# version 04
# baseline (speedup 1.0000x reference)
import numpy as np
from contextlib import ExitStack

import concourse.bass as bass
import concourse.mybir as mybir
from concourse.bass_utils import run_bass_kernel_spmd

F32 = mybir.dt.float32
BF16 = mybir.dt.bfloat16
AF = mybir.ActivationFunctionType
ALU = mybir.AluOpType
AX = mybir.AxisListType

B, S, D = 4, 4096, 1024
NT = S // 128
NC = D // 128
H, HD = 16, 64
HID = 2816
NHSL = HID // 128
NE = 8
OWN0 = NT // 2
KBLK = 1024
LN_EPS = 1e-5
RMS_EPS = 1e-6
NEGBIG = -1.0e9
CONSTS = (0.0, 1.0, -0.5, -1.0, LN_EPS, RMS_EPS, 1.0 / 32.0, 0.125, 1.0 / D, 2.0)

ENGS = ("q", "g", "t", "v", "s")


class Op:
    __slots__ = ("eng", "fn", "deps", "inc", "isdma", "sem_val", "drain", "idx")

    def __init__(self, eng, fn, deps, isdma, idx):
        self.eng, self.fn, self.deps, self.isdma = eng, fn, list(deps), isdma
        self.inc = isdma
        self.sem_val = None
        self.drain = False
        self.idx = idx


class Buf:
    __slots__ = ("ap", "w", "rs")

    def __init__(self, ap):
        self.ap, self.w, self.rs = ap, None, []


class IR:
    def __init__(self):
        self.ops = []
        self.fences = []

    def op(self, eng, fn, reads=(), writes=(), isdma=False):
        deps = []
        for b in reads:
            if b.w is not None:
                deps.append(b.w)
        for b in writes:
            if b.w is not None:
                deps.append(b.w)
            deps.extend(b.rs)
        o = Op(eng, fn, deps, isdma, len(self.ops))
        self.ops.append(o)
        for b in reads:
            b.rs.append(o)
        for b in writes:
            b.w = o
            b.rs = []
        return o

    def fence(self):
        last = {}
        for o in self.ops:
            last[(o.eng, o.isdma)] = o
        self.fences.append((len(self.ops), last))

    def lower(self, nc, ctx, block):
        fence_i = 0
        pend = None
        seen_after = set()
        for o in self.ops:
            while fence_i < len(self.fences) and self.fences[fence_i][0] <= o.idx:
                pend = self.fences[fence_i][1]
                seen_after = set()
                fence_i += 1
            if pend is not None and o.eng not in seen_after:
                for (e2, _dma), d in pend.items():
                    if e2 != o.eng or _dma:
                        o.deps.append(d)
                seen_after.add(o.eng)

        for o in self.ops:
            ded, seen = [], set()
            for d in o.deps:
                if id(d) in seen or d is o:
                    continue
                seen.add(id(d))
                ded.append(d)
                if d.isdma or d.eng != o.eng:
                    d.inc = True
            o.deps = ded

        recent = {e: [] for e in ENGS}
        for o in self.ops:
            if o.eng == "v" and not o.isdma:
                for d in o.deps:
                    if (not d.isdma) and d.eng == "v" and d in recent["v"]:
                        o.drain = True
            r = recent[o.eng]
            r.append(o)
            if len(r) > 2:
                r.pop(0)

        sems = {e: ctx.enter_context(nc.semaphore(f"sem_{e}")) for e in ENGS}
        dsems = {e: ctx.enter_context(nc.semaphore(f"dma_{e}")) for e in ("q", "g")}
        cnt = {e: 0 for e in ENGS}
        dcnt = {e: 0 for e in ("q", "g")}
        for o in self.ops:
            if o.isdma:
                dcnt[o.eng] += 16
                o.sem_val = dcnt[o.eng]
            elif o.inc:
                cnt[o.eng] += 1
                o.sem_val = cnt[o.eng]

        handles = {"q": block.sync, "g": block.gpsimd, "t": block.tensor,
                   "v": block.vector, "s": block.scalar}
        ir = self

        def emit_for(eng):
            def run(h):
                waited = {}
                for o in ir.ops:
                    if o.eng != eng:
                        continue
                    for d in o.deps:
                        if (not d.isdma) and d.eng == eng:
                            continue
                        sem = dsems[d.eng] if d.isdma else sems[d.eng]
                        key = (d.isdma, d.eng)
                        if waited.get(key, -1) >= d.sem_val:
                            continue
                        h.wait_ge(sem, d.sem_val)
                        waited[key] = d.sem_val
                    if o.drain:
                        h.drain()
                    inst = o.fn(h)
                    if o.isdma:
                        inst.then_inc(dsems[o.eng], 16)
                    elif o.inc:
                        inst.then_inc(sems[o.eng], 1)
            return run

        for e in ENGS:
            handles[e](emit_for(e))


class K:
    def __init__(self, nc, ctx, ir):
        self.nc, self.ctx, self.ir = nc, ctx, ir
        self.uid = 0

    def name(self, p):
        self.uid += 1
        return f"{p}{self.uid}"

    def sb(self, cols, dtype, scope=None):
        t = (scope or self.ctx).enter_context(
            self.nc.sbuf_tensor(self.name("sb"), [128, cols], dtype))
        return Buf(t.ap())

    def ps(self, cols, dtype, scope=None):
        t = (scope or self.ctx).enter_context(
            self.nc.psum_tensor(self.name("ps"), [128, cols], dtype))
        return Buf(t.ap())

    def dram(self, name, shape, dtype, kind="Internal"):
        return self.nc.dram_tensor(name, shape, dtype, kind=kind).ap()


def build_program(upto="full"):
    nc = bass.Bass("TRN2", target_bir_lowering=False)
    ctx = ExitStack()
    ir = IR()
    k = K(nc, ctx, ir)
    op = ir.op

    ein, eout = "ExternalInput", "ExternalOutput"
    xv_d = k.dram("xv", [S, D], F32, ein)
    w_d = {n: k.dram(n, [D, D], F32, ein) for n in
           ("wq1", "wk1", "wv1", "wq2", "wk2", "wv2", "wo2")}
    wr_d = k.dram("wr", [D, NE], F32, ein)
    wg_d = k.dram("wg", [NE, D, HID], F32, ein)
    wu_d = k.dram("wu", [NE, D, HID], F32, ein)
    wd_d = k.dram("wd", [NE, HID, D], F32, ein)
    g1bc_d = k.dram("g1bc", [128, D], F32, ein)
    g2bc_d = k.dram("g2bc", [128, D], F32, ein)
    cosr_d = k.dram("cosr", [S, D], F32, ein)
    sinr_d = k.dram("sinr", [S, D], F32, ein)
    ident_d = k.dram("ident", [128, 128], BF16, ein)
    tri_d = k.dram("tri", [128, 128], BF16, ein)
    npad_d = k.dram("npad", [128, 1], F32, ein)

    consts_d = k.dram("consts", [128, len(CONSTS)], F32, ein)
    out_d = k.dram("out", [S // 2, D], F32, eout)
    x1_d = k.dram("x1_scr", [S, D], F32, eout if upto == "A2" else "Internal")
    xn_d = k.dram("xn_scr", [NT, NC, 128, 128], BF16, "Internal")
    q2_d = k.dram("q2_scr", [NT // 2, NC, 128, 128], BF16, "Internal")
    h_d = k.dram("h_scr", [S // 2, D], F32, eout if upto.startswith("B1b") else "Internal")
    r_d = k.dram("r_scr", [S // 2, D], F32, eout if upto.startswith("B1b") else "Internal")
    wgb_d = k.dram("wgb_scr", [NE, D, HID], BF16, "Internal")
    wub_d = k.dram("wub_scr", [NE, D, HID], BF16, "Internal")
    wdb_d = k.dram("wdb_scr", [NE, HID, D], BF16, "Internal")

    ident = k.sb(128, BF16)
    tri = k.sb(128, BF16)
    npad = k.sb(1, F32)
    cbuf = k.sb(len(CONSTS), F32)
    for _i, _val in enumerate(CONSTS):
        nc.const_aps.register(F32, float(_val), cbuf.ap[:, _i:_i + 1])
    st_scope = ExitStack()
    stage = [k.sb(HID, BF16, st_scope)]

    def precast_pairs():
        for dsrc, ddst in ((wg_d, wgb_d), (wu_d, wub_d)):
            for e in range(NE):
                for c in range(NC):
                    yield (dsrc, ddst, e, c, HID)
        for e in range(NE):
            for c in range(NHSL):
                yield (wd_d, wdb_d, e, c, D)

    _pc = precast_pairs()
    _pc_n = [0]

    def pump(n):
        for _ in range(n):
            try:
                dsrc, ddst, e, c, w = next(_pc)
            except StopIteration:
                return
            st = stage[0]
            _pc_n[0] += 1
            op("g", lambda g, dsrc=dsrc, e=e, c=c, st=st, w=w: g.dma_start(
                out=st.ap[:, 0:w], in_=dsrc[e, c * 128:(c + 1) * 128, :]),
                writes=(st,), isdma=True)
            op("q", lambda q, ddst=ddst, e=e, c=c, st=st, w=w: q.dma_start(
                out=ddst[e, c * 128:(c + 1) * 128, :], in_=st.ap[:, 0:w]),
                reads=(st,), isdma=True)

    def pump_all():
        while True:
            before = _pc_n[0]
            pump(8)
            if _pc_n[0] == before:
                break

    def mk_scr(scope):
        return {n: k.sb(16, F32, scope) for n in
                ("st6", "mv", "negmean", "lnv", "rstd", "ss")}

    def ln_tile(x_in, scr, eps, out_f32=None, out_bf=None):
        for hh in range(2):
            op("v", lambda v, hh=hh: v.bn_stats(
                scr["st6"].ap[:, hh * 6:hh * 6 + 6],
                x_in.ap[:, hh * 512:(hh + 1) * 512]),
               reads=(x_in,), writes=(scr["st6"],))
        op("v", lambda v: v.bn_aggr(scr["mv"].ap[:, 0:2], scr["st6"].ap[:, 0:12]),
           reads=(scr["st6"],), writes=(scr["mv"],))
        op("v", lambda v: v.tensor_scalar(scr["negmean"].ap[:, 0:1],
                                          scr["mv"].ap[:, 0:1], -1.0, None, ALU.mult),
           reads=(scr["mv"],), writes=(scr["negmean"],))
        op("s", lambda s: s.activation(scr["lnv"].ap[:, 0:1], scr["mv"].ap[:, 1:2],
                                       AF.Ln, bias=float(eps), scale=1.0),
           reads=(scr["mv"],), writes=(scr["lnv"],))
        op("s", lambda s: s.activation(scr["rstd"].ap[:, 0:1], scr["lnv"].ap[:, 0:1],
                                       AF.Exp, bias=0.0, scale=-0.5),
           reads=(scr["lnv"],), writes=(scr["rstd"],))
        for out, dt in ((out_f32, F32), (out_bf, BF16)):
            if out is not None:
                op("v", lambda v, out=out: v.tensor_scalar(
                    out.ap[:], x_in.ap[:], scr["negmean"].ap[:, 0:1],
                    scr["rstd"].ap[:, 0:1], ALU.add, ALU.mult),
                   reads=(x_in, scr["negmean"], scr["rstd"]), writes=(out,))

    def rms_tile(z_f32, scr, gain_bc, sqdump, out_bf):
        op("s", lambda s: s.activation(sqdump.ap[:], z_f32.ap[:], AF.Square,
                                       bias=0.0, scale=1.0,
                                       accum_out=scr["ss"].ap[:, 0:1]),
           reads=(z_f32,), writes=(sqdump, scr["ss"]))
        op("s", lambda s: s.activation(scr["lnv"].ap[:, 0:1], scr["ss"].ap[:, 0:1],
                                       AF.Ln, bias=float(RMS_EPS), scale=1.0 / D),
           reads=(scr["ss"],), writes=(scr["lnv"],))
        op("s", lambda s: s.activation(scr["rstd"].ap[:, 0:1], scr["lnv"].ap[:, 0:1],
                                       AF.Exp, bias=0.0, scale=-0.5),
           reads=(scr["lnv"],), writes=(scr["rstd"],))
        op("v", lambda v: v.scalar_tensor_tensor(
            out_bf.ap[:], z_f32.ap[:], scr["rstd"].ap[:, 0:1], gain_bc.ap[:],
            ALU.mult, ALU.mult),
           reads=(z_f32, scr["rstd"], gain_bc), writes=(out_bf,))

    def load_w_cast(dst, dram, cols=D):
        for c in range(NC):
            op("g", lambda g, c=c, dst=dst, dram=dram, cols=cols: g.dma_start(
                out=dst.ap[:, c * cols:(c + 1) * cols],
                in_=dram[c * 128:(c + 1) * 128, :]),
                writes=(dst,), isdma=True)

    def transposes(src_bf, col_of, dst, dcol_of, n, tp_ps, evac_batch=4,
                   scale=None, dstbuf_extra=()):
        for base in range(0, n, evac_batch):
            m = min(evac_batch, n - base)
            p = tp_ps[(base // evac_batch) % 2]
            for j in range(m):
                c = base + j
                op("t", lambda t, c=c, j=j, p=p: t.transpose(
                    p.ap[:, j * 128:(j + 1) * 128],
                    src_bf.ap[:, col_of(c):col_of(c) + 128], ident.ap[:]),
                    reads=(src_bf, ident), writes=(p,))
            dc = dcol_of(base)
            if scale is None:
                op("s", lambda s, p=p, dc=dc, m=m: s.copy(
                    dst.ap[:, dc:dc + m * 128], p.ap[:, 0:m * 128]),
                    reads=(p,), writes=(dst,) + tuple(dstbuf_extra))
            else:
                op("s", lambda s, p=p, dc=dc, m=m: s.activation(
                    dst.ap[:, dc:dc + m * 128], p.ap[:, 0:m * 128], AF.Copy,
                    bias=0.0, scale=float(scale)),
                    reads=(p,), writes=(dst,) + tuple(dstbuf_extra))

    kv_scope = ExitStack()
    KV1 = k.sb(NC * S, BF16, kv_scope)
    KV2 = k.sb(NT * D, BF16, kv_scope)

    with nc.Block() as block:
        for dst, src in ((ident, ident_d), (tri, tri_d), (npad, npad_d),
                         (cbuf, consts_d)):
            op("q", lambda q, dst=dst, src=src: q.dma_start(out=dst.ap[:], in_=src[:]),
               writes=(dst,), isdma=True)
        ir.fence()

        a1 = ExitStack()
        w1k = k.sb(NC * D, BF16, a1)
        w1v = k.sb(NC * D, BF16, a1)
        xt = [k.sb(D, F32, a1) for _ in range(2)]
        lnb = [k.sb(D, BF16, a1) for _ in range(2)]
        lnfm4 = k.sb(NC * 512, BF16, a1)
        scrA = [mk_scr(a1) for _ in range(2)]
        tp_ps = [k.ps(1024, BF16, a1) for _ in range(2)]
        pj_ps = [k.ps(512, F32, a1) for _ in range(4)]

        load_w_cast(w1k, w_d["wk1"])
        load_w_cast(w1v, w_d["wv1"])

        for grp in range(NT // 4):
            for i in range(4):
                t = grp * 4 + i
                xb = xt[t % 2]
                op("q", lambda q, t=t, xb=xb: q.dma_start(
                    out=xb.ap[:], in_=xv_d[t * 128:(t + 1) * 128, :]),
                    writes=(xb,), isdma=True)
                lb = lnb[t % 2]
                ln_tile(xb, scrA[t % 2], LN_EPS, out_bf=lb)
                for c in range(NC):
                    p = tp_ps[c % 2]
                    op("t", lambda t_, c=c, p=p, lb=lb: t_.transpose(
                        p.ap[:, 0:128], lb.ap[:, c * 128:(c + 1) * 128], ident.ap[:]),
                        reads=(lb, ident), writes=(p,))
                    op("s", lambda s, c=c, i=i, p=p: s.copy(
                        lnfm4.ap[:, c * 512 + i * 128: c * 512 + (i + 1) * 128],
                        p.ap[:, 0:128]),
                        reads=(p,), writes=(lnfm4,))
                for half in range(2):
                    p = pj_ps[2 + half]
                    for c in range(NC):
                        op("t", lambda t_, c=c, half=half, i=i, p=p: t_.matmul(
                            p.ap[:],
                            lnfm4.ap[:, c * 512 + i * 128: c * 512 + (i + 1) * 128],
                            w1v.ap[:, c * D + half * 512: c * D + (half + 1) * 512],
                            start=(c == 0), stop=(c == NC - 1)),
                            reads=(lnfm4, w1v), writes=(p,))
                    op("s", lambda s, p=p, t=t, half=half: s.copy(
                        KV2.ap[:, t * D + half * 512: t * D + (half + 1) * 512],
                        p.ap[:]),
                        reads=(p,), writes=(KV2,))
            for do in range(NC):
                p = pj_ps[do % 2]
                for c in range(NC):
                    op("t", lambda t_, do=do, c=c, p=p: t_.matmul(
                        p.ap[:], w1k.ap[:, c * D + do * 128: c * D + (do + 1) * 128],
                        lnfm4.ap[:, c * 512:(c + 1) * 512],
                        start=(c == 0), stop=(c == NC - 1)),
                        reads=(w1k, lnfm4), writes=(p,))
                op("s", lambda s, p=p, do=do, grp=grp: s.copy(
                    KV1.ap[:, do * S + grp * 512: do * S + (grp + 1) * 512], p.ap[:]),
                    reads=(p,), writes=(KV1,))
            pump(2)

        ir.fence()
        a1.close()
        if upto == "A1":
            _finish(k, block, ctx)
            return nc, ctx

        a2 = ExitStack()
        w1q = k.sb(NC * D, BF16, a2)
        xt2 = [k.sb(D, F32, a2) for _ in range(2)]
        lnb2 = [k.sb(D, BF16, a2) for _ in range(2)]
        lnfm = [k.sb(NC * 128, BF16, a2) for _ in range(2)]
        q1t = [k.sb(NC * 128, BF16, a2) for _ in range(2)]
        acc = [[k.sb(D, F32, a2) for _ in range(2)] for _ in range(2)]
        x1t = [k.sb(D, F32, a2) for _ in range(2)]
        pbuf = [k.sb(512, BF16, a2) for _ in range(2)]
        ptb = [k.sb(512, BF16, a2) for _ in range(2)]
        fls = [[k.sb(16, F32, a2), k.sb(16, F32, a2)] for _ in range(2)]
        scrB = [mk_scr(a2) for _ in range(2)]
        sc_ps = [k.ps(512, F32, a2) for _ in range(2)]
        av_ps = [k.ps(D, F32, a2) for _ in range(2)]
        tp2_ps = [k.ps(1024, BF16, a2)]

        load_w_cast(w1q, w_d["wq1"])

        guc = [0]

        def a2_prelude(t):
            par = t % 2
            xb = xt2[par]
            op("q", lambda q, t=t, xb=xb: q.dma_start(
                out=xb.ap[:], in_=xv_d[t * 128:(t + 1) * 128, :]),
                writes=(xb,), isdma=True)
            lb = lnb2[par]
            ln_tile(xb, scrB[par], LN_EPS, out_bf=lb)
            lf = lnfm[par]
            for c in range(NC):
                tp = tp2_ps[0]
                op("t", lambda t_, c=c, tp=tp, lb=lb: t_.transpose(
                    tp.ap[:, 0:128], lb.ap[:, c * 128:(c + 1) * 128], ident.ap[:]),
                    reads=(lb, ident), writes=(tp,))
                op("s", lambda s, c=c, tp=tp, lf=lf: s.copy(
                    lf.ap[:, c * 128:(c + 1) * 128], tp.ap[:, 0:128]),
                    reads=(tp,), writes=(lf,))
            qb = q1t[par]
            av = av_ps[par]
            for do in range(NC):
                for c in range(NC):
                    op("t", lambda t_, do=do, c=c, av=av, lf=lf: t_.matmul(
                        av.ap[:, (do % 2) * 128:(do % 2) * 128 + 128],
                        w1q.ap[:, c * D + do * 128: c * D + (do + 1) * 128],
                        lf.ap[:, c * 128:(c + 1) * 128],
                        start=(c == 0), stop=(c == NC - 1)),
                        reads=(w1q, lf), writes=(av,))
                op("s", lambda s, do=do, av=av, qb=qb: s.activation(
                    qb.ap[:, do * 128:(do + 1) * 128],
                    av.ap[:, (do % 2) * 128:(do % 2) * 128 + 128],
                    AF.Copy, bias=0.0, scale=1.0 / 32.0),
                    reads=(av,), writes=(qb,))
            fcur = fls[par][0]
            op("v", lambda v, f=fcur: v.memset(f.ap[:, 0:1], -1e30), writes=(fcur,))
            op("v", lambda v, f=fcur: v.memset(f.ap[:, 9:10], 0.0), writes=(fcur,))
            op("v", lambda v, a=acc[par][0]: v.memset(a.ap[:], 0.0),
               writes=(acc[par][0],))

        def a2_units(t):
            par = t % 2
            qb = q1t[par]
            nkey = (t + 1) * 128
            nb5 = (nkey + 511) // 512
            st = {"f": fls[par][0], "fn": fls[par][1],
                  "a": acc[par][0], "an": acc[par][1]}
            units = []

            def unit(kb):
                w = min(512, nkey - kb * 512)
                isdiag = (kb == nb5 - 1)
                sp = sc_ps[guc[0] % 2]
                pb = pbuf[guc[0] % 2]
                pt = ptb[guc[0] % 2]
                guc[0] += 1
                fcur, fnew = st["f"], st["fn"]
                a_cur, a_new = st["a"], st["an"]
                fl = fls[par][0]
                for c in range(NC):
                    op("t", lambda t_, c=c, kb=kb, w=w, sp=sp, qb=qb: t_.matmul(
                        sp.ap[:, 0:w],
                        qb.ap[:, c * 128:(c + 1) * 128],
                        KV1.ap[:, c * S + kb * 512: c * S + kb * 512 + w],
                        start=(c == 0), stop=(c == NC - 1 and not isdiag)),
                        reads=(qb, KV1), writes=(sp,))
                if isdiag:
                    op("t", lambda t_, w=w, sp=sp: t_.matmul(
                        sp.ap[:, w - 128:w], ident.ap[:], tri.ap[:],
                        start=False, stop=True),
                        reads=(ident, tri), writes=(sp,))
                op("v", lambda v, sp=sp, w=w: v.reduce_max(
                    fl.ap[:, 2:3], sp.ap[:, 0:w], axis=AX.X),
                    reads=(sp,), writes=(fl,))
                op("v", lambda v, f=fcur, fn=fnew: v.tensor_tensor(
                    fn.ap[:, 0:1], f.ap[:, 0:1], fl.ap[:, 2:3], ALU.max),
                    reads=(fcur, fl), writes=(fnew,))
                op("v", lambda v, fn=fnew: v.tensor_scalar(
                    fl.ap[:, 3:4], fn.ap[:, 0:1], -1.0, None, ALU.mult),
                    reads=(fnew,), writes=(fl,))
                op("s", lambda s, f=fcur: s.activation(
                    fl.ap[:, 4:5], f.ap[:, 0:1], AF.Exp,
                    bias=fl.ap[:, 3:4], scale=1.0),
                    reads=(fcur, fl), writes=(fl,))
                op("s", lambda s, sp=sp, w=w, pb=pb: s.activation(
                    pb.ap[:, 0:w], sp.ap[:, 0:w], AF.Exp,
                    bias=fl.ap[:, 3:4], scale=1.0,
                    accum_out=fl.ap[:, 5:6]),
                    reads=(sp, fl), writes=(pb, fl))
                padw = min(2048 - kb * 512, w) if kb * 512 < 2048 else 0
                if padw > 0:
                    op("s", lambda s, fn=fnew: s.activation(
                        fl.ap[:, 7:8], fn.ap[:, 0:1], AF.Exp, bias=0.0, scale=-1.0),
                        reads=(fnew,), writes=(fl,))
                    op("v", lambda v, padw=padw: v.tensor_scalar(
                        fl.ap[:, 8:9], fl.ap[:, 7:8], npad.ap[:, 0:1],
                        -float(padw) / 1024.0, ALU.mult, ALU.mult),
                        reads=(fl, npad), writes=(fl,))
                    op("v", lambda v: v.tensor_tensor(
                        fl.ap[:, 5:6], fl.ap[:, 5:6], fl.ap[:, 8:9], ALU.add),
                        reads=(fl,), writes=(fl,))
                op("v", lambda v, f=fcur, fn=fnew: v.scalar_tensor_tensor(
                    fn.ap[:, 9:10], f.ap[:, 9:10], fl.ap[:, 4:5],
                    fl.ap[:, 5:6], ALU.mult, ALU.add),
                    reads=(fcur, fl), writes=(fnew,))
                ng = w // 128
                tp = tp2_ps[0]
                for j in range(ng):
                    op("t", lambda t_, j=j, pb=pb, tp=tp: t_.transpose(
                        tp.ap[:, j * 128:(j + 1) * 128],
                        pb.ap[:, j * 128:(j + 1) * 128], ident.ap[:]),
                        reads=(pb, ident), writes=(tp,))
                op("s", lambda s, tp=tp, ng=ng, pt=pt: s.copy(
                    pt.ap[:, 0:ng * 128], tp.ap[:, 0:ng * 128]),
                    reads=(tp,), writes=(pt,))
                av = av_ps[par]
                for half in range(2):
                    for gr in range(ng):
                        kt = kb * 4 + gr
                        op("t", lambda t_, gr=gr, kt=kt, half=half, av=av,
                           pt=pt, ng=ng: t_.matmul(
                            av.ap[:, half * 512:(half + 1) * 512],
                            pt.ap[:, gr * 128:(gr + 1) * 128],
                            KV2.ap[:, kt * D + half * 512: kt * D + (half + 1) * 512],
                            start=(gr == 0), stop=(gr == ng - 1)),
                            reads=(pt, KV2), writes=(av,))
                for half in range(2):
                    op("v", lambda v, a=a_cur, an=a_new, half=half, av=av: v.scalar_tensor_tensor(
                        an.ap[:, half * 512:(half + 1) * 512],
                        a.ap[:, half * 512:(half + 1) * 512],
                        fl.ap[:, 4:5], av.ap[:, half * 512:(half + 1) * 512],
                        ALU.mult, ALU.add),
                        reads=(a_cur, fl, av), writes=(a_new,))
                st["f"], st["fn"] = st["fn"], st["f"]
                st["a"], st["an"] = st["an"], st["a"]
            for kb in range(nb5):
                units.append(lambda kb=kb: unit(kb))
            return units, st

        def a2_epilogue(t, st):
            par = t % 2
            fl = fls[par][0]
            fcur = st["f"]
            a_cur = st["a"]
            xb = xt2[par]
            op("v", lambda v, f=fcur: v.tensor_scalar(
                fl.ap[:, 11:12], f.ap[:, 9:10], 1e-30, None, ALU.add),
                reads=(fcur,), writes=(fl,))
            op("v", lambda v: v.reciprocal(fl.ap[:, 11:12], fl.ap[:, 11:12]),
                reads=(fl,), writes=(fl,))
            xo = x1t[par]
            op("v", lambda v, a=a_cur, xb=xb, xo=xo: v.scalar_tensor_tensor(
                xo.ap[:], a.ap[:], fl.ap[:, 11:12], xb.ap[:], ALU.mult, ALU.add),
                reads=(a_cur, fl, xb), writes=(xo,))
            op("q", lambda q, t=t, xo=xo: q.dma_start(
                out=x1_d[t * 128:(t + 1) * 128, :], in_=xo.ap[:]),
                reads=(xo,), isdma=True)

        for t0 in range(0, NT, 2):
            t1 = t0 + 1
            a2_prelude(t0)
            a2_prelude(t1)
            u0, st0 = a2_units(t0)
            u1, st1 = a2_units(t1)
            n0, n1 = len(u0), len(u1)
            i0 = i1 = 0
            while i0 < n0 or i1 < n1:
                if i0 < n0 and (i1 >= n1 or i0 * n1 <= i1 * n0):
                    u0[i0](); i0 += 1
                else:
                    u1[i1](); i1 += 1
            a2_epilogue(t0, st0)
            a2_epilogue(t1, st1)
            pump(4)

        ir.fence()
        a2.close()
        if upto in ("A2", "A2dbg"):
            _finish(k, block, ctx)
            return nc, ctx

        b1k = ExitStack()
        w2k = k.sb(NC * D, BF16, b1k)
        w2q = k.sb(NC * D, BF16, b1k)
        g1bc = k.sb(D, F32, b1k)
        x1b = [k.sb(D, F32, b1k) for _ in range(2)]
        zb = [k.sb(D, F32, b1k) for _ in range(2)]
        xnb = k.sb(D, BF16, b1k)
        xnfm = k.sb(NC * 128, BF16, b1k)
        cosb = [k.sb(D, BF16, b1k)]
        sinb = [k.sb(D, BF16, b1k)]
        rt1 = k.sb(D, F32, b1k)
        rt2 = k.sb(D, F32, b1k)
        krb = k.sb(D, BF16, b1k)
        sqd = xnb
        scrC = [mk_scr(b1k) for _ in range(2)]
        kq_ps = [k.ps(D, F32, b1k) for _ in range(2)]
        tp3_ps = [k.ps(1024, BF16, b1k) for _ in range(2)]

        load_w_cast(w2k, w_d["wk2"])
        load_w_cast(w2q, w_d["wq2"])
        op("g", lambda g: g.dma_start(out=g1bc.ap[:], in_=g1bc_d[:]),
           writes=(g1bc,), isdma=True)

        def rope(src_ps, cosb_, sinb_, out_bf):
            vsrc = src_ps.ap.rearrange("p (h j) -> p h j", j=HD)
            vcos = cosb_.ap.rearrange("p (h j) -> p h j", j=HD)
            vsin = sinb_.ap.rearrange("p (h j) -> p h j", j=HD)
            vr1 = rt1.ap.rearrange("p (h j) -> p h j", j=HD)
            vr2 = rt2.ap.rearrange("p (h j) -> p h j", j=HD)
            op("v", lambda v: v.tensor_tensor(rt1.ap[:], src_ps.ap[:], cosb_.ap[:],
                                              ALU.mult),
               reads=(src_ps, cosb_), writes=(rt1,))
            op("v", lambda v: v.tensor_tensor(vr2[:, :, 0:32], vsrc[:, :, 32:64],
                                              vsin[:, :, 0:32], ALU.mult),
               reads=(src_ps, sinb_), writes=(rt2,))
            op("v", lambda v: v.tensor_tensor(vr2[:, :, 32:64], vsrc[:, :, 0:32],
                                              vsin[:, :, 32:64], ALU.mult),
               reads=(src_ps, sinb_), writes=(rt2,))
            op("v", lambda v: v.tensor_tensor(out_bf.ap[:], rt1.ap[:], rt2.ap[:],
                                              ALU.add),
               reads=(rt1, rt2), writes=(out_bf,))

        for t in range(NT):
            xb = x1b[t % 2]
            op("q", lambda q, t=t, xb=xb: q.dma_start(
                out=xb.ap[:], in_=x1_d[t * 128:(t + 1) * 128, :]),
                writes=(xb,), isdma=True)
            cb, sb_ = cosb[0], sinb[0]
            op("g", lambda g, t=t, cb=cb: g.dma_start(
                out=cb.ap[:], in_=cosr_d[t * 128:(t + 1) * 128, :]),
                writes=(cb,), isdma=True)
            op("g", lambda g, t=t, sb_=sb_: g.dma_start(
                out=sb_.ap[:], in_=sinr_d[t * 128:(t + 1) * 128, :]),
                writes=(sb_,), isdma=True)
            ln_tile(xb, scrC[t % 2], LN_EPS, out_f32=zb[t % 2])
            rms_tile(zb[t % 2], scrC[t % 2], g1bc, sqd, xnb)
            for c in range(NC):
                p = tp3_ps[c % 2]
                op("t", lambda t_, c=c, p=p: t_.transpose(
                    p.ap[:, 0:128], xnb.ap[:, c * 128:(c + 1) * 128], ident.ap[:]),
                    reads=(xnb, ident), writes=(p,))
                op("s", lambda s, c=c, p=p: s.copy(
                    xnfm.ap[:, c * 128:(c + 1) * 128], p.ap[:, 0:128]),
                    reads=(p,), writes=(xnfm,))
            op("q", lambda q, t=t: q.dma_start(
                out=xn_d[t, :, :, :].rearrange("a p j -> p a j"),
                in_=xnfm.ap[:]),
                reads=(xnfm,), isdma=True)
            kp = kq_ps[0]
            for half in range(2):
                for c in range(NC):
                    op("t", lambda t_, c=c, half=half, kp=kp: t_.matmul(
                        kp.ap[:, half * 512:(half + 1) * 512],
                        xnfm.ap[:, c * 128:(c + 1) * 128],
                        w2k.ap[:, c * D + half * 512: c * D + (half + 1) * 512],
                        start=(c == 0), stop=(c == NC - 1)),
                        reads=(xnfm, w2k), writes=(kp,))
            rope(kp, cb, sb_, krb)
            for c in range(NC):
                p = tp3_ps[c % 2]
                op("t", lambda t_, c=c, p=p: t_.transpose(
                    p.ap[:, 0:128], krb.ap[:, c * 128:(c + 1) * 128], ident.ap[:]),
                    reads=(krb, ident), writes=(p,))
                op("s", lambda s, c=c, p=p, t=t: s.copy(
                    KV1.ap[:, c * S + t * 128: c * S + (t + 1) * 128], p.ap[:, 0:128]),
                    reads=(p,), writes=(KV1,))
            if t >= OWN0:
                qp = kq_ps[1]
                for half in range(2):
                    for c in range(NC):
                        op("t", lambda t_, c=c, half=half, qp=qp: t_.matmul(
                            qp.ap[:, half * 512:(half + 1) * 512],
                            xnfm.ap[:, c * 128:(c + 1) * 128],
                            w2q.ap[:, c * D + half * 512: c * D + (half + 1) * 512],
                            start=(c == 0), stop=(c == NC - 1)),
                            reads=(xnfm, w2q), writes=(qp,))
                rope(qp, cb, sb_, krb)
                for c in range(NC):
                    p = tp3_ps[c % 2]
                    op("t", lambda t_, c=c, p=p: t_.transpose(
                        p.ap[:, 0:128], krb.ap[:, c * 128:(c + 1) * 128], ident.ap[:]),
                        reads=(krb, ident), writes=(p,))
                    op("s", lambda s, c=c, p=p: s.activation(
                        sqd.ap[:, c * 128:(c + 1) * 128], p.ap[:, 0:128],
                        AF.Copy, bias=0.0, scale=0.125),
                        reads=(p,), writes=(sqd,))
                op("q", lambda q, t=t: q.dma_start(
                    out=q2_d[t - OWN0, :, :, :].rearrange("a p j -> p a j"),
                    in_=sqd.ap[:]),
                    reads=(sqd,), isdma=True)
            pump(2)

        ir.fence()
        b1k.close()

        b1v = ExitStack()
        w2v = k.sb(NC * D, BF16, b1v)
        xnf2 = [k.sb(NC * 128, BF16, b1v) for _ in range(2)]
        v2_ps = [k.ps(D, F32, b1v) for _ in range(2)]
        load_w_cast(w2v, w_d["wv2"])
        for t in range(NT):
            xf = xnf2[t % 2]
            op("q", lambda q, t=t, xf=xf: q.dma_start(
                out=xf.ap[:], in_=xn_d[t, :, :, :].rearrange("a p j -> p a j")),
                writes=(xf,), isdma=True)
            vp = v2_ps[t % 2]
            for half in range(2):
                for c in range(NC):
                    op("t", lambda t_, c=c, half=half, vp=vp, xf=xf: t_.matmul(
                        vp.ap[:, half * 512:(half + 1) * 512],
                        xf.ap[:, c * 128:(c + 1) * 128],
                        w2v.ap[:, c * D + half * 512: c * D + (half + 1) * 512],
                        start=(c == 0), stop=(c == NC - 1)),
                        reads=(xf, w2v), writes=(vp,))
            for half in range(2):
                op("s", lambda s, vp=vp, t=t, half=half: s.copy(
                    KV2.ap[:, t * D + half * 512: t * D + (half + 1) * 512],
                    vp.ap[:, half * 512:(half + 1) * 512]),
                    reads=(vp,), writes=(KV2,))
            pump(4)

        ir.fence()
        b1v.close()
        if upto == "B1v":
            _finish(k, block, ctx)
            return nc, ctx

        b1b = ExitStack()
        wo2 = k.sb(NC * D, BF16, b1b)
        x1c = [k.sb(D, F32, b1b) for _ in range(2)]
        q2t = [k.sb(NC * 128, BF16, b1b) for _ in range(2)]
        zb2 = [k.sb(D, F32, b1b) for _ in range(2)]
        pb2 = [k.sb(KBLK, BF16, b1b) for _ in range(2)]
        ptb2 = [k.sb(KBLK, BF16, b1b) for _ in range(2)]
        at2 = [k.sb(D, BF16, b1b)] * 2
        atf = k.sb(D, BF16, b1b)
        hb = [k.sb(D, F32, b1b) for _ in range(2)]
        rb_ = [k.sb(D, F32, b1b) for _ in range(2)]
        NB5 = NT // 4
        bms = [k.sb(H * NB5, F32, b1b) for _ in range(2)]
        bss = [k.sb(H * NB5, F32, b1b) for _ in range(2)]
        mf = [k.sb(16, F32, b1b) for _ in range(2)]
        nmf = [k.sb(16, F32, b1b) for _ in range(2)]
        e0b = [k.sb(16, F32, b1b) for _ in range(2)]
        corb = [k.sb(16, F32, b1b) for _ in range(2)]
        lsum = [k.sb(16, F32, b1b) for _ in range(2)]
        linv = [k.sb(16, F32, b1b) for _ in range(2)]
        scrD = [mk_scr(b1b) for _ in range(2)]
        sp1_ps = [k.ps(512, F32, b1b)]
        sp2_ps = [k.ps(KBLK, F32, b1b) for _ in range(2)]
        ava_ps = k.ps(D, F32, b1b)
        tp4_ps = [k.ps(1024, BF16, b1b)]

        load_w_cast(wo2, w_d["wo2"])

        def b1b_scmm(sp, qb, h, blk0, w, masked, isdiag):
            c, po = h // 2, (h % 2) * 64
            nhv = (w + 511) // 512
            for half in range(nhv):
                hw = min(512, w - half * 512)
                dg = isdiag and (half == nhv - 1)
                op("t", lambda t_, c=c, po=po, blk0=blk0, half=half, hw=hw,
                   sp=sp, qb=qb, dg=dg: t_.matmul(
                    sp.ap[:, half * 512: half * 512 + hw],
                    qb.ap[po:po + 64, c * 128:(c + 1) * 128],
                    KV1.ap[po:po + 64, c * S + blk0 + half * 512:
                           c * S + blk0 + half * 512 + hw],
                    start=True, stop=not dg),
                    reads=(qb, KV1), writes=(sp,))
                if dg:
                    off = half * 512 + hw - 128
                    op("t", lambda t_, off=off, sp=sp: t_.matmul(
                        sp.ap[:, off:off + 128], ident.ap[:], tri.ap[:],
                        start=False, stop=True),
                        reads=(ident, tri), writes=(sp,))

        def mk_pass1(t, p, qb):
            nkey = (t + 1) * 128
            nk5 = (nkey + 511) // 512
            units = []

            def unit(h, kb5, w):
                sp = sp1_ps[0]
                b1b_scmm(sp, qb, h, kb5 * 512, w, False, False)
                op("v", lambda v, sp=sp, w=w, h=h, kb5=kb5: v.reduce_max(
                    bms[p].ap[:, h * NB5 + kb5: h * NB5 + kb5 + 1],
                    sp.ap[:, 0:w], axis=AX.X),
                    reads=(sp,), writes=(bms[p],))
            for kb5 in range(nk5):
                w = min(512, nkey - kb5 * 512)
                for h in range(H):
                    units.append(lambda h=h, kb5=kb5, w=w: unit(h, kb5, w))
            return units

        def emit_pass1_tail(t, p):
            vb = bms[p].ap.rearrange("p (h kb) -> p h kb", kb=NB5)
            op("v", lambda v: v.reduce_max(mf[p].ap[:], vb[:, :, 0:NB5], axis=AX.X),
               reads=(bms[p],), writes=(mf[p],))
            op("v", lambda v: v.tensor_scalar(nmf[p].ap[:], mf[p].ap[:], -1.0, None,
                                              ALU.mult),
               reads=(mf[p],), writes=(nmf[p],))
            op("s", lambda s: s.activation(e0b[p].ap[:], mf[p].ap[:], AF.Exp,
                                           bias=0.0, scale=-1.0),
               reads=(mf[p],), writes=(e0b[p],))
            op("v", lambda v: v.tensor_scalar(corb[p].ap[:], e0b[p].ap[:],
                                              npad.ap[:, 0:1], 2.0, ALU.mult, ALU.mult),
               reads=(e0b[p], npad), writes=(corb[p],))
            op("v", lambda v: v.memset(bss[p].ap[:], 0.0), writes=(bss[p],))

        def mk_pass2(t, p, qb):
            nkey = (t + 1) * 128
            nkb = (nkey + KBLK - 1) // KBLK
            units = []
            cnt = [0]

            def unit(h, kb, w):
                sp = sp2_ps[cnt[0] % 2]
                pb = pb2[cnt[0] % 2]
                pt = ptb2[cnt[0] % 2]
                tp = tp4_ps[0]
                evac_eng = "s" if cnt[0] % 2 == 0 else "v"
                cnt[0] += 1
                isdiag = (kb == nkb - 1)
                b1b_scmm(sp, qb, h, kb * KBLK, w, True, isdiag)
                nhv = (w + 511) // 512
                for half in range(nhv):
                    hw = min(512, w - half * 512)
                    kb5 = kb * 2 + half
                    op("s", lambda s, sp=sp, half=half, hw=hw, h=h, kb5=kb5,
                       pb=pb: s.activation(
                        pb.ap[:, half * 512: half * 512 + hw],
                        sp.ap[:, half * 512: half * 512 + hw], AF.Exp,
                        bias=nmf[p].ap[:, h:h + 1], scale=1.0,
                        accum_out=bss[p].ap[:, h * NB5 + kb5: h * NB5 + kb5 + 1]),
                        reads=(sp, nmf[p]), writes=(pb, bss[p]))
                ng = w // 128
                for j in range(ng):
                    op("t", lambda t_, j=j, pb=pb, tp=tp: t_.transpose(
                        tp.ap[:, j * 128:(j + 1) * 128],
                        pb.ap[:, j * 128:(j + 1) * 128], ident.ap[:]),
                        reads=(pb, ident), writes=(tp,))
                if evac_eng == "s":
                    op("s", lambda s, tp=tp, ng=ng, pt=pt: s.copy(
                        pt.ap[:, 0:ng * 128], tp.ap[:, 0:ng * 128]),
                        reads=(tp,), writes=(pt,))
                else:
                    op("v", lambda v, tp=tp, ng=ng, pt=pt: v.tensor_copy(
                        pt.ap[:, 0:ng * 128], tp.ap[:, 0:ng * 128]),
                        reads=(tp,), writes=(pt,))
                for gr in range(ng):
                    kt = kb * 8 + gr
                    op("t", lambda t_, gr=gr, kt=kt, h=h, kb=kb, ng=ng, pt=pt: t_.matmul(
                        ava_ps.ap[:, h * 64:(h + 1) * 64],
                        pt.ap[:, gr * 128:(gr + 1) * 128],
                        KV2.ap[:, kt * D + h * 64: kt * D + (h + 1) * 64],
                        start=(kb == 0 and gr == 0),
                        stop=(kb == nkb - 1 and gr == ng - 1)),
                        reads=(pt, KV2), writes=(ava_ps,))
            for h in range(H):
                for kb in range(nkb):
                    w = min(KBLK, nkey - kb * KBLK)
                    units.append(lambda h=h, kb=kb, w=w: unit(h, kb, w))
            return units

        def emit_pass2_tail(t, p, xb):
            ti = t - OWN0
            vs = bss[p].ap.rearrange("p (h kb) -> p h kb", kb=NB5)
            op("v", lambda v: v.reduce_sum(lsum[p].ap[:], vs[:, :, 0:NB5], axis=AX.X),
               reads=(bss[p],), writes=(lsum[p],))
            op("v", lambda v: v.tensor_tensor(lsum[p].ap[:], lsum[p].ap[:],
                                              corb[p].ap[:], ALU.subtract),
               reads=(corb[p],), writes=(lsum[p],))
            op("v", lambda v: v.tensor_scalar(linv[p].ap[:], lsum[p].ap[:], 1e-30,
                                              None, ALU.add),
               reads=(lsum[p],), writes=(linv[p],))
            op("v", lambda v: v.reciprocal(linv[p].ap[:], linv[p].ap[:]),
               reads=(linv[p],), writes=(linv[p],))
            for h in range(H):
                op("v", lambda v, h=h: v.tensor_scalar(
                    at2[p].ap[:, h * 64:(h + 1) * 64],
                    ava_ps.ap[:, h * 64:(h + 1) * 64],
                    linv[p].ap[:, h:h + 1], None, ALU.mult),
                    reads=(ava_ps, linv[p]), writes=(at2[p],))
            tp = tp4_ps[0]
            for cb4 in range(0, NC, 4):
                for j in range(4):
                    c = cb4 + j
                    op("t", lambda t_, c=c, j=j, tp=tp: t_.transpose(
                        tp.ap[:, j * 128:(j + 1) * 128],
                        at2[p].ap[:, c * 128:(c + 1) * 128], ident.ap[:]),
                        reads=(at2[p], ident), writes=(tp,))
                op("s", lambda s, cb4=cb4, tp=tp: s.copy(
                    atf.ap[:, cb4 * 128:(cb4 + 4) * 128], tp.ap[:, 0:512]),
                    reads=(tp,), writes=(atf,))
            for c in range(NC):
                for half in range(2):
                    op("t", lambda t_, c=c, half=half: t_.matmul(
                        ava_ps.ap[:, half * 512:(half + 1) * 512],
                        atf.ap[:, c * 128:(c + 1) * 128],
                        wo2.ap[:, c * D + half * 512: c * D + (half + 1) * 512],
                        start=(c == 0), stop=(c == NC - 1)),
                        reads=(atf, wo2), writes=(ava_ps,))
            for half in range(2):
                op("v", lambda v, half=half: v.tensor_tensor(
                    hb[p].ap[:, half * 512:(half + 1) * 512],
                    zb2[p].ap[:, half * 512:(half + 1) * 512],
                    ava_ps.ap[:, half * 512:(half + 1) * 512], ALU.add),
                    reads=(zb2[p], ava_ps), writes=(hb[p],))
            op("v", lambda v, xb=xb: v.tensor_tensor(rb_[p].ap[:], xb.ap[:],
                                                     hb[p].ap[:], ALU.add),
               reads=(xb, hb[p]), writes=(rb_[p],))
            op("q", lambda q, ti=ti: q.dma_start(
                out=h_d[ti * 128:(ti + 1) * 128, :], in_=hb[p].ap[:]),
                reads=(hb[p],), isdma=True)
            op("q", lambda q, ti=ti: q.dma_start(
                out=r_d[ti * 128:(ti + 1) * 128, :], in_=rb_[p].ap[:]),
                reads=(rb_[p],), isdma=True)

        prev = None
        for t in range(OWN0, NT):
            ti = t - OWN0
            p = ti % 2
            xb = x1c[p]
            op("q", lambda q, t=t, xb=xb: q.dma_start(
                out=xb.ap[:], in_=x1_d[t * 128:(t + 1) * 128, :]),
                writes=(xb,), isdma=True)
            qb = q2t[p]
            op("q", lambda q, ti=ti, qb=qb: q.dma_start(
                out=qb.ap[:], in_=q2_d[ti, :, :, :].rearrange("a p j -> p a j")),
                writes=(qb,), isdma=True)
            ln_tile(xb, scrD[p], LN_EPS, out_f32=zb2[p])
            op("v", lambda v, p=p: v.memset(bms[p].ap[:], -1e30), writes=(bms[p],))
            u1 = mk_pass1(t, p, qb)
            u2 = mk_pass2(prev[0], prev[1], prev[2]) if prev else []
            n1, n2 = len(u1), len(u2)
            i1 = i2 = 0
            while i1 < n1 or i2 < n2:
                if i2 < n2 and (i1 >= n1 or i2 * max(n1, 1) <= i1 * max(n2, 1)):
                    u2[i2](); i2 += 1
                else:
                    u1[i1](); i1 += 1
            emit_pass1_tail(t, p)
            if prev:
                emit_pass2_tail(prev[0], prev[1], prev[3])
            prev = (t, p, qb, xb)
            pump(6)
        for u in mk_pass2(prev[0], prev[1], prev[2]):
            u()
        emit_pass2_tail(prev[0], prev[1], prev[3])

        pump_all()
        ir.fence()
        b1b.close()
        kv_scope.close()
        st_scope.close()
        if upto in ("B1b", "B1bdbg"):
            _finish(k, block, ctx)
            return nc, ctx

        b2s = ExitStack()
        b2 = ExitStack()
        xf = k.sb(NC * 2048, BF16, b2)
        comb = k.sb(16 * NE, F32, b2)
        g2bc = k.sb(D, F32, b2s)
        hb2 = [k.sb(D, F32, b2s) for _ in range(2)]
        xfb = k.sb(D, BF16, b2s)
        sqd3 = k.sb(D, BF16, b2s)
        p8 = k.sb(NE, F32, b2s)
        r8 = {n: k.sb(8, F32, b2s) for n in
              ("m8", "nm8", "m1", "m2", "den", "dinv")}
        msk1 = k.sb(NE, F32, b2s)
        p8b = k.sb(NE, F32, b2s)
        sel8 = k.sb(NE, F32, b2s)
        scrE = mk_scr(b2s)
        rt_ps = k.ps(512, F32, b2s)
        tp5_ps = [k.ps(1024, BF16, b2s) for _ in range(2)]

        op("g", lambda g: g.dma_start(out=g2bc.ap[:], in_=g2bc_d[:]),
           writes=(g2bc,), isdma=True)
        wrb2 = k.sb(NC * NE, BF16, b2s)
        wrf = k.sb(NC * NE, F32, b2s)
        wrlo = k.sb(NC * NE, BF16, b2s)
        xff = k.sb(D, F32, b2s)
        xlo = k.sb(D, BF16, b2s)
        xlofm = k.sb(D, BF16, b2s)
        for c in range(NC):
            op("g", lambda g, c=c: g.dma_start(
                out=wrb2.ap[:, c * NE:(c + 1) * NE], in_=wr_d[c * 128:(c + 1) * 128, :]),
               writes=(wrb2,), isdma=True)
            op("q", lambda q, c=c: q.dma_start(
                out=wrf.ap[:, c * NE:(c + 1) * NE], in_=wr_d[c * 128:(c + 1) * 128, :]),
               writes=(wrf,), isdma=True)
        op("v", lambda v: v.tensor_tensor(wrlo.ap[:], wrf.ap[:], wrb2.ap[:],
                                          ALU.subtract),
           reads=(wrf, wrb2), writes=(wrlo,))

        for ti in range(16):
            hbb = hb2[ti % 2]
            op("q", lambda q, ti=ti, hbb=hbb: q.dma_start(
                out=hbb.ap[:], in_=h_d[ti * 128:(ti + 1) * 128, :]),
                writes=(hbb,), isdma=True)
            rms_tile(hbb, scrE, g2bc, sqd3, xff)
            op("v", lambda v: v.tensor_copy(xfb.ap[:], xff.ap[:]),
               reads=(xff,), writes=(xfb,))
            op("v", lambda v: v.tensor_tensor(xlo.ap[:], xff.ap[:], xfb.ap[:],
                                              ALU.subtract),
               reads=(xff, xfb), writes=(xlo,))
            for c in range(NC):
                p = tp5_ps[c % 2]
                op("t", lambda t_, c=c, p=p: t_.transpose(
                    p.ap[:, 0:128], xfb.ap[:, c * 128:(c + 1) * 128], ident.ap[:]),
                    reads=(xfb, ident), writes=(p,))
                op("s", lambda s, c=c, p=p, ti=ti: s.copy(
                    xf.ap[:, c * 2048 + ti * 128: c * 2048 + (ti + 1) * 128],
                    p.ap[:, 0:128]),
                    reads=(p,), writes=(xf,))
            for c in range(NC):
                p = tp5_ps[c % 2]
                op("t", lambda t_, c=c, p=p: t_.transpose(
                    p.ap[:, 0:128], xlo.ap[:, c * 128:(c + 1) * 128], ident.ap[:]),
                    reads=(xlo, ident), writes=(p,))
                op("s", lambda s, c=c, p=p: s.copy(
                    xlofm.ap[:, c * 128:(c + 1) * 128], p.ap[:, 0:128]),
                    reads=(p,), writes=(xlofm,))
            nmm = 0
            for c in range(NC):
                for lh, rh in (("hi", "hi"), ("hi", "lo"), ("lo", "hi")):
                    lop = (xf if lh == "hi" else xlofm)
                    lsl = (slice(c * 2048 + ti * 128, c * 2048 + (ti + 1) * 128)
                           if lh == "hi" else slice(c * 128, (c + 1) * 128))
                    rop = wrb2 if rh == "hi" else wrlo
                    op("t", lambda t_, c=c, lop=lop, lsl=lsl, rop=rop, nmm=nmm: t_.matmul(
                        rt_ps.ap[:, 0:NE], lop.ap[:, lsl],
                        rop.ap[:, c * NE:(c + 1) * NE],
                        start=(nmm == 0), stop=(nmm == 3 * NC - 1)),
                        reads=(lop, rop), writes=(rt_ps,))
                    nmm += 1
            op("v", lambda v: v.reduce_max(r8["m8"].ap[:, 0:1], rt_ps.ap[:, 0:NE], axis=AX.X),
               reads=(rt_ps,), writes=(r8["m8"],))
            op("v", lambda v: v.tensor_scalar(r8["nm8"].ap[:, 0:1], r8["m8"].ap[:, 0:1],
                                              -1.0, None, ALU.mult),
               reads=(r8["m8"],), writes=(r8["nm8"],))
            op("s", lambda s: s.activation(p8.ap[:], rt_ps.ap[:, 0:NE], AF.Exp,
                                           bias=r8["nm8"].ap[:, 0:1], scale=1.0),
               reads=(rt_ps, r8["nm8"]), writes=(p8,))
            op("v", lambda v: v.reduce_max(r8["m1"].ap[:, 0:1], p8.ap[:], axis=AX.X),
               reads=(p8,), writes=(r8["m1"],))
            op("v", lambda v: v.tensor_scalar(msk1.ap[:], p8.ap[:], r8["m1"].ap[:, 0:1],
                                              None, ALU.is_equal),
               reads=(p8, r8["m1"]), writes=(msk1,))
            op("v", lambda v: v.scalar_tensor_tensor(p8b.ap[:], msk1.ap[:], -1e30,
                                                     p8.ap[:], ALU.mult, ALU.add),
               reads=(msk1, p8), writes=(p8b,))
            op("v", lambda v: v.reduce_max(r8["m2"].ap[:, 0:1], p8b.ap[:], axis=AX.X),
               reads=(p8b,), writes=(r8["m2"],))
            op("v", lambda v: v.tensor_scalar(sel8.ap[:], p8.ap[:], r8["m2"].ap[:, 0:1],
                                              None, ALU.is_ge),
               reads=(p8, r8["m2"]), writes=(sel8,))
            op("v", lambda v: v.tensor_tensor(r8["den"].ap[:, 0:1], r8["m1"].ap[:, 0:1],
                                              r8["m2"].ap[:, 0:1], ALU.add),
               reads=(r8["m1"], r8["m2"]), writes=(r8["den"],))
            op("v", lambda v: v.reciprocal(r8["dinv"].ap[:, 0:1], r8["den"].ap[:, 0:1]),
               reads=(r8["den"],), writes=(r8["dinv"],))
            op("v", lambda v, ti=ti: v.scalar_tensor_tensor(
                comb.ap[:, ti * NE:(ti + 1) * NE], p8.ap[:], r8["dinv"].ap[:, 0:1],
                sel8.ap[:], ALU.mult, ALU.mult),
               reads=(p8, r8["dinv"], sel8), writes=(comb,))

        ir.fence()
        b2s.close()

        wdm = [k.sb(D, BF16, b2) for _ in range(NHSL)]
        wgs = [k.sb(NC * 256, BF16, b2) for _ in range(2)]
        wus = [k.sb(NC * 256, BF16, b2) for _ in range(2)]
        mall = k.sb(NHSL * 1024, BF16, b2)
        gsb = [k.sb(512, BF16, b2) for _ in range(2)]
        yac = k.sb(8 * D, F32, b2)
        rbt = [k.sb(D, F32, b2) for _ in range(2)]
        outb = [k.sb(D, F32, b2) for _ in range(2)]
        gu_ps = [k.ps(512, F32, b2) for _ in range(4)]
        dn_ps = [k.ps(512, F32, b2) for _ in range(4)]

        NHC = NHSL // 2

        for thalf in range(2):
            op("v", lambda v: v.memset(yac.ap[:], 0.0), writes=(yac,))
            for e in range(NE):
                for c in range(NHSL):
                    op("q", lambda q, e=e, c=c: q.dma_start(
                        out=wdm[c].ap[:], in_=wdb_d[e, c * 128:(c + 1) * 128, :]),
                        writes=(wdm[c],), isdma=True)
                for hc in range(NHC):
                    wgb = wgs[hc % 2]
                    wub = wus[hc % 2]
                    op("q", lambda q, e=e, hc=hc, wgb=wgb: q.dma_start(
                        out=wgb.ap[:],
                        in_=wgb_d[e, :, hc * 256:(hc + 1) * 256]
                        .rearrange("(a p) j -> p a j", p=128)),
                        writes=(wgb,), isdma=True)
                    op("q", lambda q, e=e, hc=hc, wub=wub: q.dma_start(
                        out=wub.ap[:],
                        in_=wub_d[e, :, hc * 256:(hc + 1) * 256]
                        .rearrange("(a p) j -> p a j", p=128)),
                        writes=(wub,), isdma=True)
                    for hs in range(2):
                        for c in range(NC):
                            for t5 in range(2):
                                op("t", lambda t_, c=c, hs=hs, t5=t5, wgb=wgb,
                                   thalf=thalf: t_.matmul(
                                    gu_ps[2 * t5 + 0].ap[:],
                                    wgb.ap[:, c * 256 + hs * 128: c * 256 + (hs + 1) * 128],
                                    xf.ap[:, c * 2048 + thalf * 1024 + t5 * 512:
                                          c * 2048 + thalf * 1024 + (t5 + 1) * 512],
                                    start=(c == 0), stop=(c == NC - 1)),
                                    reads=(wgb, xf), writes=(gu_ps[2 * t5 + 0],))
                        for c in range(NC):
                            for t5 in range(2):
                                op("t", lambda t_, c=c, hs=hs, t5=t5, wub=wub,
                                   thalf=thalf: t_.matmul(
                                    gu_ps[2 * t5 + 1].ap[:],
                                    wub.ap[:, c * 256 + hs * 128: c * 256 + (hs + 1) * 128],
                                    xf.ap[:, c * 2048 + thalf * 1024 + t5 * 512:
                                          c * 2048 + thalf * 1024 + (t5 + 1) * 512],
                                    start=(c == 0), stop=(c == NC - 1)),
                                    reads=(wub, xf), writes=(gu_ps[2 * t5 + 1],))
                        hsl = hc * 2 + hs
                        for t5 in range(2):
                            gs = gsb[t5]
                            op("s", lambda s, t5=t5, gs=gs: s.activation(
                                gs.ap[:], gu_ps[2 * t5 + 0].ap[:], AF.Silu,
                                bias=0.0, scale=1.0),
                                reads=(gu_ps[2 * t5 + 0],), writes=(gs,))
                            op("v", lambda v, gs=gs, hsl=hsl, t5=t5: v.tensor_tensor(
                                mall.ap[:, hsl * 1024 + t5 * 512: hsl * 1024 + (t5 + 1) * 512],
                                gs.ap[:], gu_ps[2 * t5 + 1].ap[:], ALU.mult),
                                reads=(gs, gu_ps[2 * t5 + 1]), writes=(mall,))
                for dh in range(2):
                    for hsl in range(NHSL):
                        for t128 in range(8):
                            dp = (dn_ps + gu_ps)[t128]
                            op("t", lambda t_, hsl=hsl, t128=t128, dh=dh, dp=dp: t_.matmul(
                                dp.ap[:],
                                mall.ap[:, hsl * 1024 + t128 * 128: hsl * 1024 + (t128 + 1) * 128],
                                wdm[hsl].ap[:, dh * 512:(dh + 1) * 512],
                                start=(hsl == 0), stop=(hsl == NHSL - 1)),
                                reads=(mall, wdm[hsl]), writes=(dp,))
                    for t128 in range(8):
                        dp = (dn_ps + gu_ps)[t128]
                        ci = (thalf * 8 + t128) * NE + e
                        op("v", lambda v, dp=dp, t128=t128, dh=dh, ci=ci: v.scalar_tensor_tensor(
                            yac.ap[:, t128 * D + dh * 512: t128 * D + (dh + 1) * 512],
                            dp.ap[:], comb.ap[:, ci:ci + 1],
                            yac.ap[:, t128 * D + dh * 512: t128 * D + (dh + 1) * 512],
                            ALU.mult, ALU.add),
                            reads=(dp, comb, yac), writes=(yac,))
            for t128 in range(8):
                ti = thalf * 8 + t128
                rb2 = rbt[t128 % 2]
                ob = outb[t128 % 2]
                op("q", lambda q, ti=ti, rb2=rb2: q.dma_start(
                    out=rb2.ap[:], in_=r_d[ti * 128:(ti + 1) * 128, :]),
                    writes=(rb2,), isdma=True)
                op("v", lambda v, t128=t128, rb2=rb2, ob=ob: v.tensor_tensor(
                    ob.ap[:], rb2.ap[:], yac.ap[:, t128 * D:(t128 + 1) * D], ALU.add),
                    reads=(rb2, yac), writes=(ob,))
                op("q", lambda q, ti=ti, ob=ob: q.dma_start(
                    out=out_d[ti * 128:(ti + 1) * 128, :], in_=ob.ap[:]),
                    reads=(ob,), isdma=True)

        ir.fence()
        _finish(k, block, ctx)
        return nc, ctx


def _finish(k, block, ctx):
    ir = k.ir
    ir.fence()
    ir.op("g", lambda g: g.engine_nop())
    ir.lower(k.nc, k.ctx, block)


_ROPE = {}


def _rope_tables(role):
    if role in _ROPE:
        return _ROPE[role]
    pos = np.arange(S)
    if role == 0:
        pos = np.clip(pos - 2048, 0, None)
    inv = 1.0 / (10000.0 ** (np.arange(0, HD, 2, dtype=np.float32) / HD))
    freqs = np.einsum("i,j->ij", pos.astype(np.float32), inv)
    emb = np.concatenate([freqs, freqs], axis=-1)
    cos = np.cos(emb).astype(np.float32)
    sin = np.sin(emb).astype(np.float32)
    sin_signed = np.concatenate([-sin[:, :32], sin[:, 32:]], axis=-1).astype(np.float32)
    cosr = np.ascontiguousarray(np.tile(cos, (1, H)))
    sinr = np.ascontiguousarray(np.tile(sin_signed, (1, H)))
    _ROPE[role] = (cosr, sinr)
    return _ROPE[role]


def make_in_maps(inputs):
    import ml_dtypes
    x = np.asarray(inputs["L0_attn_embeddings"], np.float32)
    wg = np.ascontiguousarray(np.pad(np.asarray(inputs["L0_ffn_wg"], np.float32),
                                     ((0, 0), (0, 0), (0, HID - 2752))))
    wu = np.ascontiguousarray(np.pad(np.asarray(inputs["L0_ffn_wu"], np.float32),
                                     ((0, 0), (0, 0), (0, HID - 2752))))
    wd = np.ascontiguousarray(np.pad(np.asarray(inputs["L0_ffn_wd"], np.float32),
                                     ((0, 0), (0, HID - 2752), (0, 0))))
    ident = np.eye(128, dtype=np.float32).astype(ml_dtypes.bfloat16)
    tri = np.triu(np.full((128, 128), NEGBIG, np.float32), k=1).astype(ml_dtypes.bfloat16)
    g1bc = np.ascontiguousarray(np.broadcast_to(
        np.asarray(inputs["L0_ffn_g1"], np.float32), (128, D)))
    g2bc = np.ascontiguousarray(np.broadcast_to(
        np.asarray(inputs["L0_ffn_g2"], np.float32), (128, D)))
    shared = dict(
        wq1=np.asarray(inputs["L0_attn_W_Q"], np.float32),
        wk1=np.asarray(inputs["L0_attn_W_K"], np.float32),
        wv1=np.asarray(inputs["L0_attn_W_V"], np.float32),
        wq2=np.asarray(inputs["L0_ffn_wq"], np.float32),
        wk2=np.asarray(inputs["L0_ffn_wk"], np.float32),
        wv2=np.asarray(inputs["L0_ffn_wv"], np.float32),
        wo2=np.asarray(inputs["L0_ffn_wo"], np.float32),
        wr=np.asarray(inputs["L0_ffn_w_router"], np.float32),
        wg=wg, wu=wu, wd=wd, ident=ident, tri=tri, g1bc=g1bc, g2bc=g2bc,
    )
    in_maps = []
    zeros = np.zeros((2048, D), np.float32)
    for c in range(8):
        b, role = c // 2, c % 2
        xv = x[b] if role == 1 else np.ascontiguousarray(
            np.concatenate([zeros, x[b][:2048]], axis=0))
        cosr, sinr = _rope_tables(role)
        m = dict(shared)
        m["consts"] = np.ascontiguousarray(
            np.broadcast_to(np.asarray(CONSTS, np.float32), (128, len(CONSTS))))
        m["xv"] = xv
        m["cosr"] = cosr
        m["sinr"] = sinr
        m["npad"] = np.full((128, 1), 1024.0 if role == 0 else 0.0, np.float32)
        in_maps.append(m)
    return in_maps


def kernel(**inputs):
    in_maps = make_in_maps(inputs)
    nc, ctx = build_program("full")
    res = run_bass_kernel_spmd(nc, in_maps, core_ids=list(range(8)))
    out = np.empty((B, S, D), np.float32)
    for c in range(8):
        b, role = c // 2, c % 2
        rows = slice(0, 2048) if role == 0 else slice(2048, 4096)
        out[b, rows] = res.results[c]["out"]
    return out


# version 05
# speedup vs baseline: 1.1110x; 1.1110x over previous
import numpy as np
from contextlib import ExitStack

import concourse.bass as bass
import concourse.mybir as mybir
from concourse.bass_utils import run_bass_kernel_spmd

F32 = mybir.dt.float32
BF16 = mybir.dt.bfloat16
AF = mybir.ActivationFunctionType
ALU = mybir.AluOpType
AX = mybir.AxisListType

B, S, D = 4, 4096, 1024
NT = S // 128
NC = D // 128
H, HD = 16, 64
HID = 2816
NHSL = HID // 128
NE = 8
OWN0 = NT // 2
KBLK = 1024
LN_EPS = 1e-5
RMS_EPS = 1e-6
NEGBIG = -1.0e9
CONSTS = (0.0, 1.0, -0.5, -1.0, LN_EPS, RMS_EPS, 1.0 / 32.0, 0.125, 1.0 / D, 2.0)

ENGS = ("q", "g", "t", "v", "s")


class Op:
    __slots__ = ("eng", "fn", "deps", "inc", "isdma", "sem_val", "drain", "idx")

    def __init__(self, eng, fn, deps, isdma, idx):
        self.eng, self.fn, self.deps, self.isdma = eng, fn, list(deps), isdma
        self.inc = isdma
        self.sem_val = None
        self.drain = False
        self.idx = idx


class Buf:
    __slots__ = ("ap", "w", "rs")

    def __init__(self, ap):
        self.ap, self.w, self.rs = ap, None, []


class IR:
    def __init__(self):
        self.ops = []
        self.fences = []

    def op(self, eng, fn, reads=(), writes=(), isdma=False):
        deps = []
        for b in reads:
            if b.w is not None:
                deps.append(b.w)
        for b in writes:
            if b.w is not None:
                deps.append(b.w)
            deps.extend(b.rs)
        o = Op(eng, fn, deps, isdma, len(self.ops))
        self.ops.append(o)
        for b in reads:
            b.rs.append(o)
        for b in writes:
            b.w = o
            b.rs = []
        return o

    def fence(self):
        last = {}
        for o in self.ops:
            last[(o.eng, o.isdma)] = o
        self.fences.append((len(self.ops), last))

    def lower(self, nc, ctx, block):
        fence_i = 0
        pend = None
        seen_after = set()
        for o in self.ops:
            while fence_i < len(self.fences) and self.fences[fence_i][0] <= o.idx:
                pend = self.fences[fence_i][1]
                seen_after = set()
                fence_i += 1
            if pend is not None and o.eng not in seen_after:
                for (e2, _dma), d in pend.items():
                    if e2 != o.eng or _dma:
                        o.deps.append(d)
                seen_after.add(o.eng)

        for o in self.ops:
            ded, seen = [], set()
            for d in o.deps:
                if id(d) in seen or d is o:
                    continue
                seen.add(id(d))
                ded.append(d)
                if d.isdma or d.eng != o.eng:
                    d.inc = True
            o.deps = ded

        recent = {e: [] for e in ENGS}
        for o in self.ops:
            if o.eng == "v" and not o.isdma:
                for d in o.deps:
                    if (not d.isdma) and d.eng == "v" and d in recent["v"]:
                        o.drain = True
            r = recent[o.eng]
            r.append(o)
            if len(r) > 2:
                r.pop(0)

        sems = {e: ctx.enter_context(nc.semaphore(f"sem_{e}")) for e in ENGS}
        dsems = {e: ctx.enter_context(nc.semaphore(f"dma_{e}")) for e in ("q", "g")}
        cnt = {e: 0 for e in ENGS}
        dcnt = {e: 0 for e in ("q", "g")}
        for o in self.ops:
            if o.isdma:
                dcnt[o.eng] += 16
                o.sem_val = dcnt[o.eng]
            elif o.inc:
                cnt[o.eng] += 1
                o.sem_val = cnt[o.eng]

        handles = {"q": block.sync, "g": block.gpsimd, "t": block.tensor,
                   "v": block.vector, "s": block.scalar}
        ir = self

        def emit_for(eng):
            def run(h):
                waited = {}
                for o in ir.ops:
                    if o.eng != eng:
                        continue
                    for d in o.deps:
                        if (not d.isdma) and d.eng == eng:
                            continue
                        sem = dsems[d.eng] if d.isdma else sems[d.eng]
                        key = (d.isdma, d.eng)
                        if waited.get(key, -1) >= d.sem_val:
                            continue
                        h.wait_ge(sem, d.sem_val)
                        waited[key] = d.sem_val
                    if o.drain:
                        h.drain()
                    inst = o.fn(h)
                    if o.isdma:
                        inst.then_inc(dsems[o.eng], 16)
                    elif o.inc:
                        inst.then_inc(sems[o.eng], 1)
            return run

        for e in ENGS:
            handles[e](emit_for(e))


class K:
    def __init__(self, nc, ctx, ir):
        self.nc, self.ctx, self.ir = nc, ctx, ir
        self.uid = 0

    def name(self, p):
        self.uid += 1
        return f"{p}{self.uid}"

    def sb(self, cols, dtype, scope=None):
        t = (scope or self.ctx).enter_context(
            self.nc.sbuf_tensor(self.name("sb"), [128, cols], dtype))
        return Buf(t.ap())

    def ps(self, cols, dtype, scope=None):
        t = (scope or self.ctx).enter_context(
            self.nc.psum_tensor(self.name("ps"), [128, cols], dtype))
        return Buf(t.ap())

    def dram(self, name, shape, dtype, kind="Internal"):
        return self.nc.dram_tensor(name, shape, dtype, kind=kind).ap()


def build_program(upto="full"):
    nc = bass.Bass("TRN2", target_bir_lowering=False)
    ctx = ExitStack()
    ir = IR()
    k = K(nc, ctx, ir)
    op = ir.op

    ein, eout = "ExternalInput", "ExternalOutput"
    xv_d = k.dram("xv", [S, D], F32, ein)
    w_d = {n: k.dram(n, [D, D], F32, ein) for n in
           ("wq1", "wk1", "wv1", "wq2", "wk2", "wv2", "wo2")}
    wr_d = k.dram("wr", [D, NE], F32, ein)
    wg_d = k.dram("wg", [NE, D, HID], F32, ein)
    wu_d = k.dram("wu", [NE, D, HID], F32, ein)
    wd_d = k.dram("wd", [NE, HID, D], F32, ein)
    g1bc_d = k.dram("g1bc", [128, D], F32, ein)
    g2bc_d = k.dram("g2bc", [128, D], F32, ein)
    cosr_d = k.dram("cosr", [S, D], F32, ein)
    sinr_d = k.dram("sinr", [S, D], F32, ein)
    ident_d = k.dram("ident", [128, 128], BF16, ein)
    tri_d = k.dram("tri", [128, 128], BF16, ein)
    npad_d = k.dram("npad", [128, 1], F32, ein)

    consts_d = k.dram("consts", [128, len(CONSTS)], F32, ein)
    out_d = k.dram("out", [S // 2, D], F32, eout)
    x1_d = k.dram("x1_scr", [S, D], F32, eout if upto == "A2" else "Internal")
    xn_d = k.dram("xn_scr", [NT, NC, 128, 128], BF16, "Internal")
    q2_d = k.dram("q2_scr", [NT // 2, NC, 128, 128], BF16, "Internal")
    h_d = k.dram("h_scr", [S // 2, D], F32, eout if upto.startswith("B1b") else "Internal")
    r_d = k.dram("r_scr", [S // 2, D], F32, eout if upto.startswith("B1b") else "Internal")
    wgb_d = k.dram("wgb_scr", [NE, D, HID], BF16, "Internal")
    wub_d = k.dram("wub_scr", [NE, D, HID], BF16, "Internal")
    wdb_d = k.dram("wdb_scr", [NE, HID, D], BF16, "Internal")

    ident = k.sb(128, BF16)
    tri = k.sb(128, BF16)
    npad = k.sb(1, F32)
    cbuf = k.sb(len(CONSTS), F32)
    for _i, _val in enumerate(CONSTS):
        nc.const_aps.register(F32, float(_val), cbuf.ap[:, _i:_i + 1])
    st_scope = ExitStack()
    stage = [k.sb(HID, BF16, st_scope)]

    def precast_pairs():
        for dsrc, ddst in ((wg_d, wgb_d), (wu_d, wub_d)):
            for e in range(NE):
                for c in range(NC):
                    yield (dsrc, ddst, e, c, HID)
        for e in range(NE):
            for c in range(NHSL):
                yield (wd_d, wdb_d, e, c, D)

    _pc = precast_pairs()
    _pc_n = [0]

    def pump(n):
        for _ in range(n):
            try:
                dsrc, ddst, e, c, w = next(_pc)
            except StopIteration:
                return
            st = stage[0]
            _pc_n[0] += 1
            op("g", lambda g, dsrc=dsrc, e=e, c=c, st=st, w=w: g.dma_start(
                out=st.ap[:, 0:w], in_=dsrc[e, c * 128:(c + 1) * 128, :]),
                writes=(st,), isdma=True)
            op("q", lambda q, ddst=ddst, e=e, c=c, st=st, w=w: q.dma_start(
                out=ddst[e, c * 128:(c + 1) * 128, :], in_=st.ap[:, 0:w]),
                reads=(st,), isdma=True)

    def pump_all():
        while True:
            before = _pc_n[0]
            pump(8)
            if _pc_n[0] == before:
                break

    def mk_scr(scope):
        return {n: k.sb(16, F32, scope) for n in
                ("st6", "mv", "negmean", "lnv", "rstd", "ss")}

    def ln_tile(x_in, scr, eps, out_f32=None, out_bf=None):
        for hh in range(2):
            op("v", lambda v, hh=hh: v.bn_stats(
                scr["st6"].ap[:, hh * 6:hh * 6 + 6],
                x_in.ap[:, hh * 512:(hh + 1) * 512]),
               reads=(x_in,), writes=(scr["st6"],))
        op("v", lambda v: v.bn_aggr(scr["mv"].ap[:, 0:2], scr["st6"].ap[:, 0:12]),
           reads=(scr["st6"],), writes=(scr["mv"],))
        op("v", lambda v: v.tensor_scalar(scr["negmean"].ap[:, 0:1],
                                          scr["mv"].ap[:, 0:1], -1.0, None, ALU.mult),
           reads=(scr["mv"],), writes=(scr["negmean"],))
        op("s", lambda s: s.activation(scr["lnv"].ap[:, 0:1], scr["mv"].ap[:, 1:2],
                                       AF.Ln, bias=float(eps), scale=1.0),
           reads=(scr["mv"],), writes=(scr["lnv"],))
        op("s", lambda s: s.activation(scr["rstd"].ap[:, 0:1], scr["lnv"].ap[:, 0:1],
                                       AF.Exp, bias=0.0, scale=-0.5),
           reads=(scr["lnv"],), writes=(scr["rstd"],))
        for out, dt in ((out_f32, F32), (out_bf, BF16)):
            if out is not None:
                op("v", lambda v, out=out: v.tensor_scalar(
                    out.ap[:], x_in.ap[:], scr["negmean"].ap[:, 0:1],
                    scr["rstd"].ap[:, 0:1], ALU.add, ALU.mult),
                   reads=(x_in, scr["negmean"], scr["rstd"]), writes=(out,))

    def rms_tile(z_f32, scr, gain_bc, sqdump, out_bf):
        op("s", lambda s: s.activation(sqdump.ap[:], z_f32.ap[:], AF.Square,
                                       bias=0.0, scale=1.0,
                                       accum_out=scr["ss"].ap[:, 0:1]),
           reads=(z_f32,), writes=(sqdump, scr["ss"]))
        op("s", lambda s: s.activation(scr["lnv"].ap[:, 0:1], scr["ss"].ap[:, 0:1],
                                       AF.Ln, bias=float(RMS_EPS), scale=1.0 / D),
           reads=(scr["ss"],), writes=(scr["lnv"],))
        op("s", lambda s: s.activation(scr["rstd"].ap[:, 0:1], scr["lnv"].ap[:, 0:1],
                                       AF.Exp, bias=0.0, scale=-0.5),
           reads=(scr["lnv"],), writes=(scr["rstd"],))
        op("v", lambda v: v.scalar_tensor_tensor(
            out_bf.ap[:], z_f32.ap[:], scr["rstd"].ap[:, 0:1], gain_bc.ap[:],
            ALU.mult, ALU.mult),
           reads=(z_f32, scr["rstd"], gain_bc), writes=(out_bf,))

    def load_w_cast(dst, dram, cols=D):
        for c in range(NC):
            op("g", lambda g, c=c, dst=dst, dram=dram, cols=cols: g.dma_start(
                out=dst.ap[:, c * cols:(c + 1) * cols],
                in_=dram[c * 128:(c + 1) * 128, :]),
                writes=(dst,), isdma=True)

    def transposes(src_bf, col_of, dst, dcol_of, n, tp_ps, evac_batch=4,
                   scale=None, dstbuf_extra=()):
        for base in range(0, n, evac_batch):
            m = min(evac_batch, n - base)
            p = tp_ps[(base // evac_batch) % 2]
            for j in range(m):
                c = base + j
                op("t", lambda t, c=c, j=j, p=p: t.transpose(
                    p.ap[:, j * 128:(j + 1) * 128],
                    src_bf.ap[:, col_of(c):col_of(c) + 128], ident.ap[:]),
                    reads=(src_bf, ident), writes=(p,))
            dc = dcol_of(base)
            if scale is None:
                op("s", lambda s, p=p, dc=dc, m=m: s.copy(
                    dst.ap[:, dc:dc + m * 128], p.ap[:, 0:m * 128]),
                    reads=(p,), writes=(dst,) + tuple(dstbuf_extra))
            else:
                op("s", lambda s, p=p, dc=dc, m=m: s.activation(
                    dst.ap[:, dc:dc + m * 128], p.ap[:, 0:m * 128], AF.Copy,
                    bias=0.0, scale=float(scale)),
                    reads=(p,), writes=(dst,) + tuple(dstbuf_extra))

    kv_scope = ExitStack()
    KV1 = k.sb(NC * S, BF16, kv_scope)
    KV2 = k.sb(NT * D, BF16, kv_scope)

    with nc.Block() as block:
        for dst, src in ((ident, ident_d), (tri, tri_d), (npad, npad_d),
                         (cbuf, consts_d)):
            op("q", lambda q, dst=dst, src=src: q.dma_start(out=dst.ap[:], in_=src[:]),
               writes=(dst,), isdma=True)
        ir.fence()

        a1 = ExitStack()
        w1k = k.sb(NC * D, BF16, a1)
        w1v = k.sb(NC * D, BF16, a1)
        xt = [k.sb(D, F32, a1) for _ in range(2)]
        lnb = [k.sb(D, BF16, a1) for _ in range(2)]
        lnfm4 = k.sb(NC * 512, BF16, a1)
        scrA = [mk_scr(a1) for _ in range(2)]
        tp_ps = [k.ps(1024, BF16, a1) for _ in range(2)]
        pj_ps = [k.ps(512, F32, a1) for _ in range(4)]

        load_w_cast(w1k, w_d["wk1"])
        load_w_cast(w1v, w_d["wv1"])

        for grp in range(NT // 4):
            for i in range(4):
                t = grp * 4 + i
                xb = xt[t % 2]
                op("q", lambda q, t=t, xb=xb: q.dma_start(
                    out=xb.ap[:], in_=xv_d[t * 128:(t + 1) * 128, :]),
                    writes=(xb,), isdma=True)
                lb = lnb[t % 2]
                ln_tile(xb, scrA[t % 2], LN_EPS, out_bf=lb)
                for c in range(NC):
                    p = tp_ps[c % 2]
                    op("t", lambda t_, c=c, p=p, lb=lb: t_.transpose(
                        p.ap[:, 0:128], lb.ap[:, c * 128:(c + 1) * 128], ident.ap[:]),
                        reads=(lb, ident), writes=(p,))
                    op("s", lambda s, c=c, i=i, p=p: s.copy(
                        lnfm4.ap[:, c * 512 + i * 128: c * 512 + (i + 1) * 128],
                        p.ap[:, 0:128]),
                        reads=(p,), writes=(lnfm4,))
                for half in range(2):
                    p = pj_ps[2 + half]
                    for c in range(NC):
                        op("t", lambda t_, c=c, half=half, i=i, p=p: t_.matmul(
                            p.ap[:],
                            lnfm4.ap[:, c * 512 + i * 128: c * 512 + (i + 1) * 128],
                            w1v.ap[:, c * D + half * 512: c * D + (half + 1) * 512],
                            start=(c == 0), stop=(c == NC - 1)),
                            reads=(lnfm4, w1v), writes=(p,))
                    op("s", lambda s, p=p, t=t, half=half: s.copy(
                        KV2.ap[:, t * D + half * 512: t * D + (half + 1) * 512],
                        p.ap[:]),
                        reads=(p,), writes=(KV2,))
            for do in range(NC):
                p = pj_ps[do % 2]
                for c in range(NC):
                    op("t", lambda t_, do=do, c=c, p=p: t_.matmul(
                        p.ap[:], w1k.ap[:, c * D + do * 128: c * D + (do + 1) * 128],
                        lnfm4.ap[:, c * 512:(c + 1) * 512],
                        start=(c == 0), stop=(c == NC - 1)),
                        reads=(w1k, lnfm4), writes=(p,))
                op("s", lambda s, p=p, do=do, grp=grp: s.copy(
                    KV1.ap[:, do * S + grp * 512: do * S + (grp + 1) * 512], p.ap[:]),
                    reads=(p,), writes=(KV1,))
            pump(2)

        ir.fence()
        a1.close()
        if upto == "A1":
            _finish(k, block, ctx)
            return nc, ctx

        a2 = ExitStack()
        w1q = k.sb(NC * D, BF16, a2)
        xt2 = [k.sb(D, F32, a2) for _ in range(2)]
        lnb2 = [k.sb(D, BF16, a2) for _ in range(2)]
        lnfm = [k.sb(NC * 128, BF16, a2) for _ in range(2)]
        q1t = [k.sb(NC * 128, BF16, a2) for _ in range(2)]
        acc = [[k.sb(D, F32, a2) for _ in range(2)] for _ in range(2)]
        x1t = [k.sb(D, F32, a2) for _ in range(2)]
        pbuf = [k.sb(512, BF16, a2) for _ in range(2)]
        ptb = [k.sb(512, BF16, a2) for _ in range(2)]
        fls = [[k.sb(16, F32, a2), k.sb(16, F32, a2)] for _ in range(2)]
        scrB = [mk_scr(a2) for _ in range(2)]
        sc_ps = [k.ps(512, F32, a2) for _ in range(3)]
        av_ps = [k.ps(D, F32, a2) for _ in range(2)]
        tp2_ps = [k.ps(1024, BF16, a2)]

        load_w_cast(w1q, w_d["wq1"])

        guc = [0, 0]

        def a2_prelude(t):
            par = t % 2
            xb = xt2[par]
            op("q", lambda q, t=t, xb=xb: q.dma_start(
                out=xb.ap[:], in_=xv_d[t * 128:(t + 1) * 128, :]),
                writes=(xb,), isdma=True)
            lb = lnb2[par]
            ln_tile(xb, scrB[par], LN_EPS, out_bf=lb)
            lf = lnfm[par]
            for c in range(NC):
                tp = tp2_ps[0]
                op("t", lambda t_, c=c, tp=tp, lb=lb: t_.transpose(
                    tp.ap[:, 0:128], lb.ap[:, c * 128:(c + 1) * 128], ident.ap[:]),
                    reads=(lb, ident), writes=(tp,))
                op("s", lambda s, c=c, tp=tp, lf=lf: s.copy(
                    lf.ap[:, c * 128:(c + 1) * 128], tp.ap[:, 0:128]),
                    reads=(tp,), writes=(lf,))
            qb = q1t[par]
            av = av_ps[par]
            for do in range(NC):
                for c in range(NC):
                    op("t", lambda t_, do=do, c=c, av=av, lf=lf: t_.matmul(
                        av.ap[:, (do % 2) * 128:(do % 2) * 128 + 128],
                        w1q.ap[:, c * D + do * 128: c * D + (do + 1) * 128],
                        lf.ap[:, c * 128:(c + 1) * 128],
                        start=(c == 0), stop=(c == NC - 1)),
                        reads=(w1q, lf), writes=(av,))
                op("s", lambda s, do=do, av=av, qb=qb: s.activation(
                    qb.ap[:, do * 128:(do + 1) * 128],
                    av.ap[:, (do % 2) * 128:(do % 2) * 128 + 128],
                    AF.Copy, bias=0.0, scale=1.0 / 32.0),
                    reads=(av,), writes=(qb,))
            fcur = fls[par][0]
            op("v", lambda v, f=fcur: v.memset(f.ap[:, 0:1], -1e30), writes=(fcur,))
            op("v", lambda v, f=fcur: v.memset(f.ap[:, 9:10], 0.0), writes=(fcur,))
            op("v", lambda v, a=acc[par][0]: v.memset(a.ap[:], 0.0),
               writes=(acc[par][0],))

        def a2_units(t):
            par = t % 2
            qb = q1t[par]
            nkey = (t + 1) * 128
            nb5 = (nkey + 511) // 512
            st = {"f": fls[par][0], "fn": fls[par][1],
                  "a": acc[par][0], "an": acc[par][1]}
            ua, ub = [], []
            spof = {}

            def unitA(kb):
                w = min(512, nkey - kb * 512)
                isdiag = (kb == nb5 - 1)
                sp = sc_ps[guc[0] % 3]
                guc[0] += 1
                spof[kb] = sp
                fl = fls[par][0]
                for c in range(NC):
                    op("t", lambda t_, c=c, kb=kb, w=w, sp=sp, qb=qb: t_.matmul(
                        sp.ap[:, 0:w],
                        qb.ap[:, c * 128:(c + 1) * 128],
                        KV1.ap[:, c * S + kb * 512: c * S + kb * 512 + w],
                        start=(c == 0), stop=(c == NC - 1 and not isdiag)),
                        reads=(qb, KV1), writes=(sp,))
                if isdiag:
                    op("t", lambda t_, w=w, sp=sp: t_.matmul(
                        sp.ap[:, w - 128:w], ident.ap[:], tri.ap[:],
                        start=False, stop=True),
                        reads=(ident, tri), writes=(sp,))
                op("v", lambda v, sp=sp, w=w, kb=kb: v.reduce_max(
                    fl.ap[:, 2 + (kb % 2):3 + (kb % 2)], sp.ap[:, 0:w], axis=AX.X),
                    reads=(sp,), writes=(fl,))

            def unitB(kb):
                w = min(512, nkey - kb * 512)
                sp = spof.pop(kb)
                pb = pbuf[guc[1] % 2]
                pt = ptb[guc[1] % 2]
                guc[1] += 1
                fcur, fnew = st["f"], st["fn"]
                a_cur, a_new = st["a"], st["an"]
                fl = fls[par][0]
                op("v", lambda v, f=fcur, fn=fnew, kb=kb: v.tensor_tensor(
                    fn.ap[:, 0:1], f.ap[:, 0:1], fl.ap[:, 2 + (kb % 2):3 + (kb % 2)],
                    ALU.max),
                    reads=(fcur, fl), writes=(fnew,))
                op("v", lambda v, fn=fnew: v.tensor_scalar(
                    fl.ap[:, 12:13], fn.ap[:, 0:1], -1.0, None, ALU.mult),
                    reads=(fnew,), writes=(fl,))
                op("s", lambda s, f=fcur: s.activation(
                    fl.ap[:, 4:5], f.ap[:, 0:1], AF.Exp,
                    bias=fl.ap[:, 12:13], scale=1.0),
                    reads=(fcur, fl), writes=(fl,))
                op("s", lambda s, sp=sp, w=w, pb=pb: s.activation(
                    pb.ap[:, 0:w], sp.ap[:, 0:w], AF.Exp,
                    bias=fl.ap[:, 12:13], scale=1.0,
                    accum_out=fl.ap[:, 5:6]),
                    reads=(sp, fl), writes=(pb, fl))
                padw = min(2048 - kb * 512, w) if kb * 512 < 2048 else 0
                if padw > 0:
                    op("s", lambda s, fn=fnew: s.activation(
                        fl.ap[:, 7:8], fn.ap[:, 0:1], AF.Exp, bias=0.0, scale=-1.0),
                        reads=(fnew,), writes=(fl,))
                    op("v", lambda v, padw=padw: v.tensor_scalar(
                        fl.ap[:, 8:9], fl.ap[:, 7:8], npad.ap[:, 0:1],
                        -float(padw) / 1024.0, ALU.mult, ALU.mult),
                        reads=(fl, npad), writes=(fl,))
                    op("v", lambda v: v.tensor_tensor(
                        fl.ap[:, 5:6], fl.ap[:, 5:6], fl.ap[:, 8:9], ALU.add),
                        reads=(fl,), writes=(fl,))
                op("v", lambda v, f=fcur, fn=fnew: v.scalar_tensor_tensor(
                    fn.ap[:, 9:10], f.ap[:, 9:10], fl.ap[:, 4:5],
                    fl.ap[:, 5:6], ALU.mult, ALU.add),
                    reads=(fcur, fl), writes=(fnew,))
                ng = w // 128
                tp = tp2_ps[0]
                for j in range(ng):
                    op("t", lambda t_, j=j, pb=pb, tp=tp: t_.transpose(
                        tp.ap[:, j * 128:(j + 1) * 128],
                        pb.ap[:, j * 128:(j + 1) * 128], ident.ap[:]),
                        reads=(pb, ident), writes=(tp,))
                op("s", lambda s, tp=tp, ng=ng, pt=pt: s.copy(
                    pt.ap[:, 0:ng * 128], tp.ap[:, 0:ng * 128]),
                    reads=(tp,), writes=(pt,))
                av = av_ps[par]
                for half in range(2):
                    for gr in range(ng):
                        kt = kb * 4 + gr
                        op("t", lambda t_, gr=gr, kt=kt, half=half, av=av,
                           pt=pt, ng=ng: t_.matmul(
                            av.ap[:, half * 512:(half + 1) * 512],
                            pt.ap[:, gr * 128:(gr + 1) * 128],
                            KV2.ap[:, kt * D + half * 512: kt * D + (half + 1) * 512],
                            start=(gr == 0), stop=(gr == ng - 1)),
                            reads=(pt, KV2), writes=(av,))
                for half in range(2):
                    op("v", lambda v, a=a_cur, an=a_new, half=half, av=av: v.scalar_tensor_tensor(
                        an.ap[:, half * 512:(half + 1) * 512],
                        a.ap[:, half * 512:(half + 1) * 512],
                        fl.ap[:, 4:5], av.ap[:, half * 512:(half + 1) * 512],
                        ALU.mult, ALU.add),
                        reads=(a_cur, fl, av), writes=(a_new,))
                st["f"], st["fn"] = st["fn"], st["f"]
                st["a"], st["an"] = st["an"], st["a"]
            for kb in range(nb5):
                ua.append(lambda kb=kb: unitA(kb))
                ub.append(lambda kb=kb: unitB(kb))
            return ua, ub, st

        def a2_epilogue(t, st):
            par = t % 2
            fl = fls[par][0]
            fcur = st["f"]
            a_cur = st["a"]
            xb = xt2[par]
            op("v", lambda v, f=fcur: v.tensor_scalar(
                fl.ap[:, 11:12], f.ap[:, 9:10], 1e-30, None, ALU.add),
                reads=(fcur,), writes=(fl,))
            op("v", lambda v: v.reciprocal(fl.ap[:, 11:12], fl.ap[:, 11:12]),
                reads=(fl,), writes=(fl,))
            xo = x1t[par]
            op("v", lambda v, a=a_cur, xb=xb, xo=xo: v.scalar_tensor_tensor(
                xo.ap[:], a.ap[:], fl.ap[:, 11:12], xb.ap[:], ALU.mult, ALU.add),
                reads=(a_cur, fl, xb), writes=(xo,))
            op("q", lambda q, t=t, xo=xo: q.dma_start(
                out=x1_d[t * 128:(t + 1) * 128, :], in_=xo.ap[:]),
                reads=(xo,), isdma=True)

        for t0 in range(0, NT, 2):
            t1 = t0 + 1
            a2_prelude(t0)
            a2_prelude(t1)
            a0, b0, st0 = a2_units(t0)
            a1, b1, st1 = a2_units(t1)
            n0, n1 = len(a0), len(a1)
            for kb in range(max(n0, n1) + 1):
                if kb < n0: a0[kb]()
                if 0 <= kb - 1 < n0: b0[kb - 1]()
                if kb < n1: a1[kb]()
                if 0 <= kb - 1 < n1: b1[kb - 1]()
            a2_epilogue(t0, st0)
            a2_epilogue(t1, st1)
            pump(4)

        ir.fence()
        a2.close()
        if upto in ("A2", "A2dbg"):
            _finish(k, block, ctx)
            return nc, ctx

        b1k = ExitStack()
        w2k = k.sb(NC * D, BF16, b1k)
        w2q = k.sb(NC * D, BF16, b1k)
        g1bc = k.sb(D, F32, b1k)
        x1b = [k.sb(D, F32, b1k) for _ in range(2)]
        zb = [k.sb(D, F32, b1k) for _ in range(2)]
        xnb = k.sb(D, BF16, b1k)
        xnfm = k.sb(NC * 128, BF16, b1k)
        cosb = [k.sb(D, BF16, b1k)]
        sinb = [k.sb(D, BF16, b1k)]
        rt1 = k.sb(D, F32, b1k)
        rt2 = k.sb(D, F32, b1k)
        krb = k.sb(D, BF16, b1k)
        sqd = xnb
        scrC = [mk_scr(b1k) for _ in range(2)]
        kq_ps = [k.ps(D, F32, b1k) for _ in range(2)]
        tp3_ps = [k.ps(1024, BF16, b1k) for _ in range(2)]

        load_w_cast(w2k, w_d["wk2"])
        load_w_cast(w2q, w_d["wq2"])
        op("g", lambda g: g.dma_start(out=g1bc.ap[:], in_=g1bc_d[:]),
           writes=(g1bc,), isdma=True)

        def rope(src_ps, cosb_, sinb_, out_bf):
            vsrc = src_ps.ap.rearrange("p (h j) -> p h j", j=HD)
            vcos = cosb_.ap.rearrange("p (h j) -> p h j", j=HD)
            vsin = sinb_.ap.rearrange("p (h j) -> p h j", j=HD)
            vr1 = rt1.ap.rearrange("p (h j) -> p h j", j=HD)
            vr2 = rt2.ap.rearrange("p (h j) -> p h j", j=HD)
            op("v", lambda v: v.tensor_tensor(rt1.ap[:], src_ps.ap[:], cosb_.ap[:],
                                              ALU.mult),
               reads=(src_ps, cosb_), writes=(rt1,))
            op("v", lambda v: v.tensor_tensor(vr2[:, :, 0:32], vsrc[:, :, 32:64],
                                              vsin[:, :, 0:32], ALU.mult),
               reads=(src_ps, sinb_), writes=(rt2,))
            op("v", lambda v: v.tensor_tensor(vr2[:, :, 32:64], vsrc[:, :, 0:32],
                                              vsin[:, :, 32:64], ALU.mult),
               reads=(src_ps, sinb_), writes=(rt2,))
            op("v", lambda v: v.tensor_tensor(out_bf.ap[:], rt1.ap[:], rt2.ap[:],
                                              ALU.add),
               reads=(rt1, rt2), writes=(out_bf,))

        for t in range(NT):
            xb = x1b[t % 2]
            op("q", lambda q, t=t, xb=xb: q.dma_start(
                out=xb.ap[:], in_=x1_d[t * 128:(t + 1) * 128, :]),
                writes=(xb,), isdma=True)
            cb, sb_ = cosb[0], sinb[0]
            op("g", lambda g, t=t, cb=cb: g.dma_start(
                out=cb.ap[:], in_=cosr_d[t * 128:(t + 1) * 128, :]),
                writes=(cb,), isdma=True)
            op("g", lambda g, t=t, sb_=sb_: g.dma_start(
                out=sb_.ap[:], in_=sinr_d[t * 128:(t + 1) * 128, :]),
                writes=(sb_,), isdma=True)
            ln_tile(xb, scrC[t % 2], LN_EPS, out_f32=zb[t % 2])
            rms_tile(zb[t % 2], scrC[t % 2], g1bc, sqd, xnb)
            for c in range(NC):
                p = tp3_ps[c % 2]
                op("t", lambda t_, c=c, p=p: t_.transpose(
                    p.ap[:, 0:128], xnb.ap[:, c * 128:(c + 1) * 128], ident.ap[:]),
                    reads=(xnb, ident), writes=(p,))
                op("s", lambda s, c=c, p=p: s.copy(
                    xnfm.ap[:, c * 128:(c + 1) * 128], p.ap[:, 0:128]),
                    reads=(p,), writes=(xnfm,))
            op("q", lambda q, t=t: q.dma_start(
                out=xn_d[t, :, :, :].rearrange("a p j -> p a j"),
                in_=xnfm.ap[:]),
                reads=(xnfm,), isdma=True)
            kp = kq_ps[0]
            for half in range(2):
                for c in range(NC):
                    op("t", lambda t_, c=c, half=half, kp=kp: t_.matmul(
                        kp.ap[:, half * 512:(half + 1) * 512],
                        xnfm.ap[:, c * 128:(c + 1) * 128],
                        w2k.ap[:, c * D + half * 512: c * D + (half + 1) * 512],
                        start=(c == 0), stop=(c == NC - 1)),
                        reads=(xnfm, w2k), writes=(kp,))
            rope(kp, cb, sb_, krb)
            for c in range(NC):
                p = tp3_ps[c % 2]
                op("t", lambda t_, c=c, p=p: t_.transpose(
                    p.ap[:, 0:128], krb.ap[:, c * 128:(c + 1) * 128], ident.ap[:]),
                    reads=(krb, ident), writes=(p,))
                op("s", lambda s, c=c, p=p, t=t: s.copy(
                    KV1.ap[:, c * S + t * 128: c * S + (t + 1) * 128], p.ap[:, 0:128]),
                    reads=(p,), writes=(KV1,))
            if t >= OWN0:
                qp = kq_ps[1]
                for half in range(2):
                    for c in range(NC):
                        op("t", lambda t_, c=c, half=half, qp=qp: t_.matmul(
                            qp.ap[:, half * 512:(half + 1) * 512],
                            xnfm.ap[:, c * 128:(c + 1) * 128],
                            w2q.ap[:, c * D + half * 512: c * D + (half + 1) * 512],
                            start=(c == 0), stop=(c == NC - 1)),
                            reads=(xnfm, w2q), writes=(qp,))
                rope(qp, cb, sb_, krb)
                for c in range(NC):
                    p = tp3_ps[c % 2]
                    op("t", lambda t_, c=c, p=p: t_.transpose(
                        p.ap[:, 0:128], krb.ap[:, c * 128:(c + 1) * 128], ident.ap[:]),
                        reads=(krb, ident), writes=(p,))
                    op("s", lambda s, c=c, p=p: s.activation(
                        sqd.ap[:, c * 128:(c + 1) * 128], p.ap[:, 0:128],
                        AF.Copy, bias=0.0, scale=0.125),
                        reads=(p,), writes=(sqd,))
                op("q", lambda q, t=t: q.dma_start(
                    out=q2_d[t - OWN0, :, :, :].rearrange("a p j -> p a j"),
                    in_=sqd.ap[:]),
                    reads=(sqd,), isdma=True)
            pump(2)

        ir.fence()
        b1k.close()

        b1v = ExitStack()
        w2v = k.sb(NC * D, BF16, b1v)
        xnf2 = [k.sb(NC * 128, BF16, b1v) for _ in range(2)]
        v2_ps = [k.ps(D, F32, b1v) for _ in range(2)]
        load_w_cast(w2v, w_d["wv2"])
        for t in range(NT):
            xf = xnf2[t % 2]
            op("q", lambda q, t=t, xf=xf: q.dma_start(
                out=xf.ap[:], in_=xn_d[t, :, :, :].rearrange("a p j -> p a j")),
                writes=(xf,), isdma=True)
            vp = v2_ps[t % 2]
            for half in range(2):
                for c in range(NC):
                    op("t", lambda t_, c=c, half=half, vp=vp, xf=xf: t_.matmul(
                        vp.ap[:, half * 512:(half + 1) * 512],
                        xf.ap[:, c * 128:(c + 1) * 128],
                        w2v.ap[:, c * D + half * 512: c * D + (half + 1) * 512],
                        start=(c == 0), stop=(c == NC - 1)),
                        reads=(xf, w2v), writes=(vp,))
            for half in range(2):
                op("s", lambda s, vp=vp, t=t, half=half: s.copy(
                    KV2.ap[:, t * D + half * 512: t * D + (half + 1) * 512],
                    vp.ap[:, half * 512:(half + 1) * 512]),
                    reads=(vp,), writes=(KV2,))
            pump(4)

        ir.fence()
        b1v.close()
        if upto == "B1v":
            _finish(k, block, ctx)
            return nc, ctx

        b1b = ExitStack()
        wo2 = k.sb(NC * D, BF16, b1b)
        x1c = [k.sb(D, F32, b1b) for _ in range(2)]
        q2t = [k.sb(NC * 128, BF16, b1b) for _ in range(2)]
        zb2 = [k.sb(D, F32, b1b) for _ in range(2)]
        pb2 = [k.sb(KBLK, BF16, b1b) for _ in range(3)]
        ptb2 = [k.sb(KBLK, BF16, b1b) for _ in range(3)]
        at2 = [k.sb(D, BF16, b1b)] * 2
        atf = k.sb(D, BF16, b1b)
        hb = [k.sb(D, F32, b1b) for _ in range(2)]
        rb_ = [k.sb(D, F32, b1b) for _ in range(2)]
        NB5 = NT // 4
        bms = [k.sb(H * NB5, F32, b1b) for _ in range(2)]
        bss = [k.sb(H * NB5, F32, b1b) for _ in range(2)]
        mf = [k.sb(16, F32, b1b) for _ in range(2)]
        nmf = [k.sb(16, F32, b1b) for _ in range(2)]
        e0b = [k.sb(16, F32, b1b) for _ in range(2)]
        corb = [k.sb(16, F32, b1b) for _ in range(2)]
        lsum = [k.sb(16, F32, b1b) for _ in range(2)]
        linv = [k.sb(16, F32, b1b) for _ in range(2)]
        scrD = [mk_scr(b1b) for _ in range(2)]
        sp1_ps = [k.ps(512, F32, b1b)]
        sp2_ps = [k.ps(KBLK, F32, b1b) for _ in range(2)]
        ava_ps = k.ps(D, F32, b1b)
        tp4_ps = [k.ps(1024, BF16, b1b)]

        load_w_cast(wo2, w_d["wo2"])

        def b1b_scmm(sp, qb, h, blk0, w, masked, isdiag):
            c, po = h // 2, (h % 2) * 64
            nhv = (w + 511) // 512
            for half in range(nhv):
                hw = min(512, w - half * 512)
                dg = isdiag and (half == nhv - 1)
                op("t", lambda t_, c=c, po=po, blk0=blk0, half=half, hw=hw,
                   sp=sp, qb=qb, dg=dg: t_.matmul(
                    sp.ap[:, half * 512: half * 512 + hw],
                    qb.ap[po:po + 64, c * 128:(c + 1) * 128],
                    KV1.ap[po:po + 64, c * S + blk0 + half * 512:
                           c * S + blk0 + half * 512 + hw],
                    start=True, stop=not dg),
                    reads=(qb, KV1), writes=(sp,))
                if dg:
                    off = half * 512 + hw - 128
                    op("t", lambda t_, off=off, sp=sp: t_.matmul(
                        sp.ap[:, off:off + 128], ident.ap[:], tri.ap[:],
                        start=False, stop=True),
                        reads=(ident, tri), writes=(sp,))

        def mk_pass1(t, p, qb):
            nkey = (t + 1) * 128
            nk5 = (nkey + 511) // 512
            units = []

            def unit(h, kb5, w):
                sp = sp1_ps[0]
                b1b_scmm(sp, qb, h, kb5 * 512, w, False, False)
                op("v", lambda v, sp=sp, w=w, h=h, kb5=kb5: v.reduce_max(
                    bms[p].ap[:, h * NB5 + kb5: h * NB5 + kb5 + 1],
                    sp.ap[:, 0:w], axis=AX.X),
                    reads=(sp,), writes=(bms[p],))
            for kb5 in range(nk5):
                w = min(512, nkey - kb5 * 512)
                for h in range(H):
                    units.append(lambda h=h, kb5=kb5, w=w: unit(h, kb5, w))
            return units

        def emit_pass1_tail(t, p):
            vb = bms[p].ap.rearrange("p (h kb) -> p h kb", kb=NB5)
            op("v", lambda v: v.reduce_max(mf[p].ap[:], vb[:, :, 0:NB5], axis=AX.X),
               reads=(bms[p],), writes=(mf[p],))
            op("v", lambda v: v.tensor_scalar(nmf[p].ap[:], mf[p].ap[:], -1.0, None,
                                              ALU.mult),
               reads=(mf[p],), writes=(nmf[p],))
            op("s", lambda s: s.activation(e0b[p].ap[:], mf[p].ap[:], AF.Exp,
                                           bias=0.0, scale=-1.0),
               reads=(mf[p],), writes=(e0b[p],))
            op("v", lambda v: v.tensor_scalar(corb[p].ap[:], e0b[p].ap[:],
                                              npad.ap[:, 0:1], 2.0, ALU.mult, ALU.mult),
               reads=(e0b[p], npad), writes=(corb[p],))
            op("v", lambda v: v.memset(bss[p].ap[:], 0.0), writes=(bss[p],))

        def mk_pass2(t, p, qb):
            nkey = (t + 1) * 128
            nkb = (nkey + KBLK - 1) // KBLK
            cnt = [0]

            def unitA(h, kb, w, g):
                sp = sp2_ps[g % 2]
                pb = pb2[g % 3]
                isdiag = (kb == nkb - 1)
                b1b_scmm(sp, qb, h, kb * KBLK, w, True, isdiag)
                nhv = (w + 511) // 512
                for half in range(nhv):
                    hw = min(512, w - half * 512)
                    kb5 = kb * 2 + half
                    op("s", lambda s, sp=sp, half=half, hw=hw, h=h, kb5=kb5,
                       pb=pb: s.activation(
                        pb.ap[:, half * 512: half * 512 + hw],
                        sp.ap[:, half * 512: half * 512 + hw], AF.Exp,
                        bias=nmf[p].ap[:, h:h + 1], scale=1.0,
                        accum_out=bss[p].ap[:, h * NB5 + kb5: h * NB5 + kb5 + 1]),
                        reads=(sp, nmf[p]), writes=(pb, bss[p]))

            def unitB(h, kb, w, g):
                pb = pb2[g % 3]
                pt = ptb2[g % 3]
                tp = tp4_ps[0]
                ng = w // 128
                for j in range(ng):
                    op("t", lambda t_, j=j, pb=pb, tp=tp: t_.transpose(
                        tp.ap[:, j * 128:(j + 1) * 128],
                        pb.ap[:, j * 128:(j + 1) * 128], ident.ap[:]),
                        reads=(pb, ident), writes=(tp,))
                if g % 2 == 0:
                    op("s", lambda s, tp=tp, ng=ng, pt=pt: s.copy(
                        pt.ap[:, 0:ng * 128], tp.ap[:, 0:ng * 128]),
                        reads=(tp,), writes=(pt,))
                else:
                    op("v", lambda v, tp=tp, ng=ng, pt=pt: v.tensor_copy(
                        pt.ap[:, 0:ng * 128], tp.ap[:, 0:ng * 128]),
                        reads=(tp,), writes=(pt,))
                for gr in range(ng):
                    kt = kb * 8 + gr
                    op("t", lambda t_, gr=gr, kt=kt, h=h, kb=kb, ng=ng, pt=pt: t_.matmul(
                        ava_ps.ap[:, h * 64:(h + 1) * 64],
                        pt.ap[:, gr * 128:(gr + 1) * 128],
                        KV2.ap[:, kt * D + h * 64: kt * D + (h + 1) * 64],
                        start=(kb == 0 and gr == 0),
                        stop=(kb == nkb - 1 and gr == ng - 1)),
                        reads=(pt, KV2), writes=(ava_ps,))

            hw_list = []
            for h in range(H):
                for kb in range(nkb):
                    hw_list.append((h, kb, min(KBLK, nkey - kb * KBLK)))
            n = len(hw_list)
            LAG = 2
            units = []
            for g in range(n + LAG):
                if g < n:
                    h, kb, w = hw_list[g]
                    units.append(lambda h=h, kb=kb, w=w, g=g: unitA(h, kb, w, g))
                if g - LAG >= 0:
                    h, kb, w = hw_list[g - LAG]
                    units.append(lambda h=h, kb=kb, w=w, g=g - LAG: unitB(h, kb, w, g))
            return units

        def emit_pass2_tail(t, p, xb):
            ti = t - OWN0
            vs = bss[p].ap.rearrange("p (h kb) -> p h kb", kb=NB5)
            op("v", lambda v: v.reduce_sum(lsum[p].ap[:], vs[:, :, 0:NB5], axis=AX.X),
               reads=(bss[p],), writes=(lsum[p],))
            op("v", lambda v: v.tensor_tensor(lsum[p].ap[:], lsum[p].ap[:],
                                              corb[p].ap[:], ALU.subtract),
               reads=(corb[p],), writes=(lsum[p],))
            op("v", lambda v: v.tensor_scalar(linv[p].ap[:], lsum[p].ap[:], 1e-30,
                                              None, ALU.add),
               reads=(lsum[p],), writes=(linv[p],))
            op("v", lambda v: v.reciprocal(linv[p].ap[:], linv[p].ap[:]),
               reads=(linv[p],), writes=(linv[p],))
            for h in range(H):
                op("v", lambda v, h=h: v.tensor_scalar(
                    at2[p].ap[:, h * 64:(h + 1) * 64],
                    ava_ps.ap[:, h * 64:(h + 1) * 64],
                    linv[p].ap[:, h:h + 1], None, ALU.mult),
                    reads=(ava_ps, linv[p]), writes=(at2[p],))
            tp = tp4_ps[0]
            for cb4 in range(0, NC, 4):
                for j in range(4):
                    c = cb4 + j
                    op("t", lambda t_, c=c, j=j, tp=tp: t_.transpose(
                        tp.ap[:, j * 128:(j + 1) * 128],
                        at2[p].ap[:, c * 128:(c + 1) * 128], ident.ap[:]),
                        reads=(at2[p], ident), writes=(tp,))
                op("s", lambda s, cb4=cb4, tp=tp: s.copy(
                    atf.ap[:, cb4 * 128:(cb4 + 4) * 128], tp.ap[:, 0:512]),
                    reads=(tp,), writes=(atf,))
            for c in range(NC):
                for half in range(2):
                    op("t", lambda t_, c=c, half=half: t_.matmul(
                        ava_ps.ap[:, half * 512:(half + 1) * 512],
                        atf.ap[:, c * 128:(c + 1) * 128],
                        wo2.ap[:, c * D + half * 512: c * D + (half + 1) * 512],
                        start=(c == 0), stop=(c == NC - 1)),
                        reads=(atf, wo2), writes=(ava_ps,))
            for half in range(2):
                op("v", lambda v, half=half: v.tensor_tensor(
                    hb[p].ap[:, half * 512:(half + 1) * 512],
                    zb2[p].ap[:, half * 512:(half + 1) * 512],
                    ava_ps.ap[:, half * 512:(half + 1) * 512], ALU.add),
                    reads=(zb2[p], ava_ps), writes=(hb[p],))
            op("v", lambda v, xb=xb: v.tensor_tensor(rb_[p].ap[:], xb.ap[:],
                                                     hb[p].ap[:], ALU.add),
               reads=(xb, hb[p]), writes=(rb_[p],))
            op("q", lambda q, ti=ti: q.dma_start(
                out=h_d[ti * 128:(ti + 1) * 128, :], in_=hb[p].ap[:]),
                reads=(hb[p],), isdma=True)
            op("q", lambda q, ti=ti: q.dma_start(
                out=r_d[ti * 128:(ti + 1) * 128, :], in_=rb_[p].ap[:]),
                reads=(rb_[p],), isdma=True)

        prev = None
        for t in range(OWN0, NT):
            ti = t - OWN0
            p = ti % 2
            xb = x1c[p]
            op("q", lambda q, t=t, xb=xb: q.dma_start(
                out=xb.ap[:], in_=x1_d[t * 128:(t + 1) * 128, :]),
                writes=(xb,), isdma=True)
            qb = q2t[p]
            op("q", lambda q, ti=ti, qb=qb: q.dma_start(
                out=qb.ap[:], in_=q2_d[ti, :, :, :].rearrange("a p j -> p a j")),
                writes=(qb,), isdma=True)
            ln_tile(xb, scrD[p], LN_EPS, out_f32=zb2[p])
            op("v", lambda v, p=p: v.memset(bms[p].ap[:], -1e30), writes=(bms[p],))
            u1 = mk_pass1(t, p, qb)
            u2 = mk_pass2(prev[0], prev[1], prev[2]) if prev else []
            n1, n2 = len(u1), len(u2)
            i1 = i2 = 0
            while i1 < n1 or i2 < n2:
                if i2 < n2 and (i1 >= n1 or i2 * max(n1, 1) <= i1 * max(n2, 1)):
                    u2[i2](); i2 += 1
                else:
                    u1[i1](); i1 += 1
            emit_pass1_tail(t, p)
            if prev:
                emit_pass2_tail(prev[0], prev[1], prev[3])
            prev = (t, p, qb, xb)
            pump(6)
        for u in mk_pass2(prev[0], prev[1], prev[2]):
            u()
        emit_pass2_tail(prev[0], prev[1], prev[3])

        pump_all()
        ir.fence()
        b1b.close()
        kv_scope.close()
        st_scope.close()
        if upto in ("B1b", "B1bdbg"):
            _finish(k, block, ctx)
            return nc, ctx

        b2s = ExitStack()
        b2 = ExitStack()
        xf = k.sb(NC * 2048, BF16, b2)
        comb = k.sb(16 * NE, F32, b2)
        g2bc = k.sb(D, F32, b2s)
        hb2 = [k.sb(D, F32, b2s) for _ in range(2)]
        xfb = k.sb(D, BF16, b2s)
        p8 = k.sb(NE, F32, b2s)
        r8 = {n: k.sb(8, F32, b2s) for n in
              ("m8", "nm8", "m1", "m2", "den", "dinv")}
        msk1 = k.sb(NE, F32, b2s)
        p8b = k.sb(NE, F32, b2s)
        sel8 = k.sb(NE, F32, b2s)
        scrE = mk_scr(b2s)
        rt_ps = k.ps(512, F32, b2s)
        tp5_ps = [k.ps(1024, BF16, b2s) for _ in range(2)]

        op("g", lambda g: g.dma_start(out=g2bc.ap[:], in_=g2bc_d[:]),
           writes=(g2bc,), isdma=True)
        wrb2 = k.sb(NC * NE, BF16, b2s)
        wrf = k.sb(NC * NE, F32, b2s)
        wrlo = k.sb(NC * NE, BF16, b2s)
        xff = k.sb(D, F32, b2s)
        xlo = k.sb(D, BF16, b2s)
        xlofm = k.sb(D, BF16, b2s)
        for c in range(NC):
            op("g", lambda g, c=c: g.dma_start(
                out=wrb2.ap[:, c * NE:(c + 1) * NE], in_=wr_d[c * 128:(c + 1) * 128, :]),
               writes=(wrb2,), isdma=True)
            op("q", lambda q, c=c: q.dma_start(
                out=wrf.ap[:, c * NE:(c + 1) * NE], in_=wr_d[c * 128:(c + 1) * 128, :]),
               writes=(wrf,), isdma=True)
        op("v", lambda v: v.tensor_tensor(wrlo.ap[:], wrf.ap[:], wrb2.ap[:],
                                          ALU.subtract),
           reads=(wrf, wrb2), writes=(wrlo,))

        for ti in range(16):
            hbb = hb2[ti % 2]
            op("q", lambda q, ti=ti, hbb=hbb: q.dma_start(
                out=hbb.ap[:], in_=h_d[ti * 128:(ti + 1) * 128, :]),
                writes=(hbb,), isdma=True)
            rms_tile(hbb, scrE, g2bc, xlofm, xff)
            op("v", lambda v: v.tensor_copy(xfb.ap[:], xff.ap[:]),
               reads=(xff,), writes=(xfb,))
            op("v", lambda v: v.tensor_tensor(xlo.ap[:], xff.ap[:], xfb.ap[:],
                                              ALU.subtract),
               reads=(xff, xfb), writes=(xlo,))
            for c in range(NC):
                p = tp5_ps[c % 2]
                op("t", lambda t_, c=c, p=p: t_.transpose(
                    p.ap[:, 0:128], xfb.ap[:, c * 128:(c + 1) * 128], ident.ap[:]),
                    reads=(xfb, ident), writes=(p,))
                op("s", lambda s, c=c, p=p, ti=ti: s.copy(
                    xf.ap[:, c * 2048 + ti * 128: c * 2048 + (ti + 1) * 128],
                    p.ap[:, 0:128]),
                    reads=(p,), writes=(xf,))
            for c in range(NC):
                p = tp5_ps[c % 2]
                op("t", lambda t_, c=c, p=p: t_.transpose(
                    p.ap[:, 0:128], xlo.ap[:, c * 128:(c + 1) * 128], ident.ap[:]),
                    reads=(xlo, ident), writes=(p,))
                op("s", lambda s, c=c, p=p: s.copy(
                    xlofm.ap[:, c * 128:(c + 1) * 128], p.ap[:, 0:128]),
                    reads=(p,), writes=(xlofm,))
            nmm = 0
            for c in range(NC):
                for lh, rh in (("hi", "hi"), ("hi", "lo"), ("lo", "hi")):
                    lop = (xf if lh == "hi" else xlofm)
                    lsl = (slice(c * 2048 + ti * 128, c * 2048 + (ti + 1) * 128)
                           if lh == "hi" else slice(c * 128, (c + 1) * 128))
                    rop = wrb2 if rh == "hi" else wrlo
                    op("t", lambda t_, c=c, lop=lop, lsl=lsl, rop=rop, nmm=nmm: t_.matmul(
                        rt_ps.ap[:, 0:NE], lop.ap[:, lsl],
                        rop.ap[:, c * NE:(c + 1) * NE],
                        start=(nmm == 0), stop=(nmm == 3 * NC - 1)),
                        reads=(lop, rop), writes=(rt_ps,))
                    nmm += 1
            op("v", lambda v: v.reduce_max(r8["m8"].ap[:, 0:1], rt_ps.ap[:, 0:NE], axis=AX.X),
               reads=(rt_ps,), writes=(r8["m8"],))
            op("v", lambda v: v.tensor_scalar(r8["nm8"].ap[:, 0:1], r8["m8"].ap[:, 0:1],
                                              -1.0, None, ALU.mult),
               reads=(r8["m8"],), writes=(r8["nm8"],))
            op("s", lambda s: s.activation(p8.ap[:], rt_ps.ap[:, 0:NE], AF.Exp,
                                           bias=r8["nm8"].ap[:, 0:1], scale=1.0),
               reads=(rt_ps, r8["nm8"]), writes=(p8,))
            op("v", lambda v: v.reduce_max(r8["m1"].ap[:, 0:1], p8.ap[:], axis=AX.X),
               reads=(p8,), writes=(r8["m1"],))
            op("v", lambda v: v.tensor_scalar(msk1.ap[:], p8.ap[:], r8["m1"].ap[:, 0:1],
                                              None, ALU.is_equal),
               reads=(p8, r8["m1"]), writes=(msk1,))
            op("v", lambda v: v.scalar_tensor_tensor(p8b.ap[:], msk1.ap[:], -1e30,
                                                     p8.ap[:], ALU.mult, ALU.add),
               reads=(msk1, p8), writes=(p8b,))
            op("v", lambda v: v.reduce_max(r8["m2"].ap[:, 0:1], p8b.ap[:], axis=AX.X),
               reads=(p8b,), writes=(r8["m2"],))
            op("v", lambda v: v.tensor_scalar(sel8.ap[:], p8.ap[:], r8["m2"].ap[:, 0:1],
                                              None, ALU.is_ge),
               reads=(p8, r8["m2"]), writes=(sel8,))
            op("v", lambda v: v.tensor_tensor(r8["den"].ap[:, 0:1], r8["m1"].ap[:, 0:1],
                                              r8["m2"].ap[:, 0:1], ALU.add),
               reads=(r8["m1"], r8["m2"]), writes=(r8["den"],))
            op("v", lambda v: v.reciprocal(r8["dinv"].ap[:, 0:1], r8["den"].ap[:, 0:1]),
               reads=(r8["den"],), writes=(r8["dinv"],))
            op("v", lambda v, ti=ti: v.scalar_tensor_tensor(
                comb.ap[:, ti * NE:(ti + 1) * NE], p8.ap[:], r8["dinv"].ap[:, 0:1],
                sel8.ap[:], ALU.mult, ALU.mult),
               reads=(p8, r8["dinv"], sel8), writes=(comb,))

        ir.fence()
        b2s.close()

        wdm = [k.sb(D, BF16, b2) for _ in range(NHSL)]
        wgs = [k.sb(NC * 256, BF16, b2) for _ in range(2)]
        wus = [k.sb(NC * 256, BF16, b2) for _ in range(2)]
        mall = k.sb(NHSL * 1024, BF16, b2)
        gsb = [k.sb(512, BF16, b2) for _ in range(2)]
        yac = k.sb(8 * D, F32, b2)
        rbt = [k.sb(D, F32, b2) for _ in range(2)]
        outb = [k.sb(D, F32, b2) for _ in range(2)]
        gu_ps = [k.ps(512, F32, b2) for _ in range(4)]
        dn_ps = [k.ps(512, F32, b2) for _ in range(4)]

        NHC = NHSL // 2

        for thalf in range(2):
            op("v", lambda v: v.memset(yac.ap[:], 0.0), writes=(yac,))
            for e in range(NE):
                for c in range(NHSL):
                    op("q", lambda q, e=e, c=c: q.dma_start(
                        out=wdm[c].ap[:], in_=wdb_d[e, c * 128:(c + 1) * 128, :]),
                        writes=(wdm[c],), isdma=True)
                for hc in range(NHC):
                    wgb = wgs[hc % 2]
                    wub = wus[hc % 2]
                    op("q", lambda q, e=e, hc=hc, wgb=wgb: q.dma_start(
                        out=wgb.ap[:],
                        in_=wgb_d[e, :, hc * 256:(hc + 1) * 256]
                        .rearrange("(a p) j -> p a j", p=128)),
                        writes=(wgb,), isdma=True)
                    op("q", lambda q, e=e, hc=hc, wub=wub: q.dma_start(
                        out=wub.ap[:],
                        in_=wub_d[e, :, hc * 256:(hc + 1) * 256]
                        .rearrange("(a p) j -> p a j", p=128)),
                        writes=(wub,), isdma=True)
                    for hs in range(2):
                        for c in range(NC):
                            for t5 in range(2):
                                op("t", lambda t_, c=c, hs=hs, t5=t5, wgb=wgb,
                                   thalf=thalf: t_.matmul(
                                    gu_ps[2 * t5 + 0].ap[:],
                                    wgb.ap[:, c * 256 + hs * 128: c * 256 + (hs + 1) * 128],
                                    xf.ap[:, c * 2048 + thalf * 1024 + t5 * 512:
                                          c * 2048 + thalf * 1024 + (t5 + 1) * 512],
                                    start=(c == 0), stop=(c == NC - 1)),
                                    reads=(wgb, xf), writes=(gu_ps[2 * t5 + 0],))
                        for c in range(NC):
                            for t5 in range(2):
                                op("t", lambda t_, c=c, hs=hs, t5=t5, wub=wub,
                                   thalf=thalf: t_.matmul(
                                    gu_ps[2 * t5 + 1].ap[:],
                                    wub.ap[:, c * 256 + hs * 128: c * 256 + (hs + 1) * 128],
                                    xf.ap[:, c * 2048 + thalf * 1024 + t5 * 512:
                                          c * 2048 + thalf * 1024 + (t5 + 1) * 512],
                                    start=(c == 0), stop=(c == NC - 1)),
                                    reads=(wub, xf), writes=(gu_ps[2 * t5 + 1],))
                        hsl = hc * 2 + hs
                        for t5 in range(2):
                            gs = gsb[t5]
                            op("s", lambda s, t5=t5, gs=gs: s.activation(
                                gs.ap[:], gu_ps[2 * t5 + 0].ap[:], AF.Silu,
                                bias=0.0, scale=1.0),
                                reads=(gu_ps[2 * t5 + 0],), writes=(gs,))
                            op("v", lambda v, gs=gs, hsl=hsl, t5=t5: v.tensor_tensor(
                                mall.ap[:, hsl * 1024 + t5 * 512: hsl * 1024 + (t5 + 1) * 512],
                                gs.ap[:], gu_ps[2 * t5 + 1].ap[:], ALU.mult),
                                reads=(gs, gu_ps[2 * t5 + 1]), writes=(mall,))
                for dh in range(2):
                    for hsl in range(NHSL):
                        for t128 in range(8):
                            dp = (dn_ps + gu_ps)[t128]
                            op("t", lambda t_, hsl=hsl, t128=t128, dh=dh, dp=dp: t_.matmul(
                                dp.ap[:],
                                mall.ap[:, hsl * 1024 + t128 * 128: hsl * 1024 + (t128 + 1) * 128],
                                wdm[hsl].ap[:, dh * 512:(dh + 1) * 512],
                                start=(hsl == 0), stop=(hsl == NHSL - 1)),
                                reads=(mall, wdm[hsl]), writes=(dp,))
                    for t128 in range(8):
                        dp = (dn_ps + gu_ps)[t128]
                        ci = (thalf * 8 + t128) * NE + e
                        op("v", lambda v, dp=dp, t128=t128, dh=dh, ci=ci: v.scalar_tensor_tensor(
                            yac.ap[:, t128 * D + dh * 512: t128 * D + (dh + 1) * 512],
                            dp.ap[:], comb.ap[:, ci:ci + 1],
                            yac.ap[:, t128 * D + dh * 512: t128 * D + (dh + 1) * 512],
                            ALU.mult, ALU.add),
                            reads=(dp, comb, yac), writes=(yac,))
            for t128 in range(8):
                ti = thalf * 8 + t128
                rb2 = rbt[t128 % 2]
                ob = outb[t128 % 2]
                op("q", lambda q, ti=ti, rb2=rb2: q.dma_start(
                    out=rb2.ap[:], in_=r_d[ti * 128:(ti + 1) * 128, :]),
                    writes=(rb2,), isdma=True)
                op("v", lambda v, t128=t128, rb2=rb2, ob=ob: v.tensor_tensor(
                    ob.ap[:], rb2.ap[:], yac.ap[:, t128 * D:(t128 + 1) * D], ALU.add),
                    reads=(rb2, yac), writes=(ob,))
                op("q", lambda q, ti=ti, ob=ob: q.dma_start(
                    out=out_d[ti * 128:(ti + 1) * 128, :], in_=ob.ap[:]),
                    reads=(ob,), isdma=True)

        ir.fence()
        _finish(k, block, ctx)
        return nc, ctx


def _finish(k, block, ctx):
    ir = k.ir
    ir.fence()
    ir.op("g", lambda g: g.engine_nop())
    ir.lower(k.nc, k.ctx, block)


_ROPE = {}


def _rope_tables(role):
    if role in _ROPE:
        return _ROPE[role]
    pos = np.arange(S)
    if role == 0:
        pos = np.clip(pos - 2048, 0, None)
    inv = 1.0 / (10000.0 ** (np.arange(0, HD, 2, dtype=np.float32) / HD))
    freqs = np.einsum("i,j->ij", pos.astype(np.float32), inv)
    emb = np.concatenate([freqs, freqs], axis=-1)
    cos = np.cos(emb).astype(np.float32)
    sin = np.sin(emb).astype(np.float32)
    sin_signed = np.concatenate([-sin[:, :32], sin[:, 32:]], axis=-1).astype(np.float32)
    cosr = np.ascontiguousarray(np.tile(cos, (1, H)))
    sinr = np.ascontiguousarray(np.tile(sin_signed, (1, H)))
    _ROPE[role] = (cosr, sinr)
    return _ROPE[role]


def make_in_maps(inputs):
    import ml_dtypes
    x = np.asarray(inputs["L0_attn_embeddings"], np.float32)
    wg = np.ascontiguousarray(np.pad(np.asarray(inputs["L0_ffn_wg"], np.float32),
                                     ((0, 0), (0, 0), (0, HID - 2752))))
    wu = np.ascontiguousarray(np.pad(np.asarray(inputs["L0_ffn_wu"], np.float32),
                                     ((0, 0), (0, 0), (0, HID - 2752))))
    wd = np.ascontiguousarray(np.pad(np.asarray(inputs["L0_ffn_wd"], np.float32),
                                     ((0, 0), (0, HID - 2752), (0, 0))))
    ident = np.eye(128, dtype=np.float32).astype(ml_dtypes.bfloat16)
    tri = np.triu(np.full((128, 128), NEGBIG, np.float32), k=1).astype(ml_dtypes.bfloat16)
    g1bc = np.ascontiguousarray(np.broadcast_to(
        np.asarray(inputs["L0_ffn_g1"], np.float32), (128, D)))
    g2bc = np.ascontiguousarray(np.broadcast_to(
        np.asarray(inputs["L0_ffn_g2"], np.float32), (128, D)))
    shared = dict(
        wq1=np.asarray(inputs["L0_attn_W_Q"], np.float32),
        wk1=np.asarray(inputs["L0_attn_W_K"], np.float32),
        wv1=np.asarray(inputs["L0_attn_W_V"], np.float32),
        wq2=np.asarray(inputs["L0_ffn_wq"], np.float32),
        wk2=np.asarray(inputs["L0_ffn_wk"], np.float32),
        wv2=np.asarray(inputs["L0_ffn_wv"], np.float32),
        wo2=np.asarray(inputs["L0_ffn_wo"], np.float32),
        wr=np.asarray(inputs["L0_ffn_w_router"], np.float32),
        wg=wg, wu=wu, wd=wd, ident=ident, tri=tri, g1bc=g1bc, g2bc=g2bc,
    )
    in_maps = []
    zeros = np.zeros((2048, D), np.float32)
    for c in range(8):
        b, role = c // 2, c % 2
        xv = x[b] if role == 1 else np.ascontiguousarray(
            np.concatenate([zeros, x[b][:2048]], axis=0))
        cosr, sinr = _rope_tables(role)
        m = dict(shared)
        m["consts"] = np.ascontiguousarray(
            np.broadcast_to(np.asarray(CONSTS, np.float32), (128, len(CONSTS))))
        m["xv"] = xv
        m["cosr"] = cosr
        m["sinr"] = sinr
        m["npad"] = np.full((128, 1), 1024.0 if role == 0 else 0.0, np.float32)
        in_maps.append(m)
    return in_maps


def kernel(**inputs):
    in_maps = make_in_maps(inputs)
    nc, ctx = build_program("full")
    res = run_bass_kernel_spmd(nc, in_maps, core_ids=list(range(8)))
    out = np.empty((B, S, D), np.float32)
    for c in range(8):
        b, role = c // 2, c % 2
        rows = slice(0, 2048) if role == 0 else slice(2048, 4096)
        out[b, rows] = res.results[c]["out"]
    return out


# version 06
# speedup vs baseline: 1.1654x; 1.0489x over previous
import numpy as np
from contextlib import ExitStack

import concourse.bass as bass
import concourse.mybir as mybir
from concourse.bass_utils import run_bass_kernel_spmd

F32 = mybir.dt.float32
BF16 = mybir.dt.bfloat16
AF = mybir.ActivationFunctionType
ALU = mybir.AluOpType
AX = mybir.AxisListType

B, S, D = 4, 4096, 1024
NT = S // 128
NC = D // 128
H, HD = 16, 64
HID = 2816
NHSL = HID // 128
NE = 8
OWN0 = NT // 2
KBLK = 1024
LN_EPS = 1e-5
RMS_EPS = 1e-6
NEGBIG = -1.0e9
CONSTS = (0.0, 1.0, -0.5, -1.0, LN_EPS, RMS_EPS, 1.0 / 32.0, 0.125, 1.0 / D, 2.0)

ENGS = ("q", "g", "t", "v", "s")


class Op:
    __slots__ = ("eng", "fn", "deps", "inc", "isdma", "sem_val", "drain", "idx")

    def __init__(self, eng, fn, deps, isdma, idx):
        self.eng, self.fn, self.deps, self.isdma = eng, fn, list(deps), isdma
        self.inc = isdma
        self.sem_val = None
        self.drain = False
        self.idx = idx


class Buf:
    __slots__ = ("ap", "w", "rs")

    def __init__(self, ap):
        self.ap, self.w, self.rs = ap, None, []


class IR:
    def __init__(self):
        self.ops = []
        self.fences = []

    def op(self, eng, fn, reads=(), writes=(), isdma=False):
        deps = []
        for b in reads:
            if b.w is not None:
                deps.append(b.w)
        for b in writes:
            if b.w is not None:
                deps.append(b.w)
            deps.extend(b.rs)
        o = Op(eng, fn, deps, isdma, len(self.ops))
        self.ops.append(o)
        for b in reads:
            b.rs.append(o)
        for b in writes:
            b.w = o
            b.rs = []
        return o

    def fence(self):
        last = {}
        for o in self.ops:
            last[(o.eng, o.isdma)] = o
        self.fences.append((len(self.ops), last))

    def lower(self, nc, ctx, block):
        fence_i = 0
        pend = None
        seen_after = set()
        for o in self.ops:
            while fence_i < len(self.fences) and self.fences[fence_i][0] <= o.idx:
                pend = self.fences[fence_i][1]
                seen_after = set()
                fence_i += 1
            if pend is not None and o.eng not in seen_after:
                for (e2, _dma), d in pend.items():
                    if e2 != o.eng or _dma:
                        o.deps.append(d)
                seen_after.add(o.eng)

        for o in self.ops:
            ded, seen = [], set()
            for d in o.deps:
                if id(d) in seen or d is o:
                    continue
                seen.add(id(d))
                ded.append(d)
                if d.isdma or d.eng != o.eng:
                    d.inc = True
            o.deps = ded

        recent = {e: [] for e in ENGS}
        for o in self.ops:
            if o.eng == "v" and not o.isdma:
                for d in o.deps:
                    if (not d.isdma) and d.eng == "v" and d in recent["v"]:
                        o.drain = True
            r = recent[o.eng]
            r.append(o)
            if len(r) > 2:
                r.pop(0)

        sems = {e: ctx.enter_context(nc.semaphore(f"sem_{e}")) for e in ENGS}
        dsems = {e: ctx.enter_context(nc.semaphore(f"dma_{e}")) for e in ("q", "g")}
        cnt = {e: 0 for e in ENGS}
        dcnt = {e: 0 for e in ("q", "g")}
        for o in self.ops:
            if o.isdma:
                dcnt[o.eng] += 16
                o.sem_val = dcnt[o.eng]
            elif o.inc:
                cnt[o.eng] += 1
                o.sem_val = cnt[o.eng]

        handles = {"q": block.sync, "g": block.gpsimd, "t": block.tensor,
                   "v": block.vector, "s": block.scalar}
        ir = self

        def emit_for(eng):
            def run(h):
                waited = {}
                for o in ir.ops:
                    if o.eng != eng:
                        continue
                    for d in o.deps:
                        if (not d.isdma) and d.eng == eng:
                            continue
                        sem = dsems[d.eng] if d.isdma else sems[d.eng]
                        key = (d.isdma, d.eng)
                        if waited.get(key, -1) >= d.sem_val:
                            continue
                        h.wait_ge(sem, d.sem_val)
                        waited[key] = d.sem_val
                    if o.drain:
                        h.drain()
                    inst = o.fn(h)
                    if o.isdma:
                        inst.then_inc(dsems[o.eng], 16)
                    elif o.inc:
                        inst.then_inc(sems[o.eng], 1)
            return run

        for e in ENGS:
            handles[e](emit_for(e))


class K:
    def __init__(self, nc, ctx, ir):
        self.nc, self.ctx, self.ir = nc, ctx, ir
        self.uid = 0

    def name(self, p):
        self.uid += 1
        return f"{p}{self.uid}"

    def sb(self, cols, dtype, scope=None):
        t = (scope or self.ctx).enter_context(
            self.nc.sbuf_tensor(self.name("sb"), [128, cols], dtype))
        return Buf(t.ap())

    def ps(self, cols, dtype, scope=None):
        t = (scope or self.ctx).enter_context(
            self.nc.psum_tensor(self.name("ps"), [128, cols], dtype))
        return Buf(t.ap())

    def dram(self, name, shape, dtype, kind="Internal"):
        return self.nc.dram_tensor(name, shape, dtype, kind=kind).ap()


def build_program(upto="full"):
    nc = bass.Bass("TRN2", target_bir_lowering=False)
    ctx = ExitStack()
    ir = IR()
    k = K(nc, ctx, ir)
    op = ir.op

    ein, eout = "ExternalInput", "ExternalOutput"
    xv_d = k.dram("xv", [S, D], F32, ein)
    w_d = {n: k.dram(n, [D, D], F32, ein) for n in
           ("wq1", "wk1", "wv1", "wq2", "wk2", "wv2", "wo2")}
    wr_d = k.dram("wr", [D, NE], F32, ein)
    wg_d = k.dram("wg", [NE, D, HID], F32, ein)
    wu_d = k.dram("wu", [NE, D, HID], F32, ein)
    wd_d = k.dram("wd", [NE, HID, D], F32, ein)
    g1bc_d = k.dram("g1bc", [128, D], F32, ein)
    g2bc_d = k.dram("g2bc", [128, D], F32, ein)
    cosr_d = k.dram("cosr", [S, D], F32, ein)
    sinr_d = k.dram("sinr", [S, D], F32, ein)
    ident_d = k.dram("ident", [128, 128], BF16, ein)
    tri_d = k.dram("tri", [128, 128], BF16, ein)
    npad_d = k.dram("npad", [128, 1], F32, ein)

    consts_d = k.dram("consts", [128, len(CONSTS)], F32, ein)
    out_d = k.dram("out", [S // 2, D], F32, eout)
    x1_d = k.dram("x1_scr", [S, D], F32, eout if upto == "A2" else "Internal")
    xn_d = k.dram("xn_scr", [NT, NC, 128, 128], BF16, "Internal")
    q2_d = k.dram("q2_scr", [NT // 2, NC, 128, 128], BF16, "Internal")
    h_d = k.dram("h_scr", [S // 2, D], F32, eout if upto.startswith("B1b") else "Internal")
    r_d = k.dram("r_scr", [S // 2, D], F32, eout if upto.startswith("B1b") else "Internal")
    wgb_d = k.dram("wgb_scr", [NE, D, HID], BF16, "Internal")
    wub_d = k.dram("wub_scr", [NE, D, HID], BF16, "Internal")
    wdb_d = k.dram("wdb_scr", [NE, HID, D], BF16, "Internal")

    ident = k.sb(128, BF16)
    tri = k.sb(128, BF16)
    npad = k.sb(1, F32)
    cbuf = k.sb(len(CONSTS), F32)
    for _i, _val in enumerate(CONSTS):
        nc.const_aps.register(F32, float(_val), cbuf.ap[:, _i:_i + 1])
    st_scope = ExitStack()
    stage = [k.sb(HID, BF16, st_scope)]

    def precast_pairs():
        for dsrc, ddst in ((wg_d, wgb_d), (wu_d, wub_d)):
            for e in range(NE):
                for c in range(NC):
                    yield (dsrc, ddst, e, c, HID)
        for e in range(NE):
            for c in range(NHSL):
                yield (wd_d, wdb_d, e, c, D)

    _pc = precast_pairs()
    _pc_n = [0]

    def pump(n):
        for _ in range(n):
            try:
                dsrc, ddst, e, c, w = next(_pc)
            except StopIteration:
                return
            st = stage[0]
            _pc_n[0] += 1
            op("g", lambda g, dsrc=dsrc, e=e, c=c, st=st, w=w: g.dma_start(
                out=st.ap[:, 0:w], in_=dsrc[e, c * 128:(c + 1) * 128, :]),
                writes=(st,), isdma=True)
            op("q", lambda q, ddst=ddst, e=e, c=c, st=st, w=w: q.dma_start(
                out=ddst[e, c * 128:(c + 1) * 128, :], in_=st.ap[:, 0:w]),
                reads=(st,), isdma=True)

    def pump_all():
        while True:
            before = _pc_n[0]
            pump(8)
            if _pc_n[0] == before:
                break

    def mk_scr(scope):
        return {n: k.sb(16, F32, scope) for n in
                ("st6", "mv", "negmean", "lnv", "rstd", "ss")}

    def ln_tile(x_in, scr, eps, out_f32=None, out_bf=None):
        for hh in range(2):
            op("v", lambda v, hh=hh: v.bn_stats(
                scr["st6"].ap[:, hh * 6:hh * 6 + 6],
                x_in.ap[:, hh * 512:(hh + 1) * 512]),
               reads=(x_in,), writes=(scr["st6"],))
        op("v", lambda v: v.bn_aggr(scr["mv"].ap[:, 0:2], scr["st6"].ap[:, 0:12]),
           reads=(scr["st6"],), writes=(scr["mv"],))
        op("v", lambda v: v.tensor_scalar(scr["negmean"].ap[:, 0:1],
                                          scr["mv"].ap[:, 0:1], -1.0, None, ALU.mult),
           reads=(scr["mv"],), writes=(scr["negmean"],))
        op("s", lambda s: s.activation(scr["lnv"].ap[:, 0:1], scr["mv"].ap[:, 1:2],
                                       AF.Ln, bias=float(eps), scale=1.0),
           reads=(scr["mv"],), writes=(scr["lnv"],))
        op("s", lambda s: s.activation(scr["rstd"].ap[:, 0:1], scr["lnv"].ap[:, 0:1],
                                       AF.Exp, bias=0.0, scale=-0.5),
           reads=(scr["lnv"],), writes=(scr["rstd"],))
        for out, dt in ((out_f32, F32), (out_bf, BF16)):
            if out is not None:
                op("v", lambda v, out=out: v.tensor_scalar(
                    out.ap[:], x_in.ap[:], scr["negmean"].ap[:, 0:1],
                    scr["rstd"].ap[:, 0:1], ALU.add, ALU.mult),
                   reads=(x_in, scr["negmean"], scr["rstd"]), writes=(out,))

    def rms_tile(z_f32, scr, gain_bc, sqdump, out_bf):
        op("s", lambda s: s.activation(sqdump.ap[:], z_f32.ap[:], AF.Square,
                                       bias=0.0, scale=1.0,
                                       accum_out=scr["ss"].ap[:, 0:1]),
           reads=(z_f32,), writes=(sqdump, scr["ss"]))
        op("s", lambda s: s.activation(scr["lnv"].ap[:, 0:1], scr["ss"].ap[:, 0:1],
                                       AF.Ln, bias=float(RMS_EPS), scale=1.0 / D),
           reads=(scr["ss"],), writes=(scr["lnv"],))
        op("s", lambda s: s.activation(scr["rstd"].ap[:, 0:1], scr["lnv"].ap[:, 0:1],
                                       AF.Exp, bias=0.0, scale=-0.5),
           reads=(scr["lnv"],), writes=(scr["rstd"],))
        op("v", lambda v: v.scalar_tensor_tensor(
            out_bf.ap[:], z_f32.ap[:], scr["rstd"].ap[:, 0:1], gain_bc.ap[:],
            ALU.mult, ALU.mult),
           reads=(z_f32, scr["rstd"], gain_bc), writes=(out_bf,))

    def load_w_cast(dst, dram, cols=D):
        for c in range(NC):
            op("g", lambda g, c=c, dst=dst, dram=dram, cols=cols: g.dma_start(
                out=dst.ap[:, c * cols:(c + 1) * cols],
                in_=dram[c * 128:(c + 1) * 128, :]),
                writes=(dst,), isdma=True)

    def transposes(src_bf, col_of, dst, dcol_of, n, tp_ps, evac_batch=4,
                   scale=None, dstbuf_extra=()):
        for base in range(0, n, evac_batch):
            m = min(evac_batch, n - base)
            p = tp_ps[(base // evac_batch) % 2]
            for j in range(m):
                c = base + j
                op("t", lambda t, c=c, j=j, p=p: t.transpose(
                    p.ap[:, j * 128:(j + 1) * 128],
                    src_bf.ap[:, col_of(c):col_of(c) + 128], ident.ap[:]),
                    reads=(src_bf, ident), writes=(p,))
            dc = dcol_of(base)
            if scale is None:
                op("s", lambda s, p=p, dc=dc, m=m: s.copy(
                    dst.ap[:, dc:dc + m * 128], p.ap[:, 0:m * 128]),
                    reads=(p,), writes=(dst,) + tuple(dstbuf_extra))
            else:
                op("s", lambda s, p=p, dc=dc, m=m: s.activation(
                    dst.ap[:, dc:dc + m * 128], p.ap[:, 0:m * 128], AF.Copy,
                    bias=0.0, scale=float(scale)),
                    reads=(p,), writes=(dst,) + tuple(dstbuf_extra))

    kv_scope = ExitStack()
    KV1 = k.sb(NC * S, BF16, kv_scope)
    KV2 = k.sb(NT * D, BF16, kv_scope)

    with nc.Block() as block:
        for dst, src in ((ident, ident_d), (tri, tri_d), (npad, npad_d),
                         (cbuf, consts_d)):
            op("q", lambda q, dst=dst, src=src: q.dma_start(out=dst.ap[:], in_=src[:]),
               writes=(dst,), isdma=True)
        ir.fence()

        a1 = ExitStack()
        w1k = k.sb(NC * D, BF16, a1)
        w1v = k.sb(NC * D, BF16, a1)
        xt = [k.sb(D, F32, a1) for _ in range(2)]
        lnb = [k.sb(D, BF16, a1) for _ in range(2)]
        lnfm4 = k.sb(NC * 512, BF16, a1)
        scrA = [mk_scr(a1) for _ in range(2)]
        tp_ps = [k.ps(1024, BF16, a1) for _ in range(2)]
        pj_ps = [k.ps(512, F32, a1) for _ in range(4)]

        load_w_cast(w1k, w_d["wk1"])
        load_w_cast(w1v, w_d["wv1"])

        for grp in range(NT // 4):
            for i in range(4):
                t = grp * 4 + i
                xb = xt[t % 2]
                op("q", lambda q, t=t, xb=xb: q.dma_start(
                    out=xb.ap[:], in_=xv_d[t * 128:(t + 1) * 128, :]),
                    writes=(xb,), isdma=True)
                lb = lnb[t % 2]
                ln_tile(xb, scrA[t % 2], LN_EPS, out_bf=lb)
                for c in range(NC):
                    p = tp_ps[c % 2]
                    op("t", lambda t_, c=c, p=p, lb=lb: t_.transpose(
                        p.ap[:, 0:128], lb.ap[:, c * 128:(c + 1) * 128], ident.ap[:]),
                        reads=(lb, ident), writes=(p,))
                    op("s", lambda s, c=c, i=i, p=p: s.copy(
                        lnfm4.ap[:, c * 512 + i * 128: c * 512 + (i + 1) * 128],
                        p.ap[:, 0:128]),
                        reads=(p,), writes=(lnfm4,))
                for half in range(2):
                    p = pj_ps[2 + half]
                    for c in range(NC):
                        op("t", lambda t_, c=c, half=half, i=i, p=p: t_.matmul(
                            p.ap[:],
                            lnfm4.ap[:, c * 512 + i * 128: c * 512 + (i + 1) * 128],
                            w1v.ap[:, c * D + half * 512: c * D + (half + 1) * 512],
                            start=(c == 0), stop=(c == NC - 1)),
                            reads=(lnfm4, w1v), writes=(p,))
                    op("s", lambda s, p=p, t=t, half=half: s.copy(
                        KV2.ap[:, t * D + half * 512: t * D + (half + 1) * 512],
                        p.ap[:]),
                        reads=(p,), writes=(KV2,))
            for do in range(NC):
                p = pj_ps[do % 2]
                for c in range(NC):
                    op("t", lambda t_, do=do, c=c, p=p: t_.matmul(
                        p.ap[:], w1k.ap[:, c * D + do * 128: c * D + (do + 1) * 128],
                        lnfm4.ap[:, c * 512:(c + 1) * 512],
                        start=(c == 0), stop=(c == NC - 1)),
                        reads=(w1k, lnfm4), writes=(p,))
                op("s", lambda s, p=p, do=do, grp=grp: s.copy(
                    KV1.ap[:, do * S + grp * 512: do * S + (grp + 1) * 512], p.ap[:]),
                    reads=(p,), writes=(KV1,))
            pump(2)

        ir.fence()
        a1.close()
        if upto == "A1":
            _finish(k, block, ctx)
            return nc, ctx

        a2 = ExitStack()
        w1q = k.sb(NC * D, BF16, a2)
        xt2 = [k.sb(D, F32, a2) for _ in range(3)]
        lnb2 = [k.sb(D, BF16, a2) for _ in range(2)]
        lnfm = [k.sb(NC * 128, BF16, a2) for _ in range(2)]
        q1t = [k.sb(NC * 128, BF16, a2) for _ in range(2)]
        x1t = [k.sb(D, F32, a2) for _ in range(2)]
        pbuf = [k.sb(512, BF16, a2) for _ in range(3)]
        ptb = [k.sb(512, BF16, a2) for _ in range(3)]
        NBO = NT // 4
        bmx = [k.sb(NBO, F32, a2) for _ in range(2)]
        bsx = [k.sb(NBO, F32, a2) for _ in range(2)]
        nmO = [k.sb(8, F32, a2) for _ in range(2)]
        scrB = [mk_scr(a2) for _ in range(2)]
        sp1_ps = [k.ps(512, F32, a2) for _ in range(2)]
        sp2_ps = [k.ps(512, F32, a2) for _ in range(2)]
        av_ps = k.ps(D, F32, a2)
        tp2_ps = [k.ps(1024, BF16, a2)]

        load_w_cast(w1q, w_d["wq1"])
        guc = [0, 0]

        def a2_prelude(t):
            par = t % 2
            xb = xt2[t % 3]
            op("q", lambda q, t=t, xb=xb: q.dma_start(
                out=xb.ap[:], in_=xv_d[t * 128:(t + 1) * 128, :]),
                writes=(xb,), isdma=True)
            lb = lnb2[par]
            ln_tile(xb, scrB[par], LN_EPS, out_bf=lb)
            lf = lnfm[par]
            tp = tp2_ps[0]
            for cb4 in range(0, NC, 4):
                for j in range(4):
                    c = cb4 + j
                    op("t", lambda t_, c=c, j=j, tp=tp, lb=lb: t_.transpose(
                        tp.ap[:, j * 128:(j + 1) * 128],
                        lb.ap[:, c * 128:(c + 1) * 128], ident.ap[:]),
                        reads=(lb, ident), writes=(tp,))
                op("s", lambda s, cb4=cb4, tp=tp, lf=lf: s.copy(
                    lf.ap[:, cb4 * 128:(cb4 + 4) * 128], tp.ap[:, 0:512]),
                    reads=(tp,), writes=(lf,))
            qb = q1t[par]
            for do in range(NC):
                sp = sp1_ps[do % 2]
                for c in range(NC):
                    op("t", lambda t_, do=do, c=c, sp=sp, lf=lf: t_.matmul(
                        sp.ap[:, 0:128],
                        w1q.ap[:, c * D + do * 128: c * D + (do + 1) * 128],
                        lf.ap[:, c * 128:(c + 1) * 128],
                        start=(c == 0), stop=(c == NC - 1)),
                        reads=(w1q, lf), writes=(sp,))
                op("s", lambda s, do=do, sp=sp, qb=qb: s.activation(
                    qb.ap[:, do * 128:(do + 1) * 128], sp.ap[:, 0:128],
                    AF.Copy, bias=0.0, scale=1.0 / 32.0),
                    reads=(sp,), writes=(qb,))

        def a2_scores(sp, qb, kb, w, isdiag):
            for c in range(NC):
                op("t", lambda t_, c=c, kb=kb, w=w, sp=sp, qb=qb: t_.matmul(
                    sp.ap[:, 0:w],
                    qb.ap[:, c * 128:(c + 1) * 128],
                    KV1.ap[:, c * S + kb * 512: c * S + kb * 512 + w],
                    start=(c == 0), stop=(c == NC - 1 and not isdiag)),
                    reads=(qb, KV1), writes=(sp,))
            if isdiag:
                op("t", lambda t_, w=w, sp=sp: t_.matmul(
                    sp.ap[:, w - 128:w], ident.ap[:], tri.ap[:],
                    start=False, stop=True),
                    reads=(ident, tri), writes=(sp,))

        def a2_pass1(t):
            par = t % 2
            qb = q1t[par]
            nkey = (t + 1) * 128
            nb5 = (nkey + 511) // 512
            units = []

            def unit(kb):
                w = min(512, nkey - kb * 512)
                sp = sp1_ps[guc[0] % 2]
                guc[0] += 1
                a2_scores(sp, qb, kb, w, False)
                op("v", lambda v, sp=sp, w=w, kb=kb: v.reduce_max(
                    bmx[par].ap[:, kb:kb + 1], sp.ap[:, 0:w], axis=AX.X),
                    reads=(sp,), writes=(bmx[par],))
            for kb in range(nb5):
                units.append(lambda kb=kb: unit(kb))
            return units

        def a2_pass1_tail(t):
            par = t % 2
            nkey = (t + 1) * 128
            nb5 = (nkey + 511) // 512
            op("v", lambda v, nb5=nb5: v.reduce_max(
                nmO[par].ap[:, 0:1], bmx[par].ap[:, 0:nb5], axis=AX.X, negate=True),
                reads=(bmx[par],), writes=(nmO[par],))
            padtot = min(2048, nkey)
            op("s", lambda s: s.activation(nmO[par].ap[:, 1:2], nmO[par].ap[:, 0:1],
                                           AF.Exp, bias=0.0, scale=1.0),
               reads=(nmO[par],), writes=(nmO[par],))
            op("v", lambda v, padtot=padtot: v.tensor_scalar(
                nmO[par].ap[:, 2:3], nmO[par].ap[:, 1:2], npad.ap[:, 0:1],
                float(padtot) / 1024.0, ALU.mult, ALU.mult),
                reads=(nmO[par], npad), writes=(nmO[par],))

        def a2_pass2(t):
            par = t % 2
            qb = q1t[par]
            nkey = (t + 1) * 128
            nb5 = (nkey + 511) // 512
            cnt = [0]

            def unitA(kb, g):
                w = min(512, nkey - kb * 512)
                sp = sp2_ps[g % 2]
                pb = pbuf[g % 3]
                a2_scores(sp, qb, kb, w, kb == nb5 - 1)
                op("s", lambda s, sp=sp, w=w, kb=kb, pb=pb: s.activation(
                    pb.ap[:, 0:w], sp.ap[:, 0:w], AF.Exp,
                    bias=nmO[par].ap[:, 0:1], scale=1.0,
                    accum_out=bsx[par].ap[:, kb:kb + 1]),
                    reads=(sp, nmO[par]), writes=(pb, bsx[par]))

            def unitB(kb, g):
                w = min(512, nkey - kb * 512)
                pb = pbuf[g % 3]
                pt = ptb[g % 3]
                tp = tp2_ps[0]
                ng = w // 128
                for j in range(ng):
                    op("t", lambda t_, j=j, pb=pb, tp=tp: t_.transpose(
                        tp.ap[:, j * 128:(j + 1) * 128],
                        pb.ap[:, j * 128:(j + 1) * 128], ident.ap[:]),
                        reads=(pb, ident), writes=(tp,))
                if g % 2 == 0:
                    op("s", lambda s, tp=tp, ng=ng, pt=pt: s.copy(
                        pt.ap[:, 0:ng * 128], tp.ap[:, 0:ng * 128]),
                        reads=(tp,), writes=(pt,))
                else:
                    op("v", lambda v, tp=tp, ng=ng, pt=pt: v.tensor_copy(
                        pt.ap[:, 0:ng * 128], tp.ap[:, 0:ng * 128]),
                        reads=(tp,), writes=(pt,))
                for half in range(2):
                    for gr in range(ng):
                        kt = kb * 4 + gr
                        op("t", lambda t_, gr=gr, kt=kt, half=half, pt=pt,
                           kb=kb, ng=ng: t_.matmul(
                            av_ps.ap[:, half * 512:(half + 1) * 512],
                            pt.ap[:, gr * 128:(gr + 1) * 128],
                            KV2.ap[:, kt * D + half * 512: kt * D + (half + 1) * 512],
                            start=(kb == 0 and gr == 0),
                            stop=(kb == nb5 - 1 and gr == ng - 1)),
                            reads=(pt, KV2), writes=(av_ps,))

            units = []
            LAG = 2
            for g in range(nb5 + LAG):
                if g < nb5:
                    units.append(lambda kb=g, g=g: unitA(kb, g))
                if g - LAG >= 0:
                    units.append(lambda kb=g - LAG, g=g - LAG: unitB(kb, g))
            return units

        def a2_epilogue(t):
            par = t % 2
            nkey = (t + 1) * 128
            nb5 = (nkey + 511) // 512
            sc = scrB[par]
            xb = xt2[t % 3]
            op("v", lambda v, nb5=nb5: v.reduce_sum(
                sc["ss"].ap[:, 0:1], bsx[par].ap[:, 0:nb5], axis=AX.X),
                reads=(bsx[par],), writes=(sc["ss"],))
            op("v", lambda v: v.tensor_tensor(
                sc["ss"].ap[:, 0:1], sc["ss"].ap[:, 0:1], nmO[par].ap[:, 2:3],
                ALU.subtract),
                reads=(nmO[par],), writes=(sc["ss"],))
            op("v", lambda v: v.tensor_scalar(
                sc["ss"].ap[:, 1:2], sc["ss"].ap[:, 0:1], 1e-30, None, ALU.add),
                reads=(sc["ss"],), writes=(sc["ss"],))
            op("v", lambda v: v.reciprocal(sc["ss"].ap[:, 1:2], sc["ss"].ap[:, 1:2]),
                reads=(sc["ss"],), writes=(sc["ss"],))
            xo = x1t[par]
            for half in range(2):
                op("v", lambda v, xb=xb, xo=xo, half=half: v.scalar_tensor_tensor(
                    xo.ap[:, half * 512:(half + 1) * 512],
                    av_ps.ap[:, half * 512:(half + 1) * 512],
                    sc["ss"].ap[:, 1:2],
                    xb.ap[:, half * 512:(half + 1) * 512], ALU.mult, ALU.add),
                    reads=(av_ps, sc["ss"], xb), writes=(xo,))
            op("q", lambda q, t=t, xo=xo: q.dma_start(
                out=x1_d[t * 128:(t + 1) * 128, :], in_=xo.ap[:]),
                reads=(xo,), isdma=True)

        prevt = None
        for t in range(NT):
            a2_prelude(t)
            u1 = a2_pass1(t)
            u2 = a2_pass2(prevt) if prevt is not None else []
            n1, n2 = len(u1), len(u2)
            i1 = i2 = 0
            while i1 < n1 or i2 < n2:
                if i2 < n2 and (i1 >= n1 or i2 * max(n1, 1) <= i1 * max(n2, 1)):
                    u2[i2](); i2 += 1
                else:
                    u1[i1](); i1 += 1
            a2_pass1_tail(t)
            if prevt is not None:
                a2_epilogue(prevt)
            prevt = t
            pump(2)
        for u in a2_pass2(prevt):
            u()
        a2_epilogue(prevt)

        ir.fence()
        a2.close()
        if upto in ("A2", "A2dbg"):
            _finish(k, block, ctx)
            return nc, ctx

        b1k = ExitStack()
        w2k = k.sb(NC * D, BF16, b1k)
        w2q = k.sb(NC * D, BF16, b1k)
        g1bc = k.sb(D, F32, b1k)
        x1b = [k.sb(D, F32, b1k) for _ in range(2)]
        zb = [k.sb(D, F32, b1k) for _ in range(2)]
        xnb = k.sb(D, BF16, b1k)
        xnfm = k.sb(NC * 128, BF16, b1k)
        cosb = [k.sb(D, BF16, b1k)]
        sinb = [k.sb(D, BF16, b1k)]
        rt1 = k.sb(D, F32, b1k)
        rt2 = k.sb(D, F32, b1k)
        krb = k.sb(D, BF16, b1k)
        sqd = xnb
        scrC = [mk_scr(b1k) for _ in range(2)]
        kq_ps = [k.ps(D, F32, b1k) for _ in range(2)]
        tp3_ps = [k.ps(1024, BF16, b1k) for _ in range(2)]

        load_w_cast(w2k, w_d["wk2"])
        load_w_cast(w2q, w_d["wq2"])
        op("g", lambda g: g.dma_start(out=g1bc.ap[:], in_=g1bc_d[:]),
           writes=(g1bc,), isdma=True)

        def rope(src_ps, cosb_, sinb_, out_bf):
            vsrc = src_ps.ap.rearrange("p (h j) -> p h j", j=HD)
            vcos = cosb_.ap.rearrange("p (h j) -> p h j", j=HD)
            vsin = sinb_.ap.rearrange("p (h j) -> p h j", j=HD)
            vr1 = rt1.ap.rearrange("p (h j) -> p h j", j=HD)
            vr2 = rt2.ap.rearrange("p (h j) -> p h j", j=HD)
            op("v", lambda v: v.tensor_tensor(rt1.ap[:], src_ps.ap[:], cosb_.ap[:],
                                              ALU.mult),
               reads=(src_ps, cosb_), writes=(rt1,))
            op("v", lambda v: v.tensor_tensor(vr2[:, :, 0:32], vsrc[:, :, 32:64],
                                              vsin[:, :, 0:32], ALU.mult),
               reads=(src_ps, sinb_), writes=(rt2,))
            op("v", lambda v: v.tensor_tensor(vr2[:, :, 32:64], vsrc[:, :, 0:32],
                                              vsin[:, :, 32:64], ALU.mult),
               reads=(src_ps, sinb_), writes=(rt2,))
            op("v", lambda v: v.tensor_tensor(out_bf.ap[:], rt1.ap[:], rt2.ap[:],
                                              ALU.add),
               reads=(rt1, rt2), writes=(out_bf,))

        for t in range(NT):
            xb = x1b[t % 2]
            op("q", lambda q, t=t, xb=xb: q.dma_start(
                out=xb.ap[:], in_=x1_d[t * 128:(t + 1) * 128, :]),
                writes=(xb,), isdma=True)
            cb, sb_ = cosb[0], sinb[0]
            op("g", lambda g, t=t, cb=cb: g.dma_start(
                out=cb.ap[:], in_=cosr_d[t * 128:(t + 1) * 128, :]),
                writes=(cb,), isdma=True)
            op("g", lambda g, t=t, sb_=sb_: g.dma_start(
                out=sb_.ap[:], in_=sinr_d[t * 128:(t + 1) * 128, :]),
                writes=(sb_,), isdma=True)
            ln_tile(xb, scrC[t % 2], LN_EPS, out_f32=zb[t % 2])
            rms_tile(zb[t % 2], scrC[t % 2], g1bc, sqd, xnb)
            for c in range(NC):
                p = tp3_ps[c % 2]
                op("t", lambda t_, c=c, p=p: t_.transpose(
                    p.ap[:, 0:128], xnb.ap[:, c * 128:(c + 1) * 128], ident.ap[:]),
                    reads=(xnb, ident), writes=(p,))
                op("s", lambda s, c=c, p=p: s.copy(
                    xnfm.ap[:, c * 128:(c + 1) * 128], p.ap[:, 0:128]),
                    reads=(p,), writes=(xnfm,))
            op("q", lambda q, t=t: q.dma_start(
                out=xn_d[t, :, :, :].rearrange("a p j -> p a j"),
                in_=xnfm.ap[:]),
                reads=(xnfm,), isdma=True)
            kp = kq_ps[0]
            for half in range(2):
                for c in range(NC):
                    op("t", lambda t_, c=c, half=half, kp=kp: t_.matmul(
                        kp.ap[:, half * 512:(half + 1) * 512],
                        xnfm.ap[:, c * 128:(c + 1) * 128],
                        w2k.ap[:, c * D + half * 512: c * D + (half + 1) * 512],
                        start=(c == 0), stop=(c == NC - 1)),
                        reads=(xnfm, w2k), writes=(kp,))
            rope(kp, cb, sb_, krb)
            for c in range(NC):
                p = tp3_ps[c % 2]
                op("t", lambda t_, c=c, p=p: t_.transpose(
                    p.ap[:, 0:128], krb.ap[:, c * 128:(c + 1) * 128], ident.ap[:]),
                    reads=(krb, ident), writes=(p,))
                op("s", lambda s, c=c, p=p, t=t: s.copy(
                    KV1.ap[:, c * S + t * 128: c * S + (t + 1) * 128], p.ap[:, 0:128]),
                    reads=(p,), writes=(KV1,))
            if t >= OWN0:
                qp = kq_ps[1]
                for half in range(2):
                    for c in range(NC):
                        op("t", lambda t_, c=c, half=half, qp=qp: t_.matmul(
                            qp.ap[:, half * 512:(half + 1) * 512],
                            xnfm.ap[:, c * 128:(c + 1) * 128],
                            w2q.ap[:, c * D + half * 512: c * D + (half + 1) * 512],
                            start=(c == 0), stop=(c == NC - 1)),
                            reads=(xnfm, w2q), writes=(qp,))
                rope(qp, cb, sb_, krb)
                for c in range(NC):
                    p = tp3_ps[c % 2]
                    op("t", lambda t_, c=c, p=p: t_.transpose(
                        p.ap[:, 0:128], krb.ap[:, c * 128:(c + 1) * 128], ident.ap[:]),
                        reads=(krb, ident), writes=(p,))
                    op("s", lambda s, c=c, p=p: s.activation(
                        sqd.ap[:, c * 128:(c + 1) * 128], p.ap[:, 0:128],
                        AF.Copy, bias=0.0, scale=0.125),
                        reads=(p,), writes=(sqd,))
                op("q", lambda q, t=t: q.dma_start(
                    out=q2_d[t - OWN0, :, :, :].rearrange("a p j -> p a j"),
                    in_=sqd.ap[:]),
                    reads=(sqd,), isdma=True)
            pump(2)

        ir.fence()
        b1k.close()

        b1v = ExitStack()
        w2v = k.sb(NC * D, BF16, b1v)
        xnf2 = [k.sb(NC * 128, BF16, b1v) for _ in range(2)]
        v2_ps = [k.ps(D, F32, b1v) for _ in range(2)]
        load_w_cast(w2v, w_d["wv2"])
        for t in range(NT):
            xf = xnf2[t % 2]
            op("q", lambda q, t=t, xf=xf: q.dma_start(
                out=xf.ap[:], in_=xn_d[t, :, :, :].rearrange("a p j -> p a j")),
                writes=(xf,), isdma=True)
            vp = v2_ps[t % 2]
            for half in range(2):
                for c in range(NC):
                    op("t", lambda t_, c=c, half=half, vp=vp, xf=xf: t_.matmul(
                        vp.ap[:, half * 512:(half + 1) * 512],
                        xf.ap[:, c * 128:(c + 1) * 128],
                        w2v.ap[:, c * D + half * 512: c * D + (half + 1) * 512],
                        start=(c == 0), stop=(c == NC - 1)),
                        reads=(xf, w2v), writes=(vp,))
            for half in range(2):
                op("s", lambda s, vp=vp, t=t, half=half: s.copy(
                    KV2.ap[:, t * D + half * 512: t * D + (half + 1) * 512],
                    vp.ap[:, half * 512:(half + 1) * 512]),
                    reads=(vp,), writes=(KV2,))
            pump(4)

        ir.fence()
        b1v.close()
        if upto == "B1v":
            _finish(k, block, ctx)
            return nc, ctx

        b1b = ExitStack()
        wo2 = k.sb(NC * D, BF16, b1b)
        x1c = [k.sb(D, F32, b1b) for _ in range(2)]
        q2t = [k.sb(NC * 128, BF16, b1b) for _ in range(2)]
        zb2 = [k.sb(D, F32, b1b) for _ in range(2)]
        pb2 = [k.sb(KBLK, BF16, b1b) for _ in range(3)]
        ptb2 = [k.sb(KBLK, BF16, b1b) for _ in range(3)]
        at2 = [k.sb(D, BF16, b1b)] * 2
        atf = k.sb(D, BF16, b1b)
        hb = [k.sb(D, F32, b1b) for _ in range(2)]
        rb_ = [k.sb(D, F32, b1b) for _ in range(2)]
        NB5 = NT // 4
        bms = [k.sb(H * NB5, F32, b1b) for _ in range(2)]
        bss = [k.sb(H * NB5, F32, b1b) for _ in range(2)]
        mf = [k.sb(16, F32, b1b) for _ in range(2)]
        nmf = [k.sb(16, F32, b1b) for _ in range(2)]
        e0b = [k.sb(16, F32, b1b) for _ in range(2)]
        corb = [k.sb(16, F32, b1b) for _ in range(2)]
        lsum = [k.sb(16, F32, b1b) for _ in range(2)]
        linv = [k.sb(16, F32, b1b) for _ in range(2)]
        scrD = [mk_scr(b1b) for _ in range(2)]
        sp1_ps = [k.ps(512, F32, b1b)]
        sp2_ps = [k.ps(KBLK, F32, b1b) for _ in range(2)]
        ava_ps = k.ps(D, F32, b1b)
        tp4_ps = [k.ps(1024, BF16, b1b)]

        load_w_cast(wo2, w_d["wo2"])

        def b1b_scmm(sp, qb, h, blk0, w, masked, isdiag):
            c, po = h // 2, (h % 2) * 64
            nhv = (w + 511) // 512
            for half in range(nhv):
                hw = min(512, w - half * 512)
                dg = isdiag and (half == nhv - 1)
                op("t", lambda t_, c=c, po=po, blk0=blk0, half=half, hw=hw,
                   sp=sp, qb=qb, dg=dg: t_.matmul(
                    sp.ap[:, half * 512: half * 512 + hw],
                    qb.ap[po:po + 64, c * 128:(c + 1) * 128],
                    KV1.ap[po:po + 64, c * S + blk0 + half * 512:
                           c * S + blk0 + half * 512 + hw],
                    start=True, stop=not dg),
                    reads=(qb, KV1), writes=(sp,))
                if dg:
                    off = half * 512 + hw - 128
                    op("t", lambda t_, off=off, sp=sp: t_.matmul(
                        sp.ap[:, off:off + 128], ident.ap[:], tri.ap[:],
                        start=False, stop=True),
                        reads=(ident, tri), writes=(sp,))

        def mk_pass1(t, p, qb):
            nkey = (t + 1) * 128
            nk5 = (nkey + 511) // 512
            units = []

            def unit(h, kb5, w):
                sp = sp1_ps[0]
                b1b_scmm(sp, qb, h, kb5 * 512, w, False, False)
                op("v", lambda v, sp=sp, w=w, h=h, kb5=kb5: v.reduce_max(
                    bms[p].ap[:, h * NB5 + kb5: h * NB5 + kb5 + 1],
                    sp.ap[:, 0:w], axis=AX.X),
                    reads=(sp,), writes=(bms[p],))
            for kb5 in range(nk5):
                w = min(512, nkey - kb5 * 512)
                for h in range(H):
                    units.append(lambda h=h, kb5=kb5, w=w: unit(h, kb5, w))
            return units

        def emit_pass1_tail(t, p):
            vb = bms[p].ap.rearrange("p (h kb) -> p h kb", kb=NB5)
            op("v", lambda v: v.reduce_max(mf[p].ap[:], vb[:, :, 0:NB5], axis=AX.X),
               reads=(bms[p],), writes=(mf[p],))
            op("v", lambda v: v.tensor_scalar(nmf[p].ap[:], mf[p].ap[:], -1.0, None,
                                              ALU.mult),
               reads=(mf[p],), writes=(nmf[p],))
            op("s", lambda s: s.activation(e0b[p].ap[:], mf[p].ap[:], AF.Exp,
                                           bias=0.0, scale=-1.0),
               reads=(mf[p],), writes=(e0b[p],))
            op("v", lambda v: v.tensor_scalar(corb[p].ap[:], e0b[p].ap[:],
                                              npad.ap[:, 0:1], 2.0, ALU.mult, ALU.mult),
               reads=(e0b[p], npad), writes=(corb[p],))
            op("v", lambda v: v.memset(bss[p].ap[:], 0.0), writes=(bss[p],))

        def mk_pass2(t, p, qb):
            nkey = (t + 1) * 128
            nkb = (nkey + KBLK - 1) // KBLK
            cnt = [0]

            def unitA(h, kb, w, g):
                sp = sp2_ps[g % 2]
                pb = pb2[g % 3]
                isdiag = (kb == nkb - 1)
                b1b_scmm(sp, qb, h, kb * KBLK, w, True, isdiag)
                nhv = (w + 511) // 512
                for half in range(nhv):
                    hw = min(512, w - half * 512)
                    kb5 = kb * 2 + half
                    op("s", lambda s, sp=sp, half=half, hw=hw, h=h, kb5=kb5,
                       pb=pb: s.activation(
                        pb.ap[:, half * 512: half * 512 + hw],
                        sp.ap[:, half * 512: half * 512 + hw], AF.Exp,
                        bias=nmf[p].ap[:, h:h + 1], scale=1.0,
                        accum_out=bss[p].ap[:, h * NB5 + kb5: h * NB5 + kb5 + 1]),
                        reads=(sp, nmf[p]), writes=(pb, bss[p]))

            def unitB(h, kb, w, g):
                pb = pb2[g % 3]
                pt = ptb2[g % 3]
                tp = tp4_ps[0]
                ng = w // 128
                for j in range(ng):
                    op("t", lambda t_, j=j, pb=pb, tp=tp: t_.transpose(
                        tp.ap[:, j * 128:(j + 1) * 128],
                        pb.ap[:, j * 128:(j + 1) * 128], ident.ap[:]),
                        reads=(pb, ident), writes=(tp,))
                if g % 2 == 0:
                    op("s", lambda s, tp=tp, ng=ng, pt=pt: s.copy(
                        pt.ap[:, 0:ng * 128], tp.ap[:, 0:ng * 128]),
                        reads=(tp,), writes=(pt,))
                else:
                    op("v", lambda v, tp=tp, ng=ng, pt=pt: v.tensor_copy(
                        pt.ap[:, 0:ng * 128], tp.ap[:, 0:ng * 128]),
                        reads=(tp,), writes=(pt,))
                for gr in range(ng):
                    kt = kb * 8 + gr
                    op("t", lambda t_, gr=gr, kt=kt, h=h, kb=kb, ng=ng, pt=pt: t_.matmul(
                        ava_ps.ap[:, h * 64:(h + 1) * 64],
                        pt.ap[:, gr * 128:(gr + 1) * 128],
                        KV2.ap[:, kt * D + h * 64: kt * D + (h + 1) * 64],
                        start=(kb == 0 and gr == 0),
                        stop=(kb == nkb - 1 and gr == ng - 1)),
                        reads=(pt, KV2), writes=(ava_ps,))

            hw_list = []
            for h in range(H):
                for kb in range(nkb):
                    hw_list.append((h, kb, min(KBLK, nkey - kb * KBLK)))
            n = len(hw_list)
            LAG = 2
            units = []
            for g in range(n + LAG):
                if g < n:
                    h, kb, w = hw_list[g]
                    units.append(lambda h=h, kb=kb, w=w, g=g: unitA(h, kb, w, g))
                if g - LAG >= 0:
                    h, kb, w = hw_list[g - LAG]
                    units.append(lambda h=h, kb=kb, w=w, g=g - LAG: unitB(h, kb, w, g))
            return units

        def emit_pass2_tail(t, p, xb):
            ti = t - OWN0
            vs = bss[p].ap.rearrange("p (h kb) -> p h kb", kb=NB5)
            op("v", lambda v: v.reduce_sum(lsum[p].ap[:], vs[:, :, 0:NB5], axis=AX.X),
               reads=(bss[p],), writes=(lsum[p],))
            op("v", lambda v: v.tensor_tensor(lsum[p].ap[:], lsum[p].ap[:],
                                              corb[p].ap[:], ALU.subtract),
               reads=(corb[p],), writes=(lsum[p],))
            op("v", lambda v: v.tensor_scalar(linv[p].ap[:], lsum[p].ap[:], 1e-30,
                                              None, ALU.add),
               reads=(lsum[p],), writes=(linv[p],))
            op("v", lambda v: v.reciprocal(linv[p].ap[:], linv[p].ap[:]),
               reads=(linv[p],), writes=(linv[p],))
            for h in range(H):
                op("v", lambda v, h=h: v.tensor_scalar(
                    at2[p].ap[:, h * 64:(h + 1) * 64],
                    ava_ps.ap[:, h * 64:(h + 1) * 64],
                    linv[p].ap[:, h:h + 1], None, ALU.mult),
                    reads=(ava_ps, linv[p]), writes=(at2[p],))
            tp = tp4_ps[0]
            for cb4 in range(0, NC, 4):
                for j in range(4):
                    c = cb4 + j
                    op("t", lambda t_, c=c, j=j, tp=tp: t_.transpose(
                        tp.ap[:, j * 128:(j + 1) * 128],
                        at2[p].ap[:, c * 128:(c + 1) * 128], ident.ap[:]),
                        reads=(at2[p], ident), writes=(tp,))
                op("s", lambda s, cb4=cb4, tp=tp: s.copy(
                    atf.ap[:, cb4 * 128:(cb4 + 4) * 128], tp.ap[:, 0:512]),
                    reads=(tp,), writes=(atf,))
            for c in range(NC):
                for half in range(2):
                    op("t", lambda t_, c=c, half=half: t_.matmul(
                        ava_ps.ap[:, half * 512:(half + 1) * 512],
                        atf.ap[:, c * 128:(c + 1) * 128],
                        wo2.ap[:, c * D + half * 512: c * D + (half + 1) * 512],
                        start=(c == 0), stop=(c == NC - 1)),
                        reads=(atf, wo2), writes=(ava_ps,))
            for half in range(2):
                op("v", lambda v, half=half: v.tensor_tensor(
                    hb[p].ap[:, half * 512:(half + 1) * 512],
                    zb2[p].ap[:, half * 512:(half + 1) * 512],
                    ava_ps.ap[:, half * 512:(half + 1) * 512], ALU.add),
                    reads=(zb2[p], ava_ps), writes=(hb[p],))
            op("v", lambda v, xb=xb: v.tensor_tensor(rb_[p].ap[:], xb.ap[:],
                                                     hb[p].ap[:], ALU.add),
               reads=(xb, hb[p]), writes=(rb_[p],))
            op("q", lambda q, ti=ti: q.dma_start(
                out=h_d[ti * 128:(ti + 1) * 128, :], in_=hb[p].ap[:]),
                reads=(hb[p],), isdma=True)
            op("q", lambda q, ti=ti: q.dma_start(
                out=r_d[ti * 128:(ti + 1) * 128, :], in_=rb_[p].ap[:]),
                reads=(rb_[p],), isdma=True)

        prev = None
        for t in range(OWN0, NT):
            ti = t - OWN0
            p = ti % 2
            xb = x1c[p]
            op("q", lambda q, t=t, xb=xb: q.dma_start(
                out=xb.ap[:], in_=x1_d[t * 128:(t + 1) * 128, :]),
                writes=(xb,), isdma=True)
            qb = q2t[p]
            op("q", lambda q, ti=ti, qb=qb: q.dma_start(
                out=qb.ap[:], in_=q2_d[ti, :, :, :].rearrange("a p j -> p a j")),
                writes=(qb,), isdma=True)
            ln_tile(xb, scrD[p], LN_EPS, out_f32=zb2[p])
            op("v", lambda v, p=p: v.memset(bms[p].ap[:], -1e30), writes=(bms[p],))
            u1 = mk_pass1(t, p, qb)
            u2 = mk_pass2(prev[0], prev[1], prev[2]) if prev else []
            n1, n2 = len(u1), len(u2)
            i1 = i2 = 0
            while i1 < n1 or i2 < n2:
                if i2 < n2 and (i1 >= n1 or i2 * max(n1, 1) <= i1 * max(n2, 1)):
                    u2[i2](); i2 += 1
                else:
                    u1[i1](); i1 += 1
            emit_pass1_tail(t, p)
            if prev:
                emit_pass2_tail(prev[0], prev[1], prev[3])
            prev = (t, p, qb, xb)
            pump(6)
        for u in mk_pass2(prev[0], prev[1], prev[2]):
            u()
        emit_pass2_tail(prev[0], prev[1], prev[3])

        pump_all()
        ir.fence()
        b1b.close()
        kv_scope.close()
        st_scope.close()
        if upto in ("B1b", "B1bdbg"):
            _finish(k, block, ctx)
            return nc, ctx

        b2s = ExitStack()
        b2 = ExitStack()
        xf = k.sb(NC * 2048, BF16, b2)
        comb = k.sb(16 * NE, F32, b2)
        g2bc = k.sb(D, F32, b2s)
        hb2 = [k.sb(D, F32, b2s) for _ in range(2)]
        xfb = k.sb(D, BF16, b2s)
        p8 = k.sb(NE, F32, b2s)
        r8 = {n: k.sb(8, F32, b2s) for n in
              ("m8", "nm8", "m1", "m2", "den", "dinv")}
        msk1 = k.sb(NE, F32, b2s)
        p8b = k.sb(NE, F32, b2s)
        sel8 = k.sb(NE, F32, b2s)
        scrE = mk_scr(b2s)
        rt_ps = k.ps(512, F32, b2s)
        tp5_ps = [k.ps(1024, BF16, b2s) for _ in range(2)]

        op("g", lambda g: g.dma_start(out=g2bc.ap[:], in_=g2bc_d[:]),
           writes=(g2bc,), isdma=True)
        wrb2 = k.sb(NC * NE, BF16, b2s)
        wrf = k.sb(NC * NE, F32, b2s)
        wrlo = k.sb(NC * NE, BF16, b2s)
        xff = k.sb(D, F32, b2s)
        xlo = k.sb(D, BF16, b2s)
        xlofm = k.sb(D, BF16, b2s)
        for c in range(NC):
            op("g", lambda g, c=c: g.dma_start(
                out=wrb2.ap[:, c * NE:(c + 1) * NE], in_=wr_d[c * 128:(c + 1) * 128, :]),
               writes=(wrb2,), isdma=True)
            op("q", lambda q, c=c: q.dma_start(
                out=wrf.ap[:, c * NE:(c + 1) * NE], in_=wr_d[c * 128:(c + 1) * 128, :]),
               writes=(wrf,), isdma=True)
        op("v", lambda v: v.tensor_tensor(wrlo.ap[:], wrf.ap[:], wrb2.ap[:],
                                          ALU.subtract),
           reads=(wrf, wrb2), writes=(wrlo,))

        for ti in range(16):
            hbb = hb2[ti % 2]
            op("q", lambda q, ti=ti, hbb=hbb: q.dma_start(
                out=hbb.ap[:], in_=h_d[ti * 128:(ti + 1) * 128, :]),
                writes=(hbb,), isdma=True)
            rms_tile(hbb, scrE, g2bc, xlofm, xff)
            op("v", lambda v: v.tensor_copy(xfb.ap[:], xff.ap[:]),
               reads=(xff,), writes=(xfb,))
            op("v", lambda v: v.tensor_tensor(xlo.ap[:], xff.ap[:], xfb.ap[:],
                                              ALU.subtract),
               reads=(xff, xfb), writes=(xlo,))
            for c in range(NC):
                p = tp5_ps[c % 2]
                op("t", lambda t_, c=c, p=p: t_.transpose(
                    p.ap[:, 0:128], xfb.ap[:, c * 128:(c + 1) * 128], ident.ap[:]),
                    reads=(xfb, ident), writes=(p,))
                op("s", lambda s, c=c, p=p, ti=ti: s.copy(
                    xf.ap[:, c * 2048 + ti * 128: c * 2048 + (ti + 1) * 128],
                    p.ap[:, 0:128]),
                    reads=(p,), writes=(xf,))
            for c in range(NC):
                p = tp5_ps[c % 2]
                op("t", lambda t_, c=c, p=p: t_.transpose(
                    p.ap[:, 0:128], xlo.ap[:, c * 128:(c + 1) * 128], ident.ap[:]),
                    reads=(xlo, ident), writes=(p,))
                op("s", lambda s, c=c, p=p: s.copy(
                    xlofm.ap[:, c * 128:(c + 1) * 128], p.ap[:, 0:128]),
                    reads=(p,), writes=(xlofm,))
            nmm = 0
            for c in range(NC):
                for lh, rh in (("hi", "hi"), ("hi", "lo"), ("lo", "hi")):
                    lop = (xf if lh == "hi" else xlofm)
                    lsl = (slice(c * 2048 + ti * 128, c * 2048 + (ti + 1) * 128)
                           if lh == "hi" else slice(c * 128, (c + 1) * 128))
                    rop = wrb2 if rh == "hi" else wrlo
                    op("t", lambda t_, c=c, lop=lop, lsl=lsl, rop=rop, nmm=nmm: t_.matmul(
                        rt_ps.ap[:, 0:NE], lop.ap[:, lsl],
                        rop.ap[:, c * NE:(c + 1) * NE],
                        start=(nmm == 0), stop=(nmm == 3 * NC - 1)),
                        reads=(lop, rop), writes=(rt_ps,))
                    nmm += 1
            op("v", lambda v: v.reduce_max(r8["m8"].ap[:, 0:1], rt_ps.ap[:, 0:NE], axis=AX.X),
               reads=(rt_ps,), writes=(r8["m8"],))
            op("v", lambda v: v.tensor_scalar(r8["nm8"].ap[:, 0:1], r8["m8"].ap[:, 0:1],
                                              -1.0, None, ALU.mult),
               reads=(r8["m8"],), writes=(r8["nm8"],))
            op("s", lambda s: s.activation(p8.ap[:], rt_ps.ap[:, 0:NE], AF.Exp,
                                           bias=r8["nm8"].ap[:, 0:1], scale=1.0),
               reads=(rt_ps, r8["nm8"]), writes=(p8,))
            op("v", lambda v: v.reduce_max(r8["m1"].ap[:, 0:1], p8.ap[:], axis=AX.X),
               reads=(p8,), writes=(r8["m1"],))
            op("v", lambda v: v.tensor_scalar(msk1.ap[:], p8.ap[:], r8["m1"].ap[:, 0:1],
                                              None, ALU.is_equal),
               reads=(p8, r8["m1"]), writes=(msk1,))
            op("v", lambda v: v.scalar_tensor_tensor(p8b.ap[:], msk1.ap[:], -1e30,
                                                     p8.ap[:], ALU.mult, ALU.add),
               reads=(msk1, p8), writes=(p8b,))
            op("v", lambda v: v.reduce_max(r8["m2"].ap[:, 0:1], p8b.ap[:], axis=AX.X),
               reads=(p8b,), writes=(r8["m2"],))
            op("v", lambda v: v.tensor_scalar(sel8.ap[:], p8.ap[:], r8["m2"].ap[:, 0:1],
                                              None, ALU.is_ge),
               reads=(p8, r8["m2"]), writes=(sel8,))
            op("v", lambda v: v.tensor_tensor(r8["den"].ap[:, 0:1], r8["m1"].ap[:, 0:1],
                                              r8["m2"].ap[:, 0:1], ALU.add),
               reads=(r8["m1"], r8["m2"]), writes=(r8["den"],))
            op("v", lambda v: v.reciprocal(r8["dinv"].ap[:, 0:1], r8["den"].ap[:, 0:1]),
               reads=(r8["den"],), writes=(r8["dinv"],))
            op("v", lambda v, ti=ti: v.scalar_tensor_tensor(
                comb.ap[:, ti * NE:(ti + 1) * NE], p8.ap[:], r8["dinv"].ap[:, 0:1],
                sel8.ap[:], ALU.mult, ALU.mult),
               reads=(p8, r8["dinv"], sel8), writes=(comb,))

        ir.fence()
        b2s.close()

        wdm = [k.sb(D, BF16, b2) for _ in range(NHSL)]
        wgs = [k.sb(NC * 256, BF16, b2) for _ in range(2)]
        wus = [k.sb(NC * 256, BF16, b2) for _ in range(2)]
        mall = k.sb(NHSL * 1024, BF16, b2)
        gsb = [k.sb(512, BF16, b2) for _ in range(2)]
        yac = k.sb(8 * D, F32, b2)
        rbt = [k.sb(D, F32, b2) for _ in range(2)]
        outb = [k.sb(D, F32, b2) for _ in range(2)]
        gu_ps = [k.ps(512, F32, b2) for _ in range(4)]
        dn_ps = [k.ps(512, F32, b2) for _ in range(4)]

        NHC = NHSL // 2

        for thalf in range(2):
            op("v", lambda v: v.memset(yac.ap[:], 0.0), writes=(yac,))
            for e in range(NE):
                for c in range(NHSL):
                    op("q", lambda q, e=e, c=c: q.dma_start(
                        out=wdm[c].ap[:], in_=wdb_d[e, c * 128:(c + 1) * 128, :]),
                        writes=(wdm[c],), isdma=True)
                for hc in range(NHC):
                    wgb = wgs[hc % 2]
                    wub = wus[hc % 2]
                    op("q", lambda q, e=e, hc=hc, wgb=wgb: q.dma_start(
                        out=wgb.ap[:],
                        in_=wgb_d[e, :, hc * 256:(hc + 1) * 256]
                        .rearrange("(a p) j -> p a j", p=128)),
                        writes=(wgb,), isdma=True)
                    op("q", lambda q, e=e, hc=hc, wub=wub: q.dma_start(
                        out=wub.ap[:],
                        in_=wub_d[e, :, hc * 256:(hc + 1) * 256]
                        .rearrange("(a p) j -> p a j", p=128)),
                        writes=(wub,), isdma=True)
                    for hs in range(2):
                        for c in range(NC):
                            for t5 in range(2):
                                op("t", lambda t_, c=c, hs=hs, t5=t5, wgb=wgb,
                                   thalf=thalf: t_.matmul(
                                    gu_ps[2 * t5 + 0].ap[:],
                                    wgb.ap[:, c * 256 + hs * 128: c * 256 + (hs + 1) * 128],
                                    xf.ap[:, c * 2048 + thalf * 1024 + t5 * 512:
                                          c * 2048 + thalf * 1024 + (t5 + 1) * 512],
                                    start=(c == 0), stop=(c == NC - 1)),
                                    reads=(wgb, xf), writes=(gu_ps[2 * t5 + 0],))
                        for c in range(NC):
                            for t5 in range(2):
                                op("t", lambda t_, c=c, hs=hs, t5=t5, wub=wub,
                                   thalf=thalf: t_.matmul(
                                    gu_ps[2 * t5 + 1].ap[:],
                                    wub.ap[:, c * 256 + hs * 128: c * 256 + (hs + 1) * 128],
                                    xf.ap[:, c * 2048 + thalf * 1024 + t5 * 512:
                                          c * 2048 + thalf * 1024 + (t5 + 1) * 512],
                                    start=(c == 0), stop=(c == NC - 1)),
                                    reads=(wub, xf), writes=(gu_ps[2 * t5 + 1],))
                        hsl = hc * 2 + hs
                        for t5 in range(2):
                            gs = gsb[t5]
                            op("s", lambda s, t5=t5, gs=gs: s.activation(
                                gs.ap[:], gu_ps[2 * t5 + 0].ap[:], AF.Silu,
                                bias=0.0, scale=1.0),
                                reads=(gu_ps[2 * t5 + 0],), writes=(gs,))
                            op("v", lambda v, gs=gs, hsl=hsl, t5=t5: v.tensor_tensor(
                                mall.ap[:, hsl * 1024 + t5 * 512: hsl * 1024 + (t5 + 1) * 512],
                                gs.ap[:], gu_ps[2 * t5 + 1].ap[:], ALU.mult),
                                reads=(gs, gu_ps[2 * t5 + 1]), writes=(mall,))
                for dh in range(2):
                    for hsl in range(NHSL):
                        for t128 in range(8):
                            dp = (dn_ps + gu_ps)[t128]
                            op("t", lambda t_, hsl=hsl, t128=t128, dh=dh, dp=dp: t_.matmul(
                                dp.ap[:],
                                mall.ap[:, hsl * 1024 + t128 * 128: hsl * 1024 + (t128 + 1) * 128],
                                wdm[hsl].ap[:, dh * 512:(dh + 1) * 512],
                                start=(hsl == 0), stop=(hsl == NHSL - 1)),
                                reads=(mall, wdm[hsl]), writes=(dp,))
                    for t128 in range(8):
                        dp = (dn_ps + gu_ps)[t128]
                        ci = (thalf * 8 + t128) * NE + e
                        op("v", lambda v, dp=dp, t128=t128, dh=dh, ci=ci: v.scalar_tensor_tensor(
                            yac.ap[:, t128 * D + dh * 512: t128 * D + (dh + 1) * 512],
                            dp.ap[:], comb.ap[:, ci:ci + 1],
                            yac.ap[:, t128 * D + dh * 512: t128 * D + (dh + 1) * 512],
                            ALU.mult, ALU.add),
                            reads=(dp, comb, yac), writes=(yac,))
            for t128 in range(8):
                ti = thalf * 8 + t128
                rb2 = rbt[t128 % 2]
                ob = outb[t128 % 2]
                op("q", lambda q, ti=ti, rb2=rb2: q.dma_start(
                    out=rb2.ap[:], in_=r_d[ti * 128:(ti + 1) * 128, :]),
                    writes=(rb2,), isdma=True)
                op("v", lambda v, t128=t128, rb2=rb2, ob=ob: v.tensor_tensor(
                    ob.ap[:], rb2.ap[:], yac.ap[:, t128 * D:(t128 + 1) * D], ALU.add),
                    reads=(rb2, yac), writes=(ob,))
                op("q", lambda q, ti=ti, ob=ob: q.dma_start(
                    out=out_d[ti * 128:(ti + 1) * 128, :], in_=ob.ap[:]),
                    reads=(ob,), isdma=True)

        ir.fence()
        _finish(k, block, ctx)
        return nc, ctx


def _finish(k, block, ctx):
    ir = k.ir
    ir.fence()
    ir.op("g", lambda g: g.engine_nop())
    ir.lower(k.nc, k.ctx, block)


_ROPE = {}


def _rope_tables(role):
    if role in _ROPE:
        return _ROPE[role]
    pos = np.arange(S)
    if role == 0:
        pos = np.clip(pos - 2048, 0, None)
    inv = 1.0 / (10000.0 ** (np.arange(0, HD, 2, dtype=np.float32) / HD))
    freqs = np.einsum("i,j->ij", pos.astype(np.float32), inv)
    emb = np.concatenate([freqs, freqs], axis=-1)
    cos = np.cos(emb).astype(np.float32)
    sin = np.sin(emb).astype(np.float32)
    sin_signed = np.concatenate([-sin[:, :32], sin[:, 32:]], axis=-1).astype(np.float32)
    cosr = np.ascontiguousarray(np.tile(cos, (1, H)))
    sinr = np.ascontiguousarray(np.tile(sin_signed, (1, H)))
    _ROPE[role] = (cosr, sinr)
    return _ROPE[role]


def make_in_maps(inputs):
    import ml_dtypes
    x = np.asarray(inputs["L0_attn_embeddings"], np.float32)
    wg = np.ascontiguousarray(np.pad(np.asarray(inputs["L0_ffn_wg"], np.float32),
                                     ((0, 0), (0, 0), (0, HID - 2752))))
    wu = np.ascontiguousarray(np.pad(np.asarray(inputs["L0_ffn_wu"], np.float32),
                                     ((0, 0), (0, 0), (0, HID - 2752))))
    wd = np.ascontiguousarray(np.pad(np.asarray(inputs["L0_ffn_wd"], np.float32),
                                     ((0, 0), (0, HID - 2752), (0, 0))))
    ident = np.eye(128, dtype=np.float32).astype(ml_dtypes.bfloat16)
    tri = np.triu(np.full((128, 128), NEGBIG, np.float32), k=1).astype(ml_dtypes.bfloat16)
    g1bc = np.ascontiguousarray(np.broadcast_to(
        np.asarray(inputs["L0_ffn_g1"], np.float32), (128, D)))
    g2bc = np.ascontiguousarray(np.broadcast_to(
        np.asarray(inputs["L0_ffn_g2"], np.float32), (128, D)))
    shared = dict(
        wq1=np.asarray(inputs["L0_attn_W_Q"], np.float32),
        wk1=np.asarray(inputs["L0_attn_W_K"], np.float32),
        wv1=np.asarray(inputs["L0_attn_W_V"], np.float32),
        wq2=np.asarray(inputs["L0_ffn_wq"], np.float32),
        wk2=np.asarray(inputs["L0_ffn_wk"], np.float32),
        wv2=np.asarray(inputs["L0_ffn_wv"], np.float32),
        wo2=np.asarray(inputs["L0_ffn_wo"], np.float32),
        wr=np.asarray(inputs["L0_ffn_w_router"], np.float32),
        wg=wg, wu=wu, wd=wd, ident=ident, tri=tri, g1bc=g1bc, g2bc=g2bc,
    )
    in_maps = []
    zeros = np.zeros((2048, D), np.float32)
    for c in range(8):
        b, role = c // 2, c % 2
        xv = x[b] if role == 1 else np.ascontiguousarray(
            np.concatenate([zeros, x[b][:2048]], axis=0))
        cosr, sinr = _rope_tables(role)
        m = dict(shared)
        m["consts"] = np.ascontiguousarray(
            np.broadcast_to(np.asarray(CONSTS, np.float32), (128, len(CONSTS))))
        m["xv"] = xv
        m["cosr"] = cosr
        m["sinr"] = sinr
        m["npad"] = np.full((128, 1), 1024.0 if role == 0 else 0.0, np.float32)
        in_maps.append(m)
    return in_maps


def kernel(**inputs):
    in_maps = make_in_maps(inputs)
    nc, ctx = build_program("full")
    res = run_bass_kernel_spmd(nc, in_maps, core_ids=list(range(8)))
    out = np.empty((B, S, D), np.float32)
    for c in range(8):
        b, role = c // 2, c % 2
        rows = slice(0, 2048) if role == 0 else slice(2048, 4096)
        out[b, rows] = res.results[c]["out"]
    return out


# version 07
# speedup vs baseline: 1.2014x; 1.0309x over previous
import numpy as np
from contextlib import ExitStack

import concourse.bass as bass
import concourse.mybir as mybir
from concourse.bass_utils import run_bass_kernel_spmd

F32 = mybir.dt.float32
BF16 = mybir.dt.bfloat16
AF = mybir.ActivationFunctionType
ALU = mybir.AluOpType
AX = mybir.AxisListType

B, S, D = 4, 4096, 1024
NT = S // 128
NC = D // 128
H, HD = 16, 64
HID = 2816
NHSL = HID // 128
NE = 8
OWN0 = NT // 2
KBLK = 1024
LN_EPS = 1e-5
RMS_EPS = 1e-6
NEGBIG = -1.0e9
CONSTS = (0.0, 1.0, -0.5, -1.0, LN_EPS, RMS_EPS, 1.0 / 32.0, 0.125, 1.0 / D, 2.0)

ENGS = ("q", "g", "t", "v", "s")


class Op:
    __slots__ = ("eng", "fn", "deps", "inc", "isdma", "sem_val", "drain", "idx")

    def __init__(self, eng, fn, deps, isdma, idx):
        self.eng, self.fn, self.deps, self.isdma = eng, fn, list(deps), isdma
        self.inc = isdma
        self.sem_val = None
        self.drain = False
        self.idx = idx


class Buf:
    __slots__ = ("ap", "w", "rs")

    def __init__(self, ap):
        self.ap, self.w, self.rs = ap, None, []


class IR:
    def __init__(self):
        self.ops = []
        self.fences = []

    def op(self, eng, fn, reads=(), writes=(), isdma=False):
        deps = []
        for b in reads:
            if b.w is not None:
                deps.append(b.w)
        for b in writes:
            if b.w is not None:
                deps.append(b.w)
            deps.extend(b.rs)
        o = Op(eng, fn, deps, isdma, len(self.ops))
        self.ops.append(o)
        for b in reads:
            b.rs.append(o)
        for b in writes:
            b.w = o
            b.rs = []
        return o

    def fence(self):
        last = {}
        for o in self.ops:
            last[(o.eng, o.isdma)] = o
        self.fences.append((len(self.ops), last))

    def lower(self, nc, ctx, block):
        fence_i = 0
        pend = None
        seen_after = set()
        for o in self.ops:
            while fence_i < len(self.fences) and self.fences[fence_i][0] <= o.idx:
                pend = self.fences[fence_i][1]
                seen_after = set()
                fence_i += 1
            if pend is not None and o.eng not in seen_after:
                for (e2, _dma), d in pend.items():
                    if e2 != o.eng or _dma:
                        o.deps.append(d)
                seen_after.add(o.eng)

        for o in self.ops:
            ded, seen = [], set()
            for d in o.deps:
                if id(d) in seen or d is o:
                    continue
                seen.add(id(d))
                ded.append(d)
                if d.isdma or d.eng != o.eng:
                    d.inc = True
            o.deps = ded

        recent = {e: [] for e in ENGS}
        for o in self.ops:
            if o.eng == "v" and not o.isdma:
                for d in o.deps:
                    if (not d.isdma) and d.eng == "v" and d in recent["v"]:
                        o.drain = True
            r = recent[o.eng]
            r.append(o)
            if len(r) > 2:
                r.pop(0)

        sems = {e: ctx.enter_context(nc.semaphore(f"sem_{e}")) for e in ENGS}
        dsems = {e: ctx.enter_context(nc.semaphore(f"dma_{e}")) for e in ("q", "g")}
        cnt = {e: 0 for e in ENGS}
        dcnt = {e: 0 for e in ("q", "g")}
        for o in self.ops:
            if o.isdma:
                dcnt[o.eng] += 16
                o.sem_val = dcnt[o.eng]
            elif o.inc:
                cnt[o.eng] += 1
                o.sem_val = cnt[o.eng]

        handles = {"q": block.sync, "g": block.gpsimd, "t": block.tensor,
                   "v": block.vector, "s": block.scalar}
        ir = self

        def emit_for(eng):
            def run(h):
                waited = {}
                for o in ir.ops:
                    if o.eng != eng:
                        continue
                    for d in o.deps:
                        if (not d.isdma) and d.eng == eng:
                            continue
                        sem = dsems[d.eng] if d.isdma else sems[d.eng]
                        key = (d.isdma, d.eng)
                        if waited.get(key, -1) >= d.sem_val:
                            continue
                        h.wait_ge(sem, d.sem_val)
                        waited[key] = d.sem_val
                    if o.drain:
                        h.drain()
                    inst = o.fn(h)
                    if o.isdma:
                        inst.then_inc(dsems[o.eng], 16)
                    elif o.inc:
                        inst.then_inc(sems[o.eng], 1)
            return run

        for e in ENGS:
            handles[e](emit_for(e))


class K:
    def __init__(self, nc, ctx, ir):
        self.nc, self.ctx, self.ir = nc, ctx, ir
        self.uid = 0

    def name(self, p):
        self.uid += 1
        return f"{p}{self.uid}"

    def sb(self, cols, dtype, scope=None):
        t = (scope or self.ctx).enter_context(
            self.nc.sbuf_tensor(self.name("sb"), [128, cols], dtype))
        return Buf(t.ap())

    def ps(self, cols, dtype, scope=None):
        t = (scope or self.ctx).enter_context(
            self.nc.psum_tensor(self.name("ps"), [128, cols], dtype))
        return Buf(t.ap())

    def dram(self, name, shape, dtype, kind="Internal"):
        return self.nc.dram_tensor(name, shape, dtype, kind=kind).ap()


def build_program(upto="full"):
    nc = bass.Bass("TRN2", target_bir_lowering=False)
    ctx = ExitStack()
    ir = IR()
    k = K(nc, ctx, ir)
    op = ir.op

    ein, eout = "ExternalInput", "ExternalOutput"
    xv_d = k.dram("xv", [S, D], F32, ein)
    w_d = {n: k.dram(n, [D, D], F32, ein) for n in
           ("wq1", "wk1", "wv1", "wq2", "wk2", "wv2", "wo2")}
    wr_d = k.dram("wr", [D, NE], F32, ein)
    wg_d = k.dram("wg", [NE, D, HID], F32, ein)
    wu_d = k.dram("wu", [NE, D, HID], F32, ein)
    wd_d = k.dram("wd", [NE, HID, D], F32, ein)
    g1bc_d = k.dram("g1bc", [128, D], F32, ein)
    g2bc_d = k.dram("g2bc", [128, D], F32, ein)
    cosr_d = k.dram("cosr", [S, D], F32, ein)
    sinr_d = k.dram("sinr", [S, D], F32, ein)
    ident_d = k.dram("ident", [128, 128], BF16, ein)
    tri_d = k.dram("tri", [128, 128], BF16, ein)
    npad_d = k.dram("npad", [128, 1], F32, ein)

    consts_d = k.dram("consts", [128, len(CONSTS)], F32, ein)
    out_d = k.dram("out", [S // 2, D], F32, eout)
    x1_d = k.dram("x1_scr", [S, D], F32, eout if upto == "A2" else "Internal")
    xn_d = k.dram("xn_scr", [NT, NC, 128, 128], BF16, "Internal")
    q2_d = k.dram("q2_scr", [NT // 2, NC, 128, 128], BF16, "Internal")
    h_d = k.dram("h_scr", [S // 2, D], F32, eout if upto.startswith("B1b") else "Internal")
    r_d = k.dram("r_scr", [S // 2, D], F32, eout if upto.startswith("B1b") else "Internal")
    wgb_d = k.dram("wgb_scr", [NE, D, HID], BF16, "Internal")
    wub_d = k.dram("wub_scr", [NE, D, HID], BF16, "Internal")
    wdb_d = k.dram("wdb_scr", [NE, HID, D], BF16, "Internal")

    ident = k.sb(128, BF16)
    tri = k.sb(128, BF16)
    npad = k.sb(1, F32)
    cbuf = k.sb(len(CONSTS), F32)
    for _i, _val in enumerate(CONSTS):
        nc.const_aps.register(F32, float(_val), cbuf.ap[:, _i:_i + 1])
    st_scope = ExitStack()
    stage = [k.sb(HID, BF16, st_scope)]

    def precast_pairs():
        for dsrc, ddst in ((wg_d, wgb_d), (wu_d, wub_d)):
            for e in range(NE):
                for c in range(NC):
                    yield (dsrc, ddst, e, c, HID)
        for e in range(NE):
            for c in range(NHSL):
                yield (wd_d, wdb_d, e, c, D)

    _pc = precast_pairs()
    _pc_n = [0]

    def pump(n):
        for _ in range(n):
            try:
                dsrc, ddst, e, c, w = next(_pc)
            except StopIteration:
                return
            st = stage[0]
            _pc_n[0] += 1
            op("g", lambda g, dsrc=dsrc, e=e, c=c, st=st, w=w: g.dma_start(
                out=st.ap[:, 0:w], in_=dsrc[e, c * 128:(c + 1) * 128, :]),
                writes=(st,), isdma=True)
            op("q", lambda q, ddst=ddst, e=e, c=c, st=st, w=w: q.dma_start(
                out=ddst[e, c * 128:(c + 1) * 128, :], in_=st.ap[:, 0:w]),
                reads=(st,), isdma=True)

    def pump_all():
        while True:
            before = _pc_n[0]
            pump(8)
            if _pc_n[0] == before:
                break

    def mk_scr(scope):
        return {n: k.sb(16, F32, scope) for n in
                ("st6", "mv", "negmean", "lnv", "rstd", "ss")}

    def ln_tile(x_in, scr, eps, out_f32=None, out_bf=None):
        for hh in range(2):
            op("v", lambda v, hh=hh: v.bn_stats(
                scr["st6"].ap[:, hh * 6:hh * 6 + 6],
                x_in.ap[:, hh * 512:(hh + 1) * 512]),
               reads=(x_in,), writes=(scr["st6"],))
        op("v", lambda v: v.bn_aggr(scr["mv"].ap[:, 0:2], scr["st6"].ap[:, 0:12]),
           reads=(scr["st6"],), writes=(scr["mv"],))
        op("v", lambda v: v.tensor_scalar(scr["negmean"].ap[:, 0:1],
                                          scr["mv"].ap[:, 0:1], -1.0, None, ALU.mult),
           reads=(scr["mv"],), writes=(scr["negmean"],))
        op("s", lambda s: s.activation(scr["lnv"].ap[:, 0:1], scr["mv"].ap[:, 1:2],
                                       AF.Ln, bias=float(eps), scale=1.0),
           reads=(scr["mv"],), writes=(scr["lnv"],))
        op("s", lambda s: s.activation(scr["rstd"].ap[:, 0:1], scr["lnv"].ap[:, 0:1],
                                       AF.Exp, bias=0.0, scale=-0.5),
           reads=(scr["lnv"],), writes=(scr["rstd"],))
        for out, dt in ((out_f32, F32), (out_bf, BF16)):
            if out is not None:
                op("v", lambda v, out=out: v.tensor_scalar(
                    out.ap[:], x_in.ap[:], scr["negmean"].ap[:, 0:1],
                    scr["rstd"].ap[:, 0:1], ALU.add, ALU.mult),
                   reads=(x_in, scr["negmean"], scr["rstd"]), writes=(out,))

    def rms_tile(z_f32, scr, gain_bc, sqdump, out_bf):
        op("s", lambda s: s.activation(sqdump.ap[:], z_f32.ap[:], AF.Square,
                                       bias=0.0, scale=1.0,
                                       accum_out=scr["ss"].ap[:, 0:1]),
           reads=(z_f32,), writes=(sqdump, scr["ss"]))
        op("s", lambda s: s.activation(scr["lnv"].ap[:, 0:1], scr["ss"].ap[:, 0:1],
                                       AF.Ln, bias=float(RMS_EPS), scale=1.0 / D),
           reads=(scr["ss"],), writes=(scr["lnv"],))
        op("s", lambda s: s.activation(scr["rstd"].ap[:, 0:1], scr["lnv"].ap[:, 0:1],
                                       AF.Exp, bias=0.0, scale=-0.5),
           reads=(scr["lnv"],), writes=(scr["rstd"],))
        op("v", lambda v: v.scalar_tensor_tensor(
            out_bf.ap[:], z_f32.ap[:], scr["rstd"].ap[:, 0:1], gain_bc.ap[:],
            ALU.mult, ALU.mult),
           reads=(z_f32, scr["rstd"], gain_bc), writes=(out_bf,))

    def load_w_cast(dst, dram, cols=D):
        for c in range(NC):
            op("g", lambda g, c=c, dst=dst, dram=dram, cols=cols: g.dma_start(
                out=dst.ap[:, c * cols:(c + 1) * cols],
                in_=dram[c * 128:(c + 1) * 128, :]),
                writes=(dst,), isdma=True)

    def transposes(src_bf, col_of, dst, dcol_of, n, tp_ps, evac_batch=4,
                   scale=None, dstbuf_extra=()):
        for base in range(0, n, evac_batch):
            m = min(evac_batch, n - base)
            p = tp_ps[(base // evac_batch) % 2]
            for j in range(m):
                c = base + j
                op("t", lambda t, c=c, j=j, p=p: t.transpose(
                    p.ap[:, j * 128:(j + 1) * 128],
                    src_bf.ap[:, col_of(c):col_of(c) + 128], ident.ap[:]),
                    reads=(src_bf, ident), writes=(p,))
            dc = dcol_of(base)
            if scale is None:
                op("s", lambda s, p=p, dc=dc, m=m: s.copy(
                    dst.ap[:, dc:dc + m * 128], p.ap[:, 0:m * 128]),
                    reads=(p,), writes=(dst,) + tuple(dstbuf_extra))
            else:
                op("s", lambda s, p=p, dc=dc, m=m: s.activation(
                    dst.ap[:, dc:dc + m * 128], p.ap[:, 0:m * 128], AF.Copy,
                    bias=0.0, scale=float(scale)),
                    reads=(p,), writes=(dst,) + tuple(dstbuf_extra))

    kv_scope = ExitStack()
    KV1 = k.sb(NC * S, BF16, kv_scope)
    KV2 = k.sb(NT * D, BF16, kv_scope)

    with nc.Block() as block:
        for dst, src in ((ident, ident_d), (tri, tri_d), (npad, npad_d),
                         (cbuf, consts_d)):
            op("q", lambda q, dst=dst, src=src: q.dma_start(out=dst.ap[:], in_=src[:]),
               writes=(dst,), isdma=True)
        ir.fence()

        a1 = ExitStack()
        w1k = k.sb(NC * D, BF16, a1)
        w1v = k.sb(NC * D, BF16, a1)
        xt = [k.sb(D, F32, a1) for _ in range(2)]
        lnb = [k.sb(D, BF16, a1) for _ in range(2)]
        lnfm4 = k.sb(NC * 512, BF16, a1)
        scrA = [mk_scr(a1) for _ in range(2)]
        tp_ps = [k.ps(1024, BF16, a1) for _ in range(2)]
        pj_ps = [k.ps(512, F32, a1) for _ in range(4)]

        load_w_cast(w1k, w_d["wk1"])
        load_w_cast(w1v, w_d["wv1"])

        for grp in range(NT // 4):
            for i in range(4):
                t = grp * 4 + i
                xb = xt[t % 2]
                op("q", lambda q, t=t, xb=xb: q.dma_start(
                    out=xb.ap[:], in_=xv_d[t * 128:(t + 1) * 128, :]),
                    writes=(xb,), isdma=True)
                lb = lnb[t % 2]
                ln_tile(xb, scrA[t % 2], LN_EPS, out_bf=lb)
                for c in range(NC):
                    p = tp_ps[c % 2]
                    op("t", lambda t_, c=c, p=p, lb=lb: t_.transpose(
                        p.ap[:, 0:128], lb.ap[:, c * 128:(c + 1) * 128], ident.ap[:]),
                        reads=(lb, ident), writes=(p,))
                    op("s", lambda s, c=c, i=i, p=p: s.copy(
                        lnfm4.ap[:, c * 512 + i * 128: c * 512 + (i + 1) * 128],
                        p.ap[:, 0:128]),
                        reads=(p,), writes=(lnfm4,))
                for half in range(2):
                    p = pj_ps[2 + half]
                    for c in range(NC):
                        op("t", lambda t_, c=c, half=half, i=i, p=p: t_.matmul(
                            p.ap[:],
                            lnfm4.ap[:, c * 512 + i * 128: c * 512 + (i + 1) * 128],
                            w1v.ap[:, c * D + half * 512: c * D + (half + 1) * 512],
                            start=(c == 0), stop=(c == NC - 1)),
                            reads=(lnfm4, w1v), writes=(p,))
                    op("s", lambda s, p=p, t=t, half=half: s.copy(
                        KV2.ap[:, t * D + half * 512: t * D + (half + 1) * 512],
                        p.ap[:]),
                        reads=(p,), writes=(KV2,))
            for do in range(NC):
                p = pj_ps[do % 2]
                for c in range(NC):
                    op("t", lambda t_, do=do, c=c, p=p: t_.matmul(
                        p.ap[:], w1k.ap[:, c * D + do * 128: c * D + (do + 1) * 128],
                        lnfm4.ap[:, c * 512:(c + 1) * 512],
                        start=(c == 0), stop=(c == NC - 1)),
                        reads=(w1k, lnfm4), writes=(p,))
                op("s", lambda s, p=p, do=do, grp=grp: s.copy(
                    KV1.ap[:, do * S + grp * 512: do * S + (grp + 1) * 512], p.ap[:]),
                    reads=(p,), writes=(KV1,))
            pump(2)

        ir.fence()
        a1.close()
        if upto == "A1":
            _finish(k, block, ctx)
            return nc, ctx

        a2 = ExitStack()
        w1q = k.sb(NC * D, BF16, a2)
        xt2 = [k.sb(D, F32, a2) for _ in range(3)]
        lnb2 = [k.sb(D, BF16, a2) for _ in range(2)]
        lnfm = [k.sb(NC * 128, BF16, a2) for _ in range(2)]
        q1t = [k.sb(NC * 128, BF16, a2) for _ in range(2)]
        x1t = [k.sb(D, F32, a2) for _ in range(2)]
        pbuf = [k.sb(512, BF16, a2) for _ in range(3)]
        ptb = [k.sb(512, BF16, a2) for _ in range(3)]
        NBO = NT // 4
        bmx = [k.sb(NBO, F32, a2) for _ in range(2)]
        bsx = [k.sb(NBO, F32, a2) for _ in range(2)]
        nmO = [k.sb(8, F32, a2) for _ in range(2)]
        scrB = [mk_scr(a2) for _ in range(2)]
        sp1_ps = [k.ps(512, F32, a2) for _ in range(2)]
        sp2_ps = [k.ps(512, F32, a2) for _ in range(2)]
        av_ps = k.ps(D, F32, a2)
        tp2_ps = [k.ps(1024, BF16, a2)]

        load_w_cast(w1q, w_d["wq1"])
        guc = [0, 0]

        def a2_prelude(t):
            par = t % 2
            xb = xt2[t % 3]
            op("q", lambda q, t=t, xb=xb: q.dma_start(
                out=xb.ap[:], in_=xv_d[t * 128:(t + 1) * 128, :]),
                writes=(xb,), isdma=True)
            lb = lnb2[par]
            ln_tile(xb, scrB[par], LN_EPS, out_bf=lb)
            lf = lnfm[par]
            tp = tp2_ps[0]
            for cb4 in range(0, NC, 4):
                for j in range(4):
                    c = cb4 + j
                    op("t", lambda t_, c=c, j=j, tp=tp, lb=lb: t_.transpose(
                        tp.ap[:, j * 128:(j + 1) * 128],
                        lb.ap[:, c * 128:(c + 1) * 128], ident.ap[:]),
                        reads=(lb, ident), writes=(tp,))
                op("s", lambda s, cb4=cb4, tp=tp, lf=lf: s.copy(
                    lf.ap[:, cb4 * 128:(cb4 + 4) * 128], tp.ap[:, 0:512]),
                    reads=(tp,), writes=(lf,))
            qb = q1t[par]
            for do in range(NC):
                sp = sp1_ps[do % 2]
                for c in range(NC):
                    op("t", lambda t_, do=do, c=c, sp=sp, lf=lf: t_.matmul(
                        sp.ap[:, 0:128],
                        w1q.ap[:, c * D + do * 128: c * D + (do + 1) * 128],
                        lf.ap[:, c * 128:(c + 1) * 128],
                        start=(c == 0), stop=(c == NC - 1)),
                        reads=(w1q, lf), writes=(sp,))
                op("s", lambda s, do=do, sp=sp, qb=qb: s.activation(
                    qb.ap[:, do * 128:(do + 1) * 128], sp.ap[:, 0:128],
                    AF.Copy, bias=0.0, scale=1.0 / 32.0),
                    reads=(sp,), writes=(qb,))

        def a2_scores(sp, qb, kb, w, isdiag):
            for c in range(NC):
                op("t", lambda t_, c=c, kb=kb, w=w, sp=sp, qb=qb: t_.matmul(
                    sp.ap[:, 0:w],
                    qb.ap[:, c * 128:(c + 1) * 128],
                    KV1.ap[:, c * S + kb * 512: c * S + kb * 512 + w],
                    start=(c == 0), stop=(c == NC - 1 and not isdiag)),
                    reads=(qb, KV1), writes=(sp,))
            if isdiag:
                op("t", lambda t_, w=w, sp=sp: t_.matmul(
                    sp.ap[:, w - 128:w], ident.ap[:], tri.ap[:],
                    start=False, stop=True),
                    reads=(ident, tri), writes=(sp,))

        def a2_pass1(t):
            par = t % 2
            qb = q1t[par]
            nkey = (t + 1) * 128
            nb5 = (nkey + 511) // 512
            units = []

            def unit(kb):
                w = min(512, nkey - kb * 512)
                sp = sp1_ps[guc[0] % 2]
                guc[0] += 1
                a2_scores(sp, qb, kb, w, False)
                op("v", lambda v, sp=sp, w=w, kb=kb: v.reduce_max(
                    bmx[par].ap[:, kb:kb + 1], sp.ap[:, 0:w], axis=AX.X),
                    reads=(sp,), writes=(bmx[par],))
            for kb in range(nb5):
                units.append(lambda kb=kb: unit(kb))
            return units

        def a2_pass1_tail(t):
            par = t % 2
            nkey = (t + 1) * 128
            nb5 = (nkey + 511) // 512
            op("v", lambda v, nb5=nb5: v.reduce_max(
                nmO[par].ap[:, 0:1], bmx[par].ap[:, 0:nb5], axis=AX.X, negate=True),
                reads=(bmx[par],), writes=(nmO[par],))
            padtot = min(2048, nkey)
            op("s", lambda s: s.activation(nmO[par].ap[:, 1:2], nmO[par].ap[:, 0:1],
                                           AF.Exp, bias=0.0, scale=1.0),
               reads=(nmO[par],), writes=(nmO[par],))
            op("v", lambda v, padtot=padtot: v.tensor_scalar(
                nmO[par].ap[:, 2:3], nmO[par].ap[:, 1:2], npad.ap[:, 0:1],
                float(padtot) / 1024.0, ALU.mult, ALU.mult),
                reads=(nmO[par], npad), writes=(nmO[par],))

        def a2_pass2(t):
            par = t % 2
            qb = q1t[par]
            nkey = (t + 1) * 128
            nb5 = (nkey + 511) // 512
            cnt = [0]

            def unitA(kb, g):
                w = min(512, nkey - kb * 512)
                sp = sp2_ps[g % 2]
                pb = pbuf[g % 3]
                a2_scores(sp, qb, kb, w, kb == nb5 - 1)
                op("s", lambda s, sp=sp, w=w, kb=kb, pb=pb: s.activation(
                    pb.ap[:, 0:w], sp.ap[:, 0:w], AF.Exp,
                    bias=nmO[par].ap[:, 0:1], scale=1.0,
                    accum_out=bsx[par].ap[:, kb:kb + 1]),
                    reads=(sp, nmO[par]), writes=(pb, bsx[par]))

            def unitB(kb, g):
                w = min(512, nkey - kb * 512)
                pb = pbuf[g % 3]
                pt = ptb[g % 3]
                tp = tp2_ps[0]
                ng = w // 128
                for j in range(ng):
                    op("t", lambda t_, j=j, pb=pb, tp=tp: t_.transpose(
                        tp.ap[:, j * 128:(j + 1) * 128],
                        pb.ap[:, j * 128:(j + 1) * 128], ident.ap[:]),
                        reads=(pb, ident), writes=(tp,))
                if g % 2 == 0:
                    op("s", lambda s, tp=tp, ng=ng, pt=pt: s.copy(
                        pt.ap[:, 0:ng * 128], tp.ap[:, 0:ng * 128]),
                        reads=(tp,), writes=(pt,))
                else:
                    op("v", lambda v, tp=tp, ng=ng, pt=pt: v.tensor_copy(
                        pt.ap[:, 0:ng * 128], tp.ap[:, 0:ng * 128]),
                        reads=(tp,), writes=(pt,))
                for half in range(2):
                    for gr in range(ng):
                        kt = kb * 4 + gr
                        op("t", lambda t_, gr=gr, kt=kt, half=half, pt=pt,
                           kb=kb, ng=ng: t_.matmul(
                            av_ps.ap[:, half * 512:(half + 1) * 512],
                            pt.ap[:, gr * 128:(gr + 1) * 128],
                            KV2.ap[:, kt * D + half * 512: kt * D + (half + 1) * 512],
                            start=(kb == 0 and gr == 0),
                            stop=(kb == nb5 - 1 and gr == ng - 1)),
                            reads=(pt, KV2), writes=(av_ps,))

            units = []
            LAG = 2
            for g in range(nb5 + LAG):
                if g < nb5:
                    units.append(lambda kb=g, g=g: unitA(kb, g))
                if g - LAG >= 0:
                    units.append(lambda kb=g - LAG, g=g - LAG: unitB(kb, g))
            return units

        def a2_epilogue(t):
            par = t % 2
            nkey = (t + 1) * 128
            nb5 = (nkey + 511) // 512
            sc = scrB[par]
            xb = xt2[t % 3]
            op("v", lambda v, nb5=nb5: v.reduce_sum(
                sc["ss"].ap[:, 0:1], bsx[par].ap[:, 0:nb5], axis=AX.X),
                reads=(bsx[par],), writes=(sc["ss"],))
            op("v", lambda v: v.tensor_tensor(
                sc["ss"].ap[:, 0:1], sc["ss"].ap[:, 0:1], nmO[par].ap[:, 2:3],
                ALU.subtract),
                reads=(nmO[par],), writes=(sc["ss"],))
            op("v", lambda v: v.tensor_scalar(
                sc["ss"].ap[:, 1:2], sc["ss"].ap[:, 0:1], 1e-30, None, ALU.add),
                reads=(sc["ss"],), writes=(sc["ss"],))
            op("v", lambda v: v.reciprocal(sc["ss"].ap[:, 1:2], sc["ss"].ap[:, 1:2]),
                reads=(sc["ss"],), writes=(sc["ss"],))
            xo = x1t[par]
            for half in range(2):
                op("v", lambda v, xb=xb, xo=xo, half=half: v.scalar_tensor_tensor(
                    xo.ap[:, half * 512:(half + 1) * 512],
                    av_ps.ap[:, half * 512:(half + 1) * 512],
                    sc["ss"].ap[:, 1:2],
                    xb.ap[:, half * 512:(half + 1) * 512], ALU.mult, ALU.add),
                    reads=(av_ps, sc["ss"], xb), writes=(xo,))
            op("q", lambda q, t=t, xo=xo: q.dma_start(
                out=x1_d[t * 128:(t + 1) * 128, :], in_=xo.ap[:]),
                reads=(xo,), isdma=True)

        prevt = None
        for t in range(NT):
            a2_prelude(t)
            u1 = a2_pass1(t)
            u2 = a2_pass2(prevt) if prevt is not None else []
            n1, n2 = len(u1), len(u2)
            i1 = i2 = 0
            while i1 < n1 or i2 < n2:
                if i2 < n2 and (i1 >= n1 or i2 * max(n1, 1) <= i1 * max(n2, 1)):
                    u2[i2](); i2 += 1
                else:
                    u1[i1](); i1 += 1
            a2_pass1_tail(t)
            if prevt is not None:
                a2_epilogue(prevt)
            prevt = t
            pump(2)
        for u in a2_pass2(prevt):
            u()
        a2_epilogue(prevt)

        ir.fence()
        a2.close()
        if upto in ("A2", "A2dbg"):
            _finish(k, block, ctx)
            return nc, ctx

        b1k = ExitStack()
        w2k = k.sb(NC * D, BF16, b1k)
        g1bc = k.sb(D, F32, b1k)
        x1b = [k.sb(D, F32, b1k) for _ in range(2)]
        zb = [k.sb(D, F32, b1k) for _ in range(2)]
        xnb = [k.sb(D, BF16, b1k) for _ in range(2)]
        xnfm = [k.sb(NC * 128, BF16, b1k) for _ in range(2)]
        cosb = [k.sb(D, BF16, b1k) for _ in range(2)]
        sinb = [k.sb(D, BF16, b1k) for _ in range(2)]
        rt1 = [k.sb(D, F32, b1k) for _ in range(2)]
        rt2 = [k.sb(D, F32, b1k) for _ in range(2)]
        krb = [k.sb(D, BF16, b1k) for _ in range(2)]
        scrC = [mk_scr(b1k) for _ in range(2)]
        kq_ps = [k.ps(D, F32, b1k) for _ in range(2)]
        tp3_ps = [k.ps(1024, BF16, b1k) for _ in range(2)]

        load_w_cast(w2k, w_d["wk2"])
        op("g", lambda g: g.dma_start(out=g1bc.ap[:], in_=g1bc_d[:]),
           writes=(g1bc,), isdma=True)

        def rope(par, src_ps, out_bf):
            r1, r2 = rt1[par], rt2[par]
            cb, sb_ = cosb[par], sinb[par]
            vsrc = src_ps.ap.rearrange("p (h j) -> p h j", j=HD)
            vsin = sb_.ap.rearrange("p (h j) -> p h j", j=HD)
            vr2 = r2.ap.rearrange("p (h j) -> p h j", j=HD)
            op("v", lambda v: v.tensor_tensor(r1.ap[:], src_ps.ap[:], cb.ap[:],
                                              ALU.mult),
               reads=(src_ps, cb), writes=(r1,))
            op("v", lambda v: v.tensor_tensor(vr2[:, :, 0:32], vsrc[:, :, 32:64],
                                              vsin[:, :, 0:32], ALU.mult),
               reads=(src_ps, sb_), writes=(r2,))
            op("v", lambda v: v.tensor_tensor(vr2[:, :, 32:64], vsrc[:, :, 0:32],
                                              vsin[:, :, 32:64], ALU.mult),
               reads=(src_ps, sb_), writes=(r2,))
            op("v", lambda v: v.tensor_tensor(out_bf.ap[:], r1.ap[:], r2.ap[:],
                                              ALU.add),
               reads=(r1, r2), writes=(out_bf,))

        def b1k_prep(t):
            par = t % 2
            xb = x1b[par]
            op("q", lambda q, t=t, xb=xb: q.dma_start(
                out=xb.ap[:], in_=x1_d[t * 128:(t + 1) * 128, :]),
                writes=(xb,), isdma=True)
            op("g", lambda g, t=t, par=par: g.dma_start(
                out=cosb[par].ap[:], in_=cosr_d[t * 128:(t + 1) * 128, :]),
                writes=(cosb[par],), isdma=True)
            op("g", lambda g, t=t, par=par: g.dma_start(
                out=sinb[par].ap[:], in_=sinr_d[t * 128:(t + 1) * 128, :]),
                writes=(sinb[par],), isdma=True)
            ln_tile(xb, scrC[par], LN_EPS, out_f32=zb[par])
            rms_tile(zb[par], scrC[par], g1bc, xnb[par], xnb[par])

        def b1k_xpose(t):
            par = t % 2
            tp = tp3_ps[0]
            for cb4 in range(0, NC, 4):
                for j in range(4):
                    c = cb4 + j
                    op("t", lambda t_, c=c, j=j, tp=tp, par=par: t_.transpose(
                        tp.ap[:, j * 128:(j + 1) * 128],
                        xnb[par].ap[:, c * 128:(c + 1) * 128], ident.ap[:]),
                        reads=(xnb[par], ident), writes=(tp,))
                op("s", lambda s, cb4=cb4, tp=tp, par=par: s.copy(
                    xnfm[par].ap[:, cb4 * 128:(cb4 + 4) * 128], tp.ap[:, 0:512]),
                    reads=(tp,), writes=(xnfm[par],))
            op("q", lambda q, t=t, par=par: q.dma_start(
                out=xn_d[t, :, :, :].rearrange("a p j -> p a j"),
                in_=xnfm[par].ap[:]),
                reads=(xnfm[par],), isdma=True)

        def b1k_k2(t):
            par = t % 2
            kp = kq_ps[par]
            for c in range(NC):
                for half in range(2):
                    op("t", lambda t_, c=c, half=half, kp=kp, par=par: t_.matmul(
                        kp.ap[:, half * 512:(half + 1) * 512],
                        xnfm[par].ap[:, c * 128:(c + 1) * 128],
                        w2k.ap[:, c * D + half * 512: c * D + (half + 1) * 512],
                        start=(c == 0), stop=(c == NC - 1)),
                        reads=(xnfm[par], w2k), writes=(kp,))

        def b1k_rope(t):
            par = t % 2
            rope(par, kq_ps[par], krb[par])

        def b1k_kt(t):
            par = t % 2
            tp = tp3_ps[1]
            for cb4 in range(0, NC, 4):
                for j in range(4):
                    c = cb4 + j
                    op("t", lambda t_, c=c, j=j, tp=tp, par=par: t_.transpose(
                        tp.ap[:, j * 128:(j + 1) * 128],
                        krb[par].ap[:, c * 128:(c + 1) * 128], ident.ap[:]),
                        reads=(krb[par], ident), writes=(tp,))
                for j in range(4):
                    c = cb4 + j
                    op("s", lambda s, c=c, j=j, tp=tp, t=t: s.copy(
                        KV1.ap[:, c * S + t * 128: c * S + (t + 1) * 128],
                        tp.ap[:, j * 128:(j + 1) * 128]),
                        reads=(tp,), writes=(KV1,))

        stages = [b1k_prep, b1k_xpose, b1k_k2, b1k_rope, b1k_kt]
        sched = []
        for t in range(NT + 2):
            if t < NT:
                sched += [(t, 0), (t, 1), (t, 2)]
            if t - 1 >= 0 and t - 1 < NT:
                sched += [(t - 1, 3)]
            if t - 2 >= 0 and t - 2 < NT:
                sched += [(t - 2, 4)]
        done = set()
        for t, s_i in sched:
            key = (t, s_i)
            if key in done:
                continue
            done.add(key)
            stages[s_i](t)
            if s_i == 0:
                pump(1)

        ir.fence()
        b1k.close()

        b1v = ExitStack()
        w2v = k.sb(NC * D, BF16, b1v)
        w2q = k.sb(NC * D, BF16, b1v)
        xnf2 = [k.sb(NC * 128, BF16, b1v) for _ in range(2)]
        cosb2 = [k.sb(D, BF16, b1v) for _ in range(2)]
        sinb2 = [k.sb(D, BF16, b1v) for _ in range(2)]
        rt1v = [k.sb(D, F32, b1v) for _ in range(2)]
        rt2v = [k.sb(D, F32, b1v) for _ in range(2)]
        krbv = [k.sb(D, BF16, b1v) for _ in range(2)]
        vq_ps = [k.ps(D, F32, b1v) for _ in range(2)]
        tp3b_ps = [k.ps(1024, BF16, b1v) for _ in range(2)]
        load_w_cast(w2v, w_d["wv2"])
        load_w_cast(w2q, w_d["wq2"])

        def ropev(par, src_ps, out_bf):
            r1, r2 = rt1v[par], rt2v[par]
            cb, sb_ = cosb2[par], sinb2[par]
            vsrc = src_ps.ap.rearrange("p (h j) -> p h j", j=HD)
            vsin = sb_.ap.rearrange("p (h j) -> p h j", j=HD)
            vr2 = r2.ap.rearrange("p (h j) -> p h j", j=HD)
            op("v", lambda v: v.tensor_tensor(r1.ap[:], src_ps.ap[:], cb.ap[:],
                                              ALU.mult),
               reads=(src_ps, cb), writes=(r1,))
            op("v", lambda v: v.tensor_tensor(vr2[:, :, 0:32], vsrc[:, :, 32:64],
                                              vsin[:, :, 0:32], ALU.mult),
               reads=(src_ps, sb_), writes=(r2,))
            op("v", lambda v: v.tensor_tensor(vr2[:, :, 32:64], vsrc[:, :, 0:32],
                                              vsin[:, :, 32:64], ALU.mult),
               reads=(src_ps, sb_), writes=(r2,))
            op("v", lambda v: v.tensor_tensor(out_bf.ap[:], r1.ap[:], r2.ap[:],
                                              ALU.add),
               reads=(r1, r2), writes=(out_bf,))

        def qv_load(t):
            par = t % 2
            xf = xnf2[par]
            op("q", lambda q, t=t, xf=xf: q.dma_start(
                out=xf.ap[:], in_=xn_d[t, :, :, :].rearrange("a p j -> p a j")),
                writes=(xf,), isdma=True)
            if t >= OWN0:
                op("g", lambda g, t=t, par=par: g.dma_start(
                    out=cosb2[par].ap[:], in_=cosr_d[t * 128:(t + 1) * 128, :]),
                    writes=(cosb2[par],), isdma=True)
                op("g", lambda g, t=t, par=par: g.dma_start(
                    out=sinb2[par].ap[:], in_=sinr_d[t * 128:(t + 1) * 128, :]),
                    writes=(sinb2[par],), isdma=True)

        def qv_v2(t):
            par = t % 2
            vp = vq_ps[par]
            xf = xnf2[par]
            for c in range(NC):
                for half in range(2):
                    op("t", lambda t_, c=c, half=half, vp=vp, xf=xf: t_.matmul(
                        vp.ap[:, half * 512:(half + 1) * 512],
                        xf.ap[:, c * 128:(c + 1) * 128],
                        w2v.ap[:, c * D + half * 512: c * D + (half + 1) * 512],
                        start=(c == 0), stop=(c == NC - 1)),
                        reads=(xf, w2v), writes=(vp,))
            for half in range(2):
                op("s", lambda s, vp=vp, t=t, half=half: s.copy(
                    KV2.ap[:, t * D + half * 512: t * D + (half + 1) * 512],
                    vp.ap[:, half * 512:(half + 1) * 512]),
                    reads=(vp,), writes=(KV2,))

        def qv_q2(t):
            par = t % 2
            qp = vq_ps[1 - par]
            xf = xnf2[par]
            for c in range(NC):
                for half in range(2):
                    op("t", lambda t_, c=c, half=half, qp=qp, xf=xf: t_.matmul(
                        qp.ap[:, half * 512:(half + 1) * 512],
                        xf.ap[:, c * 128:(c + 1) * 128],
                        w2q.ap[:, c * D + half * 512: c * D + (half + 1) * 512],
                        start=(c == 0), stop=(c == NC - 1)),
                        reads=(xf, w2q), writes=(qp,))
            ropev(par, qp, krbv[par])
            tp = tp3b_ps[par]
            for cb4 in range(0, NC, 4):
                for j in range(4):
                    c = cb4 + j
                    op("t", lambda t_, c=c, j=j, tp=tp, par=par: t_.transpose(
                        tp.ap[:, j * 128:(j + 1) * 128],
                        krbv[par].ap[:, c * 128:(c + 1) * 128], ident.ap[:]),
                        reads=(krbv[par], ident), writes=(tp,))
                op("s", lambda s, cb4=cb4, tp=tp, par=par: s.activation(
                    krbv[par].ap[:, cb4 * 128:(cb4 + 4) * 128], tp.ap[:, 0:512],
                    AF.Copy, bias=0.0, scale=0.125),
                    reads=(tp,), writes=(krbv[par],))
            op("q", lambda q, t=t, par=par: q.dma_start(
                out=q2_d[t - OWN0, :, :, :].rearrange("a p j -> p a j"),
                in_=krbv[par].ap[:]),
                reads=(krbv[par],), isdma=True)

        for t in range(NT):
            qv_load(t)
            if t >= 1:
                qv_v2(t - 1)
                if t - 1 >= OWN0:
                    qv_q2(t - 1)
            pump(2)
        qv_v2(NT - 1)
        qv_q2(NT - 1)

        ir.fence()
        b1v.close()
        if upto == "B1v":
            _finish(k, block, ctx)
            return nc, ctx

        b1b = ExitStack()
        wo2 = k.sb(NC * D, BF16, b1b)
        x1c = [k.sb(D, F32, b1b) for _ in range(2)]
        q2t = [k.sb(NC * 128, BF16, b1b) for _ in range(2)]
        zb2 = [k.sb(D, F32, b1b) for _ in range(2)]
        pb2 = [k.sb(KBLK, BF16, b1b) for _ in range(3)]
        ptb2 = [k.sb(KBLK, BF16, b1b) for _ in range(3)]
        at2 = [k.sb(D, BF16, b1b)] * 2
        atf = k.sb(D, BF16, b1b)
        hb = [k.sb(D, F32, b1b) for _ in range(2)]
        rb_ = [k.sb(D, F32, b1b) for _ in range(2)]
        NB5 = NT // 4
        bms = [k.sb(H * NB5, F32, b1b) for _ in range(2)]
        bss = [k.sb(H * NB5, F32, b1b) for _ in range(2)]
        mf = [k.sb(16, F32, b1b) for _ in range(2)]
        nmf = [k.sb(16, F32, b1b) for _ in range(2)]
        e0b = [k.sb(16, F32, b1b) for _ in range(2)]
        corb = [k.sb(16, F32, b1b) for _ in range(2)]
        lsum = [k.sb(16, F32, b1b) for _ in range(2)]
        linv = [k.sb(16, F32, b1b) for _ in range(2)]
        scrD = [mk_scr(b1b) for _ in range(2)]
        sp1_ps = [k.ps(512, F32, b1b)]
        sp2_ps = [k.ps(KBLK, F32, b1b) for _ in range(2)]
        ava_ps = k.ps(D, F32, b1b)
        tp4_ps = [k.ps(1024, BF16, b1b)]

        load_w_cast(wo2, w_d["wo2"])

        def b1b_scmm(sp, qb, h, blk0, w, masked, isdiag):
            c, po = h // 2, (h % 2) * 64
            nhv = (w + 511) // 512
            for half in range(nhv):
                hw = min(512, w - half * 512)
                dg = isdiag and (half == nhv - 1)
                op("t", lambda t_, c=c, po=po, blk0=blk0, half=half, hw=hw,
                   sp=sp, qb=qb, dg=dg: t_.matmul(
                    sp.ap[:, half * 512: half * 512 + hw],
                    qb.ap[po:po + 64, c * 128:(c + 1) * 128],
                    KV1.ap[po:po + 64, c * S + blk0 + half * 512:
                           c * S + blk0 + half * 512 + hw],
                    start=True, stop=not dg),
                    reads=(qb, KV1), writes=(sp,))
                if dg:
                    off = half * 512 + hw - 128
                    op("t", lambda t_, off=off, sp=sp: t_.matmul(
                        sp.ap[:, off:off + 128], ident.ap[:], tri.ap[:],
                        start=False, stop=True),
                        reads=(ident, tri), writes=(sp,))

        def mk_pass1(t, p, qb):
            nkey = (t + 1) * 128
            nk5 = (nkey + 511) // 512
            units = []

            def unit(h, kb5, w):
                sp = sp1_ps[0]
                b1b_scmm(sp, qb, h, kb5 * 512, w, False, False)
                op("v", lambda v, sp=sp, w=w, h=h, kb5=kb5: v.reduce_max(
                    bms[p].ap[:, h * NB5 + kb5: h * NB5 + kb5 + 1],
                    sp.ap[:, 0:w], axis=AX.X),
                    reads=(sp,), writes=(bms[p],))
            for kb5 in range(nk5):
                w = min(512, nkey - kb5 * 512)
                for h in range(H):
                    units.append(lambda h=h, kb5=kb5, w=w: unit(h, kb5, w))
            return units

        def emit_pass1_tail(t, p):
            vb = bms[p].ap.rearrange("p (h kb) -> p h kb", kb=NB5)
            op("v", lambda v: v.reduce_max(mf[p].ap[:], vb[:, :, 0:NB5], axis=AX.X),
               reads=(bms[p],), writes=(mf[p],))
            op("v", lambda v: v.tensor_scalar(nmf[p].ap[:], mf[p].ap[:], -1.0, None,
                                              ALU.mult),
               reads=(mf[p],), writes=(nmf[p],))
            op("s", lambda s: s.activation(e0b[p].ap[:], mf[p].ap[:], AF.Exp,
                                           bias=0.0, scale=-1.0),
               reads=(mf[p],), writes=(e0b[p],))
            op("v", lambda v: v.tensor_scalar(corb[p].ap[:], e0b[p].ap[:],
                                              npad.ap[:, 0:1], 2.0, ALU.mult, ALU.mult),
               reads=(e0b[p], npad), writes=(corb[p],))
            op("v", lambda v: v.memset(bss[p].ap[:], 0.0), writes=(bss[p],))

        def mk_pass2(t, p, qb):
            nkey = (t + 1) * 128
            nkb = (nkey + KBLK - 1) // KBLK
            cnt = [0]

            def unitA(h, kb, w, g):
                sp = sp2_ps[g % 2]
                pb = pb2[g % 3]
                isdiag = (kb == nkb - 1)
                b1b_scmm(sp, qb, h, kb * KBLK, w, True, isdiag)
                nhv = (w + 511) // 512
                for half in range(nhv):
                    hw = min(512, w - half * 512)
                    kb5 = kb * 2 + half
                    op("s", lambda s, sp=sp, half=half, hw=hw, h=h, kb5=kb5,
                       pb=pb: s.activation(
                        pb.ap[:, half * 512: half * 512 + hw],
                        sp.ap[:, half * 512: half * 512 + hw], AF.Exp,
                        bias=nmf[p].ap[:, h:h + 1], scale=1.0,
                        accum_out=bss[p].ap[:, h * NB5 + kb5: h * NB5 + kb5 + 1]),
                        reads=(sp, nmf[p]), writes=(pb, bss[p]))

            def unitB(h, kb, w, g):
                pb = pb2[g % 3]
                pt = ptb2[g % 3]
                tp = tp4_ps[0]
                ng = w // 128
                for j in range(ng):
                    op("t", lambda t_, j=j, pb=pb, tp=tp: t_.transpose(
                        tp.ap[:, j * 128:(j + 1) * 128],
                        pb.ap[:, j * 128:(j + 1) * 128], ident.ap[:]),
                        reads=(pb, ident), writes=(tp,))
                if g % 2 == 0:
                    op("s", lambda s, tp=tp, ng=ng, pt=pt: s.copy(
                        pt.ap[:, 0:ng * 128], tp.ap[:, 0:ng * 128]),
                        reads=(tp,), writes=(pt,))
                else:
                    op("v", lambda v, tp=tp, ng=ng, pt=pt: v.tensor_copy(
                        pt.ap[:, 0:ng * 128], tp.ap[:, 0:ng * 128]),
                        reads=(tp,), writes=(pt,))
                for gr in range(ng):
                    kt = kb * 8 + gr
                    op("t", lambda t_, gr=gr, kt=kt, h=h, kb=kb, ng=ng, pt=pt: t_.matmul(
                        ava_ps.ap[:, h * 64:(h + 1) * 64],
                        pt.ap[:, gr * 128:(gr + 1) * 128],
                        KV2.ap[:, kt * D + h * 64: kt * D + (h + 1) * 64],
                        start=(kb == 0 and gr == 0),
                        stop=(kb == nkb - 1 and gr == ng - 1)),
                        reads=(pt, KV2), writes=(ava_ps,))

            hw_list = []
            for h in range(H):
                for kb in range(nkb):
                    hw_list.append((h, kb, min(KBLK, nkey - kb * KBLK)))
            n = len(hw_list)
            LAG = 2
            units = []
            for g in range(n + LAG):
                if g < n:
                    h, kb, w = hw_list[g]
                    units.append(lambda h=h, kb=kb, w=w, g=g: unitA(h, kb, w, g))
                if g - LAG >= 0:
                    h, kb, w = hw_list[g - LAG]
                    units.append(lambda h=h, kb=kb, w=w, g=g - LAG: unitB(h, kb, w, g))
            return units

        def emit_pass2_tail(t, p, xb):
            ti = t - OWN0
            vs = bss[p].ap.rearrange("p (h kb) -> p h kb", kb=NB5)
            op("v", lambda v: v.reduce_sum(lsum[p].ap[:], vs[:, :, 0:NB5], axis=AX.X),
               reads=(bss[p],), writes=(lsum[p],))
            op("v", lambda v: v.tensor_tensor(lsum[p].ap[:], lsum[p].ap[:],
                                              corb[p].ap[:], ALU.subtract),
               reads=(corb[p],), writes=(lsum[p],))
            op("v", lambda v: v.tensor_scalar(linv[p].ap[:], lsum[p].ap[:], 1e-30,
                                              None, ALU.add),
               reads=(lsum[p],), writes=(linv[p],))
            op("v", lambda v: v.reciprocal(linv[p].ap[:], linv[p].ap[:]),
               reads=(linv[p],), writes=(linv[p],))
            for h in range(H):
                op("v", lambda v, h=h: v.tensor_scalar(
                    at2[p].ap[:, h * 64:(h + 1) * 64],
                    ava_ps.ap[:, h * 64:(h + 1) * 64],
                    linv[p].ap[:, h:h + 1], None, ALU.mult),
                    reads=(ava_ps, linv[p]), writes=(at2[p],))
            tp = tp4_ps[0]
            for cb4 in range(0, NC, 4):
                for j in range(4):
                    c = cb4 + j
                    op("t", lambda t_, c=c, j=j, tp=tp: t_.transpose(
                        tp.ap[:, j * 128:(j + 1) * 128],
                        at2[p].ap[:, c * 128:(c + 1) * 128], ident.ap[:]),
                        reads=(at2[p], ident), writes=(tp,))
                op("s", lambda s, cb4=cb4, tp=tp: s.copy(
                    atf.ap[:, cb4 * 128:(cb4 + 4) * 128], tp.ap[:, 0:512]),
                    reads=(tp,), writes=(atf,))
            for c in range(NC):
                for half in range(2):
                    op("t", lambda t_, c=c, half=half: t_.matmul(
                        ava_ps.ap[:, half * 512:(half + 1) * 512],
                        atf.ap[:, c * 128:(c + 1) * 128],
                        wo2.ap[:, c * D + half * 512: c * D + (half + 1) * 512],
                        start=(c == 0), stop=(c == NC - 1)),
                        reads=(atf, wo2), writes=(ava_ps,))
            for half in range(2):
                op("v", lambda v, half=half: v.tensor_tensor(
                    hb[p].ap[:, half * 512:(half + 1) * 512],
                    zb2[p].ap[:, half * 512:(half + 1) * 512],
                    ava_ps.ap[:, half * 512:(half + 1) * 512], ALU.add),
                    reads=(zb2[p], ava_ps), writes=(hb[p],))
            op("v", lambda v, xb=xb: v.tensor_tensor(rb_[p].ap[:], xb.ap[:],
                                                     hb[p].ap[:], ALU.add),
               reads=(xb, hb[p]), writes=(rb_[p],))
            op("q", lambda q, ti=ti: q.dma_start(
                out=h_d[ti * 128:(ti + 1) * 128, :], in_=hb[p].ap[:]),
                reads=(hb[p],), isdma=True)
            op("q", lambda q, ti=ti: q.dma_start(
                out=r_d[ti * 128:(ti + 1) * 128, :], in_=rb_[p].ap[:]),
                reads=(rb_[p],), isdma=True)

        prev = None
        for t in range(OWN0, NT):
            ti = t - OWN0
            p = ti % 2
            xb = x1c[p]
            op("q", lambda q, t=t, xb=xb: q.dma_start(
                out=xb.ap[:], in_=x1_d[t * 128:(t + 1) * 128, :]),
                writes=(xb,), isdma=True)
            qb = q2t[p]
            op("q", lambda q, ti=ti, qb=qb: q.dma_start(
                out=qb.ap[:], in_=q2_d[ti, :, :, :].rearrange("a p j -> p a j")),
                writes=(qb,), isdma=True)
            ln_tile(xb, scrD[p], LN_EPS, out_f32=zb2[p])
            op("v", lambda v, p=p: v.memset(bms[p].ap[:], -1e30), writes=(bms[p],))
            u1 = mk_pass1(t, p, qb)
            u2 = mk_pass2(prev[0], prev[1], prev[2]) if prev else []
            n1, n2 = len(u1), len(u2)
            i1 = i2 = 0
            while i1 < n1 or i2 < n2:
                if i2 < n2 and (i1 >= n1 or i2 * max(n1, 1) <= i1 * max(n2, 1)):
                    u2[i2](); i2 += 1
                else:
                    u1[i1](); i1 += 1
            emit_pass1_tail(t, p)
            if prev:
                emit_pass2_tail(prev[0], prev[1], prev[3])
            prev = (t, p, qb, xb)
            pump(6)
        for u in mk_pass2(prev[0], prev[1], prev[2]):
            u()
        emit_pass2_tail(prev[0], prev[1], prev[3])

        pump_all()
        ir.fence()
        b1b.close()
        kv_scope.close()
        st_scope.close()
        if upto in ("B1b", "B1bdbg"):
            _finish(k, block, ctx)
            return nc, ctx

        b2s = ExitStack()
        b2 = ExitStack()
        xf = k.sb(NC * 2048, BF16, b2)
        comb = k.sb(16 * NE, F32, b2)
        g2bc = k.sb(D, F32, b2s)
        hb2 = [k.sb(D, F32, b2s) for _ in range(2)]
        xfb = k.sb(D, BF16, b2s)
        p8 = k.sb(NE, F32, b2s)
        r8 = {n: k.sb(8, F32, b2s) for n in
              ("m8", "nm8", "m1", "m2", "den", "dinv")}
        msk1 = k.sb(NE, F32, b2s)
        p8b = k.sb(NE, F32, b2s)
        sel8 = k.sb(NE, F32, b2s)
        scrE = mk_scr(b2s)
        rt_ps = k.ps(512, F32, b2s)
        tp5_ps = [k.ps(1024, BF16, b2s) for _ in range(2)]

        op("g", lambda g: g.dma_start(out=g2bc.ap[:], in_=g2bc_d[:]),
           writes=(g2bc,), isdma=True)
        wrb2 = k.sb(NC * NE, BF16, b2s)
        wrf = k.sb(NC * NE, F32, b2s)
        wrlo = k.sb(NC * NE, BF16, b2s)
        xff = k.sb(D, F32, b2s)
        xlo = k.sb(D, BF16, b2s)
        xlofm = k.sb(D, BF16, b2s)
        for c in range(NC):
            op("g", lambda g, c=c: g.dma_start(
                out=wrb2.ap[:, c * NE:(c + 1) * NE], in_=wr_d[c * 128:(c + 1) * 128, :]),
               writes=(wrb2,), isdma=True)
            op("q", lambda q, c=c: q.dma_start(
                out=wrf.ap[:, c * NE:(c + 1) * NE], in_=wr_d[c * 128:(c + 1) * 128, :]),
               writes=(wrf,), isdma=True)
        op("v", lambda v: v.tensor_tensor(wrlo.ap[:], wrf.ap[:], wrb2.ap[:],
                                          ALU.subtract),
           reads=(wrf, wrb2), writes=(wrlo,))

        for ti in range(16):
            hbb = hb2[ti % 2]
            op("q", lambda q, ti=ti, hbb=hbb: q.dma_start(
                out=hbb.ap[:], in_=h_d[ti * 128:(ti + 1) * 128, :]),
                writes=(hbb,), isdma=True)
            rms_tile(hbb, scrE, g2bc, xlofm, xff)
            op("v", lambda v: v.tensor_copy(xfb.ap[:], xff.ap[:]),
               reads=(xff,), writes=(xfb,))
            op("v", lambda v: v.tensor_tensor(xlo.ap[:], xff.ap[:], xfb.ap[:],
                                              ALU.subtract),
               reads=(xff, xfb), writes=(xlo,))
            for c in range(NC):
                p = tp5_ps[c % 2]
                op("t", lambda t_, c=c, p=p: t_.transpose(
                    p.ap[:, 0:128], xfb.ap[:, c * 128:(c + 1) * 128], ident.ap[:]),
                    reads=(xfb, ident), writes=(p,))
                op("s", lambda s, c=c, p=p, ti=ti: s.copy(
                    xf.ap[:, c * 2048 + ti * 128: c * 2048 + (ti + 1) * 128],
                    p.ap[:, 0:128]),
                    reads=(p,), writes=(xf,))
            for c in range(NC):
                p = tp5_ps[c % 2]
                op("t", lambda t_, c=c, p=p: t_.transpose(
                    p.ap[:, 0:128], xlo.ap[:, c * 128:(c + 1) * 128], ident.ap[:]),
                    reads=(xlo, ident), writes=(p,))
                op("s", lambda s, c=c, p=p: s.copy(
                    xlofm.ap[:, c * 128:(c + 1) * 128], p.ap[:, 0:128]),
                    reads=(p,), writes=(xlofm,))
            nmm = 0
            for c in range(NC):
                for lh, rh in (("hi", "hi"), ("hi", "lo"), ("lo", "hi")):
                    lop = (xf if lh == "hi" else xlofm)
                    lsl = (slice(c * 2048 + ti * 128, c * 2048 + (ti + 1) * 128)
                           if lh == "hi" else slice(c * 128, (c + 1) * 128))
                    rop = wrb2 if rh == "hi" else wrlo
                    op("t", lambda t_, c=c, lop=lop, lsl=lsl, rop=rop, nmm=nmm: t_.matmul(
                        rt_ps.ap[:, 0:NE], lop.ap[:, lsl],
                        rop.ap[:, c * NE:(c + 1) * NE],
                        start=(nmm == 0), stop=(nmm == 3 * NC - 1)),
                        reads=(lop, rop), writes=(rt_ps,))
                    nmm += 1
            op("v", lambda v: v.reduce_max(r8["m8"].ap[:, 0:1], rt_ps.ap[:, 0:NE], axis=AX.X),
               reads=(rt_ps,), writes=(r8["m8"],))
            op("v", lambda v: v.tensor_scalar(r8["nm8"].ap[:, 0:1], r8["m8"].ap[:, 0:1],
                                              -1.0, None, ALU.mult),
               reads=(r8["m8"],), writes=(r8["nm8"],))
            op("s", lambda s: s.activation(p8.ap[:], rt_ps.ap[:, 0:NE], AF.Exp,
                                           bias=r8["nm8"].ap[:, 0:1], scale=1.0),
               reads=(rt_ps, r8["nm8"]), writes=(p8,))
            op("v", lambda v: v.reduce_max(r8["m1"].ap[:, 0:1], p8.ap[:], axis=AX.X),
               reads=(p8,), writes=(r8["m1"],))
            op("v", lambda v: v.tensor_scalar(msk1.ap[:], p8.ap[:], r8["m1"].ap[:, 0:1],
                                              None, ALU.is_equal),
               reads=(p8, r8["m1"]), writes=(msk1,))
            op("v", lambda v: v.scalar_tensor_tensor(p8b.ap[:], msk1.ap[:], -1e30,
                                                     p8.ap[:], ALU.mult, ALU.add),
               reads=(msk1, p8), writes=(p8b,))
            op("v", lambda v: v.reduce_max(r8["m2"].ap[:, 0:1], p8b.ap[:], axis=AX.X),
               reads=(p8b,), writes=(r8["m2"],))
            op("v", lambda v: v.tensor_scalar(sel8.ap[:], p8.ap[:], r8["m2"].ap[:, 0:1],
                                              None, ALU.is_ge),
               reads=(p8, r8["m2"]), writes=(sel8,))
            op("v", lambda v: v.tensor_tensor(r8["den"].ap[:, 0:1], r8["m1"].ap[:, 0:1],
                                              r8["m2"].ap[:, 0:1], ALU.add),
               reads=(r8["m1"], r8["m2"]), writes=(r8["den"],))
            op("v", lambda v: v.reciprocal(r8["dinv"].ap[:, 0:1], r8["den"].ap[:, 0:1]),
               reads=(r8["den"],), writes=(r8["dinv"],))
            op("v", lambda v, ti=ti: v.scalar_tensor_tensor(
                comb.ap[:, ti * NE:(ti + 1) * NE], p8.ap[:], r8["dinv"].ap[:, 0:1],
                sel8.ap[:], ALU.mult, ALU.mult),
               reads=(p8, r8["dinv"], sel8), writes=(comb,))

        ir.fence()
        b2s.close()

        wdm = [k.sb(D, BF16, b2) for _ in range(NHSL)]
        wgs = [k.sb(NC * 256, BF16, b2) for _ in range(2)]
        wus = [k.sb(NC * 256, BF16, b2) for _ in range(2)]
        mall = k.sb(NHSL * 1024, BF16, b2)
        gsb = [k.sb(512, BF16, b2) for _ in range(2)]
        yac = k.sb(8 * D, F32, b2)
        rbt = [k.sb(D, F32, b2) for _ in range(2)]
        outb = [k.sb(D, F32, b2) for _ in range(2)]
        gu_ps = [k.ps(512, F32, b2) for _ in range(4)]
        dn_ps = [k.ps(512, F32, b2) for _ in range(4)]

        NHC = NHSL // 2

        for thalf in range(2):
            op("v", lambda v: v.memset(yac.ap[:], 0.0), writes=(yac,))
            for e in range(NE):
                for c in range(NHSL):
                    op("q", lambda q, e=e, c=c: q.dma_start(
                        out=wdm[c].ap[:], in_=wdb_d[e, c * 128:(c + 1) * 128, :]),
                        writes=(wdm[c],), isdma=True)
                for hc in range(NHC):
                    wgb = wgs[hc % 2]
                    wub = wus[hc % 2]
                    op("q", lambda q, e=e, hc=hc, wgb=wgb: q.dma_start(
                        out=wgb.ap[:],
                        in_=wgb_d[e, :, hc * 256:(hc + 1) * 256]
                        .rearrange("(a p) j -> p a j", p=128)),
                        writes=(wgb,), isdma=True)
                    op("q", lambda q, e=e, hc=hc, wub=wub: q.dma_start(
                        out=wub.ap[:],
                        in_=wub_d[e, :, hc * 256:(hc + 1) * 256]
                        .rearrange("(a p) j -> p a j", p=128)),
                        writes=(wub,), isdma=True)
                    for hs in range(2):
                        for c in range(NC):
                            for t5 in range(2):
                                op("t", lambda t_, c=c, hs=hs, t5=t5, wgb=wgb,
                                   thalf=thalf: t_.matmul(
                                    gu_ps[2 * t5 + 0].ap[:],
                                    wgb.ap[:, c * 256 + hs * 128: c * 256 + (hs + 1) * 128],
                                    xf.ap[:, c * 2048 + thalf * 1024 + t5 * 512:
                                          c * 2048 + thalf * 1024 + (t5 + 1) * 512],
                                    start=(c == 0), stop=(c == NC - 1)),
                                    reads=(wgb, xf), writes=(gu_ps[2 * t5 + 0],))
                        for c in range(NC):
                            for t5 in range(2):
                                op("t", lambda t_, c=c, hs=hs, t5=t5, wub=wub,
                                   thalf=thalf: t_.matmul(
                                    gu_ps[2 * t5 + 1].ap[:],
                                    wub.ap[:, c * 256 + hs * 128: c * 256 + (hs + 1) * 128],
                                    xf.ap[:, c * 2048 + thalf * 1024 + t5 * 512:
                                          c * 2048 + thalf * 1024 + (t5 + 1) * 512],
                                    start=(c == 0), stop=(c == NC - 1)),
                                    reads=(wub, xf), writes=(gu_ps[2 * t5 + 1],))
                        hsl = hc * 2 + hs
                        for t5 in range(2):
                            gs = gsb[t5]
                            op("s", lambda s, t5=t5, gs=gs: s.activation(
                                gs.ap[:], gu_ps[2 * t5 + 0].ap[:], AF.Silu,
                                bias=0.0, scale=1.0),
                                reads=(gu_ps[2 * t5 + 0],), writes=(gs,))
                            op("v", lambda v, gs=gs, hsl=hsl, t5=t5: v.tensor_tensor(
                                mall.ap[:, hsl * 1024 + t5 * 512: hsl * 1024 + (t5 + 1) * 512],
                                gs.ap[:], gu_ps[2 * t5 + 1].ap[:], ALU.mult),
                                reads=(gs, gu_ps[2 * t5 + 1]), writes=(mall,))
                for dh in range(2):
                    for hsl in range(NHSL):
                        for t128 in range(8):
                            dp = (dn_ps + gu_ps)[t128]
                            op("t", lambda t_, hsl=hsl, t128=t128, dh=dh, dp=dp: t_.matmul(
                                dp.ap[:],
                                mall.ap[:, hsl * 1024 + t128 * 128: hsl * 1024 + (t128 + 1) * 128],
                                wdm[hsl].ap[:, dh * 512:(dh + 1) * 512],
                                start=(hsl == 0), stop=(hsl == NHSL - 1)),
                                reads=(mall, wdm[hsl]), writes=(dp,))
                    for t128 in range(8):
                        dp = (dn_ps + gu_ps)[t128]
                        ci = (thalf * 8 + t128) * NE + e
                        op("v", lambda v, dp=dp, t128=t128, dh=dh, ci=ci: v.scalar_tensor_tensor(
                            yac.ap[:, t128 * D + dh * 512: t128 * D + (dh + 1) * 512],
                            dp.ap[:], comb.ap[:, ci:ci + 1],
                            yac.ap[:, t128 * D + dh * 512: t128 * D + (dh + 1) * 512],
                            ALU.mult, ALU.add),
                            reads=(dp, comb, yac), writes=(yac,))
            for t128 in range(8):
                ti = thalf * 8 + t128
                rb2 = rbt[t128 % 2]
                ob = outb[t128 % 2]
                op("q", lambda q, ti=ti, rb2=rb2: q.dma_start(
                    out=rb2.ap[:], in_=r_d[ti * 128:(ti + 1) * 128, :]),
                    writes=(rb2,), isdma=True)
                op("v", lambda v, t128=t128, rb2=rb2, ob=ob: v.tensor_tensor(
                    ob.ap[:], rb2.ap[:], yac.ap[:, t128 * D:(t128 + 1) * D], ALU.add),
                    reads=(rb2, yac), writes=(ob,))
                op("q", lambda q, ti=ti, ob=ob: q.dma_start(
                    out=out_d[ti * 128:(ti + 1) * 128, :], in_=ob.ap[:]),
                    reads=(ob,), isdma=True)

        ir.fence()
        _finish(k, block, ctx)
        return nc, ctx


def _finish(k, block, ctx):
    ir = k.ir
    ir.fence()
    ir.op("g", lambda g: g.engine_nop())
    ir.lower(k.nc, k.ctx, block)


_ROPE = {}


def _rope_tables(role):
    if role in _ROPE:
        return _ROPE[role]
    pos = np.arange(S)
    if role == 0:
        pos = np.clip(pos - 2048, 0, None)
    inv = 1.0 / (10000.0 ** (np.arange(0, HD, 2, dtype=np.float32) / HD))
    freqs = np.einsum("i,j->ij", pos.astype(np.float32), inv)
    emb = np.concatenate([freqs, freqs], axis=-1)
    cos = np.cos(emb).astype(np.float32)
    sin = np.sin(emb).astype(np.float32)
    sin_signed = np.concatenate([-sin[:, :32], sin[:, 32:]], axis=-1).astype(np.float32)
    cosr = np.ascontiguousarray(np.tile(cos, (1, H)))
    sinr = np.ascontiguousarray(np.tile(sin_signed, (1, H)))
    _ROPE[role] = (cosr, sinr)
    return _ROPE[role]


def make_in_maps(inputs):
    import ml_dtypes
    x = np.asarray(inputs["L0_attn_embeddings"], np.float32)
    wg = np.ascontiguousarray(np.pad(np.asarray(inputs["L0_ffn_wg"], np.float32),
                                     ((0, 0), (0, 0), (0, HID - 2752))))
    wu = np.ascontiguousarray(np.pad(np.asarray(inputs["L0_ffn_wu"], np.float32),
                                     ((0, 0), (0, 0), (0, HID - 2752))))
    wd = np.ascontiguousarray(np.pad(np.asarray(inputs["L0_ffn_wd"], np.float32),
                                     ((0, 0), (0, HID - 2752), (0, 0))))
    ident = np.eye(128, dtype=np.float32).astype(ml_dtypes.bfloat16)
    tri = np.triu(np.full((128, 128), NEGBIG, np.float32), k=1).astype(ml_dtypes.bfloat16)
    g1bc = np.ascontiguousarray(np.broadcast_to(
        np.asarray(inputs["L0_ffn_g1"], np.float32), (128, D)))
    g2bc = np.ascontiguousarray(np.broadcast_to(
        np.asarray(inputs["L0_ffn_g2"], np.float32), (128, D)))
    shared = dict(
        wq1=np.asarray(inputs["L0_attn_W_Q"], np.float32),
        wk1=np.asarray(inputs["L0_attn_W_K"], np.float32),
        wv1=np.asarray(inputs["L0_attn_W_V"], np.float32),
        wq2=np.asarray(inputs["L0_ffn_wq"], np.float32),
        wk2=np.asarray(inputs["L0_ffn_wk"], np.float32),
        wv2=np.asarray(inputs["L0_ffn_wv"], np.float32),
        wo2=np.asarray(inputs["L0_ffn_wo"], np.float32),
        wr=np.asarray(inputs["L0_ffn_w_router"], np.float32),
        wg=wg, wu=wu, wd=wd, ident=ident, tri=tri, g1bc=g1bc, g2bc=g2bc,
    )
    in_maps = []
    zeros = np.zeros((2048, D), np.float32)
    for c in range(8):
        b, role = c // 2, c % 2
        xv = x[b] if role == 1 else np.ascontiguousarray(
            np.concatenate([zeros, x[b][:2048]], axis=0))
        cosr, sinr = _rope_tables(role)
        m = dict(shared)
        m["consts"] = np.ascontiguousarray(
            np.broadcast_to(np.asarray(CONSTS, np.float32), (128, len(CONSTS))))
        m["xv"] = xv
        m["cosr"] = cosr
        m["sinr"] = sinr
        m["npad"] = np.full((128, 1), 1024.0 if role == 0 else 0.0, np.float32)
        in_maps.append(m)
    return in_maps


def kernel(**inputs):
    in_maps = make_in_maps(inputs)
    nc, ctx = build_program("full")
    res = run_bass_kernel_spmd(nc, in_maps, core_ids=list(range(8)))
    out = np.empty((B, S, D), np.float32)
    for c in range(8):
        b, role = c // 2, c % 2
        rows = slice(0, 2048) if role == 0 else slice(2048, 4096)
        out[b, rows] = res.results[c]["out"]
    return out


# version 08
# speedup vs baseline: 1.2936x; 1.0767x over previous
import numpy as np
from contextlib import ExitStack

import concourse.bass as bass
import concourse.mybir as mybir
from concourse.bass_utils import run_bass_kernel_spmd

F32 = mybir.dt.float32
BF16 = mybir.dt.bfloat16
AF = mybir.ActivationFunctionType
ALU = mybir.AluOpType
AX = mybir.AxisListType

B, S, D = 4, 4096, 1024
NT = S // 128
NC = D // 128
H, HD = 16, 64
HID = 2816
NHSL = HID // 128
NE = 8
OWN0 = NT // 2
KBLK = 1024
LN_EPS = 1e-5
RMS_EPS = 1e-6
NEGBIG = -1.0e9
CONSTS = (0.0, 1.0, -0.5, -1.0, LN_EPS, RMS_EPS, 1.0 / 32.0, 0.125, 1.0 / D, 2.0)

ENGS = ("q", "g", "t", "v", "s")


class Op:
    __slots__ = ("eng", "fn", "deps", "inc", "isdma", "sem_val", "drain", "idx")

    def __init__(self, eng, fn, deps, isdma, idx):
        self.eng, self.fn, self.deps, self.isdma = eng, fn, list(deps), isdma
        self.inc = isdma
        self.sem_val = None
        self.drain = False
        self.idx = idx


class Buf:
    __slots__ = ("ap", "w", "rs")

    def __init__(self, ap):
        self.ap, self.w, self.rs = ap, None, []


class IR:
    def __init__(self):
        self.ops = []
        self.fences = []

    def op(self, eng, fn, reads=(), writes=(), isdma=False):
        deps = []
        for b in reads:
            if b.w is not None:
                deps.append(b.w)
        for b in writes:
            if b.w is not None:
                deps.append(b.w)
            deps.extend(b.rs)
        o = Op(eng, fn, deps, isdma, len(self.ops))
        self.ops.append(o)
        for b in reads:
            b.rs.append(o)
        for b in writes:
            b.w = o
            b.rs = []
        return o

    def fence(self):
        last = {}
        for o in self.ops:
            last[(o.eng, o.isdma)] = o
        self.fences.append((len(self.ops), last))

    def lower(self, nc, ctx, block):
        fence_i = 0
        pend = None
        seen_after = set()
        for o in self.ops:
            while fence_i < len(self.fences) and self.fences[fence_i][0] <= o.idx:
                pend = self.fences[fence_i][1]
                seen_after = set()
                fence_i += 1
            if pend is not None and o.eng not in seen_after:
                for (e2, _dma), d in pend.items():
                    if e2 != o.eng or _dma:
                        o.deps.append(d)
                seen_after.add(o.eng)

        for o in self.ops:
            ded, seen = [], set()
            for d in o.deps:
                if id(d) in seen or d is o:
                    continue
                seen.add(id(d))
                ded.append(d)
                if d.isdma or d.eng != o.eng:
                    d.inc = True
            o.deps = ded

        recent = {e: [] for e in ENGS}
        for o in self.ops:
            if o.eng == "v" and not o.isdma:
                for d in o.deps:
                    if (not d.isdma) and d.eng == "v" and d in recent["v"]:
                        o.drain = True
            r = recent[o.eng]
            r.append(o)
            if len(r) > 2:
                r.pop(0)

        sems = {e: ctx.enter_context(nc.semaphore(f"sem_{e}")) for e in ENGS}
        dsems = {e: ctx.enter_context(nc.semaphore(f"dma_{e}")) for e in ("q", "g")}
        cnt = {e: 0 for e in ENGS}
        dcnt = {e: 0 for e in ("q", "g")}
        for o in self.ops:
            if o.isdma:
                dcnt[o.eng] += 16
                o.sem_val = dcnt[o.eng]
            elif o.inc:
                cnt[o.eng] += 1
                o.sem_val = cnt[o.eng]

        handles = {"q": block.sync, "g": block.gpsimd, "t": block.tensor,
                   "v": block.vector, "s": block.scalar}
        ir = self

        def emit_for(eng):
            def run(h):
                waited = {}
                for o in ir.ops:
                    if o.eng != eng:
                        continue
                    for d in o.deps:
                        if (not d.isdma) and d.eng == eng:
                            continue
                        sem = dsems[d.eng] if d.isdma else sems[d.eng]
                        key = (d.isdma, d.eng)
                        if waited.get(key, -1) >= d.sem_val:
                            continue
                        h.wait_ge(sem, d.sem_val)
                        waited[key] = d.sem_val
                    if o.drain:
                        h.drain()
                    inst = o.fn(h)
                    if o.isdma:
                        inst.then_inc(dsems[o.eng], 16)
                    elif o.inc:
                        inst.then_inc(sems[o.eng], 1)
            return run

        for e in ENGS:
            handles[e](emit_for(e))


class K:
    def __init__(self, nc, ctx, ir):
        self.nc, self.ctx, self.ir = nc, ctx, ir
        self.uid = 0

    def name(self, p):
        self.uid += 1
        return f"{p}{self.uid}"

    def sb(self, cols, dtype, scope=None):
        t = (scope or self.ctx).enter_context(
            self.nc.sbuf_tensor(self.name("sb"), [128, cols], dtype))
        return Buf(t.ap())

    def ps(self, cols, dtype, scope=None):
        t = (scope or self.ctx).enter_context(
            self.nc.psum_tensor(self.name("ps"), [128, cols], dtype))
        return Buf(t.ap())

    def dram(self, name, shape, dtype, kind="Internal"):
        return self.nc.dram_tensor(name, shape, dtype, kind=kind).ap()


def build_program(upto="full"):
    nc = bass.Bass("TRN2", target_bir_lowering=False)
    ctx = ExitStack()
    ir = IR()
    k = K(nc, ctx, ir)
    op = ir.op

    ein, eout = "ExternalInput", "ExternalOutput"
    xv_d = k.dram("xv", [S, D], F32, ein)
    w_d = {n: k.dram(n, [D, D], F32, ein) for n in
           ("wq1", "wk1", "wv1", "wq2", "wk2", "wv2", "wo2")}
    wr_d = k.dram("wr", [D, NE], F32, ein)
    wg_d = k.dram("wg", [NE, D, HID], F32, ein)
    wu_d = k.dram("wu", [NE, D, HID], F32, ein)
    wd_d = k.dram("wd", [NE, HID, D], F32, ein)
    g1bc_d = k.dram("g1bc", [128, D], F32, ein)
    g2bc_d = k.dram("g2bc", [128, D], F32, ein)
    cosr_d = k.dram("cosr", [S, D], F32, ein)
    sinr_d = k.dram("sinr", [S, D], F32, ein)
    ident_d = k.dram("ident", [128, 128], BF16, ein)
    tri_d = k.dram("tri", [128, 128], BF16, ein)
    npad_d = k.dram("npad", [128, 1], F32, ein)

    consts_d = k.dram("consts", [128, len(CONSTS)], F32, ein)
    out_d = k.dram("out", [S // 2, D], F32, eout)
    x1_d = k.dram("x1_scr", [S, D], F32, eout if upto == "A2" else "Internal")
    xn_d = k.dram("xn_scr", [NT, NC, 128, 128], BF16, "Internal")
    q2_d = k.dram("q2_scr", [NT // 2, NC, 128, 128], BF16, "Internal")
    h_d = k.dram("h_scr", [S // 2, D], F32, eout if upto.startswith("B1b") else "Internal")
    r_d = k.dram("r_scr", [S // 2, D], F32, eout if upto.startswith("B1b") else "Internal")
    wgb_d = k.dram("wgb_scr", [NE, D, HID], BF16, "Internal")
    wub_d = k.dram("wub_scr", [NE, D, HID], BF16, "Internal")
    wdb_d = k.dram("wdb_scr", [NE, HID, D], BF16, "Internal")

    ident = k.sb(128, BF16)
    tri = k.sb(128, BF16)
    npad = k.sb(1, F32)
    cbuf = k.sb(len(CONSTS), F32)
    for _i, _val in enumerate(CONSTS):
        nc.const_aps.register(F32, float(_val), cbuf.ap[:, _i:_i + 1])
    st_scope = ExitStack()
    stage = [k.sb(HID, BF16, st_scope)]

    def precast_pairs():
        for dsrc, ddst in ((wg_d, wgb_d), (wu_d, wub_d)):
            for e in range(NE):
                for c in range(NC):
                    yield (dsrc, ddst, e, c, HID)
        for e in range(NE):
            for c in range(NHSL):
                yield (wd_d, wdb_d, e, c, D)

    _pc = precast_pairs()
    _pc_n = [0]

    def pump(n):
        for _ in range(n):
            try:
                dsrc, ddst, e, c, w = next(_pc)
            except StopIteration:
                return
            st = stage[0]
            _pc_n[0] += 1
            op("g", lambda g, dsrc=dsrc, e=e, c=c, st=st, w=w: g.dma_start(
                out=st.ap[:, 0:w], in_=dsrc[e, c * 128:(c + 1) * 128, :]),
                writes=(st,), isdma=True)
            op("q", lambda q, ddst=ddst, e=e, c=c, st=st, w=w: q.dma_start(
                out=ddst[e, c * 128:(c + 1) * 128, :], in_=st.ap[:, 0:w]),
                reads=(st,), isdma=True)

    def pump_all():
        while True:
            before = _pc_n[0]
            pump(8)
            if _pc_n[0] == before:
                break

    def mk_scr(scope):
        return {n: k.sb(16, F32, scope) for n in
                ("st6", "mv", "negmean", "lnv", "rstd", "ss")}

    def ln_tile(x_in, scr, eps, out_f32=None, out_bf=None):
        for hh in range(2):
            op("v", lambda v, hh=hh: v.bn_stats(
                scr["st6"].ap[:, hh * 6:hh * 6 + 6],
                x_in.ap[:, hh * 512:(hh + 1) * 512]),
               reads=(x_in,), writes=(scr["st6"],))
        op("v", lambda v: v.bn_aggr(scr["mv"].ap[:, 0:2], scr["st6"].ap[:, 0:12]),
           reads=(scr["st6"],), writes=(scr["mv"],))
        op("v", lambda v: v.tensor_scalar(scr["negmean"].ap[:, 0:1],
                                          scr["mv"].ap[:, 0:1], -1.0, None, ALU.mult),
           reads=(scr["mv"],), writes=(scr["negmean"],))
        op("s", lambda s: s.activation(scr["lnv"].ap[:, 0:1], scr["mv"].ap[:, 1:2],
                                       AF.Ln, bias=float(eps), scale=1.0),
           reads=(scr["mv"],), writes=(scr["lnv"],))
        op("s", lambda s: s.activation(scr["rstd"].ap[:, 0:1], scr["lnv"].ap[:, 0:1],
                                       AF.Exp, bias=0.0, scale=-0.5),
           reads=(scr["lnv"],), writes=(scr["rstd"],))
        for out, dt in ((out_f32, F32), (out_bf, BF16)):
            if out is not None:
                op("v", lambda v, out=out: v.tensor_scalar(
                    out.ap[:], x_in.ap[:], scr["negmean"].ap[:, 0:1],
                    scr["rstd"].ap[:, 0:1], ALU.add, ALU.mult),
                   reads=(x_in, scr["negmean"], scr["rstd"]), writes=(out,))

    def rms_tile(z_f32, scr, gain_bc, sqdump, out_bf):
        op("s", lambda s: s.activation(sqdump.ap[:], z_f32.ap[:], AF.Square,
                                       bias=0.0, scale=1.0,
                                       accum_out=scr["ss"].ap[:, 0:1]),
           reads=(z_f32,), writes=(sqdump, scr["ss"]))
        op("s", lambda s: s.activation(scr["lnv"].ap[:, 0:1], scr["ss"].ap[:, 0:1],
                                       AF.Ln, bias=float(RMS_EPS), scale=1.0 / D),
           reads=(scr["ss"],), writes=(scr["lnv"],))
        op("s", lambda s: s.activation(scr["rstd"].ap[:, 0:1], scr["lnv"].ap[:, 0:1],
                                       AF.Exp, bias=0.0, scale=-0.5),
           reads=(scr["lnv"],), writes=(scr["rstd"],))
        op("v", lambda v: v.scalar_tensor_tensor(
            out_bf.ap[:], z_f32.ap[:], scr["rstd"].ap[:, 0:1], gain_bc.ap[:],
            ALU.mult, ALU.mult),
           reads=(z_f32, scr["rstd"], gain_bc), writes=(out_bf,))

    def load_w_cast(dst, dram, cols=D):
        for c in range(NC):
            op("g", lambda g, c=c, dst=dst, dram=dram, cols=cols: g.dma_start(
                out=dst.ap[:, c * cols:(c + 1) * cols],
                in_=dram[c * 128:(c + 1) * 128, :]),
                writes=(dst,), isdma=True)

    def transposes(src_bf, col_of, dst, dcol_of, n, tp_ps, evac_batch=4,
                   scale=None, dstbuf_extra=()):
        for base in range(0, n, evac_batch):
            m = min(evac_batch, n - base)
            p = tp_ps[(base // evac_batch) % 2]
            for j in range(m):
                c = base + j
                op("t", lambda t, c=c, j=j, p=p: t.transpose(
                    p.ap[:, j * 128:(j + 1) * 128],
                    src_bf.ap[:, col_of(c):col_of(c) + 128], ident.ap[:]),
                    reads=(src_bf, ident), writes=(p,))
            dc = dcol_of(base)
            if scale is None:
                op("s", lambda s, p=p, dc=dc, m=m: s.copy(
                    dst.ap[:, dc:dc + m * 128], p.ap[:, 0:m * 128]),
                    reads=(p,), writes=(dst,) + tuple(dstbuf_extra))
            else:
                op("s", lambda s, p=p, dc=dc, m=m: s.activation(
                    dst.ap[:, dc:dc + m * 128], p.ap[:, 0:m * 128], AF.Copy,
                    bias=0.0, scale=float(scale)),
                    reads=(p,), writes=(dst,) + tuple(dstbuf_extra))

    kv_scope = ExitStack()
    KV1 = k.sb(NC * S, BF16, kv_scope)
    KV2 = k.sb(NT * D, BF16, kv_scope)

    with nc.Block() as block:
        for dst, src in ((ident, ident_d), (tri, tri_d), (npad, npad_d),
                         (cbuf, consts_d)):
            op("q", lambda q, dst=dst, src=src: q.dma_start(out=dst.ap[:], in_=src[:]),
               writes=(dst,), isdma=True)
        ir.fence()

        a1 = ExitStack()
        w1k = k.sb(NC * D, BF16, a1)
        w1v = k.sb(NC * D, BF16, a1)
        xt = [k.sb(D, F32, a1) for _ in range(2)]
        lnb = [k.sb(D, BF16, a1) for _ in range(2)]
        lnfm4 = k.sb(NC * 512, BF16, a1)
        scrA = [mk_scr(a1) for _ in range(2)]
        tp_ps = [k.ps(1024, BF16, a1) for _ in range(2)]
        pj_ps = [k.ps(512, F32, a1) for _ in range(4)]

        load_w_cast(w1k, w_d["wk1"])
        load_w_cast(w1v, w_d["wv1"])

        for grp in range(NT // 4):
            for i in range(4):
                t = grp * 4 + i
                xb = xt[t % 2]
                op("q", lambda q, t=t, xb=xb: q.dma_start(
                    out=xb.ap[:], in_=xv_d[t * 128:(t + 1) * 128, :]),
                    writes=(xb,), isdma=True)
                lb = lnb[t % 2]
                ln_tile(xb, scrA[t % 2], LN_EPS, out_bf=lb)
                for c in range(NC):
                    p = tp_ps[c % 2]
                    op("t", lambda t_, c=c, p=p, lb=lb: t_.transpose(
                        p.ap[:, 0:128], lb.ap[:, c * 128:(c + 1) * 128], ident.ap[:]),
                        reads=(lb, ident), writes=(p,))
                    op("s", lambda s, c=c, i=i, p=p: s.copy(
                        lnfm4.ap[:, c * 512 + i * 128: c * 512 + (i + 1) * 128],
                        p.ap[:, 0:128]),
                        reads=(p,), writes=(lnfm4,))
                for half in range(2):
                    p = pj_ps[2 + half]
                    for c in range(NC):
                        op("t", lambda t_, c=c, half=half, i=i, p=p: t_.matmul(
                            p.ap[:],
                            lnfm4.ap[:, c * 512 + i * 128: c * 512 + (i + 1) * 128],
                            w1v.ap[:, c * D + half * 512: c * D + (half + 1) * 512],
                            start=(c == 0), stop=(c == NC - 1)),
                            reads=(lnfm4, w1v), writes=(p,))
                    op("s", lambda s, p=p, t=t, half=half: s.copy(
                        KV2.ap[:, t * D + half * 512: t * D + (half + 1) * 512],
                        p.ap[:]),
                        reads=(p,), writes=(KV2,))
            for do in range(NC):
                p = pj_ps[do % 2]
                for c in range(NC):
                    op("t", lambda t_, do=do, c=c, p=p: t_.matmul(
                        p.ap[:], w1k.ap[:, c * D + do * 128: c * D + (do + 1) * 128],
                        lnfm4.ap[:, c * 512:(c + 1) * 512],
                        start=(c == 0), stop=(c == NC - 1)),
                        reads=(w1k, lnfm4), writes=(p,))
                op("s", lambda s, p=p, do=do, grp=grp: s.copy(
                    KV1.ap[:, do * S + grp * 512: do * S + (grp + 1) * 512], p.ap[:]),
                    reads=(p,), writes=(KV1,))
            pump(2)

        ir.fence()
        a1.close()
        if upto == "A1":
            _finish(k, block, ctx)
            return nc, ctx

        a2 = ExitStack()
        w1q = k.sb(NC * D, BF16, a2)
        xt2 = [k.sb(D, F32, a2) for _ in range(3)]
        lnb2 = [k.sb(D, BF16, a2) for _ in range(2)]
        lnfm = [k.sb(NC * 128, BF16, a2) for _ in range(2)]
        q1t = [k.sb(NC * 128, BF16, a2) for _ in range(2)]
        x1t = [k.sb(D, F32, a2) for _ in range(2)]
        pbuf = [k.sb(512, BF16, a2) for _ in range(3)]
        ptb = [k.sb(512, BF16, a2) for _ in range(3)]
        NBO = NT // 4
        bmx = [k.sb(NBO, F32, a2) for _ in range(2)]
        bsx = [k.sb(NBO, F32, a2) for _ in range(2)]
        nmO = [k.sb(8, F32, a2) for _ in range(2)]
        scrB = [mk_scr(a2) for _ in range(2)]
        sp1_ps = [k.ps(512, F32, a2) for _ in range(2)]
        sp2_ps = [k.ps(512, F32, a2) for _ in range(2)]
        av_ps = k.ps(D, F32, a2)
        tp2_ps = [k.ps(1024, BF16, a2)]

        load_w_cast(w1q, w_d["wq1"])
        guc = [0, 0]

        def a2_prelude(t):
            par = t % 2
            xb = xt2[t % 3]
            op("q", lambda q, t=t, xb=xb: q.dma_start(
                out=xb.ap[:], in_=xv_d[t * 128:(t + 1) * 128, :]),
                writes=(xb,), isdma=True)
            lb = lnb2[par]
            ln_tile(xb, scrB[par], LN_EPS, out_bf=lb)
            lf = lnfm[par]
            tp = tp2_ps[0]
            for cb4 in range(0, NC, 4):
                for j in range(4):
                    c = cb4 + j
                    op("t", lambda t_, c=c, j=j, tp=tp, lb=lb: t_.transpose(
                        tp.ap[:, j * 128:(j + 1) * 128],
                        lb.ap[:, c * 128:(c + 1) * 128], ident.ap[:]),
                        reads=(lb, ident), writes=(tp,))
                op("s", lambda s, cb4=cb4, tp=tp, lf=lf: s.copy(
                    lf.ap[:, cb4 * 128:(cb4 + 4) * 128], tp.ap[:, 0:512]),
                    reads=(tp,), writes=(lf,))
            qb = q1t[par]
            for do in range(NC):
                sp = sp1_ps[do % 2]
                for c in range(NC):
                    op("t", lambda t_, do=do, c=c, sp=sp, lf=lf: t_.matmul(
                        sp.ap[:, 0:128],
                        w1q.ap[:, c * D + do * 128: c * D + (do + 1) * 128],
                        lf.ap[:, c * 128:(c + 1) * 128],
                        start=(c == 0), stop=(c == NC - 1)),
                        reads=(w1q, lf), writes=(sp,))
                op("s", lambda s, do=do, sp=sp, qb=qb: s.activation(
                    qb.ap[:, do * 128:(do + 1) * 128], sp.ap[:, 0:128],
                    AF.Copy, bias=0.0, scale=1.0 / 32.0),
                    reads=(sp,), writes=(qb,))

        def a2_scores(sp, qb, kb, w, isdiag):
            for c in range(NC):
                op("t", lambda t_, c=c, kb=kb, w=w, sp=sp, qb=qb: t_.matmul(
                    sp.ap[:, 0:w],
                    qb.ap[:, c * 128:(c + 1) * 128],
                    KV1.ap[:, c * S + kb * 512: c * S + kb * 512 + w],
                    start=(c == 0), stop=(c == NC - 1 and not isdiag)),
                    reads=(qb, KV1), writes=(sp,))
            if isdiag:
                op("t", lambda t_, w=w, sp=sp: t_.matmul(
                    sp.ap[:, w - 128:w], ident.ap[:], tri.ap[:],
                    start=False, stop=True),
                    reads=(ident, tri), writes=(sp,))

        def a2_pass1(t):
            par = t % 2
            qb = q1t[par]
            nkey = (t + 1) * 128
            nb5 = (nkey + 511) // 512
            units = []

            def unit(kb):
                w = min(512, nkey - kb * 512)
                sp = sp1_ps[guc[0] % 2]
                guc[0] += 1
                a2_scores(sp, qb, kb, w, False)
                op("v", lambda v, sp=sp, w=w, kb=kb: v.reduce_max(
                    bmx[par].ap[:, kb:kb + 1], sp.ap[:, 0:w], axis=AX.X),
                    reads=(sp,), writes=(bmx[par],))
            for kb in range(nb5):
                units.append(lambda kb=kb: unit(kb))
            return units

        def a2_pass1_tail(t):
            par = t % 2
            nkey = (t + 1) * 128
            nb5 = (nkey + 511) // 512
            op("v", lambda v, nb5=nb5: v.reduce_max(
                nmO[par].ap[:, 0:1], bmx[par].ap[:, 0:nb5], axis=AX.X, negate=True),
                reads=(bmx[par],), writes=(nmO[par],))
            padtot = min(2048, nkey)
            op("s", lambda s: s.activation(nmO[par].ap[:, 1:2], nmO[par].ap[:, 0:1],
                                           AF.Exp, bias=0.0, scale=1.0),
               reads=(nmO[par],), writes=(nmO[par],))
            op("v", lambda v, padtot=padtot: v.tensor_scalar(
                nmO[par].ap[:, 2:3], nmO[par].ap[:, 1:2], npad.ap[:, 0:1],
                float(padtot) / 1024.0, ALU.mult, ALU.mult),
                reads=(nmO[par], npad), writes=(nmO[par],))

        def a2_pass2(t):
            par = t % 2
            qb = q1t[par]
            nkey = (t + 1) * 128
            nb5 = (nkey + 511) // 512
            cnt = [0]

            def unitA(kb, g):
                w = min(512, nkey - kb * 512)
                sp = sp2_ps[g % 2]
                pb = pbuf[g % 3]
                a2_scores(sp, qb, kb, w, kb == nb5 - 1)
                op("s", lambda s, sp=sp, w=w, kb=kb, pb=pb: s.activation(
                    pb.ap[:, 0:w], sp.ap[:, 0:w], AF.Exp,
                    bias=nmO[par].ap[:, 0:1], scale=1.0,
                    accum_out=bsx[par].ap[:, kb:kb + 1]),
                    reads=(sp, nmO[par]), writes=(pb, bsx[par]))

            def unitB(kb, g):
                w = min(512, nkey - kb * 512)
                pb = pbuf[g % 3]
                pt = ptb[g % 3]
                tp = tp2_ps[0]
                ng = w // 128
                for j in range(ng):
                    op("t", lambda t_, j=j, pb=pb, tp=tp: t_.transpose(
                        tp.ap[:, j * 128:(j + 1) * 128],
                        pb.ap[:, j * 128:(j + 1) * 128], ident.ap[:]),
                        reads=(pb, ident), writes=(tp,))
                if g % 2 == 0:
                    op("s", lambda s, tp=tp, ng=ng, pt=pt: s.copy(
                        pt.ap[:, 0:ng * 128], tp.ap[:, 0:ng * 128]),
                        reads=(tp,), writes=(pt,))
                else:
                    op("v", lambda v, tp=tp, ng=ng, pt=pt: v.tensor_copy(
                        pt.ap[:, 0:ng * 128], tp.ap[:, 0:ng * 128]),
                        reads=(tp,), writes=(pt,))
                for half in range(2):
                    for gr in range(ng):
                        kt = kb * 4 + gr
                        op("t", lambda t_, gr=gr, kt=kt, half=half, pt=pt,
                           kb=kb, ng=ng: t_.matmul(
                            av_ps.ap[:, half * 512:(half + 1) * 512],
                            pt.ap[:, gr * 128:(gr + 1) * 128],
                            KV2.ap[:, kt * D + half * 512: kt * D + (half + 1) * 512],
                            start=(kb == 0 and gr == 0),
                            stop=(kb == nb5 - 1 and gr == ng - 1)),
                            reads=(pt, KV2), writes=(av_ps,))

            units = []
            LAG = 2
            for g in range(nb5 + LAG):
                if g < nb5:
                    units.append(lambda kb=g, g=g: unitA(kb, g))
                if g - LAG >= 0:
                    units.append(lambda kb=g - LAG, g=g - LAG: unitB(kb, g))
            return units

        def a2_epilogue(t):
            par = t % 2
            nkey = (t + 1) * 128
            nb5 = (nkey + 511) // 512
            sc = scrB[par]
            xb = xt2[t % 3]
            op("v", lambda v, nb5=nb5: v.reduce_sum(
                sc["ss"].ap[:, 0:1], bsx[par].ap[:, 0:nb5], axis=AX.X),
                reads=(bsx[par],), writes=(sc["ss"],))
            op("v", lambda v: v.tensor_tensor(
                sc["ss"].ap[:, 0:1], sc["ss"].ap[:, 0:1], nmO[par].ap[:, 2:3],
                ALU.subtract),
                reads=(nmO[par],), writes=(sc["ss"],))
            op("v", lambda v: v.tensor_scalar(
                sc["ss"].ap[:, 1:2], sc["ss"].ap[:, 0:1], 1e-30, None, ALU.add),
                reads=(sc["ss"],), writes=(sc["ss"],))
            op("v", lambda v: v.reciprocal(sc["ss"].ap[:, 1:2], sc["ss"].ap[:, 1:2]),
                reads=(sc["ss"],), writes=(sc["ss"],))
            xo = x1t[par]
            for half in range(2):
                op("v", lambda v, xb=xb, xo=xo, half=half: v.scalar_tensor_tensor(
                    xo.ap[:, half * 512:(half + 1) * 512],
                    av_ps.ap[:, half * 512:(half + 1) * 512],
                    sc["ss"].ap[:, 1:2],
                    xb.ap[:, half * 512:(half + 1) * 512], ALU.mult, ALU.add),
                    reads=(av_ps, sc["ss"], xb), writes=(xo,))
            op("q", lambda q, t=t, xo=xo: q.dma_start(
                out=x1_d[t * 128:(t + 1) * 128, :], in_=xo.ap[:]),
                reads=(xo,), isdma=True)

        prevt = None
        for t in range(NT):
            a2_prelude(t)
            u1 = a2_pass1(t)
            u2 = a2_pass2(prevt) if prevt is not None else []
            n1, n2 = len(u1), len(u2)
            i1 = i2 = 0
            while i1 < n1 or i2 < n2:
                if i2 < n2 and (i1 >= n1 or i2 * max(n1, 1) <= i1 * max(n2, 1)):
                    u2[i2](); i2 += 1
                else:
                    u1[i1](); i1 += 1
            a2_pass1_tail(t)
            if prevt is not None:
                a2_epilogue(prevt)
            prevt = t
            pump(2)
        for u in a2_pass2(prevt):
            u()
        a2_epilogue(prevt)

        ir.fence()
        a2.close()
        if upto in ("A2", "A2dbg"):
            _finish(k, block, ctx)
            return nc, ctx

        b1k = ExitStack()
        w2k = k.sb(NC * D, BF16, b1k)
        g1bc = k.sb(D, F32, b1k)
        x1b = [k.sb(D, F32, b1k) for _ in range(2)]
        zb = [k.sb(D, F32, b1k) for _ in range(2)]
        xnb = [k.sb(D, BF16, b1k) for _ in range(2)]
        xnfm = [k.sb(NC * 128, BF16, b1k) for _ in range(2)]
        cosb = [k.sb(D, BF16, b1k) for _ in range(2)]
        sinb = [k.sb(D, BF16, b1k) for _ in range(2)]
        rt1 = [k.sb(D, F32, b1k) for _ in range(2)]
        rt2 = [k.sb(D, F32, b1k) for _ in range(2)]
        krb = [k.sb(D, BF16, b1k) for _ in range(2)]
        scrC = [mk_scr(b1k) for _ in range(2)]
        kq_ps = [k.ps(D, F32, b1k) for _ in range(2)]
        tp3_ps = [k.ps(1024, BF16, b1k) for _ in range(2)]

        load_w_cast(w2k, w_d["wk2"])
        op("g", lambda g: g.dma_start(out=g1bc.ap[:], in_=g1bc_d[:]),
           writes=(g1bc,), isdma=True)

        def rope(par, src_ps, out_bf):
            r1, r2 = rt1[par], rt2[par]
            cb, sb_ = cosb[par], sinb[par]
            vsrc = src_ps.ap.rearrange("p (h j) -> p h j", j=HD)
            vsin = sb_.ap.rearrange("p (h j) -> p h j", j=HD)
            vr2 = r2.ap.rearrange("p (h j) -> p h j", j=HD)
            op("v", lambda v: v.tensor_tensor(r1.ap[:], src_ps.ap[:], cb.ap[:],
                                              ALU.mult),
               reads=(src_ps, cb), writes=(r1,))
            op("v", lambda v: v.tensor_tensor(vr2[:, :, 0:32], vsrc[:, :, 32:64],
                                              vsin[:, :, 0:32], ALU.mult),
               reads=(src_ps, sb_), writes=(r2,))
            op("v", lambda v: v.tensor_tensor(vr2[:, :, 32:64], vsrc[:, :, 0:32],
                                              vsin[:, :, 32:64], ALU.mult),
               reads=(src_ps, sb_), writes=(r2,))
            op("v", lambda v: v.tensor_tensor(out_bf.ap[:], r1.ap[:], r2.ap[:],
                                              ALU.add),
               reads=(r1, r2), writes=(out_bf,))

        def b1k_prep(t):
            par = t % 2
            xb = x1b[par]
            op("q", lambda q, t=t, xb=xb: q.dma_start(
                out=xb.ap[:], in_=x1_d[t * 128:(t + 1) * 128, :]),
                writes=(xb,), isdma=True)
            op("g", lambda g, t=t, par=par: g.dma_start(
                out=cosb[par].ap[:], in_=cosr_d[t * 128:(t + 1) * 128, :]),
                writes=(cosb[par],), isdma=True)
            op("g", lambda g, t=t, par=par: g.dma_start(
                out=sinb[par].ap[:], in_=sinr_d[t * 128:(t + 1) * 128, :]),
                writes=(sinb[par],), isdma=True)
            ln_tile(xb, scrC[par], LN_EPS, out_f32=zb[par])
            rms_tile(zb[par], scrC[par], g1bc, xnb[par], xnb[par])

        def b1k_xpose(t):
            par = t % 2
            tp = tp3_ps[0]
            for cb4 in range(0, NC, 4):
                for j in range(4):
                    c = cb4 + j
                    op("t", lambda t_, c=c, j=j, tp=tp, par=par: t_.transpose(
                        tp.ap[:, j * 128:(j + 1) * 128],
                        xnb[par].ap[:, c * 128:(c + 1) * 128], ident.ap[:]),
                        reads=(xnb[par], ident), writes=(tp,))
                op("s", lambda s, cb4=cb4, tp=tp, par=par: s.copy(
                    xnfm[par].ap[:, cb4 * 128:(cb4 + 4) * 128], tp.ap[:, 0:512]),
                    reads=(tp,), writes=(xnfm[par],))
            op("q", lambda q, t=t, par=par: q.dma_start(
                out=xn_d[t, :, :, :].rearrange("a p j -> p a j"),
                in_=xnfm[par].ap[:]),
                reads=(xnfm[par],), isdma=True)

        def b1k_k2(t):
            par = t % 2
            kp = kq_ps[par]
            for c in range(NC):
                for half in range(2):
                    op("t", lambda t_, c=c, half=half, kp=kp, par=par: t_.matmul(
                        kp.ap[:, half * 512:(half + 1) * 512],
                        xnfm[par].ap[:, c * 128:(c + 1) * 128],
                        w2k.ap[:, c * D + half * 512: c * D + (half + 1) * 512],
                        start=(c == 0), stop=(c == NC - 1)),
                        reads=(xnfm[par], w2k), writes=(kp,))

        def b1k_rope(t):
            par = t % 2
            rope(par, kq_ps[par], krb[par])

        def b1k_kt(t):
            par = t % 2
            tp = tp3_ps[1]
            for cb4 in range(0, NC, 4):
                for j in range(4):
                    c = cb4 + j
                    op("t", lambda t_, c=c, j=j, tp=tp, par=par: t_.transpose(
                        tp.ap[:, j * 128:(j + 1) * 128],
                        krb[par].ap[:, c * 128:(c + 1) * 128], ident.ap[:]),
                        reads=(krb[par], ident), writes=(tp,))
                for j in range(4):
                    c = cb4 + j
                    op("s", lambda s, c=c, j=j, tp=tp, t=t: s.copy(
                        KV1.ap[:, c * S + t * 128: c * S + (t + 1) * 128],
                        tp.ap[:, j * 128:(j + 1) * 128]),
                        reads=(tp,), writes=(KV1,))

        stages = [b1k_prep, b1k_xpose, b1k_k2, b1k_rope, b1k_kt]
        sched = []
        for t in range(NT + 2):
            if t < NT:
                sched += [(t, 0), (t, 1), (t, 2)]
            if t - 1 >= 0 and t - 1 < NT:
                sched += [(t - 1, 3)]
            if t - 2 >= 0 and t - 2 < NT:
                sched += [(t - 2, 4)]
        done = set()
        for t, s_i in sched:
            key = (t, s_i)
            if key in done:
                continue
            done.add(key)
            stages[s_i](t)
            if s_i == 0:
                pump(1)

        ir.fence()
        b1k.close()

        b1v = ExitStack()
        w2v = k.sb(NC * D, BF16, b1v)
        w2q = k.sb(NC * D, BF16, b1v)
        xnf2 = [k.sb(NC * 128, BF16, b1v) for _ in range(2)]
        cosb2 = [k.sb(D, BF16, b1v) for _ in range(2)]
        sinb2 = [k.sb(D, BF16, b1v) for _ in range(2)]
        rt1v = [k.sb(D, F32, b1v) for _ in range(2)]
        rt2v = [k.sb(D, F32, b1v) for _ in range(2)]
        krbv = [k.sb(D, BF16, b1v) for _ in range(2)]
        vq_ps = [k.ps(D, F32, b1v) for _ in range(2)]
        tp3b_ps = [k.ps(1024, BF16, b1v) for _ in range(2)]
        load_w_cast(w2v, w_d["wv2"])
        load_w_cast(w2q, w_d["wq2"])

        def ropev(par, src_ps, out_bf):
            r1, r2 = rt1v[par], rt2v[par]
            cb, sb_ = cosb2[par], sinb2[par]
            vsrc = src_ps.ap.rearrange("p (h j) -> p h j", j=HD)
            vsin = sb_.ap.rearrange("p (h j) -> p h j", j=HD)
            vr2 = r2.ap.rearrange("p (h j) -> p h j", j=HD)
            op("v", lambda v: v.tensor_tensor(r1.ap[:], src_ps.ap[:], cb.ap[:],
                                              ALU.mult),
               reads=(src_ps, cb), writes=(r1,))
            op("v", lambda v: v.tensor_tensor(vr2[:, :, 0:32], vsrc[:, :, 32:64],
                                              vsin[:, :, 0:32], ALU.mult),
               reads=(src_ps, sb_), writes=(r2,))
            op("v", lambda v: v.tensor_tensor(vr2[:, :, 32:64], vsrc[:, :, 0:32],
                                              vsin[:, :, 32:64], ALU.mult),
               reads=(src_ps, sb_), writes=(r2,))
            op("v", lambda v: v.tensor_tensor(out_bf.ap[:], r1.ap[:], r2.ap[:],
                                              ALU.add),
               reads=(r1, r2), writes=(out_bf,))

        def qv_load(t):
            par = t % 2
            xf = xnf2[par]
            op("q", lambda q, t=t, xf=xf: q.dma_start(
                out=xf.ap[:], in_=xn_d[t, :, :, :].rearrange("a p j -> p a j")),
                writes=(xf,), isdma=True)
            if t >= OWN0:
                op("g", lambda g, t=t, par=par: g.dma_start(
                    out=cosb2[par].ap[:], in_=cosr_d[t * 128:(t + 1) * 128, :]),
                    writes=(cosb2[par],), isdma=True)
                op("g", lambda g, t=t, par=par: g.dma_start(
                    out=sinb2[par].ap[:], in_=sinr_d[t * 128:(t + 1) * 128, :]),
                    writes=(sinb2[par],), isdma=True)

        def qv_v2(t):
            par = t % 2
            vp = vq_ps[par]
            xf = xnf2[par]
            for c in range(NC):
                for half in range(2):
                    op("t", lambda t_, c=c, half=half, vp=vp, xf=xf: t_.matmul(
                        vp.ap[:, half * 512:(half + 1) * 512],
                        xf.ap[:, c * 128:(c + 1) * 128],
                        w2v.ap[:, c * D + half * 512: c * D + (half + 1) * 512],
                        start=(c == 0), stop=(c == NC - 1)),
                        reads=(xf, w2v), writes=(vp,))
            for half in range(2):
                op("s", lambda s, vp=vp, t=t, half=half: s.copy(
                    KV2.ap[:, t * D + half * 512: t * D + (half + 1) * 512],
                    vp.ap[:, half * 512:(half + 1) * 512]),
                    reads=(vp,), writes=(KV2,))

        def qv_q2(t):
            par = t % 2
            qp = vq_ps[1 - par]
            xf = xnf2[par]
            for c in range(NC):
                for half in range(2):
                    op("t", lambda t_, c=c, half=half, qp=qp, xf=xf: t_.matmul(
                        qp.ap[:, half * 512:(half + 1) * 512],
                        xf.ap[:, c * 128:(c + 1) * 128],
                        w2q.ap[:, c * D + half * 512: c * D + (half + 1) * 512],
                        start=(c == 0), stop=(c == NC - 1)),
                        reads=(xf, w2q), writes=(qp,))
            ropev(par, qp, krbv[par])
            tp = tp3b_ps[par]
            for cb4 in range(0, NC, 4):
                for j in range(4):
                    c = cb4 + j
                    op("t", lambda t_, c=c, j=j, tp=tp, par=par: t_.transpose(
                        tp.ap[:, j * 128:(j + 1) * 128],
                        krbv[par].ap[:, c * 128:(c + 1) * 128], ident.ap[:]),
                        reads=(krbv[par], ident), writes=(tp,))
                op("s", lambda s, cb4=cb4, tp=tp, par=par: s.activation(
                    krbv[par].ap[:, cb4 * 128:(cb4 + 4) * 128], tp.ap[:, 0:512],
                    AF.Copy, bias=0.0, scale=0.125),
                    reads=(tp,), writes=(krbv[par],))
            op("q", lambda q, t=t, par=par: q.dma_start(
                out=q2_d[t - OWN0, :, :, :].rearrange("a p j -> p a j"),
                in_=krbv[par].ap[:]),
                reads=(krbv[par],), isdma=True)

        for t in range(NT):
            qv_load(t)
            if t >= 1:
                qv_v2(t - 1)
                if t - 1 >= OWN0:
                    qv_q2(t - 1)
            pump(2)
        qv_v2(NT - 1)
        qv_q2(NT - 1)

        ir.fence()
        b1v.close()
        if upto == "B1v":
            _finish(k, block, ctx)
            return nc, ctx

        b1b = ExitStack()
        wo2 = k.sb(NC * D, BF16, b1b)
        x1c = [k.sb(D, F32, b1b) for _ in range(2)]
        q2t = [k.sb(NC * 128, BF16, b1b) for _ in range(2)]
        zb2 = [k.sb(D, F32, b1b) for _ in range(2)]
        pb2 = [k.sb(KBLK, BF16, b1b) for _ in range(3)]
        ptb2 = [k.sb(KBLK, BF16, b1b) for _ in range(3)]
        at2 = [k.sb(D, BF16, b1b)] * 2
        atf = k.sb(D, BF16, b1b)
        hb = [k.sb(D, F32, b1b) for _ in range(2)]
        rb_ = [k.sb(D, F32, b1b) for _ in range(2)]
        NB5 = NT // 4
        bms = [k.sb(H * NB5, F32, b1b) for _ in range(2)]
        bss = [k.sb(H * NB5, F32, b1b) for _ in range(2)]
        mf = [k.sb(16, F32, b1b) for _ in range(2)]
        nmf = [k.sb(16, F32, b1b) for _ in range(2)]
        e0b = [k.sb(16, F32, b1b) for _ in range(2)]
        corb = [k.sb(16, F32, b1b) for _ in range(2)]
        lsum = [k.sb(16, F32, b1b) for _ in range(2)]
        linv = [k.sb(16, F32, b1b) for _ in range(2)]
        scrD = [mk_scr(b1b) for _ in range(2)]
        scpool4 = k.ps(4 * 512, F32, b1b)
        scpool = [Buf(scpool4.ap[:, i * 512:(i + 1) * 512]) for i in range(4)]
        scpool_pair_view = [
            scpool4.ap[:, 0:1024].rearrange("p (b w) -> p b w", b=2),
            scpool4.ap[:, 1024:2048].rearrange("p (b w) -> p b w", b=2)]
        ava_ps = k.ps(D, F32, b1b)
        tp4_ps = [k.ps(1024, BF16, b1b)]
        guc1 = [0]
        jslot = {}

        load_w_cast(wo2, w_d["wo2"])

        def b1b_pair_scmm(sppair, qb, hp, blk0, w, masked, isdiag):
            c = hp
            for hip in range(2):
                po = hip * 64
                sp = sppair[hip]
                op("t", lambda t_, c=c, po=po, blk0=blk0, w=w, sp=sp, qb=qb,
                   isdiag=isdiag: t_.matmul(
                    sp.ap[:, 0:w],
                    qb.ap[po:po + 64, c * 128:(c + 1) * 128],
                    KV1.ap[po:po + 64, c * S + blk0: c * S + blk0 + w],
                    start=True, stop=not isdiag,
                    tile_position=(po, 0)),
                    reads=(qb, KV1), writes=(sp,))
            if isdiag:
                for hip in range(2):
                    sp = sppair[hip]
                    op("t", lambda t_, w=w, sp=sp: t_.matmul(
                        sp.ap[:, w - 128:w], ident.ap[:], tri.ap[:],
                        start=False, stop=True),
                        reads=(ident, tri), writes=(sp,))

        def mk_pass1(t, p, qb):
            nkey = (t + 1) * 128
            nk5 = (nkey + 511) // 512
            units = []

            def unit(hp, kb5, w):
                j = guc1[0] % 2
                guc1[0] += 1
                sppair = (scpool[2 * j], scpool[2 * j + 1])
                b1b_pair_scmm(sppair, qb, hp, kb5 * 512, w, False, False)
                vv = scpool_pair_view[j]
                op("v", lambda v, vv=vv, w=w, hp=hp, kb5=kb5: v.reduce_max(
                    bms[p].ap[:, kb5 * H + hp * 2: kb5 * H + hp * 2 + 2],
                    vv[:, :, 0:w], axis=AX.X),
                    reads=(scpool[2 * j], scpool[2 * j + 1]), writes=(bms[p],))
            for kb5 in range(nk5):
                w = min(512, nkey - kb5 * 512)
                for hp in range(H // 2):
                    units.append(lambda hp=hp, kb5=kb5, w=w: unit(hp, kb5, w))
            return units

        def emit_pass1_tail(t, p):
            vb = bms[p].ap.rearrange("p (k a) -> p a k", a=H)
            op("v", lambda v, vb=vb: v.reduce_max(mf[p].ap[:], vb[:, :, 0:NB5],
                                                  axis=AX.X),
               reads=(bms[p],), writes=(mf[p],))
            op("v", lambda v: v.tensor_scalar(nmf[p].ap[:], mf[p].ap[:], -1.0, None,
                                              ALU.mult),
               reads=(mf[p],), writes=(nmf[p],))
            op("s", lambda s: s.activation(e0b[p].ap[:], mf[p].ap[:], AF.Exp,
                                           bias=0.0, scale=-1.0),
               reads=(mf[p],), writes=(e0b[p],))
            op("v", lambda v: v.tensor_scalar(corb[p].ap[:], e0b[p].ap[:],
                                              npad.ap[:, 0:1], 2.0, ALU.mult, ALU.mult),
               reads=(e0b[p], npad), writes=(corb[p],))
            op("v", lambda v: v.memset(bss[p].ap[:], 0.0), writes=(bss[p],))

        def mk_pass2(t, p, qb):
            nkey = (t + 1) * 128
            nk5 = (nkey + 511) // 512

            def unitA(hp, kb5, w, g):
                j = guc1[0] % 2
                guc1[0] += 1
                jslot[(hp, kb5)] = j
                sppair = (scpool[2 * j], scpool[2 * j + 1])
                b1b_pair_scmm(sppair, qb, hp, kb5 * 512, w, True, kb5 == nk5 - 1)
                pb = pb2[g % 3]
                for hip in range(2):
                    h = hp * 2 + hip
                    spp = sppair[hip]
                    op("s", lambda s, sp=spp, w=w, h=h, kb5=kb5, pb=pb,
                       hip=hip, hp=hp: s.activation(
                        pb.ap[:, hip * 512: hip * 512 + w],
                        sp.ap[:, 0:w], AF.Exp,
                        bias=nmf[p].ap[:, h:h + 1], scale=1.0,
                        accum_out=bss[p].ap[:, kb5 * H + hp * 2 + hip:
                                            kb5 * H + hp * 2 + hip + 1]),
                        reads=(spp, nmf[p]), writes=(pb, bss[p]))

            def unitB(hp, kb5, w, g):
                pb = pb2[g % 3]
                pt = ptb2[g % 3]
                tp = tp4_ps[0]
                ng = w // 128
                for hip in range(2):
                    for j in range(ng):
                        op("t", lambda t_, j=j, hip=hip, pb=pb, tp=tp: t_.transpose(
                            tp.ap[:, hip * 512 + j * 128: hip * 512 + (j + 1) * 128],
                            pb.ap[:, hip * 512 + j * 128: hip * 512 + (j + 1) * 128],
                            ident.ap[:]),
                            reads=(pb, ident), writes=(tp,))
                if g % 2 == 0:
                    op("s", lambda s, tp=tp, pt=pt: s.copy(pt.ap[:], tp.ap[:]),
                        reads=(tp,), writes=(pt,))
                else:
                    op("v", lambda v, tp=tp, pt=pt: v.tensor_copy(pt.ap[:], tp.ap[:]),
                        reads=(tp,), writes=(pt,))
                for hip in range(2):
                    h = hp * 2 + hip
                    for gr in range(ng):
                        kt = kb5 * 4 + gr
                        op("t", lambda t_, gr=gr, kt=kt, h=h, hip=hip, kb5=kb5,
                           ng=ng, pt=pt: t_.matmul(
                            ava_ps.ap[:, h * 64:(h + 1) * 64],
                            pt.ap[:, hip * 512 + gr * 128: hip * 512 + (gr + 1) * 128],
                            KV2.ap[:, kt * D + h * 64: kt * D + (h + 1) * 64],
                            start=(kb5 == 0 and gr == 0 and hip == 0),
                            stop=(kb5 == nk5 - 1 and gr == ng - 1 and hip == 1)),
                            reads=(pt, KV2), writes=(ava_ps,))

            hw_list = []
            for hp in range(H // 2):
                for kb5 in range(nk5):
                    hw_list.append((hp, kb5, min(512, nkey - kb5 * 512)))
            n = len(hw_list)
            LAG = 2
            units = []
            for g in range(n + LAG):
                if g < n:
                    hp, kb5, w = hw_list[g]
                    units.append(lambda hp=hp, kb5=kb5, w=w, g=g: unitA(hp, kb5, w, g))
                if g - LAG >= 0:
                    hp, kb5, w = hw_list[g - LAG]
                    units.append(lambda hp=hp, kb5=kb5, w=w, g=g - LAG: unitB(hp, kb5, w, g))
            return units

        def emit_pass2_tail(t, p, xb):
            ti = t - OWN0
            vs = bss[p].ap.rearrange("p (k a) -> p a k", a=H)
            op("v", lambda v, vs=vs: v.reduce_sum(lsum[p].ap[:], vs[:, :, 0:NB5], axis=AX.X),
               reads=(bss[p],), writes=(lsum[p],))
            op("v", lambda v: v.tensor_tensor(lsum[p].ap[:], lsum[p].ap[:],
                                              corb[p].ap[:], ALU.subtract),
               reads=(corb[p],), writes=(lsum[p],))
            op("v", lambda v: v.tensor_scalar(linv[p].ap[:], lsum[p].ap[:], 1e-30,
                                              None, ALU.add),
               reads=(lsum[p],), writes=(linv[p],))
            op("v", lambda v: v.reciprocal(linv[p].ap[:], linv[p].ap[:]),
               reads=(linv[p],), writes=(linv[p],))
            for h in range(H):
                op("v", lambda v, h=h: v.tensor_scalar(
                    at2[p].ap[:, h * 64:(h + 1) * 64],
                    ava_ps.ap[:, h * 64:(h + 1) * 64],
                    linv[p].ap[:, h:h + 1], None, ALU.mult),
                    reads=(ava_ps, linv[p]), writes=(at2[p],))
            tp = tp4_ps[0]
            for cb4 in range(0, NC, 4):
                for j in range(4):
                    c = cb4 + j
                    op("t", lambda t_, c=c, j=j, tp=tp: t_.transpose(
                        tp.ap[:, j * 128:(j + 1) * 128],
                        at2[p].ap[:, c * 128:(c + 1) * 128], ident.ap[:]),
                        reads=(at2[p], ident), writes=(tp,))
                op("s", lambda s, cb4=cb4, tp=tp: s.copy(
                    atf.ap[:, cb4 * 128:(cb4 + 4) * 128], tp.ap[:, 0:512]),
                    reads=(tp,), writes=(atf,))
            for c in range(NC):
                for half in range(2):
                    op("t", lambda t_, c=c, half=half: t_.matmul(
                        ava_ps.ap[:, half * 512:(half + 1) * 512],
                        atf.ap[:, c * 128:(c + 1) * 128],
                        wo2.ap[:, c * D + half * 512: c * D + (half + 1) * 512],
                        start=(c == 0), stop=(c == NC - 1)),
                        reads=(atf, wo2), writes=(ava_ps,))
            for half in range(2):
                op("v", lambda v, half=half: v.tensor_tensor(
                    hb[p].ap[:, half * 512:(half + 1) * 512],
                    zb2[p].ap[:, half * 512:(half + 1) * 512],
                    ava_ps.ap[:, half * 512:(half + 1) * 512], ALU.add),
                    reads=(zb2[p], ava_ps), writes=(hb[p],))
            op("v", lambda v, xb=xb: v.tensor_tensor(rb_[p].ap[:], xb.ap[:],
                                                     hb[p].ap[:], ALU.add),
               reads=(xb, hb[p]), writes=(rb_[p],))
            op("q", lambda q, ti=ti: q.dma_start(
                out=h_d[ti * 128:(ti + 1) * 128, :], in_=hb[p].ap[:]),
                reads=(hb[p],), isdma=True)
            op("q", lambda q, ti=ti: q.dma_start(
                out=r_d[ti * 128:(ti + 1) * 128, :], in_=rb_[p].ap[:]),
                reads=(rb_[p],), isdma=True)

        prev = None
        for t in range(OWN0, NT):
            ti = t - OWN0
            p = ti % 2
            xb = x1c[p]
            op("q", lambda q, t=t, xb=xb: q.dma_start(
                out=xb.ap[:], in_=x1_d[t * 128:(t + 1) * 128, :]),
                writes=(xb,), isdma=True)
            qb = q2t[p]
            op("q", lambda q, ti=ti, qb=qb: q.dma_start(
                out=qb.ap[:], in_=q2_d[ti, :, :, :].rearrange("a p j -> p a j")),
                writes=(qb,), isdma=True)
            ln_tile(xb, scrD[p], LN_EPS, out_f32=zb2[p])
            op("v", lambda v, p=p: v.memset(bms[p].ap[:], -1e30), writes=(bms[p],))
            u1 = mk_pass1(t, p, qb)
            u2 = mk_pass2(prev[0], prev[1], prev[2]) if prev else []
            n1, n2 = len(u1), len(u2)
            i1 = i2 = 0
            while i1 < n1 or i2 < n2:
                if i2 < n2 and (i1 >= n1 or i2 * max(n1, 1) <= i1 * max(n2, 1)):
                    u2[i2](); i2 += 1
                else:
                    u1[i1](); i1 += 1
            emit_pass1_tail(t, p)
            if prev:
                emit_pass2_tail(prev[0], prev[1], prev[3])
            prev = (t, p, qb, xb)
            pump(6)
        for u in mk_pass2(prev[0], prev[1], prev[2]):
            u()
        emit_pass2_tail(prev[0], prev[1], prev[3])

        pump_all()
        ir.fence()
        b1b.close()
        kv_scope.close()
        st_scope.close()
        if upto in ("B1b", "B1bdbg"):
            _finish(k, block, ctx)
            return nc, ctx

        b2s = ExitStack()
        b2 = ExitStack()
        xf = k.sb(NC * 2048, BF16, b2)
        comb = k.sb(16 * NE, F32, b2)
        g2bc = k.sb(D, F32, b2s)
        hb2 = [k.sb(D, F32, b2s) for _ in range(2)]
        xfb = k.sb(D, BF16, b2s)
        p8 = k.sb(NE, F32, b2s)
        r8 = {n: k.sb(8, F32, b2s) for n in
              ("m8", "nm8", "m1", "m2", "den", "dinv")}
        msk1 = k.sb(NE, F32, b2s)
        p8b = k.sb(NE, F32, b2s)
        sel8 = k.sb(NE, F32, b2s)
        scrE = mk_scr(b2s)
        rt_ps = k.ps(512, F32, b2s)
        tp5_ps = [k.ps(1024, BF16, b2s) for _ in range(2)]

        op("g", lambda g: g.dma_start(out=g2bc.ap[:], in_=g2bc_d[:]),
           writes=(g2bc,), isdma=True)
        wrb2 = k.sb(NC * NE, BF16, b2s)
        wrf = k.sb(NC * NE, F32, b2s)
        wrlo = k.sb(NC * NE, BF16, b2s)
        xff = k.sb(D, F32, b2s)
        xlo = k.sb(D, BF16, b2s)
        xlofm = k.sb(D, BF16, b2s)
        for c in range(NC):
            op("g", lambda g, c=c: g.dma_start(
                out=wrb2.ap[:, c * NE:(c + 1) * NE], in_=wr_d[c * 128:(c + 1) * 128, :]),
               writes=(wrb2,), isdma=True)
            op("q", lambda q, c=c: q.dma_start(
                out=wrf.ap[:, c * NE:(c + 1) * NE], in_=wr_d[c * 128:(c + 1) * 128, :]),
               writes=(wrf,), isdma=True)
        op("v", lambda v: v.tensor_tensor(wrlo.ap[:], wrf.ap[:], wrb2.ap[:],
                                          ALU.subtract),
           reads=(wrf, wrb2), writes=(wrlo,))

        for ti in range(16):
            hbb = hb2[ti % 2]
            op("q", lambda q, ti=ti, hbb=hbb: q.dma_start(
                out=hbb.ap[:], in_=h_d[ti * 128:(ti + 1) * 128, :]),
                writes=(hbb,), isdma=True)
            rms_tile(hbb, scrE, g2bc, xlofm, xff)
            op("v", lambda v: v.tensor_copy(xfb.ap[:], xff.ap[:]),
               reads=(xff,), writes=(xfb,))
            op("v", lambda v: v.tensor_tensor(xlo.ap[:], xff.ap[:], xfb.ap[:],
                                              ALU.subtract),
               reads=(xff, xfb), writes=(xlo,))
            for c in range(NC):
                p = tp5_ps[c % 2]
                op("t", lambda t_, c=c, p=p: t_.transpose(
                    p.ap[:, 0:128], xfb.ap[:, c * 128:(c + 1) * 128], ident.ap[:]),
                    reads=(xfb, ident), writes=(p,))
                op("s", lambda s, c=c, p=p, ti=ti: s.copy(
                    xf.ap[:, c * 2048 + ti * 128: c * 2048 + (ti + 1) * 128],
                    p.ap[:, 0:128]),
                    reads=(p,), writes=(xf,))
            for c in range(NC):
                p = tp5_ps[c % 2]
                op("t", lambda t_, c=c, p=p: t_.transpose(
                    p.ap[:, 0:128], xlo.ap[:, c * 128:(c + 1) * 128], ident.ap[:]),
                    reads=(xlo, ident), writes=(p,))
                op("s", lambda s, c=c, p=p: s.copy(
                    xlofm.ap[:, c * 128:(c + 1) * 128], p.ap[:, 0:128]),
                    reads=(p,), writes=(xlofm,))
            nmm = 0
            for c in range(NC):
                for lh, rh in (("hi", "hi"), ("hi", "lo"), ("lo", "hi")):
                    lop = (xf if lh == "hi" else xlofm)
                    lsl = (slice(c * 2048 + ti * 128, c * 2048 + (ti + 1) * 128)
                           if lh == "hi" else slice(c * 128, (c + 1) * 128))
                    rop = wrb2 if rh == "hi" else wrlo
                    op("t", lambda t_, c=c, lop=lop, lsl=lsl, rop=rop, nmm=nmm: t_.matmul(
                        rt_ps.ap[:, 0:NE], lop.ap[:, lsl],
                        rop.ap[:, c * NE:(c + 1) * NE],
                        start=(nmm == 0), stop=(nmm == 3 * NC - 1)),
                        reads=(lop, rop), writes=(rt_ps,))
                    nmm += 1
            op("v", lambda v: v.reduce_max(r8["m8"].ap[:, 0:1], rt_ps.ap[:, 0:NE], axis=AX.X),
               reads=(rt_ps,), writes=(r8["m8"],))
            op("v", lambda v: v.tensor_scalar(r8["nm8"].ap[:, 0:1], r8["m8"].ap[:, 0:1],
                                              -1.0, None, ALU.mult),
               reads=(r8["m8"],), writes=(r8["nm8"],))
            op("s", lambda s: s.activation(p8.ap[:], rt_ps.ap[:, 0:NE], AF.Exp,
                                           bias=r8["nm8"].ap[:, 0:1], scale=1.0),
               reads=(rt_ps, r8["nm8"]), writes=(p8,))
            op("v", lambda v: v.reduce_max(r8["m1"].ap[:, 0:1], p8.ap[:], axis=AX.X),
               reads=(p8,), writes=(r8["m1"],))
            op("v", lambda v: v.tensor_scalar(msk1.ap[:], p8.ap[:], r8["m1"].ap[:, 0:1],
                                              None, ALU.is_equal),
               reads=(p8, r8["m1"]), writes=(msk1,))
            op("v", lambda v: v.scalar_tensor_tensor(p8b.ap[:], msk1.ap[:], -1e30,
                                                     p8.ap[:], ALU.mult, ALU.add),
               reads=(msk1, p8), writes=(p8b,))
            op("v", lambda v: v.reduce_max(r8["m2"].ap[:, 0:1], p8b.ap[:], axis=AX.X),
               reads=(p8b,), writes=(r8["m2"],))
            op("v", lambda v: v.tensor_scalar(sel8.ap[:], p8.ap[:], r8["m2"].ap[:, 0:1],
                                              None, ALU.is_ge),
               reads=(p8, r8["m2"]), writes=(sel8,))
            op("v", lambda v: v.tensor_tensor(r8["den"].ap[:, 0:1], r8["m1"].ap[:, 0:1],
                                              r8["m2"].ap[:, 0:1], ALU.add),
               reads=(r8["m1"], r8["m2"]), writes=(r8["den"],))
            op("v", lambda v: v.reciprocal(r8["dinv"].ap[:, 0:1], r8["den"].ap[:, 0:1]),
               reads=(r8["den"],), writes=(r8["dinv"],))
            op("v", lambda v, ti=ti: v.scalar_tensor_tensor(
                comb.ap[:, ti * NE:(ti + 1) * NE], p8.ap[:], r8["dinv"].ap[:, 0:1],
                sel8.ap[:], ALU.mult, ALU.mult),
               reads=(p8, r8["dinv"], sel8), writes=(comb,))

        ir.fence()
        b2s.close()

        wdm = [k.sb(D, BF16, b2) for _ in range(NHSL)]
        wgs = [k.sb(NC * 256, BF16, b2) for _ in range(2)]
        wus = [k.sb(NC * 256, BF16, b2) for _ in range(2)]
        mall = k.sb(NHSL * 1024, BF16, b2)
        gsb = [k.sb(512, BF16, b2) for _ in range(2)]
        yac = k.sb(8 * D, F32, b2)
        rbt = [k.sb(D, F32, b2) for _ in range(2)]
        outb = [k.sb(D, F32, b2) for _ in range(2)]
        gu_ps = [k.ps(512, F32, b2) for _ in range(4)]
        dn_ps = [k.ps(512, F32, b2) for _ in range(4)]

        NHC = NHSL // 2

        for thalf in range(2):
            op("v", lambda v: v.memset(yac.ap[:], 0.0), writes=(yac,))
            for e in range(NE):
                for c in range(NHSL):
                    op("q", lambda q, e=e, c=c: q.dma_start(
                        out=wdm[c].ap[:], in_=wdb_d[e, c * 128:(c + 1) * 128, :]),
                        writes=(wdm[c],), isdma=True)
                for hc in range(NHC):
                    wgb = wgs[hc % 2]
                    wub = wus[hc % 2]
                    op("q", lambda q, e=e, hc=hc, wgb=wgb: q.dma_start(
                        out=wgb.ap[:],
                        in_=wgb_d[e, :, hc * 256:(hc + 1) * 256]
                        .rearrange("(a p) j -> p a j", p=128)),
                        writes=(wgb,), isdma=True)
                    op("q", lambda q, e=e, hc=hc, wub=wub: q.dma_start(
                        out=wub.ap[:],
                        in_=wub_d[e, :, hc * 256:(hc + 1) * 256]
                        .rearrange("(a p) j -> p a j", p=128)),
                        writes=(wub,), isdma=True)
                    for hs in range(2):
                        for c in range(NC):
                            for t5 in range(2):
                                op("t", lambda t_, c=c, hs=hs, t5=t5, wgb=wgb,
                                   thalf=thalf: t_.matmul(
                                    gu_ps[2 * t5 + 0].ap[:],
                                    wgb.ap[:, c * 256 + hs * 128: c * 256 + (hs + 1) * 128],
                                    xf.ap[:, c * 2048 + thalf * 1024 + t5 * 512:
                                          c * 2048 + thalf * 1024 + (t5 + 1) * 512],
                                    start=(c == 0), stop=(c == NC - 1)),
                                    reads=(wgb, xf), writes=(gu_ps[2 * t5 + 0],))
                        for c in range(NC):
                            for t5 in range(2):
                                op("t", lambda t_, c=c, hs=hs, t5=t5, wub=wub,
                                   thalf=thalf: t_.matmul(
                                    gu_ps[2 * t5 + 1].ap[:],
                                    wub.ap[:, c * 256 + hs * 128: c * 256 + (hs + 1) * 128],
                                    xf.ap[:, c * 2048 + thalf * 1024 + t5 * 512:
                                          c * 2048 + thalf * 1024 + (t5 + 1) * 512],
                                    start=(c == 0), stop=(c == NC - 1)),
                                    reads=(wub, xf), writes=(gu_ps[2 * t5 + 1],))
                        hsl = hc * 2 + hs
                        for t5 in range(2):
                            gs = gsb[t5]
                            op("s", lambda s, t5=t5, gs=gs: s.activation(
                                gs.ap[:], gu_ps[2 * t5 + 0].ap[:], AF.Silu,
                                bias=0.0, scale=1.0),
                                reads=(gu_ps[2 * t5 + 0],), writes=(gs,))
                            op("v", lambda v, gs=gs, hsl=hsl, t5=t5: v.tensor_tensor(
                                mall.ap[:, hsl * 1024 + t5 * 512: hsl * 1024 + (t5 + 1) * 512],
                                gs.ap[:], gu_ps[2 * t5 + 1].ap[:], ALU.mult),
                                reads=(gs, gu_ps[2 * t5 + 1]), writes=(mall,))
                for dh in range(2):
                    for hsl in range(NHSL):
                        for t128 in range(8):
                            dp = (dn_ps + gu_ps)[t128]
                            op("t", lambda t_, hsl=hsl, t128=t128, dh=dh, dp=dp: t_.matmul(
                                dp.ap[:],
                                mall.ap[:, hsl * 1024 + t128 * 128: hsl * 1024 + (t128 + 1) * 128],
                                wdm[hsl].ap[:, dh * 512:(dh + 1) * 512],
                                start=(hsl == 0), stop=(hsl == NHSL - 1)),
                                reads=(mall, wdm[hsl]), writes=(dp,))
                    for t128 in range(8):
                        dp = (dn_ps + gu_ps)[t128]
                        ci = (thalf * 8 + t128) * NE + e
                        op("v", lambda v, dp=dp, t128=t128, dh=dh, ci=ci: v.scalar_tensor_tensor(
                            yac.ap[:, t128 * D + dh * 512: t128 * D + (dh + 1) * 512],
                            dp.ap[:], comb.ap[:, ci:ci + 1],
                            yac.ap[:, t128 * D + dh * 512: t128 * D + (dh + 1) * 512],
                            ALU.mult, ALU.add),
                            reads=(dp, comb, yac), writes=(yac,))
            for t128 in range(8):
                ti = thalf * 8 + t128
                rb2 = rbt[t128 % 2]
                ob = outb[t128 % 2]
                op("q", lambda q, ti=ti, rb2=rb2: q.dma_start(
                    out=rb2.ap[:], in_=r_d[ti * 128:(ti + 1) * 128, :]),
                    writes=(rb2,), isdma=True)
                op("v", lambda v, t128=t128, rb2=rb2, ob=ob: v.tensor_tensor(
                    ob.ap[:], rb2.ap[:], yac.ap[:, t128 * D:(t128 + 1) * D], ALU.add),
                    reads=(rb2, yac), writes=(ob,))
                op("q", lambda q, ti=ti, ob=ob: q.dma_start(
                    out=out_d[ti * 128:(ti + 1) * 128, :], in_=ob.ap[:]),
                    reads=(ob,), isdma=True)

        ir.fence()
        _finish(k, block, ctx)
        return nc, ctx


def _finish(k, block, ctx):
    ir = k.ir
    ir.fence()
    ir.op("g", lambda g: g.engine_nop())
    ir.lower(k.nc, k.ctx, block)


_ROPE = {}


def _rope_tables(role):
    if role in _ROPE:
        return _ROPE[role]
    pos = np.arange(S)
    if role == 0:
        pos = np.clip(pos - 2048, 0, None)
    inv = 1.0 / (10000.0 ** (np.arange(0, HD, 2, dtype=np.float32) / HD))
    freqs = np.einsum("i,j->ij", pos.astype(np.float32), inv)
    emb = np.concatenate([freqs, freqs], axis=-1)
    cos = np.cos(emb).astype(np.float32)
    sin = np.sin(emb).astype(np.float32)
    sin_signed = np.concatenate([-sin[:, :32], sin[:, 32:]], axis=-1).astype(np.float32)
    cosr = np.ascontiguousarray(np.tile(cos, (1, H)))
    sinr = np.ascontiguousarray(np.tile(sin_signed, (1, H)))
    _ROPE[role] = (cosr, sinr)
    return _ROPE[role]


def make_in_maps(inputs):
    import ml_dtypes
    x = np.asarray(inputs["L0_attn_embeddings"], np.float32)
    wg = np.ascontiguousarray(np.pad(np.asarray(inputs["L0_ffn_wg"], np.float32),
                                     ((0, 0), (0, 0), (0, HID - 2752))))
    wu = np.ascontiguousarray(np.pad(np.asarray(inputs["L0_ffn_wu"], np.float32),
                                     ((0, 0), (0, 0), (0, HID - 2752))))
    wd = np.ascontiguousarray(np.pad(np.asarray(inputs["L0_ffn_wd"], np.float32),
                                     ((0, 0), (0, HID - 2752), (0, 0))))
    ident = np.eye(128, dtype=np.float32).astype(ml_dtypes.bfloat16)
    tri = np.triu(np.full((128, 128), NEGBIG, np.float32), k=1).astype(ml_dtypes.bfloat16)
    g1bc = np.ascontiguousarray(np.broadcast_to(
        np.asarray(inputs["L0_ffn_g1"], np.float32), (128, D)))
    g2bc = np.ascontiguousarray(np.broadcast_to(
        np.asarray(inputs["L0_ffn_g2"], np.float32), (128, D)))
    shared = dict(
        wq1=np.asarray(inputs["L0_attn_W_Q"], np.float32),
        wk1=np.asarray(inputs["L0_attn_W_K"], np.float32),
        wv1=np.asarray(inputs["L0_attn_W_V"], np.float32),
        wq2=np.asarray(inputs["L0_ffn_wq"], np.float32),
        wk2=np.asarray(inputs["L0_ffn_wk"], np.float32),
        wv2=np.asarray(inputs["L0_ffn_wv"], np.float32),
        wo2=np.asarray(inputs["L0_ffn_wo"], np.float32),
        wr=np.asarray(inputs["L0_ffn_w_router"], np.float32),
        wg=wg, wu=wu, wd=wd, ident=ident, tri=tri, g1bc=g1bc, g2bc=g2bc,
    )
    in_maps = []
    zeros = np.zeros((2048, D), np.float32)
    for c in range(8):
        b, role = c // 2, c % 2
        xv = x[b] if role == 1 else np.ascontiguousarray(
            np.concatenate([zeros, x[b][:2048]], axis=0))
        cosr, sinr = _rope_tables(role)
        m = dict(shared)
        m["consts"] = np.ascontiguousarray(
            np.broadcast_to(np.asarray(CONSTS, np.float32), (128, len(CONSTS))))
        m["xv"] = xv
        m["cosr"] = cosr
        m["sinr"] = sinr
        m["npad"] = np.full((128, 1), 1024.0 if role == 0 else 0.0, np.float32)
        in_maps.append(m)
    return in_maps


def kernel(**inputs):
    in_maps = make_in_maps(inputs)
    nc, ctx = build_program("full")
    res = run_bass_kernel_spmd(nc, in_maps, core_ids=list(range(8)))
    out = np.empty((B, S, D), np.float32)
    for c in range(8):
        b, role = c // 2, c % 2
        rows = slice(0, 2048) if role == 0 else slice(2048, 4096)
        out[b, rows] = res.results[c]["out"]
    return out


# version 09
# speedup vs baseline: 1.4511x; 1.1218x over previous
import numpy as np
from contextlib import ExitStack

import concourse.bass as bass
import concourse.mybir as mybir
from concourse.bass_utils import run_bass_kernel_spmd

F32 = mybir.dt.float32
BF16 = mybir.dt.bfloat16
AF = mybir.ActivationFunctionType
ALU = mybir.AluOpType
AX = mybir.AxisListType

B, S, D = 4, 4096, 1024
NT = S // 128
NC = D // 128
H, HD = 16, 64
HID = 2816
NHSL = HID // 128
NE = 8
OWN0 = NT // 2
KBLK = 1024
LN_EPS = 1e-5
RMS_EPS = 1e-6
NEGBIG = -1.0e9
CONSTS = (0.0, 1.0, -0.5, -1.0, LN_EPS, RMS_EPS, 1.0 / 32.0, 0.125, 1.0 / D, 2.0)

ENGS = ("q", "g", "t", "v", "s")


class Op:
    __slots__ = ("eng", "fn", "deps", "inc", "isdma", "sem_val", "drain", "idx")

    def __init__(self, eng, fn, deps, isdma, idx):
        self.eng, self.fn, self.deps, self.isdma = eng, fn, list(deps), isdma
        self.inc = isdma
        self.sem_val = None
        self.drain = False
        self.idx = idx


class Buf:
    __slots__ = ("ap", "w", "rs")

    def __init__(self, ap):
        self.ap, self.w, self.rs = ap, None, []


class IR:
    def __init__(self):
        self.ops = []
        self.fences = []

    def op(self, eng, fn, reads=(), writes=(), isdma=False):
        deps = []
        for b in reads:
            if b.w is not None:
                deps.append(b.w)
        for b in writes:
            if b.w is not None:
                deps.append(b.w)
            deps.extend(b.rs)
        o = Op(eng, fn, deps, isdma, len(self.ops))
        self.ops.append(o)
        for b in reads:
            b.rs.append(o)
        for b in writes:
            b.w = o
            b.rs = []
        return o

    def fence(self):
        last = {}
        for o in self.ops:
            last[(o.eng, o.isdma)] = o
        self.fences.append((len(self.ops), last))

    def lower(self, nc, ctx, block):
        fence_i = 0
        pend = None
        seen_after = set()
        for o in self.ops:
            while fence_i < len(self.fences) and self.fences[fence_i][0] <= o.idx:
                pend = self.fences[fence_i][1]
                seen_after = set()
                fence_i += 1
            if pend is not None and o.eng not in seen_after:
                for (e2, _dma), d in pend.items():
                    if e2 != o.eng or _dma:
                        o.deps.append(d)
                seen_after.add(o.eng)

        for o in self.ops:
            ded, seen = [], set()
            for d in o.deps:
                if id(d) in seen or d is o:
                    continue
                seen.add(id(d))
                ded.append(d)
                if d.isdma or d.eng != o.eng:
                    d.inc = True
            o.deps = ded

        recent = {e: [] for e in ENGS}
        for o in self.ops:
            if o.eng == "v" and not o.isdma:
                for d in o.deps:
                    if (not d.isdma) and d.eng == "v" and d in recent["v"]:
                        o.drain = True
            r = recent[o.eng]
            r.append(o)
            if len(r) > 2:
                r.pop(0)

        sems = {e: ctx.enter_context(nc.semaphore(f"sem_{e}")) for e in ENGS}
        dsems = {e: ctx.enter_context(nc.semaphore(f"dma_{e}")) for e in ("q", "g")}
        cnt = {e: 0 for e in ENGS}
        dcnt = {e: 0 for e in ("q", "g")}
        for o in self.ops:
            if o.isdma:
                dcnt[o.eng] += 16
                o.sem_val = dcnt[o.eng]
            elif o.inc:
                cnt[o.eng] += 1
                o.sem_val = cnt[o.eng]

        handles = {"q": block.sync, "g": block.gpsimd, "t": block.tensor,
                   "v": block.vector, "s": block.scalar}
        ir = self

        def emit_for(eng):
            def run(h):
                waited = {}
                for o in ir.ops:
                    if o.eng != eng:
                        continue
                    for d in o.deps:
                        if (not d.isdma) and d.eng == eng:
                            continue
                        sem = dsems[d.eng] if d.isdma else sems[d.eng]
                        key = (d.isdma, d.eng)
                        if waited.get(key, -1) >= d.sem_val:
                            continue
                        h.wait_ge(sem, d.sem_val)
                        waited[key] = d.sem_val
                    if o.drain:
                        h.drain()
                    inst = o.fn(h)
                    if o.isdma:
                        inst.then_inc(dsems[o.eng], 16)
                    elif o.inc:
                        inst.then_inc(sems[o.eng], 1)
            return run

        for e in ENGS:
            handles[e](emit_for(e))


class K:
    def __init__(self, nc, ctx, ir):
        self.nc, self.ctx, self.ir = nc, ctx, ir
        self.uid = 0

    def name(self, p):
        self.uid += 1
        return f"{p}{self.uid}"

    def sb(self, cols, dtype, scope=None):
        t = (scope or self.ctx).enter_context(
            self.nc.sbuf_tensor(self.name("sb"), [128, cols], dtype))
        return Buf(t.ap())

    def ps(self, cols, dtype, scope=None):
        t = (scope or self.ctx).enter_context(
            self.nc.psum_tensor(self.name("ps"), [128, cols], dtype))
        return Buf(t.ap())

    def dram(self, name, shape, dtype, kind="Internal"):
        return self.nc.dram_tensor(name, shape, dtype, kind=kind).ap()


def build_program(upto="full"):
    nc = bass.Bass("TRN2", target_bir_lowering=False)
    ctx = ExitStack()
    ir = IR()
    k = K(nc, ctx, ir)
    op = ir.op

    ein, eout = "ExternalInput", "ExternalOutput"
    xv_d = k.dram("xv", [S, D], F32, ein)
    w_d = {n: k.dram(n, [D, D], F32, ein) for n in
           ("wq1", "wk1", "wv1", "wq2", "wk2", "wv2", "wo2")}
    wr_d = k.dram("wr", [D, NE], F32, ein)
    wg_d = k.dram("wg", [NE, D, HID], F32, ein)
    wu_d = k.dram("wu", [NE, D, HID], F32, ein)
    wd_d = k.dram("wd", [NE, HID, D], F32, ein)
    g1bc_d = k.dram("g1bc", [128, D], F32, ein)
    g2bc_d = k.dram("g2bc", [128, D], F32, ein)
    cosr_d = k.dram("cosr", [S, D], F32, ein)
    sinr_d = k.dram("sinr", [S, D], F32, ein)
    ident_d = k.dram("ident", [128, 128], BF16, ein)
    tri_d = k.dram("tri", [128, 128], BF16, ein)
    npad_d = k.dram("npad", [128, 1], F32, ein)

    consts_d = k.dram("consts", [128, len(CONSTS)], F32, ein)
    out_d = k.dram("out", [S // 2, D], F32, eout)
    x1_d = k.dram("x1_scr", [S, D], F32, eout if upto == "A2" else "Internal")
    xn_d = k.dram("xn_scr", [NT, NC, 128, 128], BF16, "Internal")
    q2_d = k.dram("q2_scr", [NT // 2, NC, 128, 128], BF16, "Internal")
    h_d = k.dram("h_scr", [S // 2, D], F32, eout if upto.startswith("B1b") else "Internal")
    r_d = k.dram("r_scr", [S // 2, D], F32, eout if upto.startswith("B1b") else "Internal")
    wgb_d = k.dram("wgb_scr", [NE, D, HID], BF16, "Internal")
    wub_d = k.dram("wub_scr", [NE, D, HID], BF16, "Internal")
    wdb_d = k.dram("wdb_scr", [NE, HID, D], BF16, "Internal")

    ident = k.sb(128, BF16)
    tri = k.sb(128, BF16)
    npad = k.sb(1, F32)
    cbuf = k.sb(len(CONSTS), F32)
    for _i, _val in enumerate(CONSTS):
        nc.const_aps.register(F32, float(_val), cbuf.ap[:, _i:_i + 1])
    st_scope = ExitStack()
    stage = [k.sb(HID, BF16, st_scope)]

    def precast_pairs():
        for dsrc, ddst in ((wg_d, wgb_d), (wu_d, wub_d)):
            for e in range(NE):
                for c in range(NC):
                    yield (dsrc, ddst, e, c, HID)
        for e in range(NE):
            for c in range(NHSL):
                yield (wd_d, wdb_d, e, c, D)

    _pc = precast_pairs()
    _pc_n = [0]

    def pump(n):
        for _ in range(n):
            try:
                dsrc, ddst, e, c, w = next(_pc)
            except StopIteration:
                return
            st = stage[0]
            _pc_n[0] += 1
            op("g", lambda g, dsrc=dsrc, e=e, c=c, st=st, w=w: g.dma_start(
                out=st.ap[:, 0:w], in_=dsrc[e, c * 128:(c + 1) * 128, :]),
                writes=(st,), isdma=True)
            op("q", lambda q, ddst=ddst, e=e, c=c, st=st, w=w: q.dma_start(
                out=ddst[e, c * 128:(c + 1) * 128, :], in_=st.ap[:, 0:w]),
                reads=(st,), isdma=True)

    def pump_all():
        while True:
            before = _pc_n[0]
            pump(8)
            if _pc_n[0] == before:
                break

    def mk_scr(scope):
        return {n: k.sb(16, F32, scope) for n in
                ("st6", "mv", "negmean", "lnv", "rstd", "ss")}

    def ln_tile(x_in, scr, eps, out_f32=None, out_bf=None):
        for hh in range(2):
            op("v", lambda v, hh=hh: v.bn_stats(
                scr["st6"].ap[:, hh * 6:hh * 6 + 6],
                x_in.ap[:, hh * 512:(hh + 1) * 512]),
               reads=(x_in,), writes=(scr["st6"],))
        op("v", lambda v: v.bn_aggr(scr["mv"].ap[:, 0:2], scr["st6"].ap[:, 0:12]),
           reads=(scr["st6"],), writes=(scr["mv"],))
        op("v", lambda v: v.tensor_scalar(scr["negmean"].ap[:, 0:1],
                                          scr["mv"].ap[:, 0:1], -1.0, None, ALU.mult),
           reads=(scr["mv"],), writes=(scr["negmean"],))
        op("s", lambda s: s.activation(scr["lnv"].ap[:, 0:1], scr["mv"].ap[:, 1:2],
                                       AF.Ln, bias=float(eps), scale=1.0),
           reads=(scr["mv"],), writes=(scr["lnv"],))
        op("s", lambda s: s.activation(scr["rstd"].ap[:, 0:1], scr["lnv"].ap[:, 0:1],
                                       AF.Exp, bias=0.0, scale=-0.5),
           reads=(scr["lnv"],), writes=(scr["rstd"],))
        for out, dt in ((out_f32, F32), (out_bf, BF16)):
            if out is not None:
                op("v", lambda v, out=out: v.tensor_scalar(
                    out.ap[:], x_in.ap[:], scr["negmean"].ap[:, 0:1],
                    scr["rstd"].ap[:, 0:1], ALU.add, ALU.mult),
                   reads=(x_in, scr["negmean"], scr["rstd"]), writes=(out,))

    def rms_tile(z_f32, scr, gain_bc, sqdump, out_bf):
        op("s", lambda s: s.activation(sqdump.ap[:], z_f32.ap[:], AF.Square,
                                       bias=0.0, scale=1.0,
                                       accum_out=scr["ss"].ap[:, 0:1]),
           reads=(z_f32,), writes=(sqdump, scr["ss"]))
        op("s", lambda s: s.activation(scr["lnv"].ap[:, 0:1], scr["ss"].ap[:, 0:1],
                                       AF.Ln, bias=float(RMS_EPS), scale=1.0 / D),
           reads=(scr["ss"],), writes=(scr["lnv"],))
        op("s", lambda s: s.activation(scr["rstd"].ap[:, 0:1], scr["lnv"].ap[:, 0:1],
                                       AF.Exp, bias=0.0, scale=-0.5),
           reads=(scr["lnv"],), writes=(scr["rstd"],))
        op("v", lambda v: v.scalar_tensor_tensor(
            out_bf.ap[:], z_f32.ap[:], scr["rstd"].ap[:, 0:1], gain_bc.ap[:],
            ALU.mult, ALU.mult),
           reads=(z_f32, scr["rstd"], gain_bc), writes=(out_bf,))

    def load_w_cast(dst, dram, cols=D):
        for c in range(NC):
            op("g", lambda g, c=c, dst=dst, dram=dram, cols=cols: g.dma_start(
                out=dst.ap[:, c * cols:(c + 1) * cols],
                in_=dram[c * 128:(c + 1) * 128, :]),
                writes=(dst,), isdma=True)

    def transposes(src_bf, col_of, dst, dcol_of, n, tp_ps, evac_batch=4,
                   scale=None, dstbuf_extra=()):
        for base in range(0, n, evac_batch):
            m = min(evac_batch, n - base)
            p = tp_ps[(base // evac_batch) % 2]
            for j in range(m):
                c = base + j
                op("t", lambda t, c=c, j=j, p=p: t.transpose(
                    p.ap[:, j * 128:(j + 1) * 128],
                    src_bf.ap[:, col_of(c):col_of(c) + 128], ident.ap[:]),
                    reads=(src_bf, ident), writes=(p,))
            dc = dcol_of(base)
            if scale is None:
                op("s", lambda s, p=p, dc=dc, m=m: s.copy(
                    dst.ap[:, dc:dc + m * 128], p.ap[:, 0:m * 128]),
                    reads=(p,), writes=(dst,) + tuple(dstbuf_extra))
            else:
                op("s", lambda s, p=p, dc=dc, m=m: s.activation(
                    dst.ap[:, dc:dc + m * 128], p.ap[:, 0:m * 128], AF.Copy,
                    bias=0.0, scale=float(scale)),
                    reads=(p,), writes=(dst,) + tuple(dstbuf_extra))

    kv_scope = ExitStack()
    KV1 = k.sb(NC * S, BF16, kv_scope)
    KV2 = k.sb(NT * D, BF16, kv_scope)

    with nc.Block() as block:
        for dst, src in ((ident, ident_d), (tri, tri_d), (npad, npad_d),
                         (cbuf, consts_d)):
            op("q", lambda q, dst=dst, src=src: q.dma_start(out=dst.ap[:], in_=src[:]),
               writes=(dst,), isdma=True)
        ir.fence()

        a1 = ExitStack()
        w1k = k.sb(NC * D, BF16, a1)
        w1v = k.sb(NC * D, BF16, a1)
        xt = [k.sb(D, F32, a1) for _ in range(4)]
        lnb = [k.sb(D, BF16, a1) for _ in range(8)]
        lnfm4 = k.sb(NC * 512, BF16, a1)
        scrA = [mk_scr(a1) for _ in range(2)]
        tp_ps = [k.ps(1024, BF16, a1) for _ in range(2)]
        pj_ps = [k.ps(512, F32, a1) for _ in range(4)]

        load_w_cast(w1k, w_d["wk1"])
        load_w_cast(w1v, w_d["wv1"])

        def a1_ln(t):
            xb = xt[t % 4]
            op("q", lambda q, t=t, xb=xb: q.dma_start(
                out=xb.ap[:], in_=xv_d[t * 128:(t + 1) * 128, :]),
                writes=(xb,), isdma=True)
            ln_tile(xb, scrA[t % 2], LN_EPS, out_bf=lnb[t % 8])

        def a1_pe(grp):
            for i in range(4):
                t = grp * 4 + i
                lb = lnb[t % 8]
                for c in range(NC):
                    p = tp_ps[c % 2]
                    op("t", lambda t_, c=c, p=p, lb=lb: t_.transpose(
                        p.ap[:, 0:128], lb.ap[:, c * 128:(c + 1) * 128], ident.ap[:]),
                        reads=(lb, ident), writes=(p,))
                    op("s", lambda s, c=c, i=i, p=p: s.copy(
                        lnfm4.ap[:, c * 512 + i * 128: c * 512 + (i + 1) * 128],
                        p.ap[:, 0:128]),
                        reads=(p,), writes=(lnfm4,))
                for half in range(2):
                    p = pj_ps[2 + half]
                    for c in range(NC):
                        op("t", lambda t_, c=c, half=half, i=i, p=p: t_.matmul(
                            p.ap[:],
                            lnfm4.ap[:, c * 512 + i * 128: c * 512 + (i + 1) * 128],
                            w1v.ap[:, c * D + half * 512: c * D + (half + 1) * 512],
                            start=(c == 0), stop=(c == NC - 1)),
                            reads=(lnfm4, w1v), writes=(p,))
                    op("s", lambda s, p=p, t=t, half=half: s.copy(
                        KV2.ap[:, t * D + half * 512: t * D + (half + 1) * 512],
                        p.ap[:]),
                        reads=(p,), writes=(KV2,))
            for do in range(NC):
                p = pj_ps[do % 2]
                for c in range(NC):
                    op("t", lambda t_, do=do, c=c, p=p: t_.matmul(
                        p.ap[:], w1k.ap[:, c * D + do * 128: c * D + (do + 1) * 128],
                        lnfm4.ap[:, c * 512:(c + 1) * 512],
                        start=(c == 0), stop=(c == NC - 1)),
                        reads=(w1k, lnfm4), writes=(p,))
                op("s", lambda s, p=p, do=do, grp=grp: s.copy(
                    KV1.ap[:, do * S + grp * 512: do * S + (grp + 1) * 512], p.ap[:]),
                    reads=(p,), writes=(KV1,))

        for i in range(4):
            a1_ln(i)
        for grp in range(NT // 4):
            for i in range(4):
                t2 = grp * 4 + 4 + i
                if t2 < NT:
                    a1_ln(t2)
            a1_pe(grp)
            pump(2)

        ir.fence()
        a1.close()
        if upto == "A1":
            _finish(k, block, ctx)
            return nc, ctx

        a2 = ExitStack()
        w1q = k.sb(NC * D, BF16, a2)
        xt2 = [k.sb(D, F32, a2) for _ in range(3)]
        lnb2 = [k.sb(D, BF16, a2) for _ in range(2)]
        lnfm = [k.sb(NC * 128, BF16, a2) for _ in range(2)]
        q1t = [k.sb(NC * 128, BF16, a2) for _ in range(2)]
        x1t = [k.sb(D, F32, a2) for _ in range(2)]
        pbuf = [k.sb(512, BF16, a2) for _ in range(3)]
        ptb = [k.sb(512, BF16, a2) for _ in range(3)]
        NBO = NT // 4
        bmx = [k.sb(NBO, F32, a2) for _ in range(2)]
        bsx = [k.sb(NBO, F32, a2) for _ in range(2)]
        nmO = [k.sb(8, F32, a2) for _ in range(2)]
        scrB = [mk_scr(a2) for _ in range(2)]
        sp1_ps = [k.ps(512, F32, a2) for _ in range(2)]
        sp2_ps = [k.ps(512, F32, a2) for _ in range(2)]
        av_ps = k.ps(D, F32, a2)
        tp2_ps = [k.ps(1024, BF16, a2)]

        load_w_cast(w1q, w_d["wq1"])
        guc = [0, 0]

        def a2_prelude(t):
            par = t % 2
            xb = xt2[t % 3]
            op("q", lambda q, t=t, xb=xb: q.dma_start(
                out=xb.ap[:], in_=xv_d[t * 128:(t + 1) * 128, :]),
                writes=(xb,), isdma=True)
            lb = lnb2[par]
            ln_tile(xb, scrB[par], LN_EPS, out_bf=lb)
            lf = lnfm[par]
            tp = tp2_ps[0]
            for cb4 in range(0, NC, 4):
                for j in range(4):
                    c = cb4 + j
                    op("t", lambda t_, c=c, j=j, tp=tp, lb=lb: t_.transpose(
                        tp.ap[:, j * 128:(j + 1) * 128],
                        lb.ap[:, c * 128:(c + 1) * 128], ident.ap[:]),
                        reads=(lb, ident), writes=(tp,))
                op("s", lambda s, cb4=cb4, tp=tp, lf=lf: s.copy(
                    lf.ap[:, cb4 * 128:(cb4 + 4) * 128], tp.ap[:, 0:512]),
                    reads=(tp,), writes=(lf,))
            qb = q1t[par]
            for do in range(NC):
                sp = sp1_ps[do % 2]
                for c in range(NC):
                    op("t", lambda t_, do=do, c=c, sp=sp, lf=lf: t_.matmul(
                        sp.ap[:, 0:128],
                        w1q.ap[:, c * D + do * 128: c * D + (do + 1) * 128],
                        lf.ap[:, c * 128:(c + 1) * 128],
                        start=(c == 0), stop=(c == NC - 1)),
                        reads=(w1q, lf), writes=(sp,))
                op("s", lambda s, do=do, sp=sp, qb=qb: s.activation(
                    qb.ap[:, do * 128:(do + 1) * 128], sp.ap[:, 0:128],
                    AF.Copy, bias=0.0, scale=1.0 / 32.0),
                    reads=(sp,), writes=(qb,))

        def a2_scores(sp, qb, kb, w, isdiag):
            for c in range(NC):
                op("t", lambda t_, c=c, kb=kb, w=w, sp=sp, qb=qb: t_.matmul(
                    sp.ap[:, 0:w],
                    qb.ap[:, c * 128:(c + 1) * 128],
                    KV1.ap[:, c * S + kb * 512: c * S + kb * 512 + w],
                    start=(c == 0), stop=(c == NC - 1 and not isdiag)),
                    reads=(qb, KV1), writes=(sp,))
            if isdiag:
                op("t", lambda t_, w=w, sp=sp: t_.matmul(
                    sp.ap[:, w - 128:w], ident.ap[:], tri.ap[:],
                    start=False, stop=True),
                    reads=(ident, tri), writes=(sp,))

        def a2_pass1(t):
            par = t % 2
            qb = q1t[par]
            nkey = (t + 1) * 128
            nb5 = (nkey + 511) // 512
            units = []

            def unit(kb):
                w = min(512, nkey - kb * 512)
                sp = sp1_ps[guc[0] % 2]
                guc[0] += 1
                a2_scores(sp, qb, kb, w, False)
                op("v", lambda v, sp=sp, w=w, kb=kb: v.reduce_max(
                    bmx[par].ap[:, kb:kb + 1], sp.ap[:, 0:w], axis=AX.X),
                    reads=(sp,), writes=(bmx[par],))
            for kb in range(nb5):
                units.append(lambda kb=kb: unit(kb))
            return units

        def a2_pass1_tail(t):
            par = t % 2
            nkey = (t + 1) * 128
            nb5 = (nkey + 511) // 512
            op("v", lambda v, nb5=nb5: v.reduce_max(
                nmO[par].ap[:, 0:1], bmx[par].ap[:, 0:nb5], axis=AX.X, negate=True),
                reads=(bmx[par],), writes=(nmO[par],))
            padtot = min(2048, nkey)
            op("s", lambda s: s.activation(nmO[par].ap[:, 1:2], nmO[par].ap[:, 0:1],
                                           AF.Exp, bias=0.0, scale=1.0),
               reads=(nmO[par],), writes=(nmO[par],))
            op("v", lambda v, padtot=padtot: v.tensor_scalar(
                nmO[par].ap[:, 2:3], nmO[par].ap[:, 1:2], npad.ap[:, 0:1],
                float(padtot) / 1024.0, ALU.mult, ALU.mult),
                reads=(nmO[par], npad), writes=(nmO[par],))

        def a2_pass2(t):
            par = t % 2
            qb = q1t[par]
            nkey = (t + 1) * 128
            nb5 = (nkey + 511) // 512
            cnt = [0]

            def unitA(kb, g):
                w = min(512, nkey - kb * 512)
                sp = sp2_ps[g % 2]
                pb = pbuf[g % 3]
                a2_scores(sp, qb, kb, w, kb == nb5 - 1)
                op("s", lambda s, sp=sp, w=w, kb=kb, pb=pb: s.activation(
                    pb.ap[:, 0:w], sp.ap[:, 0:w], AF.Exp,
                    bias=nmO[par].ap[:, 0:1], scale=1.0,
                    accum_out=bsx[par].ap[:, kb:kb + 1]),
                    reads=(sp, nmO[par]), writes=(pb, bsx[par]))

            def unitB(kb, g):
                w = min(512, nkey - kb * 512)
                pb = pbuf[g % 3]
                pt = ptb[g % 3]
                tp = tp2_ps[0]
                ng = w // 128
                for j in range(ng):
                    op("t", lambda t_, j=j, pb=pb, tp=tp: t_.transpose(
                        tp.ap[:, j * 128:(j + 1) * 128],
                        pb.ap[:, j * 128:(j + 1) * 128], ident.ap[:]),
                        reads=(pb, ident), writes=(tp,))
                if g % 2 == 0:
                    op("s", lambda s, tp=tp, ng=ng, pt=pt: s.copy(
                        pt.ap[:, 0:ng * 128], tp.ap[:, 0:ng * 128]),
                        reads=(tp,), writes=(pt,))
                else:
                    op("v", lambda v, tp=tp, ng=ng, pt=pt: v.tensor_copy(
                        pt.ap[:, 0:ng * 128], tp.ap[:, 0:ng * 128]),
                        reads=(tp,), writes=(pt,))
                for half in range(2):
                    for gr in range(ng):
                        kt = kb * 4 + gr
                        op("t", lambda t_, gr=gr, kt=kt, half=half, pt=pt,
                           kb=kb, ng=ng: t_.matmul(
                            av_ps.ap[:, half * 512:(half + 1) * 512],
                            pt.ap[:, gr * 128:(gr + 1) * 128],
                            KV2.ap[:, kt * D + half * 512: kt * D + (half + 1) * 512],
                            start=(kb == 0 and gr == 0),
                            stop=(kb == nb5 - 1 and gr == ng - 1)),
                            reads=(pt, KV2), writes=(av_ps,))

            units = []
            LAG = 2
            for g in range(nb5 + LAG):
                if g < nb5:
                    units.append(lambda kb=g, g=g: unitA(kb, g))
                if g - LAG >= 0:
                    units.append(lambda kb=g - LAG, g=g - LAG: unitB(kb, g))
            return units

        def a2_epilogue(t):
            par = t % 2
            nkey = (t + 1) * 128
            nb5 = (nkey + 511) // 512
            sc = scrB[par]
            xb = xt2[t % 3]
            op("v", lambda v, nb5=nb5: v.reduce_sum(
                sc["ss"].ap[:, 0:1], bsx[par].ap[:, 0:nb5], axis=AX.X),
                reads=(bsx[par],), writes=(sc["ss"],))
            op("v", lambda v: v.tensor_tensor(
                sc["ss"].ap[:, 0:1], sc["ss"].ap[:, 0:1], nmO[par].ap[:, 2:3],
                ALU.subtract),
                reads=(nmO[par],), writes=(sc["ss"],))
            op("v", lambda v: v.tensor_scalar(
                sc["ss"].ap[:, 1:2], sc["ss"].ap[:, 0:1], 1e-30, None, ALU.add),
                reads=(sc["ss"],), writes=(sc["ss"],))
            op("v", lambda v: v.reciprocal(sc["ss"].ap[:, 1:2], sc["ss"].ap[:, 1:2]),
                reads=(sc["ss"],), writes=(sc["ss"],))
            xo = x1t[par]
            for half in range(2):
                op("v", lambda v, xb=xb, xo=xo, half=half: v.scalar_tensor_tensor(
                    xo.ap[:, half * 512:(half + 1) * 512],
                    av_ps.ap[:, half * 512:(half + 1) * 512],
                    sc["ss"].ap[:, 1:2],
                    xb.ap[:, half * 512:(half + 1) * 512], ALU.mult, ALU.add),
                    reads=(av_ps, sc["ss"], xb), writes=(xo,))
            op("q", lambda q, t=t, xo=xo: q.dma_start(
                out=x1_d[t * 128:(t + 1) * 128, :], in_=xo.ap[:]),
                reads=(xo,), isdma=True)

        prevt = None
        for t in range(NT):
            a2_prelude(t)
            u1 = a2_pass1(t)
            u2 = a2_pass2(prevt) if prevt is not None else []
            n1, n2 = len(u1), len(u2)
            i1 = i2 = 0
            while i1 < n1 or i2 < n2:
                if i2 < n2 and (i1 >= n1 or i2 * max(n1, 1) <= i1 * max(n2, 1)):
                    u2[i2](); i2 += 1
                else:
                    u1[i1](); i1 += 1
            a2_pass1_tail(t)
            if prevt is not None:
                a2_epilogue(prevt)
            prevt = t
            pump(2)
        for u in a2_pass2(prevt):
            u()
        a2_epilogue(prevt)

        ir.fence()
        a2.close()
        if upto in ("A2", "A2dbg"):
            _finish(k, block, ctx)
            return nc, ctx

        b1k = ExitStack()
        w2k = k.sb(NC * D, BF16, b1k)
        g1bc = k.sb(D, F32, b1k)
        x1b = [k.sb(D, F32, b1k) for _ in range(2)]
        zb = [k.sb(D, F32, b1k) for _ in range(2)]
        xnb = [k.sb(D, BF16, b1k) for _ in range(2)]
        xnfm = [k.sb(NC * 128, BF16, b1k) for _ in range(2)]
        cosb = [k.sb(D, BF16, b1k) for _ in range(2)]
        sinb = [k.sb(D, BF16, b1k) for _ in range(2)]
        rt1 = [k.sb(D, F32, b1k) for _ in range(2)]
        rt2 = [k.sb(D, F32, b1k) for _ in range(2)]
        krb = [k.sb(D, BF16, b1k) for _ in range(2)]
        scrC = [mk_scr(b1k) for _ in range(2)]
        kq_ps = [k.ps(D, F32, b1k) for _ in range(2)]
        tp3_ps = [k.ps(1024, BF16, b1k) for _ in range(2)]

        load_w_cast(w2k, w_d["wk2"])
        op("g", lambda g: g.dma_start(out=g1bc.ap[:], in_=g1bc_d[:]),
           writes=(g1bc,), isdma=True)

        def rope(par, src_ps, out_bf):
            r1, r2 = rt1[par], rt2[par]
            cb, sb_ = cosb[par], sinb[par]
            vsrc = src_ps.ap.rearrange("p (h j) -> p h j", j=HD)
            vsin = sb_.ap.rearrange("p (h j) -> p h j", j=HD)
            vr2 = r2.ap.rearrange("p (h j) -> p h j", j=HD)
            op("v", lambda v: v.tensor_tensor(r1.ap[:], src_ps.ap[:], cb.ap[:],
                                              ALU.mult),
               reads=(src_ps, cb), writes=(r1,))
            op("v", lambda v: v.tensor_tensor(vr2[:, :, 0:32], vsrc[:, :, 32:64],
                                              vsin[:, :, 0:32], ALU.mult),
               reads=(src_ps, sb_), writes=(r2,))
            op("v", lambda v: v.tensor_tensor(vr2[:, :, 32:64], vsrc[:, :, 0:32],
                                              vsin[:, :, 32:64], ALU.mult),
               reads=(src_ps, sb_), writes=(r2,))
            op("v", lambda v: v.tensor_tensor(out_bf.ap[:], r1.ap[:], r2.ap[:],
                                              ALU.add),
               reads=(r1, r2), writes=(out_bf,))

        def b1k_prep(t):
            par = t % 2
            xb = x1b[par]
            op("q", lambda q, t=t, xb=xb: q.dma_start(
                out=xb.ap[:], in_=x1_d[t * 128:(t + 1) * 128, :]),
                writes=(xb,), isdma=True)
            op("g", lambda g, t=t, par=par: g.dma_start(
                out=cosb[par].ap[:], in_=cosr_d[t * 128:(t + 1) * 128, :]),
                writes=(cosb[par],), isdma=True)
            op("g", lambda g, t=t, par=par: g.dma_start(
                out=sinb[par].ap[:], in_=sinr_d[t * 128:(t + 1) * 128, :]),
                writes=(sinb[par],), isdma=True)
            ln_tile(xb, scrC[par], LN_EPS, out_f32=zb[par])
            rms_tile(zb[par], scrC[par], g1bc, xnb[par], xnb[par])

        def b1k_xpose(t):
            par = t % 2
            tp = tp3_ps[0]
            for cb4 in range(0, NC, 4):
                for j in range(4):
                    c = cb4 + j
                    op("t", lambda t_, c=c, j=j, tp=tp, par=par: t_.transpose(
                        tp.ap[:, j * 128:(j + 1) * 128],
                        xnb[par].ap[:, c * 128:(c + 1) * 128], ident.ap[:]),
                        reads=(xnb[par], ident), writes=(tp,))
                op("s", lambda s, cb4=cb4, tp=tp, par=par: s.copy(
                    xnfm[par].ap[:, cb4 * 128:(cb4 + 4) * 128], tp.ap[:, 0:512]),
                    reads=(tp,), writes=(xnfm[par],))
            op("q", lambda q, t=t, par=par: q.dma_start(
                out=xn_d[t, :, :, :].rearrange("a p j -> p a j"),
                in_=xnfm[par].ap[:]),
                reads=(xnfm[par],), isdma=True)

        def b1k_k2(t):
            par = t % 2
            kp = kq_ps[par]
            for c in range(NC):
                for half in range(2):
                    op("t", lambda t_, c=c, half=half, kp=kp, par=par: t_.matmul(
                        kp.ap[:, half * 512:(half + 1) * 512],
                        xnfm[par].ap[:, c * 128:(c + 1) * 128],
                        w2k.ap[:, c * D + half * 512: c * D + (half + 1) * 512],
                        start=(c == 0), stop=(c == NC - 1)),
                        reads=(xnfm[par], w2k), writes=(kp,))

        def b1k_rope(t):
            par = t % 2
            rope(par, kq_ps[par], krb[par])

        def b1k_kt(t):
            par = t % 2
            tp = tp3_ps[1]
            for cb4 in range(0, NC, 4):
                for j in range(4):
                    c = cb4 + j
                    op("t", lambda t_, c=c, j=j, tp=tp, par=par: t_.transpose(
                        tp.ap[:, j * 128:(j + 1) * 128],
                        krb[par].ap[:, c * 128:(c + 1) * 128], ident.ap[:]),
                        reads=(krb[par], ident), writes=(tp,))
                for j in range(4):
                    c = cb4 + j
                    op("s", lambda s, c=c, j=j, tp=tp, t=t: s.copy(
                        KV1.ap[:, c * S + t * 128: c * S + (t + 1) * 128],
                        tp.ap[:, j * 128:(j + 1) * 128]),
                        reads=(tp,), writes=(KV1,))

        stages = [b1k_prep, b1k_xpose, b1k_k2, b1k_rope, b1k_kt]
        sched = []
        for t in range(NT + 2):
            if t < NT:
                sched += [(t, 0), (t, 1), (t, 2)]
            if t - 1 >= 0 and t - 1 < NT:
                sched += [(t - 1, 3)]
            if t - 2 >= 0 and t - 2 < NT:
                sched += [(t - 2, 4)]
        done = set()
        for t, s_i in sched:
            key = (t, s_i)
            if key in done:
                continue
            done.add(key)
            stages[s_i](t)
            if s_i == 0:
                pump(1)

        ir.fence()
        b1k.close()

        b1v = ExitStack()
        w2v = k.sb(NC * D, BF16, b1v)
        w2q = k.sb(NC * D, BF16, b1v)
        xnf2 = [k.sb(NC * 128, BF16, b1v) for _ in range(2)]
        cosb2 = [k.sb(D, BF16, b1v) for _ in range(2)]
        sinb2 = [k.sb(D, BF16, b1v) for _ in range(2)]
        rt1v = [k.sb(D, F32, b1v) for _ in range(2)]
        rt2v = [k.sb(D, F32, b1v) for _ in range(2)]
        krbv = [k.sb(D, BF16, b1v) for _ in range(2)]
        vq_ps = [k.ps(D, F32, b1v) for _ in range(2)]
        tp3b_ps = [k.ps(1024, BF16, b1v) for _ in range(2)]
        load_w_cast(w2v, w_d["wv2"])
        load_w_cast(w2q, w_d["wq2"])

        def ropev(par, src_ps, out_bf):
            r1, r2 = rt1v[par], rt2v[par]
            cb, sb_ = cosb2[par], sinb2[par]
            vsrc = src_ps.ap.rearrange("p (h j) -> p h j", j=HD)
            vsin = sb_.ap.rearrange("p (h j) -> p h j", j=HD)
            vr2 = r2.ap.rearrange("p (h j) -> p h j", j=HD)
            op("v", lambda v: v.tensor_tensor(r1.ap[:], src_ps.ap[:], cb.ap[:],
                                              ALU.mult),
               reads=(src_ps, cb), writes=(r1,))
            op("v", lambda v: v.tensor_tensor(vr2[:, :, 0:32], vsrc[:, :, 32:64],
                                              vsin[:, :, 0:32], ALU.mult),
               reads=(src_ps, sb_), writes=(r2,))
            op("v", lambda v: v.tensor_tensor(vr2[:, :, 32:64], vsrc[:, :, 0:32],
                                              vsin[:, :, 32:64], ALU.mult),
               reads=(src_ps, sb_), writes=(r2,))
            op("v", lambda v: v.tensor_tensor(out_bf.ap[:], r1.ap[:], r2.ap[:],
                                              ALU.add),
               reads=(r1, r2), writes=(out_bf,))

        def qv_load(t):
            par = t % 2
            xf = xnf2[par]
            op("q", lambda q, t=t, xf=xf: q.dma_start(
                out=xf.ap[:], in_=xn_d[t, :, :, :].rearrange("a p j -> p a j")),
                writes=(xf,), isdma=True)
            if t >= OWN0:
                op("g", lambda g, t=t, par=par: g.dma_start(
                    out=cosb2[par].ap[:], in_=cosr_d[t * 128:(t + 1) * 128, :]),
                    writes=(cosb2[par],), isdma=True)
                op("g", lambda g, t=t, par=par: g.dma_start(
                    out=sinb2[par].ap[:], in_=sinr_d[t * 128:(t + 1) * 128, :]),
                    writes=(sinb2[par],), isdma=True)

        def qv_v2(t):
            par = t % 2
            vp = vq_ps[par]
            xf = xnf2[par]
            for c in range(NC):
                for half in range(2):
                    op("t", lambda t_, c=c, half=half, vp=vp, xf=xf: t_.matmul(
                        vp.ap[:, half * 512:(half + 1) * 512],
                        xf.ap[:, c * 128:(c + 1) * 128],
                        w2v.ap[:, c * D + half * 512: c * D + (half + 1) * 512],
                        start=(c == 0), stop=(c == NC - 1)),
                        reads=(xf, w2v), writes=(vp,))
            for half in range(2):
                op("s", lambda s, vp=vp, t=t, half=half: s.copy(
                    KV2.ap[:, t * D + half * 512: t * D + (half + 1) * 512],
                    vp.ap[:, half * 512:(half + 1) * 512]),
                    reads=(vp,), writes=(KV2,))

        def qv_q2(t):
            par = t % 2
            qp = vq_ps[1 - par]
            xf = xnf2[par]
            for c in range(NC):
                for half in range(2):
                    op("t", lambda t_, c=c, half=half, qp=qp, xf=xf: t_.matmul(
                        qp.ap[:, half * 512:(half + 1) * 512],
                        xf.ap[:, c * 128:(c + 1) * 128],
                        w2q.ap[:, c * D + half * 512: c * D + (half + 1) * 512],
                        start=(c == 0), stop=(c == NC - 1)),
                        reads=(xf, w2q), writes=(qp,))
            ropev(par, qp, krbv[par])
            tp = tp3b_ps[par]
            for cb4 in range(0, NC, 4):
                for j in range(4):
                    c = cb4 + j
                    op("t", lambda t_, c=c, j=j, tp=tp, par=par: t_.transpose(
                        tp.ap[:, j * 128:(j + 1) * 128],
                        krbv[par].ap[:, c * 128:(c + 1) * 128], ident.ap[:]),
                        reads=(krbv[par], ident), writes=(tp,))
                op("s", lambda s, cb4=cb4, tp=tp, par=par: s.activation(
                    krbv[par].ap[:, cb4 * 128:(cb4 + 4) * 128], tp.ap[:, 0:512],
                    AF.Copy, bias=0.0, scale=0.125),
                    reads=(tp,), writes=(krbv[par],))
            op("q", lambda q, t=t, par=par: q.dma_start(
                out=q2_d[t - OWN0, :, :, :].rearrange("a p j -> p a j"),
                in_=krbv[par].ap[:]),
                reads=(krbv[par],), isdma=True)

        for t in range(NT):
            qv_load(t)
            if t >= 1:
                qv_v2(t - 1)
                if t - 1 >= OWN0:
                    qv_q2(t - 1)
            pump(2)
        qv_v2(NT - 1)
        qv_q2(NT - 1)

        ir.fence()
        b1v.close()
        if upto == "B1v":
            _finish(k, block, ctx)
            return nc, ctx

        b1b = ExitStack()
        wo2 = k.sb(NC * D, BF16, b1b)
        x1c = [k.sb(D, F32, b1b) for _ in range(2)]
        q2t = [k.sb(NC * 128, BF16, b1b) for _ in range(2)]
        zb2 = [k.sb(D, F32, b1b) for _ in range(2)]
        pb2 = [k.sb(KBLK, BF16, b1b) for _ in range(3)]
        ptb2 = [k.sb(KBLK, BF16, b1b) for _ in range(3)]
        at2 = [k.sb(D, BF16, b1b)] * 2
        atf = k.sb(D, BF16, b1b)
        hb = [k.sb(D, F32, b1b) for _ in range(2)]
        rb_ = [k.sb(D, F32, b1b) for _ in range(2)]
        NB5 = NT // 4
        bms = [k.sb(H * NB5, F32, b1b) for _ in range(2)]
        bss = [k.sb(H * NB5, F32, b1b) for _ in range(2)]
        mf = [k.sb(16, F32, b1b) for _ in range(2)]
        nmf = [k.sb(16, F32, b1b) for _ in range(2)]
        e0b = [k.sb(16, F32, b1b) for _ in range(2)]
        corb = [k.sb(16, F32, b1b) for _ in range(2)]
        lsum = [k.sb(16, F32, b1b) for _ in range(2)]
        linv = [k.sb(16, F32, b1b) for _ in range(2)]
        scrD = [mk_scr(b1b) for _ in range(2)]
        scpool4 = k.ps(4 * 512, F32, b1b)
        scpool = [Buf(scpool4.ap[:, i * 512:(i + 1) * 512]) for i in range(4)]
        scpool_pair_view = [
            scpool4.ap[:, 0:1024].rearrange("p (b w) -> p b w", b=2),
            scpool4.ap[:, 1024:2048].rearrange("p (b w) -> p b w", b=2)]
        ava_ps = k.ps(D, F32, b1b)
        tp4_ps = [k.ps(1024, BF16, b1b)]
        guc1 = [0]
        jslot = {}

        load_w_cast(wo2, w_d["wo2"])

        def b1b_pair_scmm(sppair, qb, hp, blk0, w, masked, isdiag):
            c = hp
            for hip in range(2):
                po = hip * 64
                sp = sppair[hip]
                op("t", lambda t_, c=c, po=po, blk0=blk0, w=w, sp=sp, qb=qb,
                   isdiag=isdiag: t_.matmul(
                    sp.ap[:, 0:w],
                    qb.ap[po:po + 64, c * 128:(c + 1) * 128],
                    KV1.ap[po:po + 64, c * S + blk0: c * S + blk0 + w],
                    start=True, stop=not isdiag,
                    tile_position=(po, 0)),
                    reads=(qb, KV1), writes=(sp,))
            if isdiag:
                for hip in range(2):
                    sp = sppair[hip]
                    op("t", lambda t_, w=w, sp=sp: t_.matmul(
                        sp.ap[:, w - 128:w], ident.ap[:], tri.ap[:],
                        start=False, stop=True),
                        reads=(ident, tri), writes=(sp,))

        def mk_pass1(t, p, qb):
            nkey = (t + 1) * 128
            nk5 = (nkey + 511) // 512
            units = []

            def unit(hp, kb5, w):
                j = guc1[0] % 2
                guc1[0] += 1
                sppair = (scpool[2 * j], scpool[2 * j + 1])
                b1b_pair_scmm(sppair, qb, hp, kb5 * 512, w, False, False)
                vv = scpool_pair_view[j]
                op("v", lambda v, vv=vv, w=w, hp=hp, kb5=kb5: v.reduce_max(
                    bms[p].ap[:, kb5 * H + hp * 2: kb5 * H + hp * 2 + 2],
                    vv[:, :, 0:w], axis=AX.X),
                    reads=(scpool[2 * j], scpool[2 * j + 1]), writes=(bms[p],))
            for kb5 in range(nk5):
                w = min(512, nkey - kb5 * 512)
                for hp in range(H // 2):
                    units.append(lambda hp=hp, kb5=kb5, w=w: unit(hp, kb5, w))
            return units

        def emit_pass1_tail(t, p):
            vb = bms[p].ap.rearrange("p (k a) -> p a k", a=H)
            op("v", lambda v, vb=vb: v.reduce_max(mf[p].ap[:], vb[:, :, 0:NB5],
                                                  axis=AX.X),
               reads=(bms[p],), writes=(mf[p],))
            op("v", lambda v: v.tensor_scalar(nmf[p].ap[:], mf[p].ap[:], -1.0, None,
                                              ALU.mult),
               reads=(mf[p],), writes=(nmf[p],))
            op("s", lambda s: s.activation(e0b[p].ap[:], mf[p].ap[:], AF.Exp,
                                           bias=0.0, scale=-1.0),
               reads=(mf[p],), writes=(e0b[p],))
            op("v", lambda v: v.tensor_scalar(corb[p].ap[:], e0b[p].ap[:],
                                              npad.ap[:, 0:1], 2.0, ALU.mult, ALU.mult),
               reads=(e0b[p], npad), writes=(corb[p],))
            op("v", lambda v: v.memset(bss[p].ap[:], 0.0), writes=(bss[p],))

        def mk_pass2(t, p, qb):
            nkey = (t + 1) * 128
            nk5 = (nkey + 511) // 512

            def unitA(hp, kb5, w, g):
                j = guc1[0] % 2
                guc1[0] += 1
                jslot[(hp, kb5)] = j
                sppair = (scpool[2 * j], scpool[2 * j + 1])
                b1b_pair_scmm(sppair, qb, hp, kb5 * 512, w, True, kb5 == nk5 - 1)
                pb = pb2[g % 3]
                for hip in range(2):
                    h = hp * 2 + hip
                    spp = sppair[hip]
                    op("s", lambda s, sp=spp, w=w, h=h, kb5=kb5, pb=pb,
                       hip=hip, hp=hp: s.activation(
                        pb.ap[:, hip * 512: hip * 512 + w],
                        sp.ap[:, 0:w], AF.Exp,
                        bias=nmf[p].ap[:, h:h + 1], scale=1.0,
                        accum_out=bss[p].ap[:, kb5 * H + hp * 2 + hip:
                                            kb5 * H + hp * 2 + hip + 1]),
                        reads=(spp, nmf[p]), writes=(pb, bss[p]))

            def unitT(hp, kb5, w, g):
                pb = pb2[g % 3]
                pt = ptb2[g % 3]
                tp = tp4_ps[0]
                ng = w // 128
                for hip in range(2):
                    for j in range(ng):
                        op("t", lambda t_, j=j, hip=hip, pb=pb, tp=tp: t_.transpose(
                            tp.ap[:, hip * 512 + j * 128: hip * 512 + (j + 1) * 128],
                            pb.ap[:, hip * 512 + j * 128: hip * 512 + (j + 1) * 128],
                            ident.ap[:]),
                            reads=(pb, ident), writes=(tp,))
                if g % 2 == 0:
                    op("s", lambda s, tp=tp, pt=pt: s.copy(pt.ap[:], tp.ap[:]),
                        reads=(tp,), writes=(pt,))
                else:
                    op("v", lambda v, tp=tp, pt=pt: v.tensor_copy(pt.ap[:], tp.ap[:]),
                        reads=(tp,), writes=(pt,))

            def unitV(hp, kb5, w, g):
                pt = ptb2[g % 3]
                ng = w // 128
                for hip in range(2):
                    h = hp * 2 + hip
                    for gr in range(ng):
                        kt = kb5 * 4 + gr
                        op("t", lambda t_, gr=gr, kt=kt, h=h, hip=hip, kb5=kb5,
                           ng=ng, pt=pt: t_.matmul(
                            ava_ps.ap[:, h * 64:(h + 1) * 64],
                            pt.ap[:, hip * 512 + gr * 128: hip * 512 + (gr + 1) * 128],
                            KV2.ap[:, kt * D + h * 64: kt * D + (h + 1) * 64],
                            start=(kb5 == 0 and gr == 0 and hip == 0),
                            stop=(kb5 == nk5 - 1 and gr == ng - 1 and hip == 1)),
                            reads=(pt, KV2), writes=(ava_ps,))

            hw_list = []
            for hp in range(H // 2):
                for kb5 in range(nk5):
                    hw_list.append((hp, kb5, min(512, nkey - kb5 * 512)))
            n = len(hw_list)
            units = []
            for g in range(n + 3):
                if g < n:
                    hp, kb5, w = hw_list[g]
                    units.append(lambda hp=hp, kb5=kb5, w=w, g=g: unitA(hp, kb5, w, g))
                if 0 <= g - 2 < n:
                    hp, kb5, w = hw_list[g - 2]
                    units.append(lambda hp=hp, kb5=kb5, w=w, g=g - 2: unitT(hp, kb5, w, g))
                if 0 <= g - 3 < n:
                    hp, kb5, w = hw_list[g - 3]
                    units.append(lambda hp=hp, kb5=kb5, w=w, g=g - 3: unitV(hp, kb5, w, g))
            return units

        def emit_pass2_tail(t, p, xb):
            ti = t - OWN0
            vs = bss[p].ap.rearrange("p (k a) -> p a k", a=H)
            op("v", lambda v, vs=vs: v.reduce_sum(lsum[p].ap[:], vs[:, :, 0:NB5], axis=AX.X),
               reads=(bss[p],), writes=(lsum[p],))
            op("v", lambda v: v.tensor_tensor(lsum[p].ap[:], lsum[p].ap[:],
                                              corb[p].ap[:], ALU.subtract),
               reads=(corb[p],), writes=(lsum[p],))
            op("v", lambda v: v.tensor_scalar(linv[p].ap[:], lsum[p].ap[:], 1e-30,
                                              None, ALU.add),
               reads=(lsum[p],), writes=(linv[p],))
            op("v", lambda v: v.reciprocal(linv[p].ap[:], linv[p].ap[:]),
               reads=(linv[p],), writes=(linv[p],))
            for h in range(H):
                op("v", lambda v, h=h: v.tensor_scalar(
                    at2[p].ap[:, h * 64:(h + 1) * 64],
                    ava_ps.ap[:, h * 64:(h + 1) * 64],
                    linv[p].ap[:, h:h + 1], None, ALU.mult),
                    reads=(ava_ps, linv[p]), writes=(at2[p],))
            tp = tp4_ps[0]
            for cb4 in range(0, NC, 4):
                for j in range(4):
                    c = cb4 + j
                    op("t", lambda t_, c=c, j=j, tp=tp: t_.transpose(
                        tp.ap[:, j * 128:(j + 1) * 128],
                        at2[p].ap[:, c * 128:(c + 1) * 128], ident.ap[:]),
                        reads=(at2[p], ident), writes=(tp,))
                op("s", lambda s, cb4=cb4, tp=tp: s.copy(
                    atf.ap[:, cb4 * 128:(cb4 + 4) * 128], tp.ap[:, 0:512]),
                    reads=(tp,), writes=(atf,))
            for c in range(NC):
                for half in range(2):
                    op("t", lambda t_, c=c, half=half: t_.matmul(
                        ava_ps.ap[:, half * 512:(half + 1) * 512],
                        atf.ap[:, c * 128:(c + 1) * 128],
                        wo2.ap[:, c * D + half * 512: c * D + (half + 1) * 512],
                        start=(c == 0), stop=(c == NC - 1)),
                        reads=(atf, wo2), writes=(ava_ps,))
            for half in range(2):
                op("v", lambda v, half=half: v.tensor_tensor(
                    hb[p].ap[:, half * 512:(half + 1) * 512],
                    zb2[p].ap[:, half * 512:(half + 1) * 512],
                    ava_ps.ap[:, half * 512:(half + 1) * 512], ALU.add),
                    reads=(zb2[p], ava_ps), writes=(hb[p],))
            op("v", lambda v, xb=xb: v.tensor_tensor(rb_[p].ap[:], xb.ap[:],
                                                     hb[p].ap[:], ALU.add),
               reads=(xb, hb[p]), writes=(rb_[p],))
            op("q", lambda q, ti=ti: q.dma_start(
                out=h_d[ti * 128:(ti + 1) * 128, :], in_=hb[p].ap[:]),
                reads=(hb[p],), isdma=True)
            op("q", lambda q, ti=ti: q.dma_start(
                out=r_d[ti * 128:(ti + 1) * 128, :], in_=rb_[p].ap[:]),
                reads=(rb_[p],), isdma=True)

        prev = None
        for t in range(OWN0, NT):
            ti = t - OWN0
            p = ti % 2
            xb = x1c[p]
            op("q", lambda q, t=t, xb=xb: q.dma_start(
                out=xb.ap[:], in_=x1_d[t * 128:(t + 1) * 128, :]),
                writes=(xb,), isdma=True)
            qb = q2t[p]
            op("q", lambda q, ti=ti, qb=qb: q.dma_start(
                out=qb.ap[:], in_=q2_d[ti, :, :, :].rearrange("a p j -> p a j")),
                writes=(qb,), isdma=True)
            ln_tile(xb, scrD[p], LN_EPS, out_f32=zb2[p])
            op("v", lambda v, p=p: v.memset(bms[p].ap[:], -1e30), writes=(bms[p],))
            u1 = mk_pass1(t, p, qb)
            u2 = mk_pass2(prev[0], prev[1], prev[2]) if prev else []
            n1, n2 = len(u1), len(u2)
            i1 = i2 = 0
            while i1 < n1 or i2 < n2:
                if i2 < n2 and (i1 >= n1 or i2 * max(n1, 1) <= i1 * max(n2, 1)):
                    u2[i2](); i2 += 1
                else:
                    u1[i1](); i1 += 1
            emit_pass1_tail(t, p)
            if prev:
                emit_pass2_tail(prev[0], prev[1], prev[3])
            prev = (t, p, qb, xb)
            pump(6)
        for u in mk_pass2(prev[0], prev[1], prev[2]):
            u()
        emit_pass2_tail(prev[0], prev[1], prev[3])

        pump_all()
        ir.fence()
        b1b.close()
        kv_scope.close()
        st_scope.close()
        if upto in ("B1b", "B1bdbg"):
            _finish(k, block, ctx)
            return nc, ctx

        b2s = ExitStack()
        b2 = ExitStack()
        xf = k.sb(NC * 2048, BF16, b2)
        comb = k.sb(16 * NE, F32, b2)
        g2bc = k.sb(D, F32, b2s)
        hb2 = [k.sb(D, F32, b2s) for _ in range(2)]
        xfb = k.sb(D, BF16, b2s)
        p8 = k.sb(NE, F32, b2s)
        r8 = {n: k.sb(8, F32, b2s) for n in
              ("m8", "nm8", "m1", "m2", "den", "dinv")}
        msk1 = k.sb(NE, F32, b2s)
        p8b = k.sb(NE, F32, b2s)
        sel8 = k.sb(NE, F32, b2s)
        scrE = mk_scr(b2s)
        rt_ps = k.ps(512, F32, b2s)
        tp5_ps = [k.ps(1024, BF16, b2s) for _ in range(2)]

        op("g", lambda g: g.dma_start(out=g2bc.ap[:], in_=g2bc_d[:]),
           writes=(g2bc,), isdma=True)
        wrb2 = k.sb(NC * NE, BF16, b2s)
        wrf = k.sb(NC * NE, F32, b2s)
        wrlo = k.sb(NC * NE, BF16, b2s)
        xff = k.sb(D, F32, b2s)
        xlo = k.sb(D, BF16, b2s)
        xlofm = k.sb(D, BF16, b2s)
        for c in range(NC):
            op("g", lambda g, c=c: g.dma_start(
                out=wrb2.ap[:, c * NE:(c + 1) * NE], in_=wr_d[c * 128:(c + 1) * 128, :]),
               writes=(wrb2,), isdma=True)
            op("q", lambda q, c=c: q.dma_start(
                out=wrf.ap[:, c * NE:(c + 1) * NE], in_=wr_d[c * 128:(c + 1) * 128, :]),
               writes=(wrf,), isdma=True)
        op("v", lambda v: v.tensor_tensor(wrlo.ap[:], wrf.ap[:], wrb2.ap[:],
                                          ALU.subtract),
           reads=(wrf, wrb2), writes=(wrlo,))

        for ti in range(16):
            hbb = hb2[ti % 2]
            op("q", lambda q, ti=ti, hbb=hbb: q.dma_start(
                out=hbb.ap[:], in_=h_d[ti * 128:(ti + 1) * 128, :]),
                writes=(hbb,), isdma=True)
            rms_tile(hbb, scrE, g2bc, xlofm, xff)
            op("v", lambda v: v.tensor_copy(xfb.ap[:], xff.ap[:]),
               reads=(xff,), writes=(xfb,))
            op("v", lambda v: v.tensor_tensor(xlo.ap[:], xff.ap[:], xfb.ap[:],
                                              ALU.subtract),
               reads=(xff, xfb), writes=(xlo,))
            for c in range(NC):
                p = tp5_ps[c % 2]
                op("t", lambda t_, c=c, p=p: t_.transpose(
                    p.ap[:, 0:128], xfb.ap[:, c * 128:(c + 1) * 128], ident.ap[:]),
                    reads=(xfb, ident), writes=(p,))
                op("s", lambda s, c=c, p=p, ti=ti: s.copy(
                    xf.ap[:, c * 2048 + ti * 128: c * 2048 + (ti + 1) * 128],
                    p.ap[:, 0:128]),
                    reads=(p,), writes=(xf,))
            for c in range(NC):
                p = tp5_ps[c % 2]
                op("t", lambda t_, c=c, p=p: t_.transpose(
                    p.ap[:, 0:128], xlo.ap[:, c * 128:(c + 1) * 128], ident.ap[:]),
                    reads=(xlo, ident), writes=(p,))
                op("s", lambda s, c=c, p=p: s.copy(
                    xlofm.ap[:, c * 128:(c + 1) * 128], p.ap[:, 0:128]),
                    reads=(p,), writes=(xlofm,))
            nmm = 0
            for c in range(NC):
                for lh, rh in (("hi", "hi"), ("hi", "lo"), ("lo", "hi")):
                    lop = (xf if lh == "hi" else xlofm)
                    lsl = (slice(c * 2048 + ti * 128, c * 2048 + (ti + 1) * 128)
                           if lh == "hi" else slice(c * 128, (c + 1) * 128))
                    rop = wrb2 if rh == "hi" else wrlo
                    op("t", lambda t_, c=c, lop=lop, lsl=lsl, rop=rop, nmm=nmm: t_.matmul(
                        rt_ps.ap[:, 0:NE], lop.ap[:, lsl],
                        rop.ap[:, c * NE:(c + 1) * NE],
                        start=(nmm == 0), stop=(nmm == 3 * NC - 1)),
                        reads=(lop, rop), writes=(rt_ps,))
                    nmm += 1
            op("v", lambda v: v.reduce_max(r8["m8"].ap[:, 0:1], rt_ps.ap[:, 0:NE], axis=AX.X),
               reads=(rt_ps,), writes=(r8["m8"],))
            op("v", lambda v: v.tensor_scalar(r8["nm8"].ap[:, 0:1], r8["m8"].ap[:, 0:1],
                                              -1.0, None, ALU.mult),
               reads=(r8["m8"],), writes=(r8["nm8"],))
            op("s", lambda s: s.activation(p8.ap[:], rt_ps.ap[:, 0:NE], AF.Exp,
                                           bias=r8["nm8"].ap[:, 0:1], scale=1.0),
               reads=(rt_ps, r8["nm8"]), writes=(p8,))
            op("v", lambda v: v.reduce_max(r8["m1"].ap[:, 0:1], p8.ap[:], axis=AX.X),
               reads=(p8,), writes=(r8["m1"],))
            op("v", lambda v: v.tensor_scalar(msk1.ap[:], p8.ap[:], r8["m1"].ap[:, 0:1],
                                              None, ALU.is_equal),
               reads=(p8, r8["m1"]), writes=(msk1,))
            op("v", lambda v: v.scalar_tensor_tensor(p8b.ap[:], msk1.ap[:], -1e30,
                                                     p8.ap[:], ALU.mult, ALU.add),
               reads=(msk1, p8), writes=(p8b,))
            op("v", lambda v: v.reduce_max(r8["m2"].ap[:, 0:1], p8b.ap[:], axis=AX.X),
               reads=(p8b,), writes=(r8["m2"],))
            op("v", lambda v: v.tensor_scalar(sel8.ap[:], p8.ap[:], r8["m2"].ap[:, 0:1],
                                              None, ALU.is_ge),
               reads=(p8, r8["m2"]), writes=(sel8,))
            op("v", lambda v: v.tensor_tensor(r8["den"].ap[:, 0:1], r8["m1"].ap[:, 0:1],
                                              r8["m2"].ap[:, 0:1], ALU.add),
               reads=(r8["m1"], r8["m2"]), writes=(r8["den"],))
            op("v", lambda v: v.reciprocal(r8["dinv"].ap[:, 0:1], r8["den"].ap[:, 0:1]),
               reads=(r8["den"],), writes=(r8["dinv"],))
            op("v", lambda v, ti=ti: v.scalar_tensor_tensor(
                comb.ap[:, ti * NE:(ti + 1) * NE], p8.ap[:], r8["dinv"].ap[:, 0:1],
                sel8.ap[:], ALU.mult, ALU.mult),
               reads=(p8, r8["dinv"], sel8), writes=(comb,))

        ir.fence()
        b2s.close()

        wdm = [k.sb(D, BF16, b2) for _ in range(NHSL)]
        wgs = [k.sb(NC * 256, BF16, b2) for _ in range(2)]
        wus = [k.sb(NC * 256, BF16, b2) for _ in range(2)]
        mall = k.sb(NHSL * 1024, BF16, b2)
        gsb = [k.sb(512, BF16, b2) for _ in range(2)]
        yac = k.sb(8 * D, F32, b2)
        rbt = [k.sb(D, F32, b2) for _ in range(2)]
        outb = [k.sb(D, F32, b2) for _ in range(2)]
        gu_ps = [k.ps(512, F32, b2) for _ in range(4)]
        dn_ps = [k.ps(512, F32, b2) for _ in range(4)]

        NHC = NHSL // 2

        for thalf in range(2):
            op("v", lambda v: v.memset(yac.ap[:], 0.0), writes=(yac,))
            for e in range(NE):
                for c in range(NHSL):
                    op("q", lambda q, e=e, c=c: q.dma_start(
                        out=wdm[c].ap[:], in_=wdb_d[e, c * 128:(c + 1) * 128, :]),
                        writes=(wdm[c],), isdma=True)
                for hc in range(NHC):
                    wgb = wgs[hc % 2]
                    wub = wus[hc % 2]
                    op("q", lambda q, e=e, hc=hc, wgb=wgb: q.dma_start(
                        out=wgb.ap[:],
                        in_=wgb_d[e, :, hc * 256:(hc + 1) * 256]
                        .rearrange("(a p) j -> p a j", p=128)),
                        writes=(wgb,), isdma=True)
                    op("q", lambda q, e=e, hc=hc, wub=wub: q.dma_start(
                        out=wub.ap[:],
                        in_=wub_d[e, :, hc * 256:(hc + 1) * 256]
                        .rearrange("(a p) j -> p a j", p=128)),
                        writes=(wub,), isdma=True)
                    for hs in range(2):
                        for c in range(NC):
                            for t5 in range(2):
                                op("t", lambda t_, c=c, hs=hs, t5=t5, wgb=wgb,
                                   thalf=thalf: t_.matmul(
                                    gu_ps[2 * t5 + 0].ap[:],
                                    wgb.ap[:, c * 256 + hs * 128: c * 256 + (hs + 1) * 128],
                                    xf.ap[:, c * 2048 + thalf * 1024 + t5 * 512:
                                          c * 2048 + thalf * 1024 + (t5 + 1) * 512],
                                    start=(c == 0), stop=(c == NC - 1)),
                                    reads=(wgb, xf), writes=(gu_ps[2 * t5 + 0],))
                        for c in range(NC):
                            for t5 in range(2):
                                op("t", lambda t_, c=c, hs=hs, t5=t5, wub=wub,
                                   thalf=thalf: t_.matmul(
                                    gu_ps[2 * t5 + 1].ap[:],
                                    wub.ap[:, c * 256 + hs * 128: c * 256 + (hs + 1) * 128],
                                    xf.ap[:, c * 2048 + thalf * 1024 + t5 * 512:
                                          c * 2048 + thalf * 1024 + (t5 + 1) * 512],
                                    start=(c == 0), stop=(c == NC - 1)),
                                    reads=(wub, xf), writes=(gu_ps[2 * t5 + 1],))
                        hsl = hc * 2 + hs
                        for t5 in range(2):
                            gs = gsb[t5]
                            op("s", lambda s, t5=t5, gs=gs: s.activation(
                                gs.ap[:], gu_ps[2 * t5 + 0].ap[:], AF.Silu,
                                bias=0.0, scale=1.0),
                                reads=(gu_ps[2 * t5 + 0],), writes=(gs,))
                            op("v", lambda v, gs=gs, hsl=hsl, t5=t5: v.tensor_tensor(
                                mall.ap[:, hsl * 1024 + t5 * 512: hsl * 1024 + (t5 + 1) * 512],
                                gs.ap[:], gu_ps[2 * t5 + 1].ap[:], ALU.mult),
                                reads=(gs, gu_ps[2 * t5 + 1]), writes=(mall,))
                for dh in range(2):
                    for hsl in range(NHSL):
                        for t128 in range(8):
                            dp = (dn_ps + gu_ps)[t128]
                            op("t", lambda t_, hsl=hsl, t128=t128, dh=dh, dp=dp: t_.matmul(
                                dp.ap[:],
                                mall.ap[:, hsl * 1024 + t128 * 128: hsl * 1024 + (t128 + 1) * 128],
                                wdm[hsl].ap[:, dh * 512:(dh + 1) * 512],
                                start=(hsl == 0), stop=(hsl == NHSL - 1)),
                                reads=(mall, wdm[hsl]), writes=(dp,))
                    for t128 in range(8):
                        dp = (dn_ps + gu_ps)[t128]
                        ci = (thalf * 8 + t128) * NE + e
                        op("v", lambda v, dp=dp, t128=t128, dh=dh, ci=ci: v.scalar_tensor_tensor(
                            yac.ap[:, t128 * D + dh * 512: t128 * D + (dh + 1) * 512],
                            dp.ap[:], comb.ap[:, ci:ci + 1],
                            yac.ap[:, t128 * D + dh * 512: t128 * D + (dh + 1) * 512],
                            ALU.mult, ALU.add),
                            reads=(dp, comb, yac), writes=(yac,))
            for t128 in range(8):
                ti = thalf * 8 + t128
                rb2 = rbt[t128 % 2]
                ob = outb[t128 % 2]
                op("q", lambda q, ti=ti, rb2=rb2: q.dma_start(
                    out=rb2.ap[:], in_=r_d[ti * 128:(ti + 1) * 128, :]),
                    writes=(rb2,), isdma=True)
                op("v", lambda v, t128=t128, rb2=rb2, ob=ob: v.tensor_tensor(
                    ob.ap[:], rb2.ap[:], yac.ap[:, t128 * D:(t128 + 1) * D], ALU.add),
                    reads=(rb2, yac), writes=(ob,))
                op("q", lambda q, ti=ti, ob=ob: q.dma_start(
                    out=out_d[ti * 128:(ti + 1) * 128, :], in_=ob.ap[:]),
                    reads=(ob,), isdma=True)

        ir.fence()
        _finish(k, block, ctx)
        return nc, ctx


def _finish(k, block, ctx):
    ir = k.ir
    ir.fence()
    ir.op("g", lambda g: g.engine_nop())
    ir.lower(k.nc, k.ctx, block)


_ROPE = {}


def _rope_tables(role):
    if role in _ROPE:
        return _ROPE[role]
    pos = np.arange(S)
    if role == 0:
        pos = np.clip(pos - 2048, 0, None)
    inv = 1.0 / (10000.0 ** (np.arange(0, HD, 2, dtype=np.float32) / HD))
    freqs = np.einsum("i,j->ij", pos.astype(np.float32), inv)
    emb = np.concatenate([freqs, freqs], axis=-1)
    cos = np.cos(emb).astype(np.float32)
    sin = np.sin(emb).astype(np.float32)
    sin_signed = np.concatenate([-sin[:, :32], sin[:, 32:]], axis=-1).astype(np.float32)
    cosr = np.ascontiguousarray(np.tile(cos, (1, H)))
    sinr = np.ascontiguousarray(np.tile(sin_signed, (1, H)))
    _ROPE[role] = (cosr, sinr)
    return _ROPE[role]


def make_in_maps(inputs):
    import ml_dtypes
    x = np.asarray(inputs["L0_attn_embeddings"], np.float32)
    wg = np.ascontiguousarray(np.pad(np.asarray(inputs["L0_ffn_wg"], np.float32),
                                     ((0, 0), (0, 0), (0, HID - 2752))))
    wu = np.ascontiguousarray(np.pad(np.asarray(inputs["L0_ffn_wu"], np.float32),
                                     ((0, 0), (0, 0), (0, HID - 2752))))
    wd = np.ascontiguousarray(np.pad(np.asarray(inputs["L0_ffn_wd"], np.float32),
                                     ((0, 0), (0, HID - 2752), (0, 0))))
    ident = np.eye(128, dtype=np.float32).astype(ml_dtypes.bfloat16)
    tri = np.triu(np.full((128, 128), NEGBIG, np.float32), k=1).astype(ml_dtypes.bfloat16)
    g1bc = np.ascontiguousarray(np.broadcast_to(
        np.asarray(inputs["L0_ffn_g1"], np.float32), (128, D)))
    g2bc = np.ascontiguousarray(np.broadcast_to(
        np.asarray(inputs["L0_ffn_g2"], np.float32), (128, D)))
    shared = dict(
        wq1=np.asarray(inputs["L0_attn_W_Q"], np.float32),
        wk1=np.asarray(inputs["L0_attn_W_K"], np.float32),
        wv1=np.asarray(inputs["L0_attn_W_V"], np.float32),
        wq2=np.asarray(inputs["L0_ffn_wq"], np.float32),
        wk2=np.asarray(inputs["L0_ffn_wk"], np.float32),
        wv2=np.asarray(inputs["L0_ffn_wv"], np.float32),
        wo2=np.asarray(inputs["L0_ffn_wo"], np.float32),
        wr=np.asarray(inputs["L0_ffn_w_router"], np.float32),
        wg=wg, wu=wu, wd=wd, ident=ident, tri=tri, g1bc=g1bc, g2bc=g2bc,
    )
    in_maps = []
    zeros = np.zeros((2048, D), np.float32)
    for c in range(8):
        b, role = c // 2, c % 2
        xv = x[b] if role == 1 else np.ascontiguousarray(
            np.concatenate([zeros, x[b][:2048]], axis=0))
        cosr, sinr = _rope_tables(role)
        m = dict(shared)
        m["consts"] = np.ascontiguousarray(
            np.broadcast_to(np.asarray(CONSTS, np.float32), (128, len(CONSTS))))
        m["xv"] = xv
        m["cosr"] = cosr
        m["sinr"] = sinr
        m["npad"] = np.full((128, 1), 1024.0 if role == 0 else 0.0, np.float32)
        in_maps.append(m)
    return in_maps


def kernel(**inputs):
    in_maps = make_in_maps(inputs)
    nc, ctx = build_program("full")
    res = run_bass_kernel_spmd(nc, in_maps, core_ids=list(range(8)))
    out = np.empty((B, S, D), np.float32)
    for c in range(8):
        b, role = c // 2, c % 2
        rows = slice(0, 2048) if role == 0 else slice(2048, 4096)
        out[b, rows] = res.results[c]["out"]
    return out
